# version 01
# baseline (speedup 1.0000x reference)
import numpy as np
import ml_dtypes
import concourse.bass as bass
import concourse.mybir as mybir
from concourse.bass_utils import run_bass_kernel_spmd

F32 = mybir.dt.float32
BF16 = mybir.dt.bfloat16
U8 = mybir.dt.uint8
AF = mybir.ActivationFunctionType
ALU = mybir.AluOpType
AX = mybir.AxisListType
BF_NP = ml_dtypes.bfloat16


class CFG:
    def __init__(self, nseq=2, S=2048, ncc=8):
        self.nseq, self.S, self.D = nseq, S, 1024
        self.T = nseq * S
        self.NT = self.T // 128
        self.NTS = S // 128
        self.W = 256
        self.NB = S // self.W
        self.NLAT = 64
        self.TPL = S // self.NLAT
        self.F = 4096
        self.NH = 4
        self.GH = 1024
        self.GTS = min(8, self.NTS)
        self.LPG = self.GTS * 128 // self.TPL
        self.cc_group = list(range(ncc))


_AUTOSEM = {"vector": "dve", "scalar": "act", "gpsimd": "pool", "tensor": "pe"}
_NOWRAP = {"wait_ge", "sem_inc"}
_NOINC = {"dma_start", "indirect_dma_start", "collective_compute", "dma_start_transpose"}


class InstProxy:
    __slots__ = ("q",)
    def __init__(self):
        self.q = []
    def then_inc(self, semh, n=1):
        self.q.append((semh, n))
        return self


class EP:
    def __init__(self, blk, name, sem, cnt, buf):
        object.__setattr__(self, "_blk", blk)
        object.__setattr__(self, "_name", name)
        object.__setattr__(self, "_sem", sem)
        object.__setattr__(self, "_cnt", cnt)
        object.__setattr__(self, "_buf", buf)
    def __getattr__(self, attr):
        name, sem, cnt, buf = self._name, self._sem, self._cnt, self._buf
        own = _AUTOSEM.get(name)
        def call(*a, **k):
            gates = []
            selfwait = 0
            auto = False
            if attr in _NOINC:
                for s_ in ("dve", "act", "pool", "pe", "sq", "sg", "out"):
                    if cnt[s_] > 0:
                        gates.append((s_, cnt[s_]))
            elif own is not None and attr not in _NOWRAP:
                if name != "tensor":
                    selfwait = cnt[own]
                auto = True
                cnt[own] += 1
            px = InstProxy()
            def emit(e, attr=attr, a=a, k=k, gates=gates, selfwait=selfwait,
                     auto=auto, px=px):
                for s_, v_ in gates:
                    e.wait_ge(sem[s_], v_)
                ins = getattr(e, attr)(*a, **k)
                if selfwait > 0:
                    ins._wait_ge(sem[own], selfwait)
                if auto:
                    ins.then_inc(sem[own], 1)
                for semh, n_ in px.q:
                    ins.then_inc(semh, n_)
                return ins
            buf[name].append(emit)
            return px
        return call


def host_consts(cfg):
    c = {}
    c["ident"] = np.eye(128, dtype=BF_NP)
    i = np.arange(cfg.W)[:, None]
    j = np.arange(2 * cfg.W)[None, :]
    base = (j > i) & (j <= i + cfg.W)
    c["winm"] = base.astype(BF_NP)
    c["winm_first"] = (base & (j >= cfg.W)).astype(BF_NP)
    pp = np.zeros((cfg.GTS, 128, cfg.LPG), np.float32)
    for g in range(cfg.GTS):
        for ii in range(128):
            pp[g, ii, g * (128 // cfg.TPL) + ii // cfg.TPL] = 1.0 / cfg.TPL
    c["poolpat"] = pp.astype(BF_NP)
    return c


def build(cfg):
    nc = bass.Bass("TRN2", target_bir_lowering=False)
    T, S, D, F, NH = cfg.T, cfg.S, cfg.D, cfg.F, cfg.NH
    lpg = cfg.LPG

    ap = {}
    def di(name, shape, dt=F32):
        ap[name] = nc.dram_tensor(name, shape, dt, kind="ExternalInput").ap()
    di("x", [T, D])
    for n, sh in [("wq", [D, D]), ("wk", [D, D]), ("wv", [D, D]),
                  ("wgq", [D, 512]), ("wgk", [D, 512]), ("wgv", [D, 512]),
                  ("r_out_w", [512, D]), ("glw", [16, 64]), ("grw", [16, 64]),
                  ("w_out", [D, D]), ("bWq", [D, D]), ("bWk", [D, D]),
                  ("bWv", [D, D]), ("cwq", [D, 4]), ("cwk", [D, 4]),
                  ("cwv", [D, 4]), ("bWb", [D, NH]), ("bWid", [D, D]),
                  ("alpha_id", [NH]), ("firs", [NH, 256, 3]), ("firl", [NH, 256, 63]),
                  ("gW1", [D + 16, cfg.GH]), ("gb1", [cfg.GH]), ("gW2", [cfg.GH, NH]),
                  ("gb2", [NH]), ("log_temp", [NH, 4]), ("floor_param", [NH, 4]),
                  ("crl", [NH]), ("onw", [256]), ("bWo", [D, D]),
                  ("w1", [F, D]), ("b1", [F]), ("w2", [D, F]), ("b2", [D])]:
        di(n, sh)
    di("ident", [128, 128], BF16)
    di("winm", [256, 512], BF16)
    di("winm_first", [256, 512], BF16)
    di("poolpat", [cfg.GTS, 128, lpg], BF16)

    out = nc.dram_tensor("out", [T, D], F32, kind="ExternalOutput").ap()

    dr = {}
    dr["h"] = nc.dram_tensor("h_sp", [F, T], F32).ap()
    dr["w1iT"] = nc.dram_tensor("w1iT", [D, F], BF16).ap()
    dr["w2qT"] = nc.dram_tensor("w2qT", [F, D], BF16).ap()
    dr["aQ"] = nc.dram_tensor("aQ", [D, T], BF16).ap()
    dr["aK"] = nc.dram_tensor("aK", [D, T], BF16).ap()
    dr["aV"] = nc.dram_tensor("aV", [T, 16 * 65], BF16).ap()
    dr["aGQ"] = nc.dram_tensor("aGQ", [512, T], BF16).ap()
    dr["bq"] = nc.dram_tensor("bq", [D, T], BF16).ap()
    dr["bk"] = nc.dram_tensor("bk", [D, T], BF16).ap()
    dr["bv"] = nc.dram_tensor("bv", [D, T], BF16).ap()
    dr["bid"] = nc.dram_tensor("bid", [T, D], BF16).ap()
    dr["bgt"] = nc.dram_tensor("bgt", [cfg.GH, T], BF16).ap()
    dr["bls"] = nc.dram_tensor("bls", [D, T], BF16).ap()
    dr["bll"] = nc.dram_tensor("bll", [D, T], BF16).ap()
    dr["bounce"] = nc.dram_tensor("bounce", [128, 128], F32).ap()
    dr["dbg"] = nc.dram_tensor("dbg", [128, 4096], BF16).ap()
    dr["dbg2"] = nc.dram_tensor("dbg2", [128, 16384], BF16).ap()
    cc_in_t = nc.dram_tensor("cc_in", [128, 128], F32)
    cc_out_t = nc.dram_tensor("cc_out", [128, 128], F32)

    ectx = []
    def sb(name, p, f, dt=BF16):
        t = nc.sbuf_tensor(name, [p, f], dt)
        h = t.__enter__(); ectx.append(t); return h
    def psal(name, p, f, dt=F32):
        t = nc.psum_tensor(name, [p, f], dt)
        h = t.__enter__(); ectx.append(t); return h
    def mksem(name):
        t = nc.semaphore(name)
        h = t.__enter__(); ectx.append(t); return h

    stream = sb("stream", 128, cfg.NT * D)
    mu = sb("mu", 128, cfg.NT, F32)
    rstd = sb("rstd", 128, cfg.NT, F32)
    maxc = sb("maxc", 128, 4, F32)
    identt = sb("identt", 128, 128)
    identf = sb("identf", 128, 128, F32)
    onesf = sb("onesf", 1, 128, F32)
    winmt = sb("winmt", 128, 2 * 512)
    winft = sb("winft", 128, 2 * 512)
    poolt = sb("poolt", 128, cfg.GTS * lpg)
    g_exp = sb("g_exp", 128, 8 * 32)
    cw = sb("cw", 128, 8 * 12, F32)
    firsc = sb("firsc", 128, 8 * 3, F32)
    firlc = sb("firlc", 128, 8 * 63, F32)
    gb1c = sb("gb1c", 128, 8, F32)
    gW1b = sb("gW1b", 16, cfg.GH)
    gW2c = sb("gW2c", 128, 8 * NH)
    bc16 = sb("bc16", 128, 16, F32)
    bcfl = sb("bcfl", 128, 16, F32)
    bcmisc = sb("bcmisc", 128, 16, F32)
    onwb = sb("onwb", 128, NH * 256, F32)
    s_w1c = sb("s_w1c", 128, F // 128, F32)
    b1c = sb("b1c", 128, F // 128, F32)
    sw2r = sb("sw2r", 1, D, F32)
    b2r = sb("b2r", 1, D, F32)
    srow = sb("srow", 1, 128, F32)
    bq2b = sb("bq2b", 128, D, F32)
    sInv = sb("sInv", 128, 4, F32)
    ccst = sb("ccst", 128, 128, F32)
    ffnc = sb("ffnc", 128, 96, F32)
    epsc = sb("epsc", 128, 4, F32)
    betac = sb("betac", 128, cfg.NT * NH, F32)
    GKs = sb("GKs", 128, 4 * cfg.NLAT)
    GVs = sb("GVs", cfg.NLAT, 8 * 65)
    pooled = sb("pooled", cfg.NLAT, D)
    pooledT = sb("pooledT", 128, 8 * cfg.NLAT)

    AB1 = sb("AB1", 128, 32768)
    AB2 = sb("AB2", 128, 6144)
    AFX = sb("AFX", 128, 5120, F32)
    AU8 = sb("AU8", 128, 2048, U8)
    AI32 = sb("AI32", 128, 1024, mybir.dt.int32)

    class Carve:
        def __init__(self, base):
            self.base = base; self.off = 0
        def take(self, ncols):
            v = self.base[:, self.off:self.off + ncols]
            self.off += ncols
            assert self.off <= self.base.shape[1], f"arena overflow {self.off}"
            return v

    PS = [psal(f"ps{i}", 128, 512, F32) for i in range(6)]
    PB = [psal(f"pb{i}", 128, 1024, BF16) for i in range(2)]

    sem = {n: mksem(n) for n in ["sq", "sg", "pe", "dve", "act", "pool", "cc", "out"]}
    cnt = {n: 0 for n in sem}

    blk_t = nc.Block()
    blk = blk_t.__enter__(); ectx.append(blk_t)
    EBUF = {n: [] for n in ["vector", "scalar", "tensor", "gpsimd", "sync"]}
    V = EP(blk, "vector", sem, cnt, EBUF); SC = EP(blk, "scalar", sem, cnt, EBUF)
    TE = EP(blk, "tensor", sem, cnt, EBUF); G = EP(blk, "gpsimd", sem, cnt, EBUF)
    SY = EP(blk, "sync", sem, cnt, EBUF)

    _AUTO = {"dve", "act", "pool", "pe"}
    def inc(instr, s, n=1):
        if s in _AUTO:
            return instr
        instr.then_inc(sem[s], n)
        cnt[s] += n
        return instr

    def snap():
        return dict(cnt)

    def wait(eng, *names, at=None):
        for s in names:
            v = cnt[s] if at is None else at[s]
            if v > 0:
                eng.wait_ge(sem[s], v)

    wstg = AFX[:, 0:1024]
    fsc = AFX[:, 1024:2048]
    fsc2 = AFX[:, 2048:3072]
    b_sc = AFX[:, 3072:3584]
    b_sc2 = AFX[:, 3584:4096]

    inc(SY.dma_start(out=identt[:], in_=ap["ident"]), "sq", 16)
    inc(SY.dma_start(out=winmt[:, 0:512], in_=ap["winm"][0:128, :]), "sq", 16)
    inc(SY.dma_start(out=winmt[:, 512:1024], in_=ap["winm"][128:256, :]), "sq", 16)
    inc(SY.dma_start(out=winft[:, 0:512], in_=ap["winm_first"][0:128, :]), "sq", 16)
    inc(SY.dma_start(out=winft[:, 512:1024], in_=ap["winm_first"][128:256, :]), "sq", 16)
    inc(SY.dma_start(out=poolt[:].rearrange("p (g l) -> p g l", g=cfg.GTS),
                     in_=ap["poolpat"].rearrange("g p l -> p g l")), "sq", 16)
    V.memset(onesf[:], 1.0)
    V.memset(maxc[:], 0.0)
    V.memset(epsc[:, 0:1], 1e-5)
    V.memset(epsc[:, 1:2], 1e-6)
    G.memset(identf[:], 1.0)
    inc(G.affine_select(identf[:], identf[:], [[-1, 128]], ALU.is_equal, 0.0,
                        base=0, channel_multiplier=1), "pool")
    for i in range(cfg.NT):
        m = snap(); wait(SY, "act", at=m)
        inc(SY.dma_start(out=wstg[:], in_=ap["x"][128 * i:128 * (i + 1), :]), "sq", 16)
        m = snap(); wait(SC, "sq", at=m)
        inc(SC.copy(stream[:, D * i:D * (i + 1)], wstg[:]), "act")

    for ti, nm in enumerate(["cwq", "cwk", "cwv"]):
        for c in range(8):
            inc(SY.dma_start(out=cw[:, (ti * 8 + c) * 4:(ti * 8 + c) * 4 + 4],
                             in_=ap[nm][128 * c:128 * (c + 1), :]), "sq", 16)
    for c in range(8):
        inc(SY.dma_start(out=firsc[:, c * 3:(c + 1) * 3],
                         in_=ap["firs"].rearrange("h d k -> (h d) k")[128 * c:128 * (c + 1), :]), "sq", 16)
        inc(SY.dma_start(out=firlc[:, c * 63:(c + 1) * 63],
                         in_=ap["firl"].rearrange("h d k -> (h d) k")[128 * c:128 * (c + 1), :]), "sq", 16)
    m = snap(); wait(SY, "act", at=m)
    inc(SY.dma_start(out=wstg[0:16, :], in_=ap["gW1"][D:D + 16, :]), "sq", 16)
    m = snap(); wait(SC, "sq", at=m)
    inc(SC.copy(gW1b[:, :], wstg[0:16, :]), "act")
    for c in range(8):
        m = snap(); wait(SY, "act", at=m)
        inc(SY.dma_start(out=wstg[:, 0:NH], in_=ap["gW2"][128 * c:128 * (c + 1), :]), "sq", 16)
        m = snap(); wait(SC, "sq", at=m)
        inc(SC.copy(gW2c[:, NH * c:NH * (c + 1)], wstg[:, 0:NH]), "act")
    m = snap(); wait(SY, "act", at=m)
    inc(SY.dma_start(out=wstg[0:8, 0:128], in_=ap["gb1"].rearrange("(a b) -> a b", b=128)), "sq", 16)
    inc(SY.dma_start(out=wstg[32:64, 0:128], in_=ap["b1"].rearrange("(a b) -> a b", b=128)), "sq", 16)
    inc(SY.dma_start(out=b2r[0:1, :], in_=ap["b2"][None, :]), "sq", 16)
    m = snap(); wait(TE, "sq", "pool", at=m)
    TE.transpose(PS[0][0:128, 0:8], wstg[0:8, 0:128], identf[0:8, 0:8])
    inc(TE.transpose(PS[1][0:128, 0:32], wstg[32:64, 0:128], identf[32:64, 32:64]), "pe")
    m = snap(); wait(V, "pe", at=m)
    V.tensor_copy(gb1c[:, :], PS[0][0:128, 0:8])
    inc(V.tensor_copy(b1c[:, :], PS[1][0:128, 0:32]), "dve")

    m = snap(); wait(V, "dve", at=m)
    inc(V.memset(g_exp[:], 0.0), "dve")
    for gi, nm in enumerate(["glw", "grw"]):
        m = snap(); wait(SY, "act", at=m)
        inc(SY.dma_start(out=wstg[0:16, 0:64], in_=ap[nm]), "sq", 16)
        m = snap(); wait(SC, "sq", at=m)
        inc(SC.copy(AB2[0:16, 0:64], wstg[0:16, 0:64]), "act")
        m = snap(); wait(TE, "act", at=m)
        inc(TE.transpose(PB[0][0:64, 0:16], AB2[0:16, 0:64], identt[0:16, 0:16]), "pe")
        m = snap(); wait(V, "pe", "dve", at=m)
        inc(V.tensor_copy(AB2[0:64, 64:80], PB[0][0:64, 0:16]), "dve")
        m = snap(); wait(V, "dve", at=m)
        for c in range(8):
            V.tensor_copy(g_exp[0:64, 32 * c + 16 * gi + 2 * c:32 * c + 16 * gi + 2 * c + 1],
                          AB2[0:64, 64 + 2 * c:64 + 2 * c + 1])
            inc(V.tensor_copy(g_exp[64:128, 32 * c + 16 * gi + 2 * c + 1:32 * c + 16 * gi + 2 * c + 2],
                              AB2[0:64, 64 + 2 * c + 1:64 + 2 * c + 2]), "dve")

    inc(SY.dma_start(out=srow[0:1, 0:16], in_=ap["log_temp"].rearrange("a b -> (a b)")[None, :]), "sq", 16)
    m = snap(); wait(SC, "sq", at=m)
    inc(SC.activation(srow[0:1, 16:32], srow[0:1, 0:16], AF.Exp), "act")
    m = snap(); wait(V, "act", at=m)
    inc(V.reciprocal(srow[0:1, 32:48], srow[0:1, 16:32]), "dve")
    m = snap(); wait(TE, "dve", at=m)
    inc(TE.matmul(PS[0][:, 0:16], onesf[0:1, :], srow[0:1, 32:48], start=True, stop=True), "pe")
    m = snap(); wait(V, "pe", at=m)
    inc(V.tensor_copy(bc16[:], PS[0][:, 0:16]), "dve")
    m = snap(); wait(SY, "dve", at=m)
    inc(SY.dma_start(out=srow[0:1, 0:16], in_=ap["floor_param"].rearrange("a b -> (a b)")[None, :]), "sq", 16)
    m = snap(); wait(SC, "sq", at=m)
    inc(SC.activation(srow[0:1, 16:32], srow[0:1, 0:16], AF.Sigmoid), "act")
    m = snap(); wait(V, "act", at=m)
    inc(V.tensor_scalar(srow[0:1, 32:48], srow[0:1, 16:32], 0.05, 0.0, ALU.mult, ALU.add), "dve")
    m = snap(); wait(V, "dve", at=m)
    inc(V.reduce_sum(srow[0:1, 48:52],
                     srow[0:1, 32:48].rearrange("p (h c) -> p h c", h=4), axis=AX.X), "dve")
    m = snap(); wait(V, "dve", at=m)
    inc(V.tensor_scalar(srow[0:1, 52:56], srow[0:1, 48:52], -1.0, 1.0, ALU.mult, ALU.add), "dve")
    inc(SY.dma_start(out=srow[0:1, 72:76], in_=ap["crl"][None, :]), "sq", 16)
    inc(SY.dma_start(out=srow[0:1, 60:64], in_=ap["alpha_id"][None, :]), "sq", 16)
    inc(SY.dma_start(out=srow[0:1, 64:68], in_=ap["gb2"][None, :]), "sq", 16)
    m = snap(); wait(SC, "sq", at=m); wait(SC, "dve", at=m)
    inc(SC.activation(srow[0:1, 56:60], srow[0:1, 72:76], AF.Sigmoid), "act")
    m = snap(); wait(TE, "act", at=m)
    inc(TE.matmul(PS[0][:, 0:16], onesf[0:1, :], srow[0:1, 32:48], start=True, stop=True), "pe")
    inc(TE.matmul(PS[1][:, 0:16], onesf[0:1, :], srow[0:1, 52:68], start=True, stop=True), "pe")
    m = snap(); wait(V, "pe", at=m)
    V.tensor_copy(bcfl[:], PS[0][:, 0:16])
    inc(V.tensor_copy(bcmisc[:], PS[1][:, 0:16]), "dve")
    m = snap(); wait(SY, "dve", at=m)
    inc(SY.dma_start(out=srow[0:1, 0:128], in_=ap["onw"][None, 0:128]), "sq", 16)
    m = snap(); wait(TE, "sq", at=m)
    inc(TE.matmul(PS[2][:, 0:128], onesf[0:1, :], srow[0:1, 0:128], start=True, stop=True), "pe")
    m = snap(); wait(SY, "pe", at=m)
    inc(SY.dma_start(out=srow[0:1, 0:128], in_=ap["onw"][None, 128:256]), "sq", 16)
    m = snap(); wait(TE, "sq", at=m)
    inc(TE.matmul(PS[2][:, 128:256], onesf[0:1, :], srow[0:1, 0:128], start=True, stop=True), "pe")
    m = snap(); wait(V, "pe", at=m)
    for h in range(NH):
        inc(V.tensor_copy(onwb[:, 256 * h:256 * (h + 1)], PS[2][:, 0:256]), "dve")

    for c in range(F // 128):
        m = snap(); wait(SY, "act", "dve", "sg", at=m)
        inc(SY.dma_start(out=wstg[:], in_=ap["w1"][128 * c:128 * (c + 1), :]), "sq", 16)
        m = snap(); wait(V, "sq", at=m)
        inc(V.tensor_reduce(fsc[:, 0:1], wstg[:], axis=AX.X, op=ALU.max,
                            apply_absolute_value=True), "dve")
        m = snap(); wait(V, "dve", at=m)
        inc(V.tensor_scalar(fsc[:, 1:2], fsc[:, 0:1], 1e-8, 1.0 / 127.0, ALU.max, ALU.mult), "dve")
        m = snap(); wait(V, "dve", at=m)
        V.tensor_copy(s_w1c[:, c:c + 1], fsc[:, 1:2])
        inc(V.reciprocal(fsc[:, 2:3], fsc[:, 1:2]), "dve")
        m = snap(); wait(V, "dve", at=m)
        inc(V.tensor_scalar(fsc2[:, 0:1024], wstg[:], fsc[:, 2:3], 0.0, ALU.mult, ALU.add), "dve")
        m = snap(); wait(V, "dve", at=m)
        inc(V.tensor_copy(AI32[:, 0:1024], fsc2[:, 0:1024]), "dve")
        m = snap(); wait(V, "dve", at=m)
        inc(V.tensor_copy(AB2[:, 0:1024], AI32[:, 0:1024]), "dve")
        m = snap(); wait(TE, "dve", at=m)
        for d8 in range(8):
            inc(TE.transpose(PB[0][:, 128 * d8:128 * (d8 + 1)],
                             AB2[:, 128 * d8:128 * (d8 + 1)], identt[:]), "pe")
        m = snap(); wait(V, "pe", at=m)
        inc(V.tensor_copy(AB2[:, 1024:2048], PB[0][:, 0:1024]), "dve")
        m = snap(); wait(G, "dve", at=m)
        for d8 in range(8):
            inc(G.dma_start(out=dr["w1iT"][128 * d8:128 * (d8 + 1), 128 * c:128 * (c + 1)],
                            in_=AB2[:, 1024 + 128 * d8:1024 + 128 * (d8 + 1)]), "sg", 16)
    for c in range(D // 128):
        for q4 in range(4):
            m = snap(); wait(SY, "dve", "sg", at=m)
            inc(SY.dma_start(out=wstg[:], in_=ap["w2"][128 * c:128 * (c + 1),
                                                       1024 * q4:1024 * (q4 + 1)]), "sq", 16)
            m = snap(); wait(V, "sq", at=m)
            inc(V.tensor_reduce(fsc[:, q4:q4 + 1], wstg[:], axis=AX.X, op=ALU.max,
                                apply_absolute_value=True), "dve")
        m = snap(); wait(V, "dve", at=m)
        inc(V.tensor_reduce(fsc[:, 4:5], fsc[:, 0:4], axis=AX.X, op=ALU.max), "dve")
        m = snap(); wait(V, "dve", at=m)
        inc(V.tensor_scalar(fsc[:, 5:6], fsc[:, 4:5], 1e-8, 1.0 / 127.0, ALU.max, ALU.mult), "dve")
        m = snap(); wait(V, "dve", at=m)
        V.tensor_copy(bq2b[:, c:c + 1], fsc[:, 5:6])
        inc(V.reciprocal(fsc[:, 6:7], fsc[:, 5:6]), "dve")
        for q4 in range(4):
            m = snap(); wait(SY, "dve", "sg", at=m)
            inc(SY.dma_start(out=wstg[:], in_=ap["w2"][128 * c:128 * (c + 1),
                                                       1024 * q4:1024 * (q4 + 1)]), "sq", 16)
            m = snap(); wait(V, "sq", at=m)
            inc(V.tensor_scalar(fsc2[:, 0:1024], wstg[:], fsc[:, 6:7], 0.0, ALU.mult, ALU.add), "dve")
            m = snap(); wait(SC, "dve", at=m)
            m = snap(); wait(V, "dve", at=m)
            inc(V.tensor_copy(AI32[:, 0:1024], fsc2[:, 0:1024]), "dve")
            m = snap(); wait(V, "dve", at=m)
            inc(V.tensor_copy(AB2[:, 0:1024], AI32[:, 0:1024]), "dve")
            m = snap(); wait(V, "dve", at=m)
            inc(V.tensor_scalar(AB2[:, 0:1024], AB2[:, 0:1024], bq2b[:, c:c + 1], 0.0,
                                ALU.mult, ALU.add), "dve")
            m = snap(); wait(TE, "dve", at=m)
            for d8 in range(8):
                inc(TE.transpose(PB[0][:, 128 * d8:128 * (d8 + 1)],
                                 AB2[:, 128 * d8:128 * (d8 + 1)], identt[:]), "pe")
            m = snap(); wait(V, "pe", at=m)
            inc(V.tensor_copy(AB2[:, 1024:2048], PB[0][:, 0:1024]), "dve")
            m = snap(); wait(G, "dve", at=m)
            for d8 in range(8):
                inc(G.dma_start(out=dr["w2qT"][1024 * q4 + 128 * d8:1024 * q4 + 128 * (d8 + 1),
                                               128 * c:128 * (c + 1)],
                                in_=AB2[:, 1024 + 128 * d8:1024 + 128 * (d8 + 1)]), "sg", 16)
    m = snap(); wait(TE, "dve", at=m)
    inc(TE.transpose(PS[3][0:8, 0:128], bq2b[:, 0:8], identf[:]), "pe")
    m = snap(); wait(V, "pe", at=m)
    inc(V.tensor_copy(fsc[0:8, 0:128], PS[3][0:8, 0:128]), "dve")
    m = snap(); wait(G, "dve", at=m)
    inc(G.dma_start(out=dr["bounce"][0:8, 0:128], in_=fsc[0:8, 0:128]), "sg", 16)
    m = snap(); wait(SY, "sg", at=m)
    inc(SY.dma_start(out=sw2r[0:1, :],
                     in_=dr["bounce"].rearrange("a b -> (a b)")[None, 0:1024]), "sq", 16)
    m = snap(); wait(V, "sq", at=m)

    def emit_ln_stats(with_rowmax):
        for i in range(cfg.NT):
            st = stream[:, D * i:D * (i + 1)]
            m0 = snap(); wait(V, "act", at=m0)
            inc(V.reduce_sum(fsc[:, 0:1], st, axis=AX.X), "dve")
            m = snap(); wait(SC, "dve", at=m)
            inc(SC.activation(fsc2[:, 0:1024], st, AF.Square, accum_out=fsc[:, 1:2]), "act")
            m = snap(); wait(V, "act", at=m)
            V.tensor_scalar(fsc[:, 2:3], fsc[:, 0:1], 1.0 / D, 0.0, ALU.mult, ALU.add)
            inc(V.tensor_scalar(fsc[:, 3:4], fsc[:, 1:2], 1.0 / D, 0.0, ALU.mult, ALU.add), "dve")
            m = snap(); wait(V, "dve", at=m)
            V.tensor_copy(mu[:, i:i + 1], fsc[:, 2:3])
            inc(V.tensor_tensor(fsc[:, 4:5], fsc[:, 2:3], fsc[:, 2:3], ALU.mult), "dve")
            m = snap(); wait(V, "dve", at=m)
            inc(V.tensor_sub(fsc[:, 5:6], fsc[:, 3:4], fsc[:, 4:5]), "dve")
            m = snap(); wait(SC, "dve", at=m)
            inc(SC.activation(fsc[:, 6:7], fsc[:, 5:6], AF.Sqrt, bias=epsc[:, 0:1]), "act")
            m = snap(); wait(V, "act", at=m)
            inc(V.reciprocal(rstd[:, i:i + 1], fsc[:, 6:7]), "dve")
            if with_rowmax:
                inc(V.reduce_max(fsc[:, 8:9], st, axis=AX.X), "dve")
                m = snap(); wait(V, "dve", at=m)
                inc(V.tensor_scalar(fsc[:, 9:10], fsc[:, 8:9], mu[:, i:i + 1],
                                    rstd[:, i:i + 1], ALU.subtract, ALU.mult), "dve")
                m = snap(); wait(V, "dve", at=m)
                inc(V.tensor_max(maxc[:, 0:1], maxc[:, 0:1], fsc[:, 9:10]), "dve")

    def norm_tile(i, dst):
        return V.tensor_scalar(dst, stream[:, D * i:D * (i + 1)],
                               mu[:, i:i + 1], rstd[:, i:i + 1], ALU.subtract, ALU.mult)

    def emit_global_max(out_recip_col, out_val_col):
        m = snap(); wait(V, "dve", at=m)
        inc(V.memset(ccst[:], 0.0), "dve")
        m = snap(); wait(V, "dve", at=m)
        inc(V.tensor_copy(ccst[:, 0:1], maxc[:, 0:1]), "dve")
        m = snap(); wait(G, "dve", at=m)
        inc(G.dma_start(out=cc_in_t.ap(), in_=ccst[:]), "sg", 16)
        m = snap(); wait(G, "sg", at=m)
        inc(G.collective_compute("AllReduce", ALU.max, [list(cfg.cc_group)],
                                 [cc_in_t.ap().opt()], [cc_out_t.ap().opt()]), "cc", 1)
        m = snap(); wait(G, "cc", at=m)
        inc(G.dma_start(out=ccst[:], in_=cc_out_t.ap()), "sg", 16)
        m = snap(); wait(TE, "sg", at=m)
        inc(TE.transpose(PS[4][0:1, 0:128], ccst[:, 0:1], identf[:]), "pe")
        m = snap(); wait(V, "pe", at=m)
        inc(V.reduce_max(srow[0:1, 0:1], PS[4][0:1, 0:128], axis=AX.X), "dve")
        m = snap(); wait(V, "dve", at=m)
        inc(V.tensor_scalar(srow[0:1, 1:2], srow[0:1, 0:1], 1e-8, 1.0 / 255.0,
                            ALU.max, ALU.mult), "dve")
        m = snap(); wait(V, "dve", at=m)
        inc(V.reciprocal(srow[0:1, 2:3], srow[0:1, 1:2]), "dve")
        m = snap(); wait(TE, "dve", at=m)
        inc(TE.matmul(PS[4][:, 0:1], onesf[0:1, :], srow[0:1, 2:3], start=True, stop=True), "pe")
        inc(TE.matmul(PS[4][:, 1:2], onesf[0:1, :], srow[0:1, 1:2], start=True, stop=True), "pe")
        m = snap(); wait(V, "pe", at=m)
        V.tensor_copy(sInv[:, out_recip_col:out_recip_col + 1], PS[4][:, 0:1])
        inc(V.tensor_copy(sInv[:, out_val_col:out_val_col + 1], PS[4][:, 1:2]), "dve")
        m = snap(); wait(V, "dve", at=m)
        inc(V.memset(maxc[:, 0:1], 0.0), "dve")

    def load_w_chunks(name, dst, fdim, nchunk):
        for c in range(nchunk):
            m = snap(); wait(SY, "act", at=m)
            inc(SY.dma_start(out=wstg[:, 0:fdim], in_=ap[name][128 * c:128 * (c + 1), :]), "sq", 16)
            m = snap(); wait(SC, "sq", "sg", "pe", "dve", at=m)
            inc(SC.copy(dst[:, fdim * c:fdim * (c + 1)], wstg[:, 0:fdim]), "act")

    def xnT_tile(i, dstb, pbi=0):
        m = snap(); wait(V, "act", at=m)
        inc(norm_tile(i, fsc[:, 0:1024]), "dve")
        m = snap(); wait(SC, "dve", "pe", at=m)
        inc(SC.copy(AB2[:, 0:1024], fsc[:, 0:1024]), "act")
        m = snap(); wait(TE, "act", "dve", at=m)
        for d8 in range(8):
            inc(TE.transpose(PB[pbi][:, 128 * d8:128 * (d8 + 1)],
                             AB2[:, 128 * d8:128 * (d8 + 1)], identt[:]), "pe")
        m = snap(); wait(V, "pe", at=m)
        for d8 in range(8):
            inc(V.tensor_copy(dstb[:, 128 * d8:128 * (d8 + 1)],
                              PB[pbi][:, 128 * d8:128 * (d8 + 1)]), "dve")

    GKs2 = sb("GKs2", 128, 4 * cfg.NLAT)
    GVs2 = sb("GVs2", cfg.NLAT, 8 * 65)

    def emit_attn_a2():
        ca = Carve(AB1)
        wres = ca.take(8 * 1024)
        xnTb = ca.take(1024)
        evb = ca.take(1040)

        emit_ln_stats(with_rowmax=False)

        ngrp = cfg.NTS // cfg.GTS
        for sq_ in range(cfg.nseq):
            t0 = sq_ * cfg.NTS
            for g in range(ngrp):
                for it in range(cfg.GTS):
                    i = t0 + g * cfg.GTS + it
                    m = snap(); wait(V, "act", at=m)
                    inc(norm_tile(i, fsc[:, 0:1024]), "dve")
                    m = snap(); wait(SC, "dve", "pe", at=m)
                    inc(SC.copy(AB2[:, 0:1024], fsc[:, 0:1024]), "act")
                    m = snap(); wait(TE, "act", at=m)
                    if it == 0:
                        wait(TE, "dve", at=m)
                    for hf in range(2):
                        inc(TE.matmul(PS[hf][0:lpg, 0:512],
                                      poolt[:, lpg * it:lpg * (it + 1)],
                                      AB2[:, 512 * hf:512 * (hf + 1)],
                                      start=(it == 0), stop=(it == cfg.GTS - 1)), "pe")
                m = snap(); wait(V, "pe", at=m)
                for hf in range(2):
                    inc(V.tensor_copy(pooled[lpg * g:lpg * (g + 1), 512 * hf:512 * (hf + 1)],
                                      PS[hf][0:lpg, 0:512]), "dve")
            m = snap(); wait(TE, "dve", at=m)
            for d8 in range(8):
                inc(TE.transpose(PB[0][0:128, 128 * d8:128 * d8 + cfg.NLAT],
                                 pooled[0:cfg.NLAT, 128 * d8:128 * (d8 + 1)],
                                 identt[0:cfg.NLAT, 0:cfg.NLAT]), "pe")
            m = snap(); wait(V, "pe", at=m)
            for d8 in range(8):
                dcol = cfg.NLAT * d8
                if sq_ == 0:
                    inc(V.tensor_copy(pooledT[:, dcol:dcol + cfg.NLAT],
                                      PB[0][0:128, 128 * d8:128 * d8 + cfg.NLAT]), "dve")
                else:
                    inc(V.tensor_copy(evb[:, dcol:dcol + cfg.NLAT],
                                      PB[0][0:128, 128 * d8:128 * d8 + cfg.NLAT]), "dve")
        pooledT2 = evb

        for wn, kind in [("wgk", "K"), ("wgv", "V")]:
            load_w_chunks(wn, wres, 512, 8)
            for sq_ in range(cfg.nseq):
                pt = pooledT if sq_ == 0 else pooledT2
                gk_dst = GKs if sq_ == 0 else GKs2
                gv_dst = GVs if sq_ == 0 else GVs2
                m = snap(); wait(TE, "act", "dve", at=m)
                if kind == "K":
                    for fc in range(4):
                        for kc in range(8):
                            inc(TE.matmul(PS[fc % 2][0:128, 0:cfg.NLAT],
                                          wres[:, 512 * kc + 128 * fc:512 * kc + 128 * (fc + 1)],
                                          pt[:, cfg.NLAT * kc:cfg.NLAT * (kc + 1)],
                                          start=(kc == 0), stop=(kc == 7)), "pe")
                        m = snap(); wait(V, "pe", at=m)
                        inc(V.tensor_copy(gk_dst[:, cfg.NLAT * fc:cfg.NLAT * (fc + 1)],
                                          PS[fc % 2][0:128, 0:cfg.NLAT]), "dve")
                        m = snap(); wait(TE, "dve", at=m)
                else:
                    for kc in range(8):
                        inc(TE.matmul(PS[2][0:cfg.NLAT, 0:512],
                                      pt[:, cfg.NLAT * kc:cfg.NLAT * (kc + 1)],
                                      wres[:, 512 * kc:512 * (kc + 1)],
                                      start=(kc == 0), stop=(kc == 7)), "pe")
                    m = snap(); wait(V, "pe", at=m)
                    for lh in range(8):
                        inc(V.tensor_copy(gv_dst[0:cfg.NLAT, 65 * lh:65 * lh + 64],
                                          PS[2][0:cfg.NLAT, 64 * lh:64 * (lh + 1)]), "dve")
                    m = snap(); wait(V, "dve", at=m)
                    for lh in range(8):
                        inc(V.memset(gv_dst[0:cfg.NLAT, 65 * lh + 64:65 * (lh + 1)], 1.0), "dve")

        caA = Carve(AB1)
        wqR = caA.take(8 * 1024)
        wkR = caA.take(8 * 1024)
        wgqR = caA.take(8 * 512)
        wvR = caA.take(8 * 1024)
        xnTblk = caA.take(8 * 512)
        evS = [AB2[:, 1024:1536], AB2[:, 1536:2048]]
        evT = AB2[:, 2048:3088]
        load_w_chunks("wq", wqR, 1024, 8)
        load_w_chunks("wk", wkR, 1024, 8)
        load_w_chunks("wgq", wgqR, 512, 8)
        load_w_chunks("wv", wvR, 1024, 8)
        m = snap(); wait(V, "act", at=m)
        for h in range(16):
            inc(V.memset(evT[:, 65 * h + 64:65 * (h + 1)], 1.0), "dve")
        for bi in range(T // 512):
            for it in range(4):
                i = 4 * bi + it
                m = snap(); wait(V, "act", at=m)
                inc(norm_tile(i, fsc[:, 0:1024]), "dve")
                m = snap(); wait(SC, "dve", "pe", at=m)
                inc(SC.copy(AB2[:, 0:1024], fsc[:, 0:1024]), "act")
                m = snap(); wait(TE, "act", "dve", at=m)
                for d8 in range(8):
                    inc(TE.transpose(PB[it % 2][:, 128 * d8:128 * (d8 + 1)],
                                     AB2[:, 128 * d8:128 * (d8 + 1)], identt[:]), "pe")
                m = snap(); wait(V, "pe", at=m)
                for d8 in range(8):
                    inc(V.tensor_copy(xnTblk[:, 512 * d8 + 128 * it:512 * d8 + 128 * (it + 1)],
                                      PB[it % 2][:, 128 * d8:128 * (d8 + 1)]), "dve")
            m = snap(); wait(TE, "dve", at=m)
            for wres, dst, fdim in [(wqR, dr["aQ"], 1024), (wkR, dr["aK"], 1024),
                                    (wgqR, dr["aGQ"], 512)]:
                for fc in range(fdim // 128):
                    ps_q = PS[fc % 2]
                    m = snap(); wait(TE, "act", at=m)
                    for kc in range(8):
                        inc(TE.matmul(ps_q[:, 0:512],
                                      wres[:, fdim * kc + 128 * fc:fdim * kc + 128 * (fc + 1)],
                                      xnTblk[:, 512 * kc:512 * (kc + 1)],
                                      start=(kc == 0), stop=(kc == 7)), "pe")
                    m = snap(); wait(SC, "pe", "sg", at=m)
                    inc(SC.copy(evS[fc % 2][:, 0:512], ps_q[:, 0:512]), "act")
                    m = snap(); wait(G, "act", at=m)
                    inc(G.dma_start(out=dst[128 * fc:128 * (fc + 1), 512 * bi:512 * (bi + 1)],
                                    in_=evS[fc % 2][:, 0:512]), "sg", 16)
            for it in range(4):
                i = 4 * bi + it
                m = snap(); wait(TE, "dve", "act", at=m)
                for nh2 in range(2):
                    ps_v = PS[2 + nh2]
                    for kc in range(8):
                        inc(TE.matmul(ps_v[:, 0:512],
                                      xnTblk[:, 512 * kc + 128 * it:512 * kc + 128 * (it + 1)],
                                      wvR[:, 1024 * kc + 512 * nh2:1024 * kc + 512 * (nh2 + 1)],
                                      start=(kc == 0), stop=(kc == 7)), "pe")
                m = snap(); wait(V, "pe", "sg", at=m)
                for h in range(16):
                    inc(V.tensor_copy(evT[:, 65 * h:65 * h + 64],
                                      PS[2 + h // 8][:, 64 * (h % 8):64 * (h % 8) + 64]), "dve")
                m = snap(); wait(G, "dve", at=m)
                inc(G.dma_start(out=dr["aV"][128 * i:128 * (i + 1), :], in_=evT[:, 0:1040]), "sg", 16)

        ca2 = Carve(AB1)
        rW = ca2.take(4 * 1024)
        oW = ca2.take(8 * 1024)
        Qb = ca2.take(8 * 256)
        Kb = [ca2.take(8 * 256), ca2.take(8 * 256)]
        Vb = [ca2.take(2 * 1040), ca2.take(2 * 1040)]
        GQb = ca2.take(4 * 256)
        Ptb = ca2.take(512)
        PTb = ca2.take(512)
        remtb = ca2.take(2 * 512)
        remfb = ca2.take(4 * 256)
        combb = ca2.take(2 * 1024)
        gat = AFX[:, 4096:4096 + 64]

        load_w_chunks("r_out_w", rW, 1024, 4)
        load_w_chunks("w_out", oW, 1024, 8)
        mA2w = snap()

        for sq_ in range(cfg.nseq):
            gk = GKs if sq_ == 0 else GKs2
            gv = GVs if sq_ == 0 else GVs2
            for b in range(cfg.NB):
                cur, prv = b % 2, 1 - (b % 2)
                trow0 = (sq_ * S + b * 256)
                m = snap()
                wait(SY, "pe", at=m)
                inc(SY.dma_start(out=Qb[:, 0:2048],
                                 in_=dr["aQ"].rearrange("(c p) t -> p c t", p=128)
                                 [:, :, trow0:trow0 + 256]), "sq", 16)
                inc(SY.dma_start(out=Kb[cur][:, 0:2048],
                                 in_=dr["aK"].rearrange("(c p) t -> p c t", p=128)
                                 [:, :, trow0:trow0 + 256]), "sq", 16)
                inc(SY.dma_start(out=GQb[:, 0:1024],
                                 in_=dr["aGQ"].rearrange("(c p) t -> p c t", p=128)
                                 [:, :, trow0:trow0 + 256]), "sq", 16)
                inc(SY.dma_start(out=Vb[cur][:, 0:1040], in_=dr["aV"][trow0:trow0 + 128, :]), "sq", 16)
                inc(SY.dma_start(out=Vb[cur][:, 1040:2080], in_=dr["aV"][trow0 + 128:trow0 + 256, :]), "sq", 16)
                mLd = snap()
                if b == 0:
                    wait(V, "sq", at=mLd)
                    inc(V.memset(Kb[prv][:, :], 0.0), "dve")
                    inc(V.memset(Vb[prv][:, :], 0.0), "dve")
                wait(TE, "sq", at=mLd)
                wait(TE, "pe", at=mA2w)
                m = snap(); wait(TE, "dve", at=m)
                for mt in range(2):
                    for kc in range(8):
                        inc(TE.matmul(PS[4][:, 32 * mt:32 * (mt + 1)],
                                      Qb[:, 256 * kc + 128 * mt:256 * kc + 128 * (mt + 1)],
                                      g_exp[:, 32 * kc:32 * (kc + 1)],
                                      start=(kc == 0), stop=(kc == 7)), "pe")
                m = snap(); wait(SC, "pe", at=m)
                inc(SC.activation(gat[:, 0:64], PS[4][:, 0:64], AF.Sigmoid), "act")
                for lh in range(8):
                    fc, of = (64 * lh) // 128, (64 * lh) % 128
                    for mt in range(2):
                        inc(TE.matmul(PS[2][0:128, 64 * mt:64 * (mt + 1)],
                                      GQb[of:of + 64, 256 * fc + 128 * mt:256 * fc + 128 * (mt + 1)],
                                      gk[of:of + 64, 64 * fc:64 * fc + cfg.NLAT],
                                      start=True, stop=True), "pe")
                    m = snap(); wait(V, "pe", at=m)
                    V.reduce_max(fsc[:, 0:1], PS[2][:, 0:64], axis=AX.X)
                    inc(V.reduce_max(fsc[:, 1:2], PS[2][:, 64:128], axis=AX.X), "dve")
                    m = snap(); wait(V, "dve", at=m)
                    inc(V.tensor_scalar(fsc[:, 2:4], fsc[:, 0:2], -0.125, 0.0, ALU.mult, ALU.add), "dve")
                    for mt in range(2):
                        m = snap(); wait(SC, "dve" if mt == 0 else "pe", at=m)
                        wait(SC, "pool")
                        inc(SC.activation(Ptb[:, 0:cfg.NLAT], PS[2][:, 64 * mt:64 * (mt + 1)],
                                          AF.Exp, bias=fsc[:, 2 + mt:3 + mt], scale=0.125), "act")
                        m = snap(); wait(G, "act", at=m)
                        tbase = 256 * b + 128 * mt
                        inc(G.affine_select(Ptb[:, 0:cfg.NLAT], Ptb[:, 0:cfg.NLAT],
                                            [[-cfg.TPL, cfg.NLAT]], ALU.is_ge, 0.0,
                                            base=tbase, channel_multiplier=1), "pool")
                        m = snap(); wait(TE, "pool", at=m)
                        inc(TE.transpose(PB[0][0:cfg.NLAT, 128 * mt:128 * (mt + 1)],
                                         Ptb[:, 0:cfg.NLAT], identt[:]), "pe")
                        m = snap(); wait(V, "pe", at=m)
                        inc(V.tensor_copy(PTb[0:cfg.NLAT, 128 * mt:128 * (mt + 1)],
                                          PB[0][0:cfg.NLAT, 128 * mt:128 * (mt + 1)]), "dve")
                    m = snap(); wait(TE, "dve", at=m)
                    for mt in range(2):
                        inc(TE.matmul(PS[3][0:128, 65 * mt:65 * (mt + 1)],
                                      PTb[0:cfg.NLAT, 128 * mt:128 * (mt + 1)],
                                      gv[0:cfg.NLAT, 65 * lh:65 * (lh + 1)],
                                      start=True, stop=True), "pe")
                    m = snap(); wait(V, "pe", at=m)
                    for mt in range(2):
                        inc(V.reciprocal(fsc[:, 4 + mt:5 + mt],
                                         PS[3][:, 65 * mt + 64:65 * mt + 65]), "dve")
                    m = snap(); wait(V, "dve", at=m)
                    for mt in range(2):
                        inc(V.tensor_scalar(remtb[:, 512 * mt + 64 * lh:512 * mt + 64 * (lh + 1)],
                                            PS[3][:, 65 * mt:65 * mt + 64],
                                            fsc[:, 4 + mt:5 + mt], 0.0, ALU.mult, ALU.add), "dve")
                    m = snap(); wait(TE, "dve", at=m)
                for mt in range(2):
                    for d4 in range(4):
                        inc(TE.transpose(PB[mt][:, 128 * d4:128 * (d4 + 1)],
                                         remtb[:, 512 * mt + 128 * d4:512 * mt + 128 * (d4 + 1)],
                                         identt[:]), "pe")
                m = snap(); wait(V, "pe", at=m)
                for mt in range(2):
                    for d4 in range(4):
                        inc(V.tensor_copy(remfb[:, 256 * d4 + 128 * mt:256 * d4 + 128 * (mt + 1)],
                                          PB[mt][:, 128 * d4:128 * (d4 + 1)]), "dve")
                m = snap(); wait(TE, "dve", at=m)
                for mt in range(2):
                    for nh2 in range(2):
                        ps_r = PS[2 * mt + nh2]
                        for kc in range(4):
                            inc(TE.matmul(ps_r[:, 0:512],
                                          remfb[:, 256 * kc + 128 * mt:256 * kc + 128 * (mt + 1)],
                                          rW[:, 1024 * kc + 512 * nh2:1024 * kc + 512 * (nh2 + 1)],
                                          start=(kc == 0), stop=(kc == 3)), "pe")
                mRem = snap()
                for h in range(16):
                    fc, of = (64 * h) // 128, (64 * h) % 128
                    for mt in range(2):
                        ps_s = PS[4]
                        inc(TE.matmul(ps_s[:, 0:256],
                                      Qb[of:of + 64, 256 * fc + 128 * mt:256 * fc + 128 * (mt + 1)],
                                      Kb[prv][of:of + 64, 256 * fc:256 * fc + 256],
                                      start=True, stop=True), "pe")
                        inc(TE.matmul(ps_s[:, 256:512],
                                      Qb[of:of + 64, 256 * fc + 128 * mt:256 * fc + 128 * (mt + 1)],
                                      Kb[cur][of:of + 64, 256 * fc:256 * fc + 256],
                                      start=True, stop=True), "pe")
                        m = snap(); wait(V, "pe", at=m)
                        inc(V.reduce_max(fsc[:, 0:1], ps_s[:, 0:512], axis=AX.X), "dve")
                        m = snap(); wait(V, "dve", at=m)
                        inc(V.tensor_scalar(fsc[:, 1:2], fsc[:, 0:1], -0.125, 0.0,
                                            ALU.mult, ALU.add), "dve")
                        m = snap(); wait(SC, "dve", at=m)
                        inc(SC.activation(Ptb[:, 0:512], ps_s[:, 0:512], AF.Exp,
                                          bias=fsc[:, 1:2], scale=0.125), "act")
                        m = snap(); wait(G, "act", at=m)
                        wm = winft if b == 0 else winmt
                        inc(G.tensor_tensor(Ptb[:, 0:512], Ptb[:, 0:512],
                                            wm[:, 512 * mt:512 * (mt + 1)], ALU.mult), "pool")
                        m = snap(); wait(TE, "pool", at=m)
                        for k4 in range(4):
                            inc(TE.transpose(PB[0][:, 128 * k4:128 * (k4 + 1)],
                                             Ptb[:, 128 * k4:128 * (k4 + 1)], identt[:]), "pe")
                        m = snap(); wait(V, "pe", at=m)
                        inc(V.tensor_copy(PTb[:, 0:512], PB[0][:, 0:512]), "dve")
                        m = snap(); wait(TE, "dve", at=m)
                        for k4 in range(4):
                            vt = Vb[prv] if k4 < 2 else Vb[cur]
                            vof = 1040 * (k4 % 2)
                            inc(TE.matmul(PS[5][:, 0:65],
                                          PTb[:, 128 * k4:128 * (k4 + 1)],
                                          vt[:, vof + 65 * h:vof + 65 * (h + 1)],
                                          start=(k4 == 0), stop=(k4 == 3)), "pe")
                        m = snap(); wait(V, "pe", at=m)
                        inc(V.reciprocal(fsc[:, 2:3], PS[5][:, 64:65]), "dve")
                        m = snap(); wait(V, "dve", at=m)
                        inc(V.tensor_scalar(fsc[:, 4:68], PS[5][:, 0:64], fsc[:, 2:3],
                                            gat[:, 32 * mt + h:32 * mt + h + 1],
                                            ALU.mult, ALU.mult), "dve")
                        m = snap()
                        wait(V, "pe", at=mRem)
                        wait(V, "dve", at=m)
                        rps = PS[2 * mt + (64 * h) // 512]
                        rof = (64 * h) % 512
                        inc(V.scalar_tensor_tensor(
                            combb[:, 1024 * mt + 64 * h:1024 * mt + 64 * (h + 1)],
                            rps[:, rof:rof + 64],
                            gat[:, 32 * mt + 16 + h:32 * mt + 16 + h + 1],
                            fsc[:, 4:68], ALU.mult, ALU.add), "dve")
                m = snap(); wait(TE, "dve", at=m)
                for mt in range(2):
                    for d8 in range(8):
                        inc(TE.transpose(PB[mt][:, 128 * d8:128 * (d8 + 1)],
                                         combb[:, 1024 * mt + 128 * d8:1024 * mt + 128 * (d8 + 1)],
                                         identt[:]), "pe")
                m = snap(); wait(V, "pe", at=m)
                for mt in range(2):
                    for d8 in range(8):
                        inc(V.tensor_copy(Qb[:, 1024 * mt + 128 * d8:1024 * mt + 128 * (d8 + 1)],
                                          PB[mt][:, 128 * d8:128 * (d8 + 1)]), "dve")
                m = snap(); wait(TE, "dve", at=m)
                for mt in range(2):
                    for nh2 in range(2):
                        ps_o = PS[2 * mt + nh2]
                        for kc in range(8):
                            inc(TE.matmul(ps_o[:, 0:512],
                                          Qb[:, 1024 * mt + 128 * kc:1024 * mt + 128 * (kc + 1)],
                                          oW[:, 1024 * kc + 512 * nh2:1024 * kc + 512 * (nh2 + 1)],
                                          start=(kc == 0), stop=(kc == 7)), "pe")
                m = snap(); wait(V, "pe", at=m)
                for mt in range(2):
                    i = (trow0 // 128) + mt
                    for nh2 in range(2):
                        inc(V.tensor_tensor(stream[:, D * i + 512 * nh2:D * i + 512 * (nh2 + 1)],
                                            stream[:, D * i + 512 * nh2:D * i + 512 * (nh2 + 1)],
                                            PS[2 * mt + nh2][:, 0:512], ALU.add), "dve")
                m = snap(); wait(TE, "dve", at=m)
                wait(SY, "dve", at=m)

    def emit_ffn(final):
        ca = Carve(AB1)
        wbig = ca.take(8 * 2048)
        wbig2 = ca.take(8 * 2048)
        caB = Carve(AB2)
        xqTb = caB.take(4096)
        xqb = caB.take(1024)
        hqb = caB.take(0)

        emit_ln_stats(with_rowmax=True)
        emit_global_max(0, 1)
        m = snap(); wait(V, "dve", at=m)
        inc(V.tensor_scalar(ffnc[:, 0:32], s_w1c[:], sInv[:, 1:2], 0.0, ALU.mult, ALU.add), "dve")
        m = snap(); wait(V, "dve", at=m)
        inc(V.reciprocal(ffnc[:, 64:96], ffnc[:, 0:32]), "dve")
        m = snap(); wait(V, "dve", at=m)
        inc(V.tensor_tensor(fsc[:, 0:32], b1c[:], ffnc[:, 64:96], ALU.mult), "dve")
        m = snap(); wait(V, "dve", at=m)
        inc(V.tensor_copy(AI32[:, 0:32], fsc[:, 0:32]), "dve")
        m = snap(); wait(V, "dve", at=m)
        inc(V.tensor_copy(fsc[:, 32:64], AI32[:, 0:32]), "dve")
        m = snap(); wait(V, "dve", at=m)
        inc(V.tensor_tensor(ffnc[:, 32:64], fsc[:, 32:64], ffnc[:, 0:32], ALU.mult), "dve")

        for d8 in range(8):
            inc(SY.dma_start(out=wbig[:, 2048 * d8:2048 * (d8 + 1)],
                             in_=dr["w1iT"][128 * d8:128 * (d8 + 1), 0:2048]), "sq", 16)
            inc(SY.dma_start(out=wbig2[:, 2048 * d8:2048 * (d8 + 1)],
                             in_=dr["w1iT"][128 * d8:128 * (d8 + 1), 2048:4096]), "sq", 16)
        mW1 = snap()
        m = snap(); wait(V, "dve", at=m)
        inc(V.memset(maxc[:, 0:1], 0.0), "dve")

        nblk = T // 512
        for bk in range(nblk):
            for it in range(4):
                i = 4 * bk + it
                m = snap(); wait(V, "pe", at=m)
                inc(norm_tile(i, fsc[:, 0:1024]), "dve")
                m = snap(); wait(V, "dve", at=m)
                inc(V.tensor_scalar(AU8[:, 0:1024], fsc[:, 0:1024], sInv[:, 0:1],
                                    0.0, ALU.mult, ALU.max), "dve")
                m = snap(); wait(V, "dve", at=m)
                inc(V.tensor_copy(xqb[:, 0:1024], AU8[:, 0:1024]), "dve")
                m = snap(); wait(TE, "dve", at=m)
                for d8 in range(8):
                    inc(TE.transpose(PB[it % 2][:, 128 * d8:128 * (d8 + 1)],
                                     xqb[:, 128 * d8:128 * (d8 + 1)], identt[:]), "pe")
                m = snap(); wait(V, "pe", at=m)
                for d8 in range(8):
                    inc(V.tensor_copy(xqTb[:, 512 * d8 + 128 * it:512 * d8 + 128 * (it + 1)],
                                      PB[it % 2][:, 128 * d8:128 * (d8 + 1)]), "dve")
            if bk == 0:
                wait(TE, "sq", at=mW1)
                if getattr(cfg, "dbg_xqt", False):
                    m = snap()
                    inc(G.dma_start(out=dr["dbg"], in_=xqTb[:, 0:4096]), "sg", 16)
            m = snap(); wait(TE, "dve", at=m)
            for fc in range(F // 128):
                wsrc = wbig if fc < 16 else wbig2
                fof = (fc % 16) * 128
                ps_f = PS[fc % 4]
                m = snap(); wait(TE, "act", at=m)
                for kc in range(8):
                    inc(TE.matmul(ps_f[:, 0:512],
                                  wsrc[:, 2048 * kc + fof:2048 * kc + fof + 128],
                                  xqTb[:, 512 * kc:512 * (kc + 1)],
                                  start=(kc == 0), stop=(kc == 7)), "pe")
                m = snap(); wait(SC, "pe", "sg", "dve", at=m)
                inc(SC.activation(b_sc[:, 0:512], ps_f[:, 0:512], AF.Relu,
                                  bias=ffnc[:, 32 + fc:33 + fc], scale=ffnc[:, fc:fc + 1]), "act")
                m = snap(); wait(V, "act", at=m)
                inc(V.reduce_max(fsc[:, 100:101], b_sc[:, 0:512], axis=AX.X), "dve")
                m = snap(); wait(V, "dve", at=m)
                inc(V.tensor_max(maxc[:, 0:1], maxc[:, 0:1], fsc[:, 100:101]), "dve")
                m = snap(); wait(G, "act", at=m)
                inc(G.dma_start(out=dr["h"][128 * fc:128 * (fc + 1), 512 * bk:512 * (bk + 1)],
                                in_=b_sc[:, 0:512]), "sg", 16)
        emit_global_max(2, 3)
        sh_sc = srow[0:1, 1:2]
        m = snap(); wait(V, "dve", at=m)
        inc(V.tensor_scalar(fsc[0:1, 0:1024], sw2r[0:1, :], sh_sc, 0.0, ALU.mult, ALU.add), "dve")
        m = snap(); wait(V, "dve", at=m)
        inc(V.reciprocal(fsc2[0:1, 0:1024], fsc[0:1, 0:1024]), "dve")
        m = snap(); wait(V, "dve", at=m)
        inc(V.tensor_tensor(fsc2[0:1, 0:1024], b2r[0:1, :], fsc2[0:1, 0:1024], ALU.mult), "dve")
        m = snap(); wait(V, "dve", at=m)
        inc(V.tensor_copy(AI32[0:1, 0:1024], fsc2[0:1, 0:1024]), "dve")
        m = snap(); wait(V, "dve", at=m)
        inc(V.tensor_copy(fsc2[0:1, 0:1024], AI32[0:1, 0:1024]), "dve")
        m = snap(); wait(V, "dve", at=m)
        inc(V.tensor_mul(fsc2[0:1, 0:1024], fsc2[0:1, 0:1024], fsc[0:1, 0:1024]), "dve")
        m = snap(); wait(TE, "dve", at=m)
        inc(TE.matmul(PS[4][:, 0:512], onesf[0:1, :], fsc2[0:1, 0:512], start=True, stop=True), "pe")
        inc(TE.matmul(PS[5][:, 0:512], onesf[0:1, :], fsc2[0:1, 512:1024], start=True, stop=True), "pe")
        m = snap(); wait(V, "pe", at=m)
        V.tensor_copy(bq2b[:, 0:512], PS[4][:, 0:512])
        inc(V.tensor_copy(bq2b[:, 512:1024], PS[5][:, 0:512]), "dve")

        m = snap(); wait(SY, "pe", at=m)
        for kc in range(16):
            inc(SY.dma_start(out=wbig[:, 1024 * kc:1024 * (kc + 1)],
                             in_=dr["w2qT"][128 * kc:128 * (kc + 1), :]), "sq", 16)
            inc(SY.dma_start(out=wbig2[:, 1024 * kc:1024 * (kc + 1)],
                             in_=dr["w2qT"][128 * (16 + kc):128 * (17 + kc), :]), "sq", 16)
        mW2 = snap()

        hqcols = xqTb
        for i in range(cfg.NT):
            for fc in range(F // 128):
                m = snap(); wait(SY, "dve", at=m)
                inc(SY.dma_start(out=b_sc[:, 0:128],
                                 in_=dr["h"][128 * fc:128 * (fc + 1), 128 * i:128 * (i + 1)]), "sq", 16)
                m = snap(); wait(V, "sq", at=m)
                inc(V.tensor_scalar(AU8[:, 0:128], b_sc[:, 0:128], sInv[:, 2:3], 0.0,
                                    ALU.mult, ALU.max), "dve")
                m = snap(); wait(V, "dve", at=m)
                inc(V.tensor_copy(hqcols[:, 128 * fc:128 * (fc + 1)], AU8[:, 0:128]), "dve")
            if i == 0:
                wait(TE, "sq", at=mW2)
            m = snap(); wait(TE, "dve", at=m)
            for nh2 in range(2):
                ps_o = PS[nh2]
                for kc in range(32):
                    wsrc = wbig if kc < 16 else wbig2
                    inc(TE.matmul(ps_o[:, 0:512],
                                  hqcols[:, 128 * kc:128 * (kc + 1)],
                                  wsrc[:, 1024 * (kc % 16) + 512 * nh2:1024 * (kc % 16) + 512 * (nh2 + 1)],
                                  start=(kc == 0), stop=(kc == 31)), "pe")
            m = snap(); wait(V, "pe", at=m)
            for nh2 in range(2):
                inc(V.scalar_tensor_tensor(fsc[:, 512 * nh2:512 * (nh2 + 1)],
                                           PS[nh2][:, 0:512], sInv[:, 3:4],
                                           bq2b[:, 512 * nh2:512 * (nh2 + 1)],
                                           ALU.mult, ALU.add), "dve")
            m = snap(); wait(V, "dve", at=m)
            if final:
                inc(V.tensor_tensor(fsc2[:, 0:1024], fsc[:, 0:1024],
                                    stream[:, D * i:D * (i + 1)], ALU.add), "dve")
                m = snap(); wait(G, "dve", at=m)
                inc(G.dma_start(out=out[128 * i:128 * (i + 1), :], in_=fsc2[:, 0:1024]), "out", 16)
                m = snap(); wait(V, "out", at=m)
            else:
                inc(V.tensor_tensor(stream[:, D * i:D * (i + 1)],
                                    stream[:, D * i:D * (i + 1)], fsc[:, 0:1024], ALU.add), "dve")
            m = snap(); wait(TE, "dve", at=m)

    def emit_attn_b():
        emit_ln_stats(with_rowmax=False)

        caB1 = Carve(AB1)
        WqR = caB1.take(8 * 1024)
        WkR = caB1.take(8 * 1024)
        WvR = caB1.take(8 * 1024)
        xnTblk2 = caB1.take(8 * 512)
        evC = AB2[:, 1024:1536]
        vdb = AB2[:, 1536:2112]
        fcast = AB2[:, 2112:2624]
        vhist = AB2[:, 2624:3136]
        chist = AFX[:, 4096:4168]
        load_w_chunks("bWq", WqR, 1024, 8)
        load_w_chunks("bWk", WkR, 1024, 8)
        load_w_chunks("bWv", WvR, 1024, 8)
        for sq_ in range(cfg.nseq):
            m = snap(); wait(V, "act", "pool", at=m)
            inc(V.memset(chist[:, :], 0.0), "dve")
            inc(V.memset(vhist[:, 0:512], 0.0), "dve")
            for bi in range(S // 512):
                blk_g = sq_ * (S // 512) + bi
                for it in range(4):
                    i = 4 * blk_g + it
                    m = snap(); wait(V, "act", at=m)
                    inc(norm_tile(i, fsc[:, 0:1024]), "dve")
                    m = snap(); wait(SC, "dve", "pe", at=m)
                    inc(SC.copy(AB2[:, 0:1024], fsc[:, 0:1024]), "act")
                    m = snap(); wait(TE, "act", "dve", at=m)
                    for d8 in range(8):
                        inc(TE.transpose(PB[it % 2][:, 128 * d8:128 * (d8 + 1)],
                                         AB2[:, 128 * d8:128 * (d8 + 1)], identt[:]), "pe")
                    m = snap(); wait(V, "pe", at=m)
                    for d8 in range(8):
                        inc(V.tensor_copy(xnTblk2[:, 512 * d8 + 128 * it:512 * d8 + 128 * (it + 1)],
                                          PB[it % 2][:, 128 * d8:128 * (d8 + 1)]), "dve")
                m = snap(); wait(TE, "dve", at=m)
                for wi, (wres2, dst) in enumerate([(WqR, dr["bq"]), (WkR, dr["bk"]),
                                                   (WvR, dr["bv"])]):
                    for fc in range(8):
                        ps_q = PS[fc % 2]
                        m = snap(); wait(TE, "dve", "act", at=m)
                        for kc in range(8):
                            inc(TE.matmul(ps_q[:, 0:512],
                                          wres2[:, 1024 * kc + 128 * fc:1024 * kc + 128 * (fc + 1)],
                                          xnTblk2[:, 512 * kc:512 * (kc + 1)],
                                          start=(kc == 0), stop=(kc == 7)), "pe")
                        cwc = cw[:, (wi * 8 + fc) * 4:(wi * 8 + fc) * 4 + 4]
                        hc3 = chist[:, (wi * 8 + fc) * 3:(wi * 8 + fc) * 3 + 3]
                        m = snap(); wait(V, "pe", "act", at=m)
                        inc(V.tensor_scalar(b_sc[:, 0:512], ps_q[:, 0:512], cwc[:, 3:4],
                                            0.0, ALU.mult, ALU.add), "dve")
                        for tp in range(1, 4):
                            inc(V.scalar_tensor_tensor(b_sc[:, tp:512], ps_q[:, 0:512 - tp],
                                                       cwc[:, 3 - tp:4 - tp],
                                                       b_sc[:, tp:512], ALU.mult, ALU.add), "dve")
                            inc(V.scalar_tensor_tensor(b_sc[:, 0:tp], hc3[:, 3 - tp:3],
                                                       cwc[:, 3 - tp:4 - tp],
                                                       b_sc[:, 0:tp], ALU.mult, ALU.add), "dve")
                        inc(V.tensor_copy(hc3[:, :], ps_q[:, 509:512]), "dve")
                        m = snap(); wait(SC, "dve", "sg", at=m)
                        inc(SC.activation(evC[:, 0:512], b_sc[:, 0:512], AF.Silu), "act")
                        m = snap(); wait(G, "act", at=m)
                        inc(G.dma_start(out=dst[128 * fc:128 * (fc + 1), 512 * blk_g:512 * (blk_g + 1)],
                                        in_=evC[:, 0:512]), "sg", 16)
                        if wi == 2:
                            m = snap(); wait(V, "act", at=m)
                            V.tensor_copy(vdb[:, 0:64], vhist[:, 64 * fc:64 * (fc + 1)])
                            inc(V.tensor_copy(vdb[:, 64:576], evC[:, 0:512]), "dve")
                            inc(V.tensor_copy(vhist[:, 64 * fc:64 * (fc + 1)],
                                              vdb[:, 512:576]), "dve")
                            for (filt, Kf, dspill) in [(firsc, 3, dr["bls"]), (firlc, 63, dr["bll"])]:
                                fco = filt[:, Kf * fc:Kf * (fc + 1)]
                                m = snap(); wait(V, "dve", "act", at=m)
                                inc(V.tensor_scalar(b_sc[:, 0:512], vdb[:, 64:576],
                                                    fco[:, Kf - 1:Kf], 0.0, ALU.mult, ALU.add), "dve")
                                for tp in range(1, Kf):
                                    inc(V.scalar_tensor_tensor(b_sc[:, 0:512],
                                                               vdb[:, 64 - tp:576 - tp],
                                                               fco[:, Kf - 1 - tp:Kf - tp],
                                                               b_sc[:, 0:512], ALU.mult, ALU.add), "dve")
                                m = snap(); wait(SC, "dve", "sg", at=m)
                                inc(SC.copy(fcast[:, 0:512], b_sc[:, 0:512]), "act")
                                m = snap(); wait(G, "act", at=m)
                                inc(G.dma_start(out=dspill[128 * fc:128 * (fc + 1),
                                                          512 * blk_g:512 * (blk_g + 1)],
                                                in_=fcast[:, 0:512]), "sg", 16)

        caB2 = Carve(AB1)
        WidR = caB2.take(8 * 1024)
        GW1R = caB2.take(8 * 1024)
        WbR = caB2.take(8 * NH)
        xnTblk3 = caB2.take(8 * 512)
        load_w_chunks("bWid", WidR, 1024, 8)
        load_w_chunks("gW1", GW1R, 1024, 8)
        load_w_chunks("bWb", WbR, NH, 8)
        for blk_g in range(T // 512):
            for it in range(4):
                i = 4 * blk_g + it
                m = snap(); wait(V, "act", at=m)
                inc(norm_tile(i, fsc[:, 0:1024]), "dve")
                m = snap(); wait(SC, "dve", "pe", at=m)
                inc(SC.copy(AB2[:, 0:1024], fsc[:, 0:1024]), "act")
                m = snap(); wait(TE, "act", "dve", at=m)
                for d8 in range(8):
                    inc(TE.transpose(PB[it % 2][:, 128 * d8:128 * (d8 + 1)],
                                     AB2[:, 128 * d8:128 * (d8 + 1)], identt[:]), "pe")
                m = snap(); wait(V, "pe", at=m)
                for d8 in range(8):
                    inc(V.tensor_copy(xnTblk3[:, 512 * d8 + 128 * it:512 * d8 + 128 * (it + 1)],
                                      PB[it % 2][:, 128 * d8:128 * (d8 + 1)]), "dve")
            m = snap(); wait(TE, "dve", at=m)
            for it in range(4):
                i = 4 * blk_g + it
                for kc in range(8):
                    inc(TE.matmul(PS[4][:, 0:NH],
                                  xnTblk3[:, 512 * kc + 128 * it:512 * kc + 128 * (it + 1)],
                                  WbR[:, NH * kc:NH * (kc + 1)],
                                  start=(kc == 0), stop=(kc == 7)), "pe")
                for nh2 in range(2):
                    ps_v = PS[2 + nh2]
                    for kc in range(8):
                        inc(TE.matmul(ps_v[:, 0:512],
                                      xnTblk3[:, 512 * kc + 128 * it:512 * kc + 128 * (it + 1)],
                                      WidR[:, 1024 * kc + 512 * nh2:1024 * kc + 512 * (nh2 + 1)],
                                      start=(kc == 0), stop=(kc == 7)), "pe")
                m = snap(); wait(SC, "pe", "sg", at=m)
                inc(SC.activation(betac[:, NH * i:NH * (i + 1)], PS[4][:, 0:NH], AF.Sigmoid), "act")
                SC.copy(evC[:, 0:512], PS[2][:, 0:512])
                inc(SC.copy(fcast[:, 0:512], PS[3][:, 0:512]), "act")
                m = snap(); wait(G, "act", at=m)
                inc(G.dma_start(out=dr["bid"][128 * i:128 * (i + 1), 0:512], in_=evC[:, 0:512]), "sg", 16)
                inc(G.dma_start(out=dr["bid"][128 * i:128 * (i + 1), 512:1024], in_=fcast[:, 0:512]), "sg", 16)
                m = snap(); wait(TE, "act", at=m)
            for fc in range(8):
                ps_q = PS[fc % 2]
                m = snap(); wait(TE, "act", at=m)
                for kc in range(8):
                    inc(TE.matmul(ps_q[:, 0:512],
                                  GW1R[:, 1024 * kc + 128 * fc:1024 * kc + 128 * (fc + 1)],
                                  xnTblk3[:, 512 * kc:512 * (kc + 1)],
                                  start=(kc == 0), stop=(kc == 7)), "pe")
                m = snap(); wait(SC, "pe", "sg", at=m)
                inc(SC.copy(evC[:, 0:512], ps_q[:, 0:512]), "act")
                m = snap(); wait(G, "act", at=m)
                inc(G.dma_start(out=dr["bgt"][128 * fc:128 * (fc + 1), 512 * blk_g:512 * (blk_g + 1)],
                                in_=evC[:, 0:512]), "sg", 16)

        ca2 = Carve(AB1)
        oW = ca2.take(8 * 1024)
        qfb = ca2.take(1024)
        kfb = ca2.take(1024)
        vfb = ca2.take(1024)
        qtb = ca2.take(1024)
        ktb = ca2.take(1024)
        vtb = ca2.take(1024)
        kbt = ca2.take(1024)
        vbt = ca2.take(1024)
        lstb = ca2.take(1024)
        lltb = ca2.take(1024)
        dotb = ca2.take(1024)
        idtb = ca2.take(1024)
        gtb = ca2.take(1024)
        gelb = ca2.take(1024)
        otb = ca2.take(1024)
        ofb = ca2.take(1024)
        statb = ca2.take(512)
        fstg = ca2.take(128)
        fstg2 = ca2.take(1024)
        Sbf_ = [ca2.take(512) for _ in range(2 * NH)]
        dl = {n: ca2.take(c) for n, c in
              [("L", 128), ("LT", 128), ("T", 128), ("TT", 128), ("P2", 128),
               ("P2T", 128), ("u", 256), ("uadj", 256), ("wTT", 256), ("attnT", 128),
               ("kbf", 256)]}
        probs_f = AFX[:, 4120:4136]
        Sf_ = [AFX[:, 512 * k:512 * (k + 1)] for k in range(8)]
        stat_sc = AFX[:, 4136:4200]
        b2fs = AFX[:, 4200:4712]
        b2fs2 = AFX[:, 4712:5120]

        load_w_chunks("bWo", oW, 1024, 8)
        mWo = snap()
        for sq_ in range(cfg.nseq):
            m = snap(); wait(V, "act", "pool", "pe", at=m)
            for k in range(NH):
                V.memset(Sf_[2 * k][:, :], 0.0)
                V.memset(Sf_[2 * k + 1][:, :], 0.0)
                inc(V.memset(Sbf_[k][:, :], 0.0), "dve")
            for sg_i in range(cfg.NTS):
                i = sq_ * cfg.NTS + sg_i
                trow0 = 128 * i
                m = snap()
                wait(SY, "pe", "dve", at=m)
                inc(SY.dma_start(out=qfb[:, 0:1024],
                                 in_=dr["bq"].rearrange("(c p) t -> p c t", p=128)
                                 [:, :, trow0:trow0 + 128]), "sq", 16)
                inc(SY.dma_start(out=kfb[:, 0:1024],
                                 in_=dr["bk"].rearrange("(c p) t -> p c t", p=128)
                                 [:, :, trow0:trow0 + 128]), "sq", 16)
                inc(SY.dma_start(out=gtb[:, 0:1024],
                                 in_=dr["bgt"].rearrange("(c p) t -> p c t", p=128)
                                 [:, :, trow0:trow0 + 128]), "sq", 16)
                inc(SY.dma_start(out=idtb[:, 0:1024], in_=dr["bid"][trow0:trow0 + 128, :]), "sq", 16)
                inc(SY.dma_start(out=vfb[:, 0:1024],
                                 in_=dr["bv"].rearrange("(c p) t -> p c t", p=128)
                                 [:, :, trow0:trow0 + 128]), "sq", 16)
                mLd = snap()
                wait(TE, "sq", at=mLd)
                m = snap(); wait(TE, "dve", at=m)
                for d8 in range(8):
                    inc(TE.transpose(PB[0][:, 128 * d8:128 * (d8 + 1)],
                                     qfb[:, 128 * d8:128 * (d8 + 1)], identt[:]), "pe")
                m = snap(); wait(V, "pe", at=m)
                for d8 in range(8):
                    inc(V.tensor_copy(qtb[:, 128 * d8:128 * (d8 + 1)],
                                      PB[0][:, 128 * d8:128 * (d8 + 1)]), "dve")
                m = snap(); wait(TE, "dve", at=m)
                for d8 in range(8):
                    inc(TE.transpose(PB[0][:, 128 * d8:128 * (d8 + 1)],
                                     kfb[:, 128 * d8:128 * (d8 + 1)], identt[:]), "pe")
                m = snap(); wait(V, "pe", at=m)
                for d8 in range(8):
                    inc(V.tensor_copy(ktb[:, 128 * d8:128 * (d8 + 1)],
                                      PB[0][:, 128 * d8:128 * (d8 + 1)]), "dve")
                m = snap(); wait(TE, "dve", at=m)
                for d8 in range(8):
                    inc(TE.transpose(PB[0][:, 128 * d8:128 * (d8 + 1)],
                                     vfb[:, 128 * d8:128 * (d8 + 1)], identt[:]), "pe")
                m = snap(); wait(V, "pe", at=m)
                for d8 in range(8):
                    inc(V.tensor_copy(vtb[:, 128 * d8:128 * (d8 + 1)],
                                      PB[0][:, 128 * d8:128 * (d8 + 1)]), "dve")
                for tb, fb in [(qtb, qfb), (ktb, kfb)]:
                    for h in range(NH):
                        m = snap(); wait(SC, "dve", at=m)
                        inc(SC.activation(b2fs[:, 0:256], tb[:, 256 * h:256 * (h + 1)],
                                          AF.Square, accum_out=stat_sc[:, h:h + 1]), "act")
                    m = snap(); wait(SC, "act", at=m)
                    inc(SC.activation(stat_sc[:, 4:8], stat_sc[:, 0:4], AF.Sqrt, bias=epsc[:, 1:2]), "act")
                    m = snap(); wait(V, "act", at=m)
                    inc(V.reciprocal(stat_sc[:, 8:12], stat_sc[:, 4:8]), "dve")
                    m = snap(); wait(V, "dve", at=m)
                    for h in range(NH):
                        inc(V.tensor_scalar(tb[:, 256 * h:256 * (h + 1)],
                                            tb[:, 256 * h:256 * (h + 1)],
                                            stat_sc[:, 8 + h:9 + h], 0.0, ALU.mult, ALU.add), "dve")
                    m = snap(); wait(TE, "dve", at=m)
                    for d8 in range(8):
                        inc(TE.transpose(PB[0][:, 128 * d8:128 * (d8 + 1)],
                                         tb[:, 128 * d8:128 * (d8 + 1)], identt[:]), "pe")
                    m = snap(); wait(V, "pe", at=m)
                    for d8 in range(8):
                        inc(V.tensor_copy(fb[:, 128 * d8:128 * (d8 + 1)],
                                          PB[0][:, 128 * d8:128 * (d8 + 1)]), "dve")
                m = snap(); wait(V, "dve", at=m)
                for h in range(NH):
                    V.tensor_scalar(kbt[:, 256 * h:256 * (h + 1)], ktb[:, 256 * h:256 * (h + 1)],
                                    betac[:, NH * i + h:NH * i + h + 1], 0.0, ALU.mult, ALU.add)
                    inc(V.tensor_scalar(vbt[:, 256 * h:256 * (h + 1)], vtb[:, 256 * h:256 * (h + 1)],
                                        betac[:, NH * i + h:NH * i + h + 1], 0.0, ALU.mult, ALU.add), "dve")
                for (dspill, dstt) in [(dr["bls"], lstb), (dr["bll"], lltb)]:
                    m = snap(); wait(SY, "pe", "dve", at=m)
                    inc(SY.dma_start(out=fstg2[:, 0:1024],
                                     in_=dspill.rearrange("(c p) t -> p c t", p=128)
                                     [:, :, trow0:trow0 + 128]), "sq", 16)
                    m = snap(); wait(TE, "sq", "dve", at=m)
                    for d8 in range(8):
                        inc(TE.transpose(PB[1][:, 128 * d8:128 * (d8 + 1)],
                                         fstg2[:, 128 * d8:128 * (d8 + 1)], identt[:]), "pe")
                    m = snap(); wait(V, "pe", at=m)
                    for d8 in range(8):
                        inc(V.tensor_copy(dstt[:, 128 * d8:128 * (d8 + 1)],
                                          PB[1][:, 128 * d8:128 * (d8 + 1)]), "dve")
                for h in range(NH):
                    Sfh = (Sf_[2 * h], Sf_[2 * h + 1])
                    Sbh = Sbf_[h]
                    dk0 = 256 * h
                    m = snap(); wait(TE, "dve", at=m)
                    for d2 in range(2):
                        inc(TE.transpose(PB[0][:, 128 * d2:128 * (d2 + 1)],
                                         kbt[:, dk0 + 128 * d2:dk0 + 128 * (d2 + 1)], identt[:]), "pe")
                    m = snap(); wait(V, "pe", at=m)
                    inc(V.tensor_copy(dl["kbf"][:, 0:256], PB[0][:, 0:256]), "dve")
                    m = snap(); wait(TE, "dve", at=m)
                    for d2 in range(2):
                        inc(TE.matmul(PS[0][:, 0:128],
                                      dl["kbf"][:, 128 * d2:128 * (d2 + 1)],
                                      kfb[:, 256 * h + 128 * d2:256 * h + 128 * (d2 + 1)],
                                      start=(d2 == 0), stop=(d2 == 1)), "pe")
                    m = snap(); wait(SC, "pe", at=m)
                    inc(SC.mul(dl["L"][:, :], PS[0][:, 0:128], -1.0), "act")
                    m = snap(); wait(G, "act", at=m)
                    inc(G.affine_select(dl["L"][:, :], dl["L"][:, :], [[-1, 128]], ALU.is_ge,
                                        0.0, base=-1, channel_multiplier=1), "pool")
                    m = snap(); wait(V, "pool", at=m)
                    inc(V.memset(dl["T"][0:64, 64:128], 0.0), "dve")
                    for hf_ in range(2):
                        b0 = 64 * hf_
                        slb = slice(b0, b0 + 64)
                        idb = identt[slb, slb]
                        m = snap(); wait(V, "pool" if hf_ == 0 else "act", at=m)
                        inc(V.tensor_add(dl["T"][slb, b0:b0 + 64],
                                         dl["L"][slb, b0:b0 + 64], idb), "dve")
                        inc(V.tensor_copy(dl["P2"][slb, 0:64], dl["L"][slb, b0:b0 + 64]), "dve")
                        m = snap(); wait(TE, "dve", at=m)
                        inc(TE.transpose(PB[0][0:64, 0:64], dl["P2"][slb, 0:64], idb), "pe")
                        m = snap(); wait(V, "pe", at=m)
                        inc(V.tensor_copy(dl["P2T"][slb, 0:64], PB[0][0:64, 0:64]), "dve")
                        for st_ in range(5):
                            m = snap(); wait(TE, "dve", at=m)
                            inc(TE.matmul(PS[1][0:64, 0:64], dl["P2T"][slb, 0:64],
                                          dl["P2"][slb, 0:64], start=True, stop=True), "pe")
                            m = snap(); wait(SC, "pe", at=m)
                            inc(SC.copy(dl["P2"][slb, 0:64], PS[1][0:64, 0:64]), "act")
                            m = snap(); wait(TE, "act", at=m)
                            inc(TE.transpose(PB[0][0:64, 0:64], dl["P2"][slb, 0:64], idb), "pe")
                            m = snap(); wait(V, "pe", at=m)
                            inc(V.tensor_copy(dl["P2T"][slb, 0:64], PB[0][0:64, 0:64]), "dve")
                            m = snap(); wait(TE, "dve", at=m)
                            inc(TE.matmul(PS[2][0:64, 0:64], dl["P2T"][slb, 0:64],
                                          dl["T"][slb, b0:b0 + 64], start=True, stop=False), "pe")
                            inc(TE.matmul(PS[2][0:64, 0:64], idb,
                                          dl["T"][slb, b0:b0 + 64], start=False, stop=True), "pe")
                            m = snap(); wait(SC, "pe", at=m)
                            inc(SC.copy(dl["T"][slb, b0:b0 + 64], PS[2][0:64, 0:64]), "act")
                            m = snap(); wait(TE, "act", at=m)
                    idb1 = identt[64:128, 64:128]
                    inc(TE.transpose(PB[0][0:64, 0:64], dl["L"][64:128, 0:64], idb1), "pe")
                    m = snap(); wait(V, "pe", at=m)
                    inc(V.tensor_copy(dl["P2"][0:64, 0:64], PB[0][0:64, 0:64]), "dve")
                    m = snap(); wait(TE, "dve", at=m)
                    inc(TE.matmul(PS[1][0:64, 0:64], dl["P2"][0:64, 0:64],
                                  dl["T"][0:64, 0:64], start=True, stop=True), "pe")
                    m = snap(); wait(SC, "pe", at=m)
                    inc(SC.copy(dl["P2T"][0:64, 0:64], PS[1][0:64, 0:64]), "act")
                    m = snap(); wait(TE, "act", at=m)
                    inc(TE.transpose(PB[0][0:64, 0:64], dl["T"][64:128, 64:128], idb1), "pe")
                    m = snap(); wait(V, "pe", at=m)
                    inc(V.tensor_copy(dl["LT"][0:64, 0:64], PB[0][0:64, 0:64]), "dve")
                    m = snap(); wait(TE, "dve", at=m)
                    inc(TE.matmul(PS[2][0:64, 0:64], dl["LT"][0:64, 0:64],
                                  dl["P2T"][0:64, 0:64], start=True, stop=True), "pe")
                    m = snap(); wait(SC, "pe", at=m)
                    inc(SC.copy(dl["T"][64:128, 0:64], PS[2][0:64, 0:64]), "act")
                    m = snap(); wait(TE, "act", "dve", at=m)
                    inc(TE.transpose(PB[0][:, 0:128], dl["T"][:, :], identt[:]), "pe")
                    m = snap(); wait(V, "pe", at=m)
                    inc(V.tensor_copy(dl["TT"][:, :], PB[0][:, 0:128]), "dve")
                    if getattr(cfg, "dbg_b", -1) == i and h == 0:
                        inc(G.dma_start(out=dr["dbg"][:, 0:128], in_=dl["T"][:, :]), "sg", 16)
                        inc(G.dma_start(out=dr["dbg"][:, 128:256], in_=dl["L"][:, :]), "sg", 16)
                        inc(G.dma_start(out=dr["dbg"][:, 256:512], in_=dl["kbf"][:, 0:256]), "sg", 16)
                        inc(G.dma_start(out=dr["dbg"][:, 1152:1408], in_=kfb[:, 0:256]), "sg", 16)
                        inc(G.dma_start(out=dr["dbg"][:, 1408:1664], in_=qfb[:, 0:256]), "sg", 16)
                        m2 = snap(); wait(SC, "sg", at=m2); wait(V, "sg", at=m2)
                    m = snap(); wait(TE, "dve", at=m)
                    inc(TE.matmul(PS[0][:, 0:256], dl["TT"][:, :],
                                  vbt[:, 256 * h:256 * (h + 1)], start=True, stop=True), "pe")
                    for d2 in range(2):
                        inc(TE.matmul(PS[1][:, 128 * d2:128 * (d2 + 1)],
                                      kbt[:, dk0 + 128 * d2:dk0 + 128 * (d2 + 1)],
                                      dl["TT"][:, :], start=True, stop=True), "pe")
                    m = snap(); wait(SC, "pe", at=m)
                    inc(SC.copy(dl["u"][:, 0:256], PS[0][:, 0:256]), "act")
                    inc(SC.copy(dl["wTT"][:, 0:256], PS[1][:, 0:256]), "act")
                    m = snap(); wait(TE, "act", at=m)
                    for d2 in range(2):
                        inc(TE.matmul(PS[2][:, 0:128],
                                      kfb[:, 256 * h + 128 * d2:256 * h + 128 * (d2 + 1)],
                                      qfb[:, 256 * h + 128 * d2:256 * h + 128 * (d2 + 1)],
                                      start=(d2 == 0), stop=(d2 == 1)), "pe")
                    m = snap(); wait(SC, "pe", at=m)
                    inc(SC.copy(dl["attnT"][:, :], PS[2][:, 0:128]), "act")
                    m = snap(); wait(G, "act", at=m)
                    inc(G.affine_select(dl["attnT"][:, :], dl["attnT"][:, :], [[1, 128]],
                                        ALU.is_ge, 0.0, base=0, channel_multiplier=-1), "pool")
                    m = snap(); wait(TE, "pool", "dve", at=m)
                    for d2 in range(2):
                        inc(TE.matmul(PS[3][:, 0:256], dl["wTT"][:, 128 * d2:128 * (d2 + 1)],
                                      Sbh[:, 256 * d2:256 * (d2 + 1)],
                                      start=(d2 == 0), stop=(d2 == 1)), "pe")
                    m = snap(); wait(V, "pe", at=m)
                    inc(V.tensor_sub(b2fs[:, 0:256], dl["u"][:, 0:256], PS[3][:, 0:256]), "dve")
                    m = snap(); wait(SC, "dve", at=m)
                    inc(SC.copy(dl["uadj"][:, 0:256], b2fs[:, 0:256]), "act")
                    m = snap(); wait(TE, "act", at=m)
                    for d2 in range(2):
                        inc(TE.matmul(PS[0][:, 0:256],
                                      qfb[:, 256 * h + 128 * d2:256 * h + 128 * (d2 + 1)],
                                      Sbh[:, 256 * d2:256 * (d2 + 1)],
                                      start=(d2 == 0), stop=False), "pe")
                    inc(TE.matmul(PS[0][:, 0:256], dl["attnT"][:, :], dl["uadj"][:, 0:256],
                                  start=False, stop=True), "pe")
                    for d2 in range(2):
                        inc(TE.matmul(PS[1 + d2][:, 0:256],
                                      ktb[:, dk0 + 128 * d2:dk0 + 128 * (d2 + 1)],
                                      dl["uadj"][:, 0:256], start=True, stop=True), "pe")
                    m = snap(); wait(V, "pe", at=m)
                    if getattr(cfg, "dbg_b", -1) == i and h == 0:
                        inc(V.tensor_copy(dl["LT"][:, :], dl["attnT"][:, :]), "dve")
                        m2 = snap(); wait(G, "dve", at=m2)
                        inc(G.dma_start(out=dr["dbg"][:, 512:640], in_=dl["LT"][:, :]), "sg", 16)
                        inc(G.dma_start(out=dr["dbg"][:, 640:896], in_=dl["u"][:, 0:256]), "sg", 16)
                        inc(G.dma_start(out=dr["dbg"][:, 896:1152], in_=dl["uadj"][:, 0:256]), "sg", 16)
                        m2 = snap(); wait(SC, "sg", at=m2); wait(V, "sg", at=m2)
                    inc(V.tensor_copy(dotb[:, 256 * h:256 * (h + 1)], PS[0][:, 0:256]), "dve")
                    for d2 in range(2):
                        inc(V.tensor_tensor(Sfh[d2][:, 0:256], Sfh[d2][:, 0:256],
                                            PS[1 + d2][:, 0:256], ALU.add), "dve")
                    m = snap(); wait(G, "dve", at=m)
                    for d2 in range(2):
                        inc(G.tensor_copy(Sbh[:, 256 * d2:256 * (d2 + 1)],
                                          Sfh[d2][:, 0:256]), "pool")
                    m = snap(); wait(TE, "pool", at=m)
                tensors = [(lstb, 0), (lltb, 4), (dotb, 8), (vtb, 12)]
                for (tt, so) in tensors:
                    for h in range(NH):
                        sl = tt[:, 256 * h:256 * (h + 1)]
                        m = snap(); wait(V, "act", at=m)
                        inc(V.reduce_sum(stat_sc[:, 16:17], sl, axis=AX.X), "dve")
                        inc(V.reduce_sum(stat_sc[:, 17:18], sl, axis=AX.X,
                                         apply_absolute_value=True), "dve")
                        m = snap(); wait(SC, "dve", at=m)
                        inc(SC.activation(b2fs[:, 0:256], sl, AF.Square,
                                          accum_out=stat_sc[:, 18:19]), "act")
                        m = snap(); wait(V, "act", at=m)
                        V.tensor_scalar(stat_sc[:, 20:21], stat_sc[:, 16:17], 1.0 / 256.0,
                                        0.0, ALU.mult, ALU.add)
                        inc(V.tensor_scalar(stat_sc[:, 22:23], stat_sc[:, 17:18], 1.0 / 256.0,
                                            0.0, ALU.mult, ALU.add), "dve")
                        m = snap(); wait(V, "dve", at=m)
                        V.tensor_scalar(stat_sc[:, 24:25], stat_sc[:, 18:19], 1.0 / 256.0,
                                        0.0, ALU.mult, ALU.add)
                        inc(V.tensor_tensor(stat_sc[:, 25:26], stat_sc[:, 20:21],
                                            stat_sc[:, 20:21], ALU.mult), "dve")
                        m = snap(); wait(V, "dve", at=m)
                        inc(V.tensor_sub(stat_sc[:, 21:22], stat_sc[:, 24:25],
                                         stat_sc[:, 25:26]), "dve")
                        m = snap(); wait(SC, "dve", at=m)
                        inc(SC.activation(stat_sc[:, 23:24], stat_sc[:, 18:19], AF.Sqrt), "act")
                        m = snap(); wait(V, "act", at=m)
                        inc(V.tensor_copy(stat_sc[:, 28:32], stat_sc[:, 20:24]), "dve")
                        m = snap(); wait(V, "dve", at=m)
                        inc(V.tensor_copy(b2fs2[:, 16 * h + so:16 * h + so + 4],
                                          stat_sc[:, 28:32]), "dve")
                m = snap(); wait(TE, "dve", at=m)
                for h in range(NH):
                    inc(TE.transpose(PS[4][0:16, 0:128], b2fs2[:, 16 * h:16 * (h + 1)],
                                     identf[:]), "pe")
                    m = snap(); wait(SC, "pe", at=m)
                    inc(SC.copy(statb[0:16, 128 * h:128 * (h + 1)], PS[4][0:16, 0:128]), "act")
                    m = snap(); wait(TE, "act", at=m)
                for h in range(NH):
                    for hc in range(8):
                        ps_g = PS[hc % 2]
                        inc(TE.matmul(ps_g[:, 0:128],
                                      gW1b[:, 128 * hc:128 * (hc + 1)],
                                      statb[0:16, 128 * h:128 * (h + 1)],
                                      start=True, stop=True), "pe")
                        m = snap(); wait(V, "pe", at=m)
                        inc(V.scalar_tensor_tensor(b2fs[:, 0:128], ps_g[:, 0:128],
                                                   gb1c[:, hc:hc + 1],
                                                   gtb[:, 128 * hc:128 * (hc + 1)],
                                                   ALU.add, ALU.add), "dve")
                        m = snap(); wait(SC, "dve", at=m)
                        inc(SC.activation(gelb[:, 128 * hc:128 * (hc + 1)], b2fs[:, 0:128],
                                          AF.Gelu), "act")
                        m = snap(); wait(TE, "act", at=m)
                    for hc in range(8):
                        inc(TE.matmul(PS[2][:, 0:NH], gelb[:, 128 * hc:128 * (hc + 1)],
                                      gW2c[:, NH * hc:NH * (hc + 1)],
                                      start=(hc == 0), stop=(hc == 7)), "pe")
                    m = snap(); wait(V, "pe", at=m)
                    inc(V.tensor_tensor(stat_sc[:, 0:4], PS[2][:, 0:NH],
                                        bcmisc[:, 12:16], ALU.add), "dve")
                    m = snap(); wait(V, "dve", at=m)
                    inc(V.tensor_tensor(stat_sc[:, 4:8], stat_sc[:, 0:4],
                                        bc16[:, 4 * h:4 * (h + 1)], ALU.mult), "dve")
                    m = snap(); wait(V, "dve", at=m)
                    inc(V.reduce_max(stat_sc[:, 8:9], stat_sc[:, 4:8], axis=AX.X), "dve")
                    m = snap(); wait(V, "dve", at=m)
                    inc(V.tensor_scalar(stat_sc[:, 9:10], stat_sc[:, 8:9], -1.0, 0.0,
                                        ALU.mult, ALU.add), "dve")
                    m = snap(); wait(SC, "dve", at=m)
                    inc(SC.activation(stat_sc[:, 12:16], stat_sc[:, 4:8], AF.Exp,
                                      bias=stat_sc[:, 9:10], accum_out=stat_sc[:, 10:11]), "act")
                    m = snap(); wait(V, "act", at=m)
                    inc(V.reciprocal(stat_sc[:, 11:12], stat_sc[:, 10:11]), "dve")
                    m = snap(); wait(V, "dve", at=m)
                    inc(V.tensor_scalar(stat_sc[:, 12:16], stat_sc[:, 12:16],
                                        stat_sc[:, 11:12], bcmisc[:, h:h + 1],
                                        ALU.mult, ALU.mult), "dve")
                    m = snap(); wait(V, "dve", at=m)
                    inc(V.tensor_tensor(probs_f[:, 4 * h:4 * (h + 1)], stat_sc[:, 12:16],
                                        bcfl[:, 4 * h:4 * (h + 1)], ALU.add), "dve")
                m = snap(); wait(V, "dve", at=m)
                for h in range(NH):
                    inc(V.tensor_scalar(stat_sc[:, 40 + h:41 + h], probs_f[:, 4 * h:4 * h + 1],
                                        bcmisc[:, 4 + h:5 + h], 0.0, ALU.add, ALU.add), "dve")
                for h in range(NH):
                    sl = slice(256 * h, 256 * (h + 1))
                    m = snap(); wait(V, "dve", at=m)
                    inc(V.tensor_scalar(b2fs[:, 0:256], lstb[:, sl], stat_sc[:, 40 + h:41 + h],
                                        0.0, ALU.mult, ALU.add), "dve")
                    inc(V.scalar_tensor_tensor(b2fs[:, 0:256], lltb[:, sl],
                                               probs_f[:, 4 * h + 1:4 * h + 2],
                                               b2fs[:, 0:256], ALU.mult, ALU.add), "dve")
                    inc(V.scalar_tensor_tensor(b2fs[:, 0:256], dotb[:, sl],
                                               probs_f[:, 4 * h + 2:4 * h + 3],
                                               b2fs[:, 0:256], ALU.mult, ALU.add), "dve")
                    inc(V.scalar_tensor_tensor(b2fs[:, 0:256], vtb[:, sl],
                                               probs_f[:, 4 * h + 3:4 * h + 4],
                                               b2fs[:, 0:256], ALU.mult, ALU.add), "dve")
                    inc(V.scalar_tensor_tensor(b2fs[:, 0:256], idtb[:, sl],
                                               bcmisc[:, 8 + h:9 + h],
                                               b2fs[:, 0:256], ALU.mult, ALU.add), "dve")
                    m = snap(); wait(SC, "dve", at=m)
                    inc(SC.activation(b2fs2[:, 0:256], b2fs[:, 0:256], AF.Square,
                                      accum_out=stat_sc[:, 48:49]), "act")
                    m = snap(); wait(V, "act", at=m)
                    inc(V.tensor_scalar(stat_sc[:, 49:50], stat_sc[:, 48:49], 1.0 / 256.0,
                                        0.0, ALU.mult, ALU.add), "dve")
                    m = snap(); wait(SC, "dve", at=m)
                    inc(SC.activation(stat_sc[:, 50:51], stat_sc[:, 49:50], AF.Sqrt,
                                      bias=epsc[:, 0:1]), "act")
                    m = snap(); wait(V, "act", at=m)
                    inc(V.reciprocal(stat_sc[:, 51:52], stat_sc[:, 50:51]), "dve")
                    m = snap(); wait(V, "dve", at=m)
                    inc(V.tensor_scalar(b2fs[:, 0:256], b2fs[:, 0:256], stat_sc[:, 51:52],
                                        0.0, ALU.mult, ALU.add), "dve")
                    m = snap(); wait(V, "dve", at=m)
                    inc(V.tensor_tensor(otb[:, sl], b2fs[:, 0:256], onwb[:, sl], ALU.mult), "dve")
                if getattr(cfg, "dbg_b", -1) == i:
                    for j_, tt_ in enumerate([qtb, ktb, vtb, kbt, vbt, lstb, lltb,
                                              dotb, idtb, otb, gelb, gtb, qfb, kfb]):
                        inc(G.dma_start(out=dr["dbg2"][:, 1024 * j_:1024 * (j_ + 1)],
                                        in_=tt_[:, 0:1024]), "sg", 16)
                m = snap(); wait(TE, "dve", at=m)
                for d8 in range(8):
                    inc(TE.transpose(PB[0][:, 128 * d8:128 * (d8 + 1)],
                                     otb[:, 128 * d8:128 * (d8 + 1)], identt[:]), "pe")
                m = snap(); wait(V, "pe", at=m)
                for d8 in range(8):
                    inc(V.tensor_copy(ofb[:, 128 * d8:128 * (d8 + 1)],
                                      PB[0][:, 128 * d8:128 * (d8 + 1)]), "dve")
                if sq_ == 0 and sg_i == 0:
                    wait(TE, "act", at=mWo)
                m = snap(); wait(TE, "dve", at=m)
                for nh2 in range(2):
                    ps_o = PS[nh2]
                    for kc in range(8):
                        inc(TE.matmul(ps_o[:, 0:512],
                                      ofb[:, 128 * kc:128 * (kc + 1)],
                                      oW[:, 1024 * kc + 512 * nh2:1024 * kc + 512 * (nh2 + 1)],
                                      start=(kc == 0), stop=(kc == 7)), "pe")
                m = snap(); wait(V, "pe", at=m)
                for nh2 in range(2):
                    inc(V.tensor_tensor(stream[:, D * i + 512 * nh2:D * i + 512 * (nh2 + 1)],
                                        stream[:, D * i + 512 * nh2:D * i + 512 * (nh2 + 1)],
                                        PS[nh2][:, 0:512], ALU.add), "dve")
                m = snap(); wait(TE, "dve", at=m)

    stages = getattr(cfg, "stages", ("a", "f1", "b", "f2"))
    for st_name in stages:
        if st_name == "a":
            emit_attn_a2()
        elif st_name == "f1":
            emit_ffn(final=False)
        elif st_name == "b":
            emit_attn_b()
        elif st_name == "f2":
            emit_ffn(final=True)
    if stages[-1] != "f2":
        for i in range(cfg.NT):
            m = snap(); wait(SC, "out", "dve", at=m)
            inc(SC.copy(fsc[:, 0:1024], stream[:, D * i:D * (i + 1)]), "act")
            m = snap(); wait(G, "act", at=m)
            inc(G.dma_start(out=out[128 * i:128 * (i + 1), :], in_=fsc[:, 0:1024]), "out", 16)

    m = snap()
    wait(G, "out", at=m)
    wait(G, "sq", "sg", "dve", "act", "pe", "pool", at=m)

    for en in ["sync", "gpsimd", "vector", "scalar", "tensor"]:
        fns = EBUF[en]
        getattr(blk, en)(lambda e, fns=fns: [f(e) for f in fns])

    blk_t.__exit__(None, None, None)
    for t in reversed(ectx[:-1]):
        try:
            t.__exit__(None, None, None)
        except Exception:
            pass
    return nc


def kernel(**inputs):
    cfg = CFG()
    consts = host_consts(cfg)
    nc = build(cfg)
    x = np.ascontiguousarray(inputs["L0_attn_a_x"], dtype=np.float32)
    namemap = {
        "wq": "L0_attn_a_wq", "wk": "L0_attn_a_wk", "wv": "L0_attn_a_wv",
        "wgq": "L0_attn_a_wgq", "wgk": "L0_attn_a_wgk", "wgv": "L0_attn_a_wgv",
        "r_out_w": "L0_attn_a_r_out_w", "glw": "L0_attn_a_gate_local_w",
        "grw": "L0_attn_a_gate_remote_w", "w_out": "L0_attn_a_w_out",
        "bWq": "L0_attn_b_Wq", "bWk": "L0_attn_b_Wk", "bWv": "L0_attn_b_Wv",
        "cwq": "L0_attn_b_cwq", "cwk": "L0_attn_b_cwk", "cwv": "L0_attn_b_cwv",
        "bWb": "L0_attn_b_Wb", "bWid": "L0_attn_b_Wid", "alpha_id": "L0_attn_b_alpha_id",
        "firs": "L0_attn_b_firs", "firl": "L0_attn_b_firl",
        "gW1": "L0_attn_b_gW1", "gb1": "L0_attn_b_gb1", "gW2": "L0_attn_b_gW2",
        "gb2": "L0_attn_b_gb2", "log_temp": "L0_attn_b_log_temp",
        "floor_param": "L0_attn_b_floor_param", "crl": "L0_attn_b_conv_res_logit",
        "onw": "L0_attn_b_o_norm_w", "bWo": "L0_attn_b_Wo",
        "w1": "L0_ffn_w1", "b1": "L0_ffn_b1", "w2": "L0_ffn_w2", "b2": "L0_ffn_b2",
    }
    shared = {k: np.ascontiguousarray(inputs[v], dtype=np.float32) for k, v in namemap.items()}
    shared.update(consts)
    in_maps = []
    for c in range(8):
        mm = dict(shared)
        mm["x"] = np.ascontiguousarray(x[2 * c:2 * c + 2].reshape(cfg.T, cfg.D))
        in_maps.append(mm)
    res = run_bass_kernel_spmd(nc, in_maps, core_ids=list(range(8)))
    outs = [res.results[c]["out"].reshape(2, cfg.S, cfg.D) for c in range(8)]
    return np.concatenate(outs, axis=0)


# version 03
# speedup vs baseline: 1.1985x; 1.1985x over previous
import numpy as np
import ml_dtypes
import concourse.bass as bass
import concourse.mybir as mybir
from concourse.bass_utils import run_bass_kernel_spmd

F32 = mybir.dt.float32
BF16 = mybir.dt.bfloat16
U8 = mybir.dt.uint8
AF = mybir.ActivationFunctionType
ALU = mybir.AluOpType
AX = mybir.AxisListType
BF_NP = ml_dtypes.bfloat16


class CFG:
    def __init__(self, nseq=2, S=2048, ncc=8):
        self.nseq, self.S, self.D = nseq, S, 1024
        self.T = nseq * S
        self.NT = self.T // 128
        self.NTS = S // 128
        self.W = 256
        self.NB = S // self.W
        self.NLAT = 64
        self.TPL = S // self.NLAT
        self.F = 4096
        self.NH = 4
        self.GH = 1024
        self.GTS = min(8, self.NTS)
        self.LPG = self.GTS * 128 // self.TPL
        self.cc_group = list(range(ncc))


_AUTOSEM = {"vector": "dve", "scalar": "act", "gpsimd": "pool", "tensor": "pe"}
_NOWRAP = {"wait_ge", "sem_inc"}
_NOINC = {"dma_start", "indirect_dma_start", "collective_compute", "dma_start_transpose"}


class InstProxy:
    __slots__ = ("q",)
    def __init__(self):
        self.q = []
    def then_inc(self, semh, n=1):
        self.q.append((semh, n))
        return self


class EP:
    def __init__(self, blk, name, sem, cnt, buf):
        object.__setattr__(self, "_blk", blk)
        object.__setattr__(self, "_name", name)
        object.__setattr__(self, "_sem", sem)
        object.__setattr__(self, "_cnt", cnt)
        object.__setattr__(self, "_buf", buf)
    def __getattr__(self, attr):
        name, sem, cnt, buf = self._name, self._sem, self._cnt, self._buf
        own = _AUTOSEM.get(name)
        def call(*a, **k):
            gates = []
            selfwait = 0
            auto = False
            if attr in _NOINC:
                for s_ in ("dve", "act", "pool", "pe", "sq", "sg", "out"):
                    if cnt[s_] > 0:
                        gates.append((s_, cnt[s_]))
            elif own is not None and attr not in _NOWRAP:
                if name != "tensor":
                    selfwait = cnt[own]
                auto = True
                cnt[own] += 1
            px = InstProxy()
            def emit(e, attr=attr, a=a, k=k, gates=gates, selfwait=selfwait,
                     auto=auto, px=px):
                for s_, v_ in gates:
                    e.wait_ge(sem[s_], v_)
                ins = getattr(e, attr)(*a, **k)
                if selfwait > 0:
                    ins._wait_ge(sem[own], selfwait)
                if auto:
                    ins.then_inc(sem[own], 1)
                for semh, n_ in px.q:
                    ins.then_inc(semh, n_)
                return ins
            buf[name].append(emit)
            return px
        return call


def host_consts(cfg):
    c = {}
    c["ident"] = np.eye(128, dtype=BF_NP)
    i = np.arange(cfg.W)[:, None]
    j = np.arange(2 * cfg.W)[None, :]
    base = (j > i) & (j <= i + cfg.W)
    c["winm"] = base.astype(BF_NP)
    c["winm_first"] = (base & (j >= cfg.W)).astype(BF_NP)
    pp = np.zeros((cfg.GTS, 128, cfg.LPG), np.float32)
    for g in range(cfg.GTS):
        for ii in range(128):
            pp[g, ii, g * (128 // cfg.TPL) + ii // cfg.TPL] = 1.0 / cfg.TPL
    c["poolpat"] = pp.astype(BF_NP)
    return c


def build(cfg):
    nc = bass.Bass("TRN2", target_bir_lowering=False)
    T, S, D, F, NH = cfg.T, cfg.S, cfg.D, cfg.F, cfg.NH
    lpg = cfg.LPG

    ap = {}
    def di(name, shape, dt=F32):
        ap[name] = nc.dram_tensor(name, shape, dt, kind="ExternalInput").ap()
    di("x", [T, D])
    for n, sh in [("wq", [D, D]), ("wk", [D, D]), ("wv", [D, D]),
                  ("wgq", [D, 512]), ("wgk", [D, 512]), ("wgv", [D, 512]),
                  ("r_out_w", [512, D]), ("glw", [16, 64]), ("grw", [16, 64]),
                  ("w_out", [D, D]), ("bWq", [D, D]), ("bWk", [D, D]),
                  ("bWv", [D, D]), ("cwq", [D, 4]), ("cwk", [D, 4]),
                  ("cwv", [D, 4]), ("bWb", [D, NH]), ("bWid", [D, D]),
                  ("alpha_id", [NH]), ("firs", [NH, 256, 3]), ("firl", [NH, 256, 63]),
                  ("gW1", [D + 16, cfg.GH]), ("gb1", [cfg.GH]), ("gW2", [cfg.GH, NH]),
                  ("gb2", [NH]), ("log_temp", [NH, 4]), ("floor_param", [NH, 4]),
                  ("crl", [NH]), ("onw", [256]), ("bWo", [D, D]),
                  ("w1", [F, D]), ("b1", [F]), ("w2", [D, F]), ("b2", [D])]:
        di(n, sh)
    di("ident", [128, 128], BF16)
    di("winm", [256, 512], BF16)
    di("winm_first", [256, 512], BF16)
    di("poolpat", [cfg.GTS, 128, lpg], BF16)

    out = nc.dram_tensor("out", [T, D], F32, kind="ExternalOutput").ap()

    dr = {}
    dr["h"] = nc.dram_tensor("h_sp", [F, T], F32).ap()
    dr["w1iT"] = nc.dram_tensor("w1iT", [D, F], BF16).ap()
    dr["w2qT"] = nc.dram_tensor("w2qT", [F, D], BF16).ap()
    dr["aQ"] = nc.dram_tensor("aQ", [D, T], BF16).ap()
    dr["aK"] = nc.dram_tensor("aK", [D, T], BF16).ap()
    dr["aV"] = nc.dram_tensor("aV", [T, 16 * 65], BF16).ap()
    dr["aGQ"] = nc.dram_tensor("aGQ", [512, T], BF16).ap()
    dr["bq"] = nc.dram_tensor("bq", [D, T], BF16).ap()
    dr["bk"] = nc.dram_tensor("bk", [D, T], BF16).ap()
    dr["bv"] = nc.dram_tensor("bv", [D, T], BF16).ap()
    dr["bid"] = nc.dram_tensor("bid", [T, D], BF16).ap()
    dr["bgt"] = nc.dram_tensor("bgt", [cfg.GH, T], BF16).ap()
    dr["bls"] = nc.dram_tensor("bls", [D, T], BF16).ap()
    dr["bll"] = nc.dram_tensor("bll", [D, T], BF16).ap()
    dr["bounce"] = nc.dram_tensor("bounce", [128, 128], F32).ap()
    dr["dbg"] = nc.dram_tensor("dbg", [128, 4096], BF16).ap()
    dr["dbg2"] = nc.dram_tensor("dbg2", [128, 16384], BF16).ap()
    cc_in_t = nc.dram_tensor("cc_in", [128, 128], F32)
    cc_out_t = nc.dram_tensor("cc_out", [128, 128], F32)

    ectx = []
    def sb(name, p, f, dt=BF16):
        t = nc.sbuf_tensor(name, [p, f], dt)
        h = t.__enter__(); ectx.append(t); return h
    def psal(name, p, f, dt=F32):
        t = nc.psum_tensor(name, [p, f], dt)
        h = t.__enter__(); ectx.append(t); return h
    def mksem(name):
        t = nc.semaphore(name)
        h = t.__enter__(); ectx.append(t); return h

    stream = sb("stream", 128, cfg.NT * D)
    mu = sb("mu", 128, cfg.NT, F32)
    rstd = sb("rstd", 128, cfg.NT, F32)
    maxc = sb("maxc", 128, 4, F32)
    identt = sb("identt", 128, 128)
    identf = sb("identf", 128, 128, F32)
    onesf = sb("onesf", 1, 128, F32)
    winmt = sb("winmt", 128, 2 * 512)
    winft = sb("winft", 128, 2 * 512)
    poolt = sb("poolt", 128, cfg.GTS * lpg)
    g_exp = sb("g_exp", 128, 8 * 32)
    cw = sb("cw", 128, 8 * 12, F32)
    firsc = sb("firsc", 128, 8 * 3, F32)
    firlc = sb("firlc", 128, 8 * 63, F32)
    gb1c = sb("gb1c", 128, 8, F32)
    gW1b = sb("gW1b", 16, cfg.GH)
    gW2c = sb("gW2c", 128, 8 * NH)
    bc16 = sb("bc16", 128, 16, F32)
    bcfl = sb("bcfl", 128, 16, F32)
    bcmisc = sb("bcmisc", 128, 16, F32)
    onwb = sb("onwb", 128, NH * 256, F32)
    s_w1c = sb("s_w1c", 128, F // 128, F32)
    b1c = sb("b1c", 128, F // 128, F32)
    sw2r = sb("sw2r", 1, D, F32)
    b2r = sb("b2r", 1, D, F32)
    srow = sb("srow", 1, 128, F32)
    bq2b = sb("bq2b", 128, D, F32)
    sInv = sb("sInv", 128, 4, F32)
    ccst = sb("ccst", 128, 128, F32)
    ffnc = sb("ffnc", 128, 96, F32)
    epsc = sb("epsc", 128, 4, F32)
    betac = sb("betac", 128, cfg.NT * NH, F32)
    GKs = sb("GKs", 128, 4 * cfg.NLAT)
    GVs = sb("GVs", cfg.NLAT, 8 * 65)
    pooled = sb("pooled", cfg.NLAT, D)
    pooledT = sb("pooledT", 128, 8 * cfg.NLAT)

    AB1 = sb("AB1", 128, 32768)
    AB2 = sb("AB2", 128, 6144)
    AFX = sb("AFX", 128, 5120, F32)
    AU8 = sb("AU8", 128, 2048, U8)
    AI32 = sb("AI32", 128, 1024, mybir.dt.int32)

    class Carve:
        def __init__(self, base):
            self.base = base; self.off = 0
        def take(self, ncols):
            v = self.base[:, self.off:self.off + ncols]
            self.off += ncols
            assert self.off <= self.base.shape[1], f"arena overflow {self.off}"
            return v

    PS = [psal(f"ps{i}", 128, 512, F32) for i in range(6)]
    PB = [psal(f"pb{i}", 128, 1024, BF16) for i in range(2)]

    sem = {n: mksem(n) for n in ["sq", "sg", "pe", "dve", "act", "pool", "cc", "out"]}
    cnt = {n: 0 for n in sem}

    blk_t = nc.Block()
    blk = blk_t.__enter__(); ectx.append(blk_t)
    EBUF = {n: [] for n in ["vector", "scalar", "tensor", "gpsimd", "sync"]}
    V = EP(blk, "vector", sem, cnt, EBUF); SC = EP(blk, "scalar", sem, cnt, EBUF)
    TE = EP(blk, "tensor", sem, cnt, EBUF); G = EP(blk, "gpsimd", sem, cnt, EBUF)
    SY = EP(blk, "sync", sem, cnt, EBUF)

    _AUTO = {"dve", "act", "pool", "pe"}
    def inc(instr, s, n=1):
        if s in _AUTO:
            return instr
        instr.then_inc(sem[s], n)
        cnt[s] += n
        return instr

    def snap():
        return dict(cnt)

    def wait(eng, *names, at=None):
        own = _AUTOSEM.get(eng._name)
        if eng._name == "tensor":
            own = None
        for s in names:
            if s == own:
                continue
            v = cnt[s] if at is None else at[s]
            if v > 0:
                eng.wait_ge(sem[s], v)

    wstg = AFX[:, 0:1024]
    fsc = AFX[:, 1024:2048]
    fsc2 = AFX[:, 2048:3072]
    b_sc = AFX[:, 3072:3584]
    b_sc2 = AFX[:, 3584:4096]

    inc(SY.dma_start(out=identt[:], in_=ap["ident"]), "sq", 16)
    inc(SY.dma_start(out=winmt[:, 0:512], in_=ap["winm"][0:128, :]), "sq", 16)
    inc(SY.dma_start(out=winmt[:, 512:1024], in_=ap["winm"][128:256, :]), "sq", 16)
    inc(SY.dma_start(out=winft[:, 0:512], in_=ap["winm_first"][0:128, :]), "sq", 16)
    inc(SY.dma_start(out=winft[:, 512:1024], in_=ap["winm_first"][128:256, :]), "sq", 16)
    inc(SY.dma_start(out=poolt[:].rearrange("p (g l) -> p g l", g=cfg.GTS),
                     in_=ap["poolpat"].rearrange("g p l -> p g l")), "sq", 16)
    V.memset(onesf[:], 1.0)
    V.memset(maxc[:], 0.0)
    V.memset(epsc[:, 0:1], 1e-5)
    V.memset(epsc[:, 1:2], 1e-6)
    G.memset(identf[:], 1.0)
    inc(G.affine_select(identf[:], identf[:], [[-1, 128]], ALU.is_equal, 0.0,
                        base=0, channel_multiplier=1), "pool")
    for i in range(cfg.NT):
        m = snap(); wait(SY, "act", at=m)
        inc(SY.dma_start(out=wstg[:], in_=ap["x"][128 * i:128 * (i + 1), :]), "sq", 16)
        m = snap(); wait(SC, "sq", at=m)
        inc(SC.copy(stream[:, D * i:D * (i + 1)], wstg[:]), "act")

    for ti, nm in enumerate(["cwq", "cwk", "cwv"]):
        for c in range(8):
            inc(SY.dma_start(out=cw[:, (ti * 8 + c) * 4:(ti * 8 + c) * 4 + 4],
                             in_=ap[nm][128 * c:128 * (c + 1), :]), "sq", 16)
    for c in range(8):
        inc(SY.dma_start(out=firsc[:, c * 3:(c + 1) * 3],
                         in_=ap["firs"].rearrange("h d k -> (h d) k")[128 * c:128 * (c + 1), :]), "sq", 16)
        inc(SY.dma_start(out=firlc[:, c * 63:(c + 1) * 63],
                         in_=ap["firl"].rearrange("h d k -> (h d) k")[128 * c:128 * (c + 1), :]), "sq", 16)
    m = snap(); wait(SY, "act", at=m)
    inc(SY.dma_start(out=wstg[0:16, :], in_=ap["gW1"][D:D + 16, :]), "sq", 16)
    m = snap(); wait(SC, "sq", at=m)
    inc(SC.copy(gW1b[:, :], wstg[0:16, :]), "act")
    for c in range(8):
        m = snap(); wait(SY, "act", at=m)
        inc(SY.dma_start(out=wstg[:, 0:NH], in_=ap["gW2"][128 * c:128 * (c + 1), :]), "sq", 16)
        m = snap(); wait(SC, "sq", at=m)
        inc(SC.copy(gW2c[:, NH * c:NH * (c + 1)], wstg[:, 0:NH]), "act")
    m = snap(); wait(SY, "act", at=m)
    inc(SY.dma_start(out=wstg[0:8, 0:128], in_=ap["gb1"].rearrange("(a b) -> a b", b=128)), "sq", 16)
    inc(SY.dma_start(out=wstg[32:64, 0:128], in_=ap["b1"].rearrange("(a b) -> a b", b=128)), "sq", 16)
    inc(SY.dma_start(out=b2r[0:1, :], in_=ap["b2"][None, :]), "sq", 16)
    m = snap(); wait(TE, "sq", "pool", at=m)
    TE.transpose(PS[0][0:128, 0:8], wstg[0:8, 0:128], identf[0:8, 0:8])
    inc(TE.transpose(PS[1][0:128, 0:32], wstg[32:64, 0:128], identf[32:64, 32:64]), "pe")
    m = snap(); wait(V, "pe", at=m)
    V.tensor_copy(gb1c[:, :], PS[0][0:128, 0:8])
    inc(V.tensor_copy(b1c[:, :], PS[1][0:128, 0:32]), "dve")

    m = snap(); wait(V, "dve", at=m)
    inc(V.memset(g_exp[:], 0.0), "dve")
    for gi, nm in enumerate(["glw", "grw"]):
        m = snap(); wait(SY, "act", at=m)
        inc(SY.dma_start(out=wstg[0:16, 0:64], in_=ap[nm]), "sq", 16)
        m = snap(); wait(SC, "sq", at=m)
        inc(SC.copy(AB2[0:16, 0:64], wstg[0:16, 0:64]), "act")
        m = snap(); wait(TE, "act", at=m)
        inc(TE.transpose(PB[0][0:64, 0:16], AB2[0:16, 0:64], identt[0:16, 0:16]), "pe")
        m = snap(); wait(V, "pe", "dve", at=m)
        inc(V.tensor_copy(AB2[0:64, 64:80], PB[0][0:64, 0:16]), "dve")
        m = snap(); wait(V, "dve", at=m)
        for c in range(8):
            V.tensor_copy(g_exp[0:64, 32 * c + 16 * gi + 2 * c:32 * c + 16 * gi + 2 * c + 1],
                          AB2[0:64, 64 + 2 * c:64 + 2 * c + 1])
            inc(V.tensor_copy(g_exp[64:128, 32 * c + 16 * gi + 2 * c + 1:32 * c + 16 * gi + 2 * c + 2],
                              AB2[0:64, 64 + 2 * c + 1:64 + 2 * c + 2]), "dve")

    inc(SY.dma_start(out=srow[0:1, 0:16], in_=ap["log_temp"].rearrange("a b -> (a b)")[None, :]), "sq", 16)
    m = snap(); wait(SC, "sq", at=m)
    inc(SC.activation(srow[0:1, 16:32], srow[0:1, 0:16], AF.Exp), "act")
    m = snap(); wait(V, "act", at=m)
    inc(V.reciprocal(srow[0:1, 32:48], srow[0:1, 16:32]), "dve")
    m = snap(); wait(TE, "dve", at=m)
    inc(TE.matmul(PS[0][:, 0:16], onesf[0:1, :], srow[0:1, 32:48], start=True, stop=True), "pe")
    m = snap(); wait(V, "pe", at=m)
    inc(V.tensor_copy(bc16[:], PS[0][:, 0:16]), "dve")
    m = snap(); wait(SY, "dve", at=m)
    inc(SY.dma_start(out=srow[0:1, 0:16], in_=ap["floor_param"].rearrange("a b -> (a b)")[None, :]), "sq", 16)
    m = snap(); wait(SC, "sq", at=m)
    inc(SC.activation(srow[0:1, 16:32], srow[0:1, 0:16], AF.Sigmoid), "act")
    m = snap(); wait(V, "act", at=m)
    inc(V.tensor_scalar(srow[0:1, 32:48], srow[0:1, 16:32], 0.05, 0.0, ALU.mult, ALU.add), "dve")
    m = snap(); wait(V, "dve", at=m)
    inc(V.reduce_sum(srow[0:1, 48:52],
                     srow[0:1, 32:48].rearrange("p (h c) -> p h c", h=4), axis=AX.X), "dve")
    m = snap(); wait(V, "dve", at=m)
    inc(V.tensor_scalar(srow[0:1, 52:56], srow[0:1, 48:52], -1.0, 1.0, ALU.mult, ALU.add), "dve")
    inc(SY.dma_start(out=srow[0:1, 72:76], in_=ap["crl"][None, :]), "sq", 16)
    inc(SY.dma_start(out=srow[0:1, 60:64], in_=ap["alpha_id"][None, :]), "sq", 16)
    inc(SY.dma_start(out=srow[0:1, 64:68], in_=ap["gb2"][None, :]), "sq", 16)
    m = snap(); wait(SC, "sq", at=m); wait(SC, "dve", at=m)
    inc(SC.activation(srow[0:1, 56:60], srow[0:1, 72:76], AF.Sigmoid), "act")
    m = snap(); wait(TE, "act", at=m)
    inc(TE.matmul(PS[0][:, 0:16], onesf[0:1, :], srow[0:1, 32:48], start=True, stop=True), "pe")
    inc(TE.matmul(PS[1][:, 0:16], onesf[0:1, :], srow[0:1, 52:68], start=True, stop=True), "pe")
    m = snap(); wait(V, "pe", at=m)
    V.tensor_copy(bcfl[:], PS[0][:, 0:16])
    inc(V.tensor_copy(bcmisc[:], PS[1][:, 0:16]), "dve")
    m = snap(); wait(SY, "dve", at=m)
    inc(SY.dma_start(out=srow[0:1, 0:128], in_=ap["onw"][None, 0:128]), "sq", 16)
    m = snap(); wait(TE, "sq", at=m)
    inc(TE.matmul(PS[2][:, 0:128], onesf[0:1, :], srow[0:1, 0:128], start=True, stop=True), "pe")
    m = snap(); wait(SY, "pe", at=m)
    inc(SY.dma_start(out=srow[0:1, 0:128], in_=ap["onw"][None, 128:256]), "sq", 16)
    m = snap(); wait(TE, "sq", at=m)
    inc(TE.matmul(PS[2][:, 128:256], onesf[0:1, :], srow[0:1, 0:128], start=True, stop=True), "pe")
    m = snap(); wait(V, "pe", at=m)
    for h in range(NH):
        inc(V.tensor_copy(onwb[:, 256 * h:256 * (h + 1)], PS[2][:, 0:256]), "dve")

    for c in range(F // 128):
        m = snap(); wait(SY, "act", "dve", "sg", at=m)
        inc(SY.dma_start(out=wstg[:], in_=ap["w1"][128 * c:128 * (c + 1), :]), "sq", 16)
        m = snap(); wait(V, "sq", at=m)
        inc(V.tensor_reduce(fsc[:, 0:1], wstg[:], axis=AX.X, op=ALU.max,
                            apply_absolute_value=True), "dve")
        m = snap(); wait(V, "dve", at=m)
        inc(V.tensor_scalar(fsc[:, 1:2], fsc[:, 0:1], 1e-8, 1.0 / 127.0, ALU.max, ALU.mult), "dve")
        m = snap(); wait(V, "dve", at=m)
        V.tensor_copy(s_w1c[:, c:c + 1], fsc[:, 1:2])
        inc(V.reciprocal(fsc[:, 2:3], fsc[:, 1:2]), "dve")
        m = snap(); wait(V, "dve", at=m)
        inc(V.tensor_scalar(fsc2[:, 0:1024], wstg[:], fsc[:, 2:3], 0.0, ALU.mult, ALU.add), "dve")
        m = snap(); wait(V, "dve", at=m)
        inc(V.tensor_copy(AI32[:, 0:1024], fsc2[:, 0:1024]), "dve")
        m = snap(); wait(V, "dve", at=m)
        inc(V.tensor_copy(AB2[:, 0:1024], AI32[:, 0:1024]), "dve")
        m = snap(); wait(TE, "dve", at=m)
        for d8 in range(8):
            inc(TE.transpose(PB[0][:, 128 * d8:128 * (d8 + 1)],
                             AB2[:, 128 * d8:128 * (d8 + 1)], identt[:]), "pe")
        m = snap(); wait(V, "pe", at=m)
        inc(V.tensor_copy(AB2[:, 1024:2048], PB[0][:, 0:1024]), "dve")
        m = snap(); wait(G, "dve", at=m)
        for d8 in range(8):
            inc(G.dma_start(out=dr["w1iT"][128 * d8:128 * (d8 + 1), 128 * c:128 * (c + 1)],
                            in_=AB2[:, 1024 + 128 * d8:1024 + 128 * (d8 + 1)]), "sg", 16)
    for c in range(D // 128):
        for q4 in range(4):
            m = snap(); wait(SY, "dve", "sg", at=m)
            inc(SY.dma_start(out=wstg[:], in_=ap["w2"][128 * c:128 * (c + 1),
                                                       1024 * q4:1024 * (q4 + 1)]), "sq", 16)
            m = snap(); wait(V, "sq", at=m)
            inc(V.tensor_reduce(fsc[:, q4:q4 + 1], wstg[:], axis=AX.X, op=ALU.max,
                                apply_absolute_value=True), "dve")
        m = snap(); wait(V, "dve", at=m)
        inc(V.tensor_reduce(fsc[:, 4:5], fsc[:, 0:4], axis=AX.X, op=ALU.max), "dve")
        m = snap(); wait(V, "dve", at=m)
        inc(V.tensor_scalar(fsc[:, 5:6], fsc[:, 4:5], 1e-8, 1.0 / 127.0, ALU.max, ALU.mult), "dve")
        m = snap(); wait(V, "dve", at=m)
        V.tensor_copy(bq2b[:, c:c + 1], fsc[:, 5:6])
        inc(V.reciprocal(fsc[:, 6:7], fsc[:, 5:6]), "dve")
        for q4 in range(4):
            m = snap(); wait(SY, "dve", "sg", at=m)
            inc(SY.dma_start(out=wstg[:], in_=ap["w2"][128 * c:128 * (c + 1),
                                                       1024 * q4:1024 * (q4 + 1)]), "sq", 16)
            m = snap(); wait(V, "sq", at=m)
            inc(V.tensor_scalar(fsc2[:, 0:1024], wstg[:], fsc[:, 6:7], 0.0, ALU.mult, ALU.add), "dve")
            m = snap(); wait(SC, "dve", at=m)
            m = snap(); wait(V, "dve", at=m)
            inc(V.tensor_copy(AI32[:, 0:1024], fsc2[:, 0:1024]), "dve")
            m = snap(); wait(V, "dve", at=m)
            inc(V.tensor_copy(AB2[:, 0:1024], AI32[:, 0:1024]), "dve")
            m = snap(); wait(V, "dve", at=m)
            inc(V.tensor_scalar(AB2[:, 0:1024], AB2[:, 0:1024], bq2b[:, c:c + 1], 0.0,
                                ALU.mult, ALU.add), "dve")
            m = snap(); wait(TE, "dve", at=m)
            for d8 in range(8):
                inc(TE.transpose(PB[0][:, 128 * d8:128 * (d8 + 1)],
                                 AB2[:, 128 * d8:128 * (d8 + 1)], identt[:]), "pe")
            m = snap(); wait(V, "pe", at=m)
            inc(V.tensor_copy(AB2[:, 1024:2048], PB[0][:, 0:1024]), "dve")
            m = snap(); wait(G, "dve", at=m)
            for d8 in range(8):
                inc(G.dma_start(out=dr["w2qT"][1024 * q4 + 128 * d8:1024 * q4 + 128 * (d8 + 1),
                                               128 * c:128 * (c + 1)],
                                in_=AB2[:, 1024 + 128 * d8:1024 + 128 * (d8 + 1)]), "sg", 16)
    m = snap(); wait(TE, "dve", at=m)
    inc(TE.transpose(PS[3][0:8, 0:128], bq2b[:, 0:8], identf[:]), "pe")
    m = snap(); wait(V, "pe", at=m)
    inc(V.tensor_copy(fsc[0:8, 0:128], PS[3][0:8, 0:128]), "dve")
    m = snap(); wait(G, "dve", at=m)
    inc(G.dma_start(out=dr["bounce"][0:8, 0:128], in_=fsc[0:8, 0:128]), "sg", 16)
    m = snap(); wait(SY, "sg", at=m)
    inc(SY.dma_start(out=sw2r[0:1, :],
                     in_=dr["bounce"].rearrange("a b -> (a b)")[None, 0:1024]), "sq", 16)
    m = snap(); wait(V, "sq", at=m)

    def emit_ln_stats(with_rowmax):
        for i in range(cfg.NT):
            st = stream[:, D * i:D * (i + 1)]
            m0 = snap(); wait(V, "act", at=m0)
            inc(V.reduce_sum(fsc[:, 0:1], st, axis=AX.X), "dve")
            m = snap(); wait(SC, "dve", at=m)
            inc(SC.activation(fsc2[:, 0:1024], st, AF.Square, accum_out=fsc[:, 1:2]), "act")
            m = snap(); wait(V, "act", at=m)
            V.tensor_scalar(fsc[:, 2:3], fsc[:, 0:1], 1.0 / D, 0.0, ALU.mult, ALU.add)
            inc(V.tensor_scalar(fsc[:, 3:4], fsc[:, 1:2], 1.0 / D, 0.0, ALU.mult, ALU.add), "dve")
            m = snap(); wait(V, "dve", at=m)
            V.tensor_copy(mu[:, i:i + 1], fsc[:, 2:3])
            inc(V.tensor_tensor(fsc[:, 4:5], fsc[:, 2:3], fsc[:, 2:3], ALU.mult), "dve")
            m = snap(); wait(V, "dve", at=m)
            inc(V.tensor_sub(fsc[:, 5:6], fsc[:, 3:4], fsc[:, 4:5]), "dve")
            m = snap(); wait(SC, "dve", at=m)
            inc(SC.activation(fsc[:, 6:7], fsc[:, 5:6], AF.Sqrt, bias=epsc[:, 0:1]), "act")
            m = snap(); wait(V, "act", at=m)
            inc(V.reciprocal(rstd[:, i:i + 1], fsc[:, 6:7]), "dve")
            if with_rowmax:
                inc(V.reduce_max(fsc[:, 8:9], st, axis=AX.X), "dve")
                m = snap(); wait(V, "dve", at=m)
                inc(V.tensor_scalar(fsc[:, 9:10], fsc[:, 8:9], mu[:, i:i + 1],
                                    rstd[:, i:i + 1], ALU.subtract, ALU.mult), "dve")
                m = snap(); wait(V, "dve", at=m)
                inc(V.tensor_max(maxc[:, 0:1], maxc[:, 0:1], fsc[:, 9:10]), "dve")

    def norm_tile(i, dst):
        return V.tensor_scalar(dst, stream[:, D * i:D * (i + 1)],
                               mu[:, i:i + 1], rstd[:, i:i + 1], ALU.subtract, ALU.mult)

    def emit_global_max(out_recip_col, out_val_col):
        m = snap(); wait(V, "dve", at=m)
        inc(V.memset(ccst[:], 0.0), "dve")
        m = snap(); wait(V, "dve", at=m)
        inc(V.tensor_copy(ccst[:, 0:1], maxc[:, 0:1]), "dve")
        m = snap(); wait(G, "dve", at=m)
        inc(G.dma_start(out=cc_in_t.ap(), in_=ccst[:]), "sg", 16)
        m = snap(); wait(G, "sg", at=m)
        inc(G.collective_compute("AllReduce", ALU.max, [list(cfg.cc_group)],
                                 [cc_in_t.ap().opt()], [cc_out_t.ap().opt()]), "cc", 1)
        m = snap(); wait(G, "cc", at=m)
        inc(G.dma_start(out=ccst[:], in_=cc_out_t.ap()), "sg", 16)
        m = snap(); wait(TE, "sg", at=m)
        inc(TE.transpose(PS[4][0:1, 0:128], ccst[:, 0:1], identf[:]), "pe")
        m = snap(); wait(V, "pe", at=m)
        inc(V.reduce_max(srow[0:1, 0:1], PS[4][0:1, 0:128], axis=AX.X), "dve")
        m = snap(); wait(V, "dve", at=m)
        inc(V.tensor_scalar(srow[0:1, 1:2], srow[0:1, 0:1], 1e-8, 1.0 / 255.0,
                            ALU.max, ALU.mult), "dve")
        m = snap(); wait(V, "dve", at=m)
        inc(V.reciprocal(srow[0:1, 2:3], srow[0:1, 1:2]), "dve")
        m = snap(); wait(TE, "dve", at=m)
        inc(TE.matmul(PS[4][:, 0:1], onesf[0:1, :], srow[0:1, 2:3], start=True, stop=True), "pe")
        inc(TE.matmul(PS[4][:, 1:2], onesf[0:1, :], srow[0:1, 1:2], start=True, stop=True), "pe")
        m = snap(); wait(V, "pe", at=m)
        V.tensor_copy(sInv[:, out_recip_col:out_recip_col + 1], PS[4][:, 0:1])
        inc(V.tensor_copy(sInv[:, out_val_col:out_val_col + 1], PS[4][:, 1:2]), "dve")
        m = snap(); wait(V, "dve", at=m)
        inc(V.memset(maxc[:, 0:1], 0.0), "dve")

    def load_w_chunks(name, dst, fdim, nchunk):
        for c in range(nchunk):
            m = snap(); wait(SY, "act", at=m)
            inc(SY.dma_start(out=wstg[:, 0:fdim], in_=ap[name][128 * c:128 * (c + 1), :]), "sq", 16)
            m = snap(); wait(SC, "sq", "sg", "pe", "dve", at=m)
            inc(SC.copy(dst[:, fdim * c:fdim * (c + 1)], wstg[:, 0:fdim]), "act")

    def xnT_tile(i, dstb, pbi=0):
        m = snap(); wait(V, "act", at=m)
        inc(norm_tile(i, fsc[:, 0:1024]), "dve")
        m = snap(); wait(SC, "dve", "pe", at=m)
        inc(SC.copy(AB2[:, 0:1024], fsc[:, 0:1024]), "act")
        m = snap(); wait(TE, "act", "dve", at=m)
        for d8 in range(8):
            inc(TE.transpose(PB[pbi][:, 128 * d8:128 * (d8 + 1)],
                             AB2[:, 128 * d8:128 * (d8 + 1)], identt[:]), "pe")
        m = snap(); wait(V, "pe", at=m)
        for d8 in range(8):
            inc(V.tensor_copy(dstb[:, 128 * d8:128 * (d8 + 1)],
                              PB[pbi][:, 128 * d8:128 * (d8 + 1)]), "dve")

    GKs2 = sb("GKs2", 128, 4 * cfg.NLAT)
    GVs2 = sb("GVs2", cfg.NLAT, 8 * 65)

    def emit_attn_a2():
        ca = Carve(AB1)
        wres = ca.take(8 * 1024)
        xnTb = ca.take(1024)
        evb = ca.take(1040)

        emit_ln_stats(with_rowmax=False)

        ngrp = cfg.NTS // cfg.GTS
        for sq_ in range(cfg.nseq):
            t0 = sq_ * cfg.NTS
            for g in range(ngrp):
                for it in range(cfg.GTS):
                    i = t0 + g * cfg.GTS + it
                    m = snap(); wait(V, "act", at=m)
                    inc(norm_tile(i, fsc[:, 0:1024]), "dve")
                    m = snap(); wait(SC, "dve", "pe", at=m)
                    inc(SC.copy(AB2[:, 0:1024], fsc[:, 0:1024]), "act")
                    m = snap(); wait(TE, "act", at=m)
                    if it == 0:
                        wait(TE, "dve", at=m)
                    for hf in range(2):
                        inc(TE.matmul(PS[hf][0:lpg, 0:512],
                                      poolt[:, lpg * it:lpg * (it + 1)],
                                      AB2[:, 512 * hf:512 * (hf + 1)],
                                      start=(it == 0), stop=(it == cfg.GTS - 1)), "pe")
                m = snap(); wait(V, "pe", at=m)
                for hf in range(2):
                    inc(V.tensor_copy(pooled[lpg * g:lpg * (g + 1), 512 * hf:512 * (hf + 1)],
                                      PS[hf][0:lpg, 0:512]), "dve")
            m = snap(); wait(TE, "dve", at=m)
            for d8 in range(8):
                inc(TE.transpose(PB[0][0:128, 128 * d8:128 * d8 + cfg.NLAT],
                                 pooled[0:cfg.NLAT, 128 * d8:128 * (d8 + 1)],
                                 identt[0:cfg.NLAT, 0:cfg.NLAT]), "pe")
            m = snap(); wait(V, "pe", at=m)
            for d8 in range(8):
                dcol = cfg.NLAT * d8
                if sq_ == 0:
                    inc(V.tensor_copy(pooledT[:, dcol:dcol + cfg.NLAT],
                                      PB[0][0:128, 128 * d8:128 * d8 + cfg.NLAT]), "dve")
                else:
                    inc(V.tensor_copy(evb[:, dcol:dcol + cfg.NLAT],
                                      PB[0][0:128, 128 * d8:128 * d8 + cfg.NLAT]), "dve")
        pooledT2 = evb

        for wn, kind in [("wgk", "K"), ("wgv", "V")]:
            load_w_chunks(wn, wres, 512, 8)
            for sq_ in range(cfg.nseq):
                pt = pooledT if sq_ == 0 else pooledT2
                gk_dst = GKs if sq_ == 0 else GKs2
                gv_dst = GVs if sq_ == 0 else GVs2
                m = snap(); wait(TE, "act", "dve", at=m)
                if kind == "K":
                    for fc in range(4):
                        for kc in range(8):
                            inc(TE.matmul(PS[fc % 2][0:128, 0:cfg.NLAT],
                                          wres[:, 512 * kc + 128 * fc:512 * kc + 128 * (fc + 1)],
                                          pt[:, cfg.NLAT * kc:cfg.NLAT * (kc + 1)],
                                          start=(kc == 0), stop=(kc == 7)), "pe")
                        m = snap(); wait(V, "pe", at=m)
                        inc(V.tensor_copy(gk_dst[:, cfg.NLAT * fc:cfg.NLAT * (fc + 1)],
                                          PS[fc % 2][0:128, 0:cfg.NLAT]), "dve")
                        m = snap(); wait(TE, "dve", at=m)
                else:
                    for kc in range(8):
                        inc(TE.matmul(PS[2][0:cfg.NLAT, 0:512],
                                      pt[:, cfg.NLAT * kc:cfg.NLAT * (kc + 1)],
                                      wres[:, 512 * kc:512 * (kc + 1)],
                                      start=(kc == 0), stop=(kc == 7)), "pe")
                    m = snap(); wait(V, "pe", at=m)
                    for lh in range(8):
                        inc(V.tensor_copy(gv_dst[0:cfg.NLAT, 65 * lh:65 * lh + 64],
                                          PS[2][0:cfg.NLAT, 64 * lh:64 * (lh + 1)]), "dve")
                    m = snap(); wait(V, "dve", at=m)
                    for lh in range(8):
                        inc(V.memset(gv_dst[0:cfg.NLAT, 65 * lh + 64:65 * (lh + 1)], 1.0), "dve")

        caA = Carve(AB1)
        wqR = caA.take(8 * 1024)
        wkR = caA.take(8 * 1024)
        wgqR = caA.take(8 * 512)
        wvR = caA.take(8 * 1024)
        xnTblk = caA.take(8 * 512)
        evS = [AB2[:, 1024:1536], AB2[:, 1536:2048]]
        evT = AB2[:, 2048:3088]
        load_w_chunks("wq", wqR, 1024, 8)
        load_w_chunks("wk", wkR, 1024, 8)
        load_w_chunks("wgq", wgqR, 512, 8)
        load_w_chunks("wv", wvR, 1024, 8)
        m = snap(); wait(V, "act", at=m)
        for h in range(16):
            inc(V.memset(evT[:, 65 * h + 64:65 * (h + 1)], 1.0), "dve")
        for bi in range(T // 512):
            for it in range(4):
                i = 4 * bi + it
                m = snap(); wait(V, "act", at=m)
                inc(norm_tile(i, fsc[:, 0:1024]), "dve")
                m = snap(); wait(SC, "dve", "pe", at=m)
                inc(SC.copy(AB2[:, 0:1024], fsc[:, 0:1024]), "act")
                m = snap(); wait(TE, "act", "dve", at=m)
                for d8 in range(8):
                    inc(TE.transpose(PB[it % 2][:, 128 * d8:128 * (d8 + 1)],
                                     AB2[:, 128 * d8:128 * (d8 + 1)], identt[:]), "pe")
                m = snap(); wait(V, "pe", at=m)
                for d8 in range(8):
                    inc(V.tensor_copy(xnTblk[:, 512 * d8 + 128 * it:512 * d8 + 128 * (it + 1)],
                                      PB[it % 2][:, 128 * d8:128 * (d8 + 1)]), "dve")
            m = snap(); wait(TE, "dve", at=m)
            for wres, dst, fdim in [(wqR, dr["aQ"], 1024), (wkR, dr["aK"], 1024),
                                    (wgqR, dr["aGQ"], 512)]:
                for fc in range(fdim // 128):
                    ps_q = PS[fc % 2]
                    m = snap(); wait(TE, "act", at=m)
                    for kc in range(8):
                        inc(TE.matmul(ps_q[:, 0:512],
                                      wres[:, fdim * kc + 128 * fc:fdim * kc + 128 * (fc + 1)],
                                      xnTblk[:, 512 * kc:512 * (kc + 1)],
                                      start=(kc == 0), stop=(kc == 7)), "pe")
                    m = snap(); wait(SC, "pe", "sg", at=m)
                    inc(SC.copy(evS[fc % 2][:, 0:512], ps_q[:, 0:512]), "act")
                    m = snap(); wait(G, "act", at=m)
                    inc(G.dma_start(out=dst[128 * fc:128 * (fc + 1), 512 * bi:512 * (bi + 1)],
                                    in_=evS[fc % 2][:, 0:512]), "sg", 16)
            for it in range(4):
                i = 4 * bi + it
                m = snap(); wait(TE, "dve", "act", at=m)
                for nh2 in range(2):
                    ps_v = PS[2 + nh2]
                    for kc in range(8):
                        inc(TE.matmul(ps_v[:, 0:512],
                                      xnTblk[:, 512 * kc + 128 * it:512 * kc + 128 * (it + 1)],
                                      wvR[:, 1024 * kc + 512 * nh2:1024 * kc + 512 * (nh2 + 1)],
                                      start=(kc == 0), stop=(kc == 7)), "pe")
                m = snap(); wait(V, "pe", "sg", at=m)
                for h in range(16):
                    inc(V.tensor_copy(evT[:, 65 * h:65 * h + 64],
                                      PS[2 + h // 8][:, 64 * (h % 8):64 * (h % 8) + 64]), "dve")
                m = snap(); wait(G, "dve", at=m)
                inc(G.dma_start(out=dr["aV"][128 * i:128 * (i + 1), :], in_=evT[:, 0:1040]), "sg", 16)

        ca2 = Carve(AB1)
        rW = ca2.take(4 * 1024)
        oW = ca2.take(8 * 1024)
        Qb = ca2.take(8 * 256)
        Kb = [ca2.take(8 * 256), ca2.take(8 * 256)]
        Vb = [ca2.take(2 * 1040), ca2.take(2 * 1040)]
        GQb = ca2.take(4 * 256)
        Ptb = ca2.take(512)
        PTb = ca2.take(512)
        remtb = ca2.take(2 * 512)
        remfb = ca2.take(4 * 256)
        combb = ca2.take(2 * 1024)
        gat = AFX[:, 4096:4096 + 64]

        load_w_chunks("r_out_w", rW, 1024, 4)
        load_w_chunks("w_out", oW, 1024, 8)
        mA2w = snap()

        for sq_ in range(cfg.nseq):
            gk = GKs if sq_ == 0 else GKs2
            gv = GVs if sq_ == 0 else GVs2
            for b in range(cfg.NB):
                cur, prv = b % 2, 1 - (b % 2)
                trow0 = (sq_ * S + b * 256)
                m = snap()
                wait(SY, "pe", at=m)
                inc(SY.dma_start(out=Qb[:, 0:2048],
                                 in_=dr["aQ"].rearrange("(c p) t -> p c t", p=128)
                                 [:, :, trow0:trow0 + 256]), "sq", 16)
                inc(SY.dma_start(out=Kb[cur][:, 0:2048],
                                 in_=dr["aK"].rearrange("(c p) t -> p c t", p=128)
                                 [:, :, trow0:trow0 + 256]), "sq", 16)
                inc(SY.dma_start(out=GQb[:, 0:1024],
                                 in_=dr["aGQ"].rearrange("(c p) t -> p c t", p=128)
                                 [:, :, trow0:trow0 + 256]), "sq", 16)
                inc(SY.dma_start(out=Vb[cur][:, 0:1040], in_=dr["aV"][trow0:trow0 + 128, :]), "sq", 16)
                inc(SY.dma_start(out=Vb[cur][:, 1040:2080], in_=dr["aV"][trow0 + 128:trow0 + 256, :]), "sq", 16)
                mLd = snap()
                if b == 0:
                    wait(V, "sq", at=mLd)
                    inc(V.memset(Kb[prv][:, :], 0.0), "dve")
                    inc(V.memset(Vb[prv][:, :], 0.0), "dve")
                wait(TE, "sq", at=mLd)
                wait(TE, "pe", at=mA2w)
                m = snap(); wait(TE, "dve", at=m)
                for mt in range(2):
                    for kc in range(8):
                        inc(TE.matmul(PS[4][:, 32 * mt:32 * (mt + 1)],
                                      Qb[:, 256 * kc + 128 * mt:256 * kc + 128 * (mt + 1)],
                                      g_exp[:, 32 * kc:32 * (kc + 1)],
                                      start=(kc == 0), stop=(kc == 7)), "pe")
                m = snap(); wait(SC, "pe", at=m)
                inc(SC.activation(gat[:, 0:64], PS[4][:, 0:64], AF.Sigmoid), "act")
                for lh in range(8):
                    fc, of = (64 * lh) // 128, (64 * lh) % 128
                    for mt in range(2):
                        inc(TE.matmul(PS[2][0:128, 64 * mt:64 * (mt + 1)],
                                      GQb[of:of + 64, 256 * fc + 128 * mt:256 * fc + 128 * (mt + 1)],
                                      gk[of:of + 64, 64 * fc:64 * fc + cfg.NLAT],
                                      start=True, stop=True), "pe")
                    m = snap(); wait(V, "pe", at=m)
                    V.reduce_max(fsc[:, 0:1], PS[2][:, 0:64], axis=AX.X)
                    inc(V.reduce_max(fsc[:, 1:2], PS[2][:, 64:128], axis=AX.X), "dve")
                    m = snap(); wait(V, "dve", at=m)
                    inc(V.tensor_scalar(fsc[:, 2:4], fsc[:, 0:2], -0.125, 0.0, ALU.mult, ALU.add), "dve")
                    for mt in range(2):
                        m = snap(); wait(SC, "dve" if mt == 0 else "pe", at=m)
                        wait(SC, "pool")
                        inc(SC.activation(Ptb[:, 0:cfg.NLAT], PS[2][:, 64 * mt:64 * (mt + 1)],
                                          AF.Exp, bias=fsc[:, 2 + mt:3 + mt], scale=0.125), "act")
                        m = snap(); wait(G, "act", at=m)
                        tbase = 256 * b + 128 * mt
                        inc(G.affine_select(Ptb[:, 0:cfg.NLAT], Ptb[:, 0:cfg.NLAT],
                                            [[-cfg.TPL, cfg.NLAT]], ALU.is_ge, 0.0,
                                            base=tbase, channel_multiplier=1), "pool")
                        m = snap(); wait(TE, "pool", at=m)
                        inc(TE.transpose(PB[0][0:cfg.NLAT, 128 * mt:128 * (mt + 1)],
                                         Ptb[:, 0:cfg.NLAT], identt[:]), "pe")
                        m = snap(); wait(V, "pe", at=m)
                        inc(V.tensor_copy(PTb[0:cfg.NLAT, 128 * mt:128 * (mt + 1)],
                                          PB[0][0:cfg.NLAT, 128 * mt:128 * (mt + 1)]), "dve")
                    m = snap(); wait(TE, "dve", at=m)
                    for mt in range(2):
                        inc(TE.matmul(PS[3][0:128, 65 * mt:65 * (mt + 1)],
                                      PTb[0:cfg.NLAT, 128 * mt:128 * (mt + 1)],
                                      gv[0:cfg.NLAT, 65 * lh:65 * (lh + 1)],
                                      start=True, stop=True), "pe")
                    m = snap(); wait(V, "pe", at=m)
                    for mt in range(2):
                        inc(V.reciprocal(fsc[:, 4 + mt:5 + mt],
                                         PS[3][:, 65 * mt + 64:65 * mt + 65]), "dve")
                    m = snap(); wait(V, "dve", at=m)
                    for mt in range(2):
                        inc(V.tensor_scalar(remtb[:, 512 * mt + 64 * lh:512 * mt + 64 * (lh + 1)],
                                            PS[3][:, 65 * mt:65 * mt + 64],
                                            fsc[:, 4 + mt:5 + mt], 0.0, ALU.mult, ALU.add), "dve")
                    m = snap(); wait(TE, "dve", at=m)
                for mt in range(2):
                    for d4 in range(4):
                        inc(TE.transpose(PB[mt][:, 128 * d4:128 * (d4 + 1)],
                                         remtb[:, 512 * mt + 128 * d4:512 * mt + 128 * (d4 + 1)],
                                         identt[:]), "pe")
                m = snap(); wait(V, "pe", at=m)
                for mt in range(2):
                    for d4 in range(4):
                        inc(V.tensor_copy(remfb[:, 256 * d4 + 128 * mt:256 * d4 + 128 * (mt + 1)],
                                          PB[mt][:, 128 * d4:128 * (d4 + 1)]), "dve")
                m = snap(); wait(TE, "dve", at=m)
                for mt in range(2):
                    for nh2 in range(2):
                        ps_r = PS[2 * mt + nh2]
                        for kc in range(4):
                            inc(TE.matmul(ps_r[:, 0:512],
                                          remfb[:, 256 * kc + 128 * mt:256 * kc + 128 * (mt + 1)],
                                          rW[:, 1024 * kc + 512 * nh2:1024 * kc + 512 * (nh2 + 1)],
                                          start=(kc == 0), stop=(kc == 3)), "pe")
                mRem = snap()
                for h in range(16):
                    fc, of = (64 * h) // 128, (64 * h) % 128
                    for mt in range(2):
                        ps_s = PS[4]
                        inc(TE.matmul(ps_s[:, 0:256],
                                      Qb[of:of + 64, 256 * fc + 128 * mt:256 * fc + 128 * (mt + 1)],
                                      Kb[prv][of:of + 64, 256 * fc:256 * fc + 256],
                                      start=True, stop=True), "pe")
                        inc(TE.matmul(ps_s[:, 256:512],
                                      Qb[of:of + 64, 256 * fc + 128 * mt:256 * fc + 128 * (mt + 1)],
                                      Kb[cur][of:of + 64, 256 * fc:256 * fc + 256],
                                      start=True, stop=True), "pe")
                        m = snap(); wait(V, "pe", at=m)
                        inc(V.reduce_max(fsc[:, 0:1], ps_s[:, 0:512], axis=AX.X), "dve")
                        m = snap(); wait(V, "dve", at=m)
                        inc(V.tensor_scalar(fsc[:, 1:2], fsc[:, 0:1], -0.125, 0.0,
                                            ALU.mult, ALU.add), "dve")
                        m = snap(); wait(SC, "dve", at=m)
                        inc(SC.activation(Ptb[:, 0:512], ps_s[:, 0:512], AF.Exp,
                                          bias=fsc[:, 1:2], scale=0.125), "act")
                        m = snap(); wait(G, "act", at=m)
                        wm = winft if b == 0 else winmt
                        inc(G.tensor_tensor(Ptb[:, 0:512], Ptb[:, 0:512],
                                            wm[:, 512 * mt:512 * (mt + 1)], ALU.mult), "pool")
                        m = snap(); wait(TE, "pool", at=m)
                        for k4 in range(4):
                            inc(TE.transpose(PB[0][:, 128 * k4:128 * (k4 + 1)],
                                             Ptb[:, 128 * k4:128 * (k4 + 1)], identt[:]), "pe")
                        m = snap(); wait(V, "pe", at=m)
                        inc(V.tensor_copy(PTb[:, 0:512], PB[0][:, 0:512]), "dve")
                        m = snap(); wait(TE, "dve", at=m)
                        for k4 in range(4):
                            vt = Vb[prv] if k4 < 2 else Vb[cur]
                            vof = 1040 * (k4 % 2)
                            inc(TE.matmul(PS[5][:, 0:65],
                                          PTb[:, 128 * k4:128 * (k4 + 1)],
                                          vt[:, vof + 65 * h:vof + 65 * (h + 1)],
                                          start=(k4 == 0), stop=(k4 == 3)), "pe")
                        m = snap(); wait(V, "pe", at=m)
                        inc(V.reciprocal(fsc[:, 2:3], PS[5][:, 64:65]), "dve")
                        m = snap(); wait(V, "dve", at=m)
                        inc(V.tensor_scalar(fsc[:, 4:68], PS[5][:, 0:64], fsc[:, 2:3],
                                            gat[:, 32 * mt + h:32 * mt + h + 1],
                                            ALU.mult, ALU.mult), "dve")
                        m = snap()
                        wait(V, "pe", at=mRem)
                        wait(V, "dve", at=m)
                        rps = PS[2 * mt + (64 * h) // 512]
                        rof = (64 * h) % 512
                        inc(V.scalar_tensor_tensor(
                            combb[:, 1024 * mt + 64 * h:1024 * mt + 64 * (h + 1)],
                            rps[:, rof:rof + 64],
                            gat[:, 32 * mt + 16 + h:32 * mt + 16 + h + 1],
                            fsc[:, 4:68], ALU.mult, ALU.add), "dve")
                m = snap(); wait(TE, "dve", at=m)
                for mt in range(2):
                    for d8 in range(8):
                        inc(TE.transpose(PB[mt][:, 128 * d8:128 * (d8 + 1)],
                                         combb[:, 1024 * mt + 128 * d8:1024 * mt + 128 * (d8 + 1)],
                                         identt[:]), "pe")
                m = snap(); wait(V, "pe", at=m)
                for mt in range(2):
                    for d8 in range(8):
                        inc(V.tensor_copy(Qb[:, 1024 * mt + 128 * d8:1024 * mt + 128 * (d8 + 1)],
                                          PB[mt][:, 128 * d8:128 * (d8 + 1)]), "dve")
                m = snap(); wait(TE, "dve", at=m)
                for mt in range(2):
                    for nh2 in range(2):
                        ps_o = PS[2 * mt + nh2]
                        for kc in range(8):
                            inc(TE.matmul(ps_o[:, 0:512],
                                          Qb[:, 1024 * mt + 128 * kc:1024 * mt + 128 * (kc + 1)],
                                          oW[:, 1024 * kc + 512 * nh2:1024 * kc + 512 * (nh2 + 1)],
                                          start=(kc == 0), stop=(kc == 7)), "pe")
                m = snap(); wait(V, "pe", at=m)
                for mt in range(2):
                    i = (trow0 // 128) + mt
                    for nh2 in range(2):
                        inc(V.tensor_tensor(stream[:, D * i + 512 * nh2:D * i + 512 * (nh2 + 1)],
                                            stream[:, D * i + 512 * nh2:D * i + 512 * (nh2 + 1)],
                                            PS[2 * mt + nh2][:, 0:512], ALU.add), "dve")
                m = snap(); wait(TE, "dve", at=m)
                wait(SY, "dve", at=m)

    def emit_ffn(final):
        ca = Carve(AB1)
        wbig = ca.take(8 * 2048)
        wbig2 = ca.take(8 * 2048)
        caB = Carve(AB2)
        xqTb = caB.take(4096)
        xqb = caB.take(1024)
        hqb = caB.take(0)

        emit_ln_stats(with_rowmax=True)
        emit_global_max(0, 1)
        m = snap(); wait(V, "dve", at=m)
        inc(V.tensor_scalar(ffnc[:, 0:32], s_w1c[:], sInv[:, 1:2], 0.0, ALU.mult, ALU.add), "dve")
        m = snap(); wait(V, "dve", at=m)
        inc(V.reciprocal(ffnc[:, 64:96], ffnc[:, 0:32]), "dve")
        m = snap(); wait(V, "dve", at=m)
        inc(V.tensor_tensor(fsc[:, 0:32], b1c[:], ffnc[:, 64:96], ALU.mult), "dve")
        m = snap(); wait(V, "dve", at=m)
        inc(V.tensor_copy(AI32[:, 0:32], fsc[:, 0:32]), "dve")
        m = snap(); wait(V, "dve", at=m)
        inc(V.tensor_copy(fsc[:, 32:64], AI32[:, 0:32]), "dve")
        m = snap(); wait(V, "dve", at=m)
        inc(V.tensor_tensor(ffnc[:, 32:64], fsc[:, 32:64], ffnc[:, 0:32], ALU.mult), "dve")

        for d8 in range(8):
            inc(SY.dma_start(out=wbig[:, 2048 * d8:2048 * (d8 + 1)],
                             in_=dr["w1iT"][128 * d8:128 * (d8 + 1), 0:2048]), "sq", 16)
            inc(SY.dma_start(out=wbig2[:, 2048 * d8:2048 * (d8 + 1)],
                             in_=dr["w1iT"][128 * d8:128 * (d8 + 1), 2048:4096]), "sq", 16)
        mW1 = snap()
        m = snap(); wait(V, "dve", at=m)
        inc(V.memset(maxc[:, 0:1], 0.0), "dve")

        nblk = T // 512
        for bk in range(nblk):
            for it in range(4):
                i = 4 * bk + it
                m = snap(); wait(V, "pe", at=m)
                inc(norm_tile(i, fsc[:, 0:1024]), "dve")
                m = snap(); wait(V, "dve", at=m)
                inc(V.tensor_scalar(AU8[:, 0:1024], fsc[:, 0:1024], sInv[:, 0:1],
                                    0.0, ALU.mult, ALU.max), "dve")
                m = snap(); wait(V, "dve", at=m)
                inc(V.tensor_copy(xqb[:, 0:1024], AU8[:, 0:1024]), "dve")
                m = snap(); wait(TE, "dve", at=m)
                for d8 in range(8):
                    inc(TE.transpose(PB[it % 2][:, 128 * d8:128 * (d8 + 1)],
                                     xqb[:, 128 * d8:128 * (d8 + 1)], identt[:]), "pe")
                m = snap(); wait(V, "pe", at=m)
                for d8 in range(8):
                    inc(V.tensor_copy(xqTb[:, 512 * d8 + 128 * it:512 * d8 + 128 * (it + 1)],
                                      PB[it % 2][:, 128 * d8:128 * (d8 + 1)]), "dve")
            if bk == 0:
                wait(TE, "sq", at=mW1)
                if getattr(cfg, "dbg_xqt", False):
                    m = snap()
                    inc(G.dma_start(out=dr["dbg"], in_=xqTb[:, 0:4096]), "sg", 16)
            m = snap(); wait(TE, "dve", at=m)
            for fc in range(F // 128):
                wsrc = wbig if fc < 16 else wbig2
                fof = (fc % 16) * 128
                ps_f = PS[fc % 4]
                m = snap(); wait(TE, "act", at=m)
                for kc in range(8):
                    inc(TE.matmul(ps_f[:, 0:512],
                                  wsrc[:, 2048 * kc + fof:2048 * kc + fof + 128],
                                  xqTb[:, 512 * kc:512 * (kc + 1)],
                                  start=(kc == 0), stop=(kc == 7)), "pe")
                m = snap(); wait(SC, "pe", "sg", "dve", at=m)
                inc(SC.activation(b_sc[:, 0:512], ps_f[:, 0:512], AF.Relu,
                                  bias=ffnc[:, 32 + fc:33 + fc], scale=ffnc[:, fc:fc + 1]), "act")
                m = snap(); wait(V, "act", at=m)
                inc(V.reduce_max(fsc[:, 100:101], b_sc[:, 0:512], axis=AX.X), "dve")
                m = snap(); wait(V, "dve", at=m)
                inc(V.tensor_max(maxc[:, 0:1], maxc[:, 0:1], fsc[:, 100:101]), "dve")
                m = snap(); wait(G, "act", at=m)
                inc(G.dma_start(out=dr["h"][128 * fc:128 * (fc + 1), 512 * bk:512 * (bk + 1)],
                                in_=b_sc[:, 0:512]), "sg", 16)
        emit_global_max(2, 3)
        sh_sc = srow[0:1, 1:2]
        m = snap(); wait(V, "dve", at=m)
        inc(V.tensor_scalar(fsc[0:1, 0:1024], sw2r[0:1, :], sh_sc, 0.0, ALU.mult, ALU.add), "dve")
        m = snap(); wait(V, "dve", at=m)
        inc(V.reciprocal(fsc2[0:1, 0:1024], fsc[0:1, 0:1024]), "dve")
        m = snap(); wait(V, "dve", at=m)
        inc(V.tensor_tensor(fsc2[0:1, 0:1024], b2r[0:1, :], fsc2[0:1, 0:1024], ALU.mult), "dve")
        m = snap(); wait(V, "dve", at=m)
        inc(V.tensor_copy(AI32[0:1, 0:1024], fsc2[0:1, 0:1024]), "dve")
        m = snap(); wait(V, "dve", at=m)
        inc(V.tensor_copy(fsc2[0:1, 0:1024], AI32[0:1, 0:1024]), "dve")
        m = snap(); wait(V, "dve", at=m)
        inc(V.tensor_mul(fsc2[0:1, 0:1024], fsc2[0:1, 0:1024], fsc[0:1, 0:1024]), "dve")
        m = snap(); wait(TE, "dve", at=m)
        inc(TE.matmul(PS[4][:, 0:512], onesf[0:1, :], fsc2[0:1, 0:512], start=True, stop=True), "pe")
        inc(TE.matmul(PS[5][:, 0:512], onesf[0:1, :], fsc2[0:1, 512:1024], start=True, stop=True), "pe")
        m = snap(); wait(V, "pe", at=m)
        V.tensor_copy(bq2b[:, 0:512], PS[4][:, 0:512])
        inc(V.tensor_copy(bq2b[:, 512:1024], PS[5][:, 0:512]), "dve")

        m = snap(); wait(SY, "pe", at=m)
        for kc in range(16):
            inc(SY.dma_start(out=wbig[:, 1024 * kc:1024 * (kc + 1)],
                             in_=dr["w2qT"][128 * kc:128 * (kc + 1), :]), "sq", 16)
            inc(SY.dma_start(out=wbig2[:, 1024 * kc:1024 * (kc + 1)],
                             in_=dr["w2qT"][128 * (16 + kc):128 * (17 + kc), :]), "sq", 16)
        mW2 = snap()

        hqcols = xqTb
        for i in range(cfg.NT):
            for f4 in range(F // 512):
                m = snap(); wait(SY, "dve", at=m)
                inc(SY.dma_start(out=fsc[:, 0:512].rearrange("p (c t) -> p c t", c=4),
                                 in_=dr["h"].rearrange("(c p) t -> p c t", p=128)
                                 [:, 4 * f4:4 * (f4 + 1), 128 * i:128 * (i + 1)]), "sq", 16)
                m = snap(); wait(V, "sq", at=m)
                inc(V.tensor_scalar(AU8[:, 0:512], fsc[:, 0:512], sInv[:, 2:3], 0.0,
                                    ALU.mult, ALU.max), "dve")
                m = snap(); wait(V, "dve", at=m)
                inc(V.tensor_copy(hqcols[:, 512 * f4:512 * (f4 + 1)], AU8[:, 0:512]), "dve")
            if i == 0:
                wait(TE, "sq", at=mW2)
            m = snap(); wait(TE, "dve", at=m)
            for nh2 in range(2):
                ps_o = PS[nh2]
                for kc in range(32):
                    wsrc = wbig if kc < 16 else wbig2
                    inc(TE.matmul(ps_o[:, 0:512],
                                  hqcols[:, 128 * kc:128 * (kc + 1)],
                                  wsrc[:, 1024 * (kc % 16) + 512 * nh2:1024 * (kc % 16) + 512 * (nh2 + 1)],
                                  start=(kc == 0), stop=(kc == 31)), "pe")
            m = snap(); wait(V, "pe", at=m)
            for nh2 in range(2):
                inc(V.scalar_tensor_tensor(fsc[:, 512 * nh2:512 * (nh2 + 1)],
                                           PS[nh2][:, 0:512], sInv[:, 3:4],
                                           bq2b[:, 512 * nh2:512 * (nh2 + 1)],
                                           ALU.mult, ALU.add), "dve")
            m = snap(); wait(V, "dve", at=m)
            if final:
                inc(V.tensor_tensor(fsc2[:, 0:1024], fsc[:, 0:1024],
                                    stream[:, D * i:D * (i + 1)], ALU.add), "dve")
                m = snap(); wait(G, "dve", at=m)
                inc(G.dma_start(out=out[128 * i:128 * (i + 1), :], in_=fsc2[:, 0:1024]), "out", 16)
                m = snap(); wait(V, "out", at=m)
            else:
                inc(V.tensor_tensor(stream[:, D * i:D * (i + 1)],
                                    stream[:, D * i:D * (i + 1)], fsc[:, 0:1024], ALU.add), "dve")
            m = snap(); wait(TE, "dve", at=m)

    def emit_attn_b():
        emit_ln_stats(with_rowmax=False)

        caB1 = Carve(AB1)
        WqR = caB1.take(8 * 1024)
        WkR = caB1.take(8 * 1024)
        WvR = caB1.take(8 * 1024)
        xnTblk2 = caB1.take(8 * 512)
        evC = AB2[:, 1024:1536]
        vdb = AB2[:, 1536:2112]
        fcast = AB2[:, 2112:2624]
        vhist = AB2[:, 2624:3136]
        chist = AFX[:, 4096:4168]
        load_w_chunks("bWq", WqR, 1024, 8)
        load_w_chunks("bWk", WkR, 1024, 8)
        load_w_chunks("bWv", WvR, 1024, 8)
        for sq_ in range(cfg.nseq):
            m = snap(); wait(V, "act", "pool", at=m)
            inc(V.memset(chist[:, :], 0.0), "dve")
            inc(V.memset(vhist[:, 0:512], 0.0), "dve")
            for bi in range(S // 512):
                blk_g = sq_ * (S // 512) + bi
                for it in range(4):
                    i = 4 * blk_g + it
                    m = snap(); wait(V, "act", at=m)
                    inc(norm_tile(i, fsc[:, 0:1024]), "dve")
                    m = snap(); wait(SC, "dve", "pe", at=m)
                    inc(SC.copy(AB2[:, 0:1024], fsc[:, 0:1024]), "act")
                    m = snap(); wait(TE, "act", "dve", at=m)
                    for d8 in range(8):
                        inc(TE.transpose(PB[it % 2][:, 128 * d8:128 * (d8 + 1)],
                                         AB2[:, 128 * d8:128 * (d8 + 1)], identt[:]), "pe")
                    m = snap(); wait(V, "pe", at=m)
                    for d8 in range(8):
                        inc(V.tensor_copy(xnTblk2[:, 512 * d8 + 128 * it:512 * d8 + 128 * (it + 1)],
                                          PB[it % 2][:, 128 * d8:128 * (d8 + 1)]), "dve")
                m = snap(); wait(TE, "dve", at=m)
                for wi, (wres2, dst) in enumerate([(WqR, dr["bq"]), (WkR, dr["bk"]),
                                                   (WvR, dr["bv"])]):
                    for fc in range(8):
                        ps_q = PS[fc % 2]
                        m = snap(); wait(TE, "dve", "act", at=m)
                        for kc in range(8):
                            inc(TE.matmul(ps_q[:, 0:512],
                                          wres2[:, 1024 * kc + 128 * fc:1024 * kc + 128 * (fc + 1)],
                                          xnTblk2[:, 512 * kc:512 * (kc + 1)],
                                          start=(kc == 0), stop=(kc == 7)), "pe")
                        cwc = cw[:, (wi * 8 + fc) * 4:(wi * 8 + fc) * 4 + 4]
                        hc3 = chist[:, (wi * 8 + fc) * 3:(wi * 8 + fc) * 3 + 3]
                        m = snap(); wait(V, "pe", "act", at=m)
                        inc(V.tensor_scalar(b_sc[:, 0:512], ps_q[:, 0:512], cwc[:, 3:4],
                                            0.0, ALU.mult, ALU.add), "dve")
                        for tp in range(1, 4):
                            inc(V.scalar_tensor_tensor(b_sc[:, tp:512], ps_q[:, 0:512 - tp],
                                                       cwc[:, 3 - tp:4 - tp],
                                                       b_sc[:, tp:512], ALU.mult, ALU.add), "dve")
                            inc(V.scalar_tensor_tensor(b_sc[:, 0:tp], hc3[:, 3 - tp:3],
                                                       cwc[:, 3 - tp:4 - tp],
                                                       b_sc[:, 0:tp], ALU.mult, ALU.add), "dve")
                        inc(V.tensor_copy(hc3[:, :], ps_q[:, 509:512]), "dve")
                        m = snap(); wait(SC, "dve", "sg", at=m)
                        inc(SC.activation(evC[:, 0:512], b_sc[:, 0:512], AF.Silu), "act")
                        m = snap(); wait(G, "act", at=m)
                        inc(G.dma_start(out=dst[128 * fc:128 * (fc + 1), 512 * blk_g:512 * (blk_g + 1)],
                                        in_=evC[:, 0:512]), "sg", 16)
                        if wi == 2:
                            m = snap(); wait(V, "act", at=m)
                            V.tensor_copy(vdb[:, 0:64], vhist[:, 64 * fc:64 * (fc + 1)])
                            inc(V.tensor_copy(vdb[:, 64:576], evC[:, 0:512]), "dve")
                            inc(V.tensor_copy(vhist[:, 64 * fc:64 * (fc + 1)],
                                              vdb[:, 512:576]), "dve")
                            for (filt, Kf, dspill) in [(firsc, 3, dr["bls"]), (firlc, 63, dr["bll"])]:
                                fco = filt[:, Kf * fc:Kf * (fc + 1)]
                                m = snap(); wait(V, "dve", "act", at=m)
                                inc(V.tensor_scalar(b_sc[:, 0:512], vdb[:, 64:576],
                                                    fco[:, Kf - 1:Kf], 0.0, ALU.mult, ALU.add), "dve")
                                for tp in range(1, Kf):
                                    inc(V.scalar_tensor_tensor(b_sc[:, 0:512],
                                                               vdb[:, 64 - tp:576 - tp],
                                                               fco[:, Kf - 1 - tp:Kf - tp],
                                                               b_sc[:, 0:512], ALU.mult, ALU.add), "dve")
                                m = snap(); wait(SC, "dve", "sg", at=m)
                                inc(SC.copy(fcast[:, 0:512], b_sc[:, 0:512]), "act")
                                m = snap(); wait(G, "act", at=m)
                                inc(G.dma_start(out=dspill[128 * fc:128 * (fc + 1),
                                                          512 * blk_g:512 * (blk_g + 1)],
                                                in_=fcast[:, 0:512]), "sg", 16)

        caB2 = Carve(AB1)
        WidR = caB2.take(8 * 1024)
        GW1R = caB2.take(8 * 1024)
        WbR = caB2.take(8 * NH)
        xnTblk3 = caB2.take(8 * 512)
        load_w_chunks("bWid", WidR, 1024, 8)
        load_w_chunks("gW1", GW1R, 1024, 8)
        load_w_chunks("bWb", WbR, NH, 8)
        for blk_g in range(T // 512):
            for it in range(4):
                i = 4 * blk_g + it
                m = snap(); wait(V, "act", at=m)
                inc(norm_tile(i, fsc[:, 0:1024]), "dve")
                m = snap(); wait(SC, "dve", "pe", at=m)
                inc(SC.copy(AB2[:, 0:1024], fsc[:, 0:1024]), "act")
                m = snap(); wait(TE, "act", "dve", at=m)
                for d8 in range(8):
                    inc(TE.transpose(PB[it % 2][:, 128 * d8:128 * (d8 + 1)],
                                     AB2[:, 128 * d8:128 * (d8 + 1)], identt[:]), "pe")
                m = snap(); wait(V, "pe", at=m)
                for d8 in range(8):
                    inc(V.tensor_copy(xnTblk3[:, 512 * d8 + 128 * it:512 * d8 + 128 * (it + 1)],
                                      PB[it % 2][:, 128 * d8:128 * (d8 + 1)]), "dve")
            m = snap(); wait(TE, "dve", at=m)
            for it in range(4):
                i = 4 * blk_g + it
                for kc in range(8):
                    inc(TE.matmul(PS[4][:, 0:NH],
                                  xnTblk3[:, 512 * kc + 128 * it:512 * kc + 128 * (it + 1)],
                                  WbR[:, NH * kc:NH * (kc + 1)],
                                  start=(kc == 0), stop=(kc == 7)), "pe")
                for nh2 in range(2):
                    ps_v = PS[2 + nh2]
                    for kc in range(8):
                        inc(TE.matmul(ps_v[:, 0:512],
                                      xnTblk3[:, 512 * kc + 128 * it:512 * kc + 128 * (it + 1)],
                                      WidR[:, 1024 * kc + 512 * nh2:1024 * kc + 512 * (nh2 + 1)],
                                      start=(kc == 0), stop=(kc == 7)), "pe")
                m = snap(); wait(SC, "pe", "sg", at=m)
                inc(SC.activation(betac[:, NH * i:NH * (i + 1)], PS[4][:, 0:NH], AF.Sigmoid), "act")
                SC.copy(evC[:, 0:512], PS[2][:, 0:512])
                inc(SC.copy(fcast[:, 0:512], PS[3][:, 0:512]), "act")
                m = snap(); wait(G, "act", at=m)
                inc(G.dma_start(out=dr["bid"][128 * i:128 * (i + 1), 0:512], in_=evC[:, 0:512]), "sg", 16)
                inc(G.dma_start(out=dr["bid"][128 * i:128 * (i + 1), 512:1024], in_=fcast[:, 0:512]), "sg", 16)
                m = snap(); wait(TE, "act", at=m)
            for fc in range(8):
                ps_q = PS[fc % 2]
                m = snap(); wait(TE, "act", at=m)
                for kc in range(8):
                    inc(TE.matmul(ps_q[:, 0:512],
                                  GW1R[:, 1024 * kc + 128 * fc:1024 * kc + 128 * (fc + 1)],
                                  xnTblk3[:, 512 * kc:512 * (kc + 1)],
                                  start=(kc == 0), stop=(kc == 7)), "pe")
                m = snap(); wait(SC, "pe", "sg", at=m)
                inc(SC.copy(evC[:, 0:512], ps_q[:, 0:512]), "act")
                m = snap(); wait(G, "act", at=m)
                inc(G.dma_start(out=dr["bgt"][128 * fc:128 * (fc + 1), 512 * blk_g:512 * (blk_g + 1)],
                                in_=evC[:, 0:512]), "sg", 16)

        ca2 = Carve(AB1)
        oW = ca2.take(8 * 1024)
        qfb = ca2.take(1024)
        kfb = ca2.take(1024)
        ktb = ca2.take(1024)
        vtb = ca2.take(1024)
        kbt = ca2.take(1024)
        vbt = ca2.take(1024)
        lstb = ca2.take(1024)
        lltb = ca2.take(1024)
        dotb = ca2.take(1024)
        idtb = ca2.take(1024)
        gtb = ca2.take(1024)
        gelb = ca2.take(1024)
        otb = ca2.take(1024)
        Sbf_ = [ca2.take(512) for _ in range(2 * NH)]
        dL = ca2.take(512)
        dT = ca2.take(512)
        dTT = ca2.take(512)
        dP2 = ca2.take(512)
        dP2T = ca2.take(512)
        dAttnT = ca2.take(512)
        dU = ca2.take(1024)
        dUadj = ca2.take(1024)
        dWTT = ca2.take(1024)
        dKbf = ca2.take(1024)
        vfb = dKbf
        qtb = dU
        fstg2 = dWTT
        ofb = otb
        statb = AB2
        probs_f = AFX[:, 4120:4136]
        stat_sc = AFX[:, 4136:4200]
        b2fs = AFX[:, 4200:4712]
        b2fs2 = AFX[:, 4712:5120]

        load_w_chunks("bWo", oW, 1024, 8)
        mWo = snap()
        for sq_ in range(cfg.nseq):
            m = snap(); wait(V, "act", "pool", "pe", at=m)
            for k in range(2 * NH):
                inc(V.memset(Sbf_[k][:, :], 0.0), "dve")
            for sg_i in range(cfg.NTS):
                i = sq_ * cfg.NTS + sg_i
                trow0 = 128 * i
                m = snap()
                wait(SY, "pe", "dve", at=m)
                inc(SY.dma_start(out=qfb[:, 0:1024],
                                 in_=dr["bq"].rearrange("(c p) t -> p c t", p=128)
                                 [:, :, trow0:trow0 + 128]), "sq", 16)
                inc(SY.dma_start(out=kfb[:, 0:1024],
                                 in_=dr["bk"].rearrange("(c p) t -> p c t", p=128)
                                 [:, :, trow0:trow0 + 128]), "sq", 16)
                inc(SY.dma_start(out=gtb[:, 0:1024],
                                 in_=dr["bgt"].rearrange("(c p) t -> p c t", p=128)
                                 [:, :, trow0:trow0 + 128]), "sq", 16)
                inc(SY.dma_start(out=idtb[:, 0:1024], in_=dr["bid"][trow0:trow0 + 128, :]), "sq", 16)
                inc(SY.dma_start(out=vfb[:, 0:1024],
                                 in_=dr["bv"].rearrange("(c p) t -> p c t", p=128)
                                 [:, :, trow0:trow0 + 128]), "sq", 16)
                mLd = snap()
                wait(TE, "sq", at=mLd)
                m = snap(); wait(TE, "dve", at=m)
                for d8 in range(8):
                    inc(TE.transpose(PB[0][:, 128 * d8:128 * (d8 + 1)],
                                     qfb[:, 128 * d8:128 * (d8 + 1)], identt[:]), "pe")
                m = snap(); wait(V, "pe", at=m)
                for d8 in range(8):
                    inc(V.tensor_copy(qtb[:, 128 * d8:128 * (d8 + 1)],
                                      PB[0][:, 128 * d8:128 * (d8 + 1)]), "dve")
                m = snap(); wait(TE, "dve", at=m)
                for d8 in range(8):
                    inc(TE.transpose(PB[0][:, 128 * d8:128 * (d8 + 1)],
                                     kfb[:, 128 * d8:128 * (d8 + 1)], identt[:]), "pe")
                m = snap(); wait(V, "pe", at=m)
                for d8 in range(8):
                    inc(V.tensor_copy(ktb[:, 128 * d8:128 * (d8 + 1)],
                                      PB[0][:, 128 * d8:128 * (d8 + 1)]), "dve")
                m = snap(); wait(TE, "dve", at=m)
                for d8 in range(8):
                    inc(TE.transpose(PB[0][:, 128 * d8:128 * (d8 + 1)],
                                     vfb[:, 128 * d8:128 * (d8 + 1)], identt[:]), "pe")
                m = snap(); wait(V, "pe", at=m)
                for d8 in range(8):
                    inc(V.tensor_copy(vtb[:, 128 * d8:128 * (d8 + 1)],
                                      PB[0][:, 128 * d8:128 * (d8 + 1)]), "dve")
                for tb, fb in [(qtb, qfb), (ktb, kfb)]:
                    for h in range(NH):
                        m = snap(); wait(SC, "dve", at=m)
                        inc(SC.activation(b2fs[:, 0:256], tb[:, 256 * h:256 * (h + 1)],
                                          AF.Square, accum_out=stat_sc[:, h:h + 1]), "act")
                    m = snap(); wait(SC, "act", at=m)
                    inc(SC.activation(stat_sc[:, 4:8], stat_sc[:, 0:4], AF.Sqrt, bias=epsc[:, 1:2]), "act")
                    m = snap(); wait(V, "act", at=m)
                    inc(V.reciprocal(stat_sc[:, 8:12], stat_sc[:, 4:8]), "dve")
                    m = snap(); wait(V, "dve", at=m)
                    for h in range(NH):
                        inc(V.tensor_scalar(tb[:, 256 * h:256 * (h + 1)],
                                            tb[:, 256 * h:256 * (h + 1)],
                                            stat_sc[:, 8 + h:9 + h], 0.0, ALU.mult, ALU.add), "dve")
                    m = snap(); wait(TE, "dve", at=m)
                    for d8 in range(8):
                        inc(TE.transpose(PB[0][:, 128 * d8:128 * (d8 + 1)],
                                         tb[:, 128 * d8:128 * (d8 + 1)], identt[:]), "pe")
                    m = snap(); wait(V, "pe", at=m)
                    for d8 in range(8):
                        inc(V.tensor_copy(fb[:, 128 * d8:128 * (d8 + 1)],
                                          PB[0][:, 128 * d8:128 * (d8 + 1)]), "dve")
                m = snap(); wait(V, "dve", at=m)
                for h in range(NH):
                    V.tensor_scalar(kbt[:, 256 * h:256 * (h + 1)], ktb[:, 256 * h:256 * (h + 1)],
                                    betac[:, NH * i + h:NH * i + h + 1], 0.0, ALU.mult, ALU.add)
                    inc(V.tensor_scalar(vbt[:, 256 * h:256 * (h + 1)], vtb[:, 256 * h:256 * (h + 1)],
                                        betac[:, NH * i + h:NH * i + h + 1], 0.0, ALU.mult, ALU.add), "dve")
                for (dspill, dstt) in [(dr["bls"], lstb), (dr["bll"], lltb)]:
                    m = snap(); wait(SY, "pe", "dve", at=m)
                    inc(SY.dma_start(out=fstg2[:, 0:1024],
                                     in_=dspill.rearrange("(c p) t -> p c t", p=128)
                                     [:, :, trow0:trow0 + 128]), "sq", 16)
                    m = snap(); wait(TE, "sq", "dve", at=m)
                    for d8 in range(8):
                        inc(TE.transpose(PB[1][:, 128 * d8:128 * (d8 + 1)],
                                         fstg2[:, 128 * d8:128 * (d8 + 1)], identt[:]), "pe")
                    m = snap(); wait(V, "pe", at=m)
                    for d8 in range(8):
                        inc(V.tensor_copy(dstt[:, 128 * d8:128 * (d8 + 1)],
                                          PB[1][:, 128 * d8:128 * (d8 + 1)]), "dve")
                if getattr(cfg, "delta_off", False):
                    m = snap(); wait(V, "pe", at=m)
                    inc(V.memset(dotb[:, 0:1024], 0.0), "dve")
                else:
                    m = snap(); wait(TE, "dve", at=m)
                    for h in range(NH):
                        for d2 in range(2):
                            inc(TE.transpose(PB[0][:, 256 * h + 128 * d2:256 * h + 128 * (d2 + 1)],
                                             kbt[:, 256 * h + 128 * d2:256 * h + 128 * (d2 + 1)],
                                             identt[:]), "pe")
                    m = snap(); wait(V, "pe", at=m)
                    inc(V.tensor_copy(dKbf[:, 0:1024], PB[0][:, 0:1024]), "dve")

                    if getattr(cfg, 'delta_upto', 99) >= 1:
                        m = snap(); wait(TE, "dve", at=m)
                        for h in range(NH):
                            for d2 in range(2):
                                inc(TE.matmul(PS[h][:, 0:128],
                                              dKbf[:, 256 * h + 128 * d2:256 * h + 128 * (d2 + 1)],
                                              kfb[:, 256 * h + 128 * d2:256 * h + 128 * (d2 + 1)],
                                              start=(d2 == 0), stop=(d2 == 1)), "pe")
                        m = snap(); wait(SC, "pe", at=m)
                        for h in range(NH):
                            inc(SC.mul(dL[:, 128 * h:128 * (h + 1)], PS[h][:, 0:128], -1.0), "act")
                        m = snap(); wait(G, "act", at=m)
                        inc(G.affine_select(dL[:, 0:512], dL[:, 0:512], [[0, 4], [-1, 128]],
                                            ALU.is_ge, 0.0, base=-1, channel_multiplier=1), "pool")


                    if getattr(cfg, 'delta_upto', 99) >= 2:
                        for h in range(NH):
                            hc0 = 128 * h
                            m = snap(); wait(V, "pool", "act", at=m)
                            inc(V.memset(dT[0:64, hc0 + 64:hc0 + 128], 0.0), "dve")
                            for hf_ in range(2):
                                b0 = 64 * hf_
                                slb = slice(b0, b0 + 64)
                                idb = identt[slb, slb]
                                m = snap(); wait(V, "act", at=m)
                                inc(V.tensor_add(dT[slb, hc0 + b0:hc0 + b0 + 64],
                                                 dL[slb, hc0 + b0:hc0 + b0 + 64], idb), "dve")
                                inc(V.tensor_copy(dP2[slb, hc0:hc0 + 64],
                                                  dL[slb, hc0 + b0:hc0 + b0 + 64]), "dve")
                                m = snap(); wait(TE, "dve", at=m)
                                inc(TE.transpose(PB[0][0:64, 0:64], dP2[slb, hc0:hc0 + 64], idb), "pe")
                                m = snap(); wait(V, "pe", at=m)
                                inc(V.tensor_copy(dP2T[slb, hc0:hc0 + 64], PB[0][0:64, 0:64]), "dve")
                                for st_ in range(5):
                                    m = snap(); wait(TE, "dve", at=m)
                                    inc(TE.matmul(PS[1][0:64, 0:64], dP2T[slb, hc0:hc0 + 64],
                                                  dP2[slb, hc0:hc0 + 64], start=True, stop=True), "pe")
                                    m = snap(); wait(SC, "pe", at=m)
                                    inc(SC.copy(dP2[slb, hc0:hc0 + 64], PS[1][0:64, 0:64]), "act")
                                    m = snap(); wait(TE, "act", at=m)
                                    inc(TE.transpose(PB[0][0:64, 0:64], dP2[slb, hc0:hc0 + 64], idb), "pe")
                                    m = snap(); wait(V, "pe", at=m)
                                    inc(V.tensor_copy(dP2T[slb, hc0:hc0 + 64], PB[0][0:64, 0:64]), "dve")
                                    m = snap(); wait(TE, "dve", "act", at=m)
                                    inc(TE.matmul(PS[2][0:64, 0:64], dP2T[slb, hc0:hc0 + 64],
                                                  dT[slb, hc0 + b0:hc0 + b0 + 64],
                                                  start=True, stop=False), "pe")
                                    inc(TE.matmul(PS[2][0:64, 0:64], idb,
                                                  dT[slb, hc0 + b0:hc0 + b0 + 64],
                                                  start=False, stop=True), "pe")
                                    m = snap(); wait(SC, "pe", at=m)
                                    inc(SC.copy(dT[slb, hc0 + b0:hc0 + b0 + 64], PS[2][0:64, 0:64]), "act")
                                    m = snap(); wait(TE, "act", at=m)
                            idb1 = identt[64:128, 64:128]
                            inc(TE.transpose(PB[0][0:64, 0:64], dL[64:128, hc0:hc0 + 64], idb1), "pe")
                            m = snap(); wait(V, "pe", at=m)
                            inc(V.tensor_copy(dP2[0:64, hc0:hc0 + 64], PB[0][0:64, 0:64]), "dve")
                            m = snap(); wait(TE, "dve", at=m)
                            inc(TE.matmul(PS[1][0:64, 0:64], dP2[0:64, hc0:hc0 + 64],
                                          dT[0:64, hc0:hc0 + 64], start=True, stop=True), "pe")
                            m = snap(); wait(SC, "pe", at=m)
                            inc(SC.copy(dP2T[0:64, hc0:hc0 + 64], PS[1][0:64, 0:64]), "act")
                            m = snap(); wait(TE, "act", at=m)
                            inc(TE.transpose(PB[0][0:64, 0:64], dT[64:128, hc0 + 64:hc0 + 128], idb1), "pe")
                            m = snap(); wait(V, "pe", at=m)
                            inc(V.tensor_copy(dP2[0:64, hc0:hc0 + 64], PB[0][0:64, 0:64]), "dve")
                            m = snap(); wait(TE, "dve", at=m)
                            inc(TE.matmul(PS[2][0:64, 0:64], dP2[0:64, hc0:hc0 + 64],
                                          dP2T[0:64, hc0:hc0 + 64], start=True, stop=True), "pe")
                            m = snap(); wait(SC, "pe", at=m)
                            inc(SC.copy(dT[64:128, hc0:hc0 + 64], PS[2][0:64, 0:64]), "act")
                    if getattr(cfg, 'delta_upto', 99) >= 5:
                        m = snap(); wait(TE, "act", "dve", at=m)
                        for h in range(NH):
                            inc(TE.transpose(PB[0][:, 128 * h:128 * (h + 1)],
                                             dT[:, 128 * h:128 * (h + 1)], identt[:]), "pe")
                        m = snap(); wait(V, "pe", at=m)
                        inc(V.tensor_copy(dTT[:, 0:512], PB[0][:, 0:512]), "dve")
                        if getattr(cfg, "dbg_b", -1) == i:
                            inc(G.dma_start(out=dr["dbg"][:, 0:128], in_=dT[:, 0:128]), "sg", 16)
                            inc(G.dma_start(out=dr["dbg"][:, 128:256], in_=dL[:, 0:128]), "sg", 16)
                            inc(G.dma_start(out=dr["dbg"][:, 256:512], in_=dKbf[:, 0:256]), "sg", 16)
                            inc(G.dma_start(out=dr["dbg"][:, 1152:1408], in_=kfb[:, 0:256]), "sg", 16)
                            inc(G.dma_start(out=dr["dbg"][:, 1408:1664], in_=qfb[:, 0:256]), "sg", 16)
                            m2 = snap(); wait(SC, "sg", at=m2); wait(V, "sg", at=m2)


                    if getattr(cfg, 'delta_upto', 99) >= 6:
                        m = snap(); wait(TE, "dve", at=m)
                        for h in range(NH):
                            inc(TE.matmul(PS[h][:, 0:256], dTT[:, 128 * h:128 * (h + 1)],
                                          vbt[:, 256 * h:256 * (h + 1)], start=True, stop=True), "pe")
                            for d2 in range(2):
                                inc(TE.matmul(PS[4 + h // 2][:, 512 * 0 + 256 * (h % 2) + 0:0]
                                              if False else
                                              PS[4 + h // 2][:, 256 * (h % 2):256 * (h % 2) + 128]
                                              if False else
                                              PS[4 + h // 2][:, 256 * (h % 2) + 128 * d2:
                                                             256 * (h % 2) + 128 * (d2 + 1)],
                                              kbt[:, 256 * h + 128 * d2:256 * h + 128 * (d2 + 1)],
                                              dTT[:, 128 * h:128 * (h + 1)], start=True, stop=True), "pe")
                        m = snap(); wait(SC, "pe", at=m)
                        for h in range(NH):
                            SC.copy(dU[:, 256 * h:256 * (h + 1)], PS[h][:, 0:256])
                            inc(SC.copy(dWTT[:, 256 * h:256 * (h + 1)],
                                        PS[4 + h // 2][:, 256 * (h % 2):256 * (h % 2) + 256]), "act")


                    if getattr(cfg, 'delta_upto', 99) >= 7:
                        m = snap(); wait(TE, "act", at=m)
                        for h in range(NH):
                            for d2 in range(2):
                                inc(TE.matmul(PS[h][:, 0:128],
                                              kfb[:, 256 * h + 128 * d2:256 * h + 128 * (d2 + 1)],
                                              qfb[:, 256 * h + 128 * d2:256 * h + 128 * (d2 + 1)],
                                              start=(d2 == 0), stop=(d2 == 1)), "pe")
                        m = snap(); wait(SC, "pe", at=m)
                        for h in range(NH):
                            inc(SC.copy(dAttnT[:, 128 * h:128 * (h + 1)], PS[h][:, 0:128]), "act")
                        m = snap(); wait(G, "act", at=m)
                        inc(G.affine_select(dAttnT[:, 0:512], dAttnT[:, 0:512], [[0, 4], [1, 128]],
                                            ALU.is_ge, 0.0, base=0, channel_multiplier=-1), "pool")


                    if getattr(cfg, 'delta_upto', 99) >= 8:
                        m = snap(); wait(TE, "pool", "dve", at=m)
                        for h in range(NH):
                            for d2 in range(2):
                                inc(TE.matmul(PS[h][:, 0:256],
                                              dWTT[:, 256 * h + 128 * d2:256 * h + 128 * (d2 + 1)],
                                              Sbf_[h][:, 256 * d2:256 * (d2 + 1)],
                                              start=(d2 == 0), stop=(d2 == 1)), "pe")
                        m = snap(); wait(V, "pe", at=m)
                        for h in range(NH):
                            inc(V.tensor_sub(dUadj[:, 256 * h:256 * (h + 1)],
                                             dU[:, 256 * h:256 * (h + 1)], PS[h][:, 0:256]), "dve")


                    if getattr(cfg, 'delta_upto', 99) >= 9:
                        m = snap(); wait(TE, "dve", at=m)
                        for h in range(NH):
                            for d2 in range(2):
                                inc(TE.matmul(PS[h][:, 0:256],
                                              qfb[:, 256 * h + 128 * d2:256 * h + 128 * (d2 + 1)],
                                              Sbf_[h][:, 256 * d2:256 * (d2 + 1)],
                                              start=(d2 == 0), stop=False), "pe")
                            inc(TE.matmul(PS[h][:, 0:256], dAttnT[:, 128 * h:128 * (h + 1)],
                                          dUadj[:, 256 * h:256 * (h + 1)],
                                          start=False, stop=True), "pe")
                        m = snap(); wait(V, "pe", at=m)
                        for h in range(NH):
                            inc(V.tensor_copy(dotb[:, 256 * h:256 * (h + 1)], PS[h][:, 0:256]), "dve")


                    if getattr(cfg, 'delta_upto', 99) >= 10:
                        for hp in range(2):
                            m = snap(); wait(TE, "dve", at=m)
                            for hh in range(2):
                                h = 2 * hp + hh
                                for d2 in range(2):
                                    inc(TE.matmul(PS[4 + hh][:, 256 * d2:256 * (d2 + 1)],
                                                  ktb[:, 256 * h + 128 * d2:256 * h + 128 * (d2 + 1)],
                                                  dUadj[:, 256 * h:256 * (h + 1)],
                                                  start=True, stop=True), "pe")
                            m = snap(); wait(V, "pe", at=m)
                            for hh in range(2):
                                h = 2 * hp + hh
                                for d2 in range(2):
                                    inc(V.tensor_tensor(Sbf_[h][:, 256 * d2:256 * (d2 + 1)],
                                                        Sbf_[h][:, 256 * d2:256 * (d2 + 1)],
                                                        PS[4 + hh][:, 256 * d2:256 * (d2 + 1)],
                                                        ALU.add), "dve")

                    if getattr(cfg, 'delta_upto', 99) < 10:
                        m = snap(); wait(V, "pe", at=m)
                        inc(V.memset(dotb[:, 0:1024], 0.0), "dve")
                if getattr(cfg, "stats_off", False):
                    m = snap(); wait(V, "act", at=m)
                    inc(V.memset(b2fs2[:, 0:64], 0.5), "dve")
                else:
                    tensors = [(lstb, 0), (lltb, 4), (dotb, 8), (vtb, 12)]
                    svw = b2fs2[:, 0:64].rearrange("p (h s) -> p s h", s=16)
                    for (tt, so) in tensors:
                        t3 = tt[:, 0:1024].rearrange("p (h d) -> p h d", h=NH)
                        m = snap(); wait(V, "act", at=m)
                        inc(V.reduce_sum(stat_sc[:, 0:4], t3, axis=AX.X), "dve")
                        inc(V.reduce_sum(stat_sc[:, 4:8], t3, axis=AX.X,
                                         apply_absolute_value=True), "dve")
                        m = snap(); wait(SC, "dve", at=m)
                        inc(SC.activation(fsc[:, 0:1024], tt[:, 0:1024], AF.Square), "act")
                        m = snap(); wait(V, "act", at=m)
                        inc(V.reduce_sum(stat_sc[:, 8:12],
                                         fsc[:, 0:1024].rearrange("p (h d) -> p h d", h=NH),
                                         axis=AX.X), "dve")
                        m = snap(); wait(V, "dve", at=m)
                        V.tensor_scalar(svw[:, so + 0, :], stat_sc[:, 0:4], 1.0 / 256.0,
                                        0.0, ALU.mult, ALU.add)
                        V.tensor_scalar(svw[:, so + 2, :], stat_sc[:, 4:8], 1.0 / 256.0,
                                        0.0, ALU.mult, ALU.add)
                        V.tensor_scalar(stat_sc[:, 12:16], stat_sc[:, 8:12], 1.0 / 256.0,
                                        0.0, ALU.mult, ALU.add)
                        inc(V.tensor_scalar(stat_sc[:, 16:20], stat_sc[:, 0:4], 1.0 / 256.0,
                                            0.0, ALU.mult, ALU.add), "dve")
                        inc(V.tensor_tensor(stat_sc[:, 20:24], stat_sc[:, 16:20],
                                            stat_sc[:, 16:20], ALU.mult), "dve")
                        inc(V.tensor_sub(svw[:, so + 1, :], stat_sc[:, 12:16],
                                         stat_sc[:, 20:24]), "dve")
                        m = snap(); wait(SC, "dve", at=m)
                        inc(SC.activation(svw[:, so + 3, :], stat_sc[:, 8:12], AF.Sqrt), "act")
                m = snap(); wait(TE, "dve", "act", at=m)
                for h in range(NH):
                    inc(TE.transpose(PS[4][0:16, 0:128], b2fs2[:, 16 * h:16 * (h + 1)],
                                     identf[:]), "pe")
                    m = snap(); wait(SC, "pe", at=m)
                    inc(SC.copy(statb[0:16, 128 * h:128 * (h + 1)], PS[4][0:16, 0:128]), "act")
                    m = snap(); wait(TE, "act", at=m)
                for h in range(NH):
                    for hc in range(8):
                        ps_g = PS[hc % 2]
                        inc(TE.matmul(ps_g[:, 0:128],
                                      gW1b[:, 128 * hc:128 * (hc + 1)],
                                      statb[0:16, 128 * h:128 * (h + 1)],
                                      start=True, stop=True), "pe")
                        m = snap(); wait(V, "pe", at=m)
                        inc(V.scalar_tensor_tensor(b2fs[:, 0:128], ps_g[:, 0:128],
                                                   gb1c[:, hc:hc + 1],
                                                   gtb[:, 128 * hc:128 * (hc + 1)],
                                                   ALU.add, ALU.add), "dve")
                        m = snap(); wait(SC, "dve", at=m)
                        inc(SC.activation(gelb[:, 128 * hc:128 * (hc + 1)], b2fs[:, 0:128],
                                          AF.Gelu), "act")
                        m = snap(); wait(TE, "act", at=m)
                    for hc in range(8):
                        inc(TE.matmul(PS[2][:, 0:NH], gelb[:, 128 * hc:128 * (hc + 1)],
                                      gW2c[:, NH * hc:NH * (hc + 1)],
                                      start=(hc == 0), stop=(hc == 7)), "pe")
                    m = snap(); wait(V, "pe", at=m)
                    inc(V.tensor_tensor(stat_sc[:, 0:4], PS[2][:, 0:NH],
                                        bcmisc[:, 12:16], ALU.add), "dve")
                    m = snap(); wait(V, "dve", at=m)
                    inc(V.tensor_tensor(stat_sc[:, 4:8], stat_sc[:, 0:4],
                                        bc16[:, 4 * h:4 * (h + 1)], ALU.mult), "dve")
                    m = snap(); wait(V, "dve", at=m)
                    inc(V.reduce_max(stat_sc[:, 8:9], stat_sc[:, 4:8], axis=AX.X), "dve")
                    m = snap(); wait(V, "dve", at=m)
                    inc(V.tensor_scalar(stat_sc[:, 9:10], stat_sc[:, 8:9], -1.0, 0.0,
                                        ALU.mult, ALU.add), "dve")
                    m = snap(); wait(SC, "dve", at=m)
                    inc(SC.activation(stat_sc[:, 12:16], stat_sc[:, 4:8], AF.Exp,
                                      bias=stat_sc[:, 9:10], accum_out=stat_sc[:, 10:11]), "act")
                    m = snap(); wait(V, "act", at=m)
                    inc(V.reciprocal(stat_sc[:, 11:12], stat_sc[:, 10:11]), "dve")
                    m = snap(); wait(V, "dve", at=m)
                    inc(V.tensor_scalar(stat_sc[:, 12:16], stat_sc[:, 12:16],
                                        stat_sc[:, 11:12], bcmisc[:, h:h + 1],
                                        ALU.mult, ALU.mult), "dve")
                    m = snap(); wait(V, "dve", at=m)
                    inc(V.tensor_tensor(probs_f[:, 4 * h:4 * (h + 1)], stat_sc[:, 12:16],
                                        bcfl[:, 4 * h:4 * (h + 1)], ALU.add), "dve")
                m = snap(); wait(V, "dve", at=m)
                for h in range(NH):
                    inc(V.tensor_scalar(stat_sc[:, 40 + h:41 + h], probs_f[:, 4 * h:4 * h + 1],
                                        bcmisc[:, 4 + h:5 + h], 0.0, ALU.add, ALU.add), "dve")
                for h in range(NH):
                    sl = slice(256 * h, 256 * (h + 1))
                    m = snap(); wait(V, "dve", at=m)
                    inc(V.tensor_scalar(b2fs[:, 0:256], lstb[:, sl], stat_sc[:, 40 + h:41 + h],
                                        0.0, ALU.mult, ALU.add), "dve")
                    inc(V.scalar_tensor_tensor(b2fs[:, 0:256], lltb[:, sl],
                                               probs_f[:, 4 * h + 1:4 * h + 2],
                                               b2fs[:, 0:256], ALU.mult, ALU.add), "dve")
                    inc(V.scalar_tensor_tensor(b2fs[:, 0:256], dotb[:, sl],
                                               probs_f[:, 4 * h + 2:4 * h + 3],
                                               b2fs[:, 0:256], ALU.mult, ALU.add), "dve")
                    inc(V.scalar_tensor_tensor(b2fs[:, 0:256], vtb[:, sl],
                                               probs_f[:, 4 * h + 3:4 * h + 4],
                                               b2fs[:, 0:256], ALU.mult, ALU.add), "dve")
                    inc(V.scalar_tensor_tensor(b2fs[:, 0:256], idtb[:, sl],
                                               bcmisc[:, 8 + h:9 + h],
                                               b2fs[:, 0:256], ALU.mult, ALU.add), "dve")
                    m = snap(); wait(SC, "dve", at=m)
                    inc(SC.activation(b2fs2[:, 0:256], b2fs[:, 0:256], AF.Square,
                                      accum_out=stat_sc[:, 48:49]), "act")
                    m = snap(); wait(V, "act", at=m)
                    inc(V.tensor_scalar(stat_sc[:, 49:50], stat_sc[:, 48:49], 1.0 / 256.0,
                                        0.0, ALU.mult, ALU.add), "dve")
                    m = snap(); wait(SC, "dve", at=m)
                    inc(SC.activation(stat_sc[:, 50:51], stat_sc[:, 49:50], AF.Sqrt,
                                      bias=epsc[:, 0:1]), "act")
                    m = snap(); wait(V, "act", at=m)
                    inc(V.reciprocal(stat_sc[:, 51:52], stat_sc[:, 50:51]), "dve")
                    m = snap(); wait(V, "dve", at=m)
                    inc(V.tensor_scalar(b2fs[:, 0:256], b2fs[:, 0:256], stat_sc[:, 51:52],
                                        0.0, ALU.mult, ALU.add), "dve")
                    m = snap(); wait(V, "dve", at=m)
                    inc(V.tensor_tensor(otb[:, sl], b2fs[:, 0:256], onwb[:, sl], ALU.mult), "dve")
                if getattr(cfg, "dbg_b", -1) == i:
                    for j_, tt_ in enumerate([qtb, ktb, vtb, kbt, vbt, lstb, lltb,
                                              dotb, idtb, otb, gelb, gtb, qfb, kfb]):
                        inc(G.dma_start(out=dr["dbg2"][:, 1024 * j_:1024 * (j_ + 1)],
                                        in_=tt_[:, 0:1024]), "sg", 16)
                m = snap(); wait(TE, "dve", at=m)
                for d8 in range(8):
                    inc(TE.transpose(PB[0][:, 128 * d8:128 * (d8 + 1)],
                                     otb[:, 128 * d8:128 * (d8 + 1)], identt[:]), "pe")
                m = snap(); wait(V, "pe", at=m)
                for d8 in range(8):
                    inc(V.tensor_copy(ofb[:, 128 * d8:128 * (d8 + 1)],
                                      PB[0][:, 128 * d8:128 * (d8 + 1)]), "dve")
                if sq_ == 0 and sg_i == 0:
                    wait(TE, "act", at=mWo)
                m = snap(); wait(TE, "dve", at=m)
                for nh2 in range(2):
                    ps_o = PS[nh2]
                    for kc in range(8):
                        inc(TE.matmul(ps_o[:, 0:512],
                                      ofb[:, 128 * kc:128 * (kc + 1)],
                                      oW[:, 1024 * kc + 512 * nh2:1024 * kc + 512 * (nh2 + 1)],
                                      start=(kc == 0), stop=(kc == 7)), "pe")
                m = snap(); wait(V, "pe", at=m)
                for nh2 in range(2):
                    inc(V.tensor_tensor(stream[:, D * i + 512 * nh2:D * i + 512 * (nh2 + 1)],
                                        stream[:, D * i + 512 * nh2:D * i + 512 * (nh2 + 1)],
                                        PS[nh2][:, 0:512], ALU.add), "dve")
                m = snap(); wait(TE, "dve", at=m)

    stages = getattr(cfg, "stages", ("a", "f1", "b", "f2"))
    for st_name in stages:
        if st_name == "a":
            emit_attn_a2()
        elif st_name == "f1":
            emit_ffn(final=False)
        elif st_name == "b":
            emit_attn_b()
        elif st_name == "f2":
            emit_ffn(final=True)
    if stages[-1] != "f2":
        for i in range(cfg.NT):
            m = snap(); wait(SC, "out", "dve", at=m)
            inc(SC.copy(fsc[:, 0:1024], stream[:, D * i:D * (i + 1)]), "act")
            m = snap(); wait(G, "act", at=m)
            inc(G.dma_start(out=out[128 * i:128 * (i + 1), :], in_=fsc[:, 0:1024]), "out", 16)

    m = snap()
    wait(G, "out", at=m)
    wait(G, "sq", "sg", "dve", "act", "pe", "pool", at=m)

    for en in ["sync", "gpsimd", "vector", "scalar", "tensor"]:
        fns = EBUF[en]
        getattr(blk, en)(lambda e, fns=fns: [f(e) for f in fns])

    blk_t.__exit__(None, None, None)
    for t in reversed(ectx[:-1]):
        try:
            t.__exit__(None, None, None)
        except Exception:
            pass
    return nc


def kernel(**inputs):
    cfg = CFG()
    consts = host_consts(cfg)
    nc = build(cfg)
    x = np.ascontiguousarray(inputs["L0_attn_a_x"], dtype=np.float32)
    namemap = {
        "wq": "L0_attn_a_wq", "wk": "L0_attn_a_wk", "wv": "L0_attn_a_wv",
        "wgq": "L0_attn_a_wgq", "wgk": "L0_attn_a_wgk", "wgv": "L0_attn_a_wgv",
        "r_out_w": "L0_attn_a_r_out_w", "glw": "L0_attn_a_gate_local_w",
        "grw": "L0_attn_a_gate_remote_w", "w_out": "L0_attn_a_w_out",
        "bWq": "L0_attn_b_Wq", "bWk": "L0_attn_b_Wk", "bWv": "L0_attn_b_Wv",
        "cwq": "L0_attn_b_cwq", "cwk": "L0_attn_b_cwk", "cwv": "L0_attn_b_cwv",
        "bWb": "L0_attn_b_Wb", "bWid": "L0_attn_b_Wid", "alpha_id": "L0_attn_b_alpha_id",
        "firs": "L0_attn_b_firs", "firl": "L0_attn_b_firl",
        "gW1": "L0_attn_b_gW1", "gb1": "L0_attn_b_gb1", "gW2": "L0_attn_b_gW2",
        "gb2": "L0_attn_b_gb2", "log_temp": "L0_attn_b_log_temp",
        "floor_param": "L0_attn_b_floor_param", "crl": "L0_attn_b_conv_res_logit",
        "onw": "L0_attn_b_o_norm_w", "bWo": "L0_attn_b_Wo",
        "w1": "L0_ffn_w1", "b1": "L0_ffn_b1", "w2": "L0_ffn_w2", "b2": "L0_ffn_b2",
    }
    shared = {k: np.ascontiguousarray(inputs[v], dtype=np.float32) for k, v in namemap.items()}
    shared.update(consts)
    in_maps = []
    for c in range(8):
        mm = dict(shared)
        mm["x"] = np.ascontiguousarray(x[2 * c:2 * c + 2].reshape(cfg.T, cfg.D))
        in_maps.append(mm)
    res = run_bass_kernel_spmd(nc, in_maps, core_ids=list(range(8)))
    outs = [res.results[c]["out"].reshape(2, cfg.S, cfg.D) for c in range(8)]
    return np.concatenate(outs, axis=0)


# version 04
# speedup vs baseline: 1.2305x; 1.0267x over previous
import numpy as np
import ml_dtypes
import concourse.bass as bass
import concourse.mybir as mybir
from concourse.bass_utils import run_bass_kernel_spmd

F32 = mybir.dt.float32
BF16 = mybir.dt.bfloat16
U8 = mybir.dt.uint8
AF = mybir.ActivationFunctionType
ALU = mybir.AluOpType
AX = mybir.AxisListType
BF_NP = ml_dtypes.bfloat16


class CFG:
    def __init__(self, nseq=2, S=2048, ncc=8):
        self.nseq, self.S, self.D = nseq, S, 1024
        self.T = nseq * S
        self.NT = self.T // 128
        self.NTS = S // 128
        self.W = 256
        self.NB = S // self.W
        self.NLAT = 64
        self.TPL = S // self.NLAT
        self.F = 4096
        self.NH = 4
        self.GH = 1024
        self.GTS = min(8, self.NTS)
        self.LPG = self.GTS * 128 // self.TPL
        self.cc_group = list(range(ncc))


_AUTOSEM = {"vector": "dve", "scalar": "act", "gpsimd": "pool", "tensor": "pe"}
_NOWRAP = {"wait_ge", "sem_inc"}
_NOINC = {"dma_start", "indirect_dma_start", "collective_compute", "dma_start_transpose"}


class InstProxy:
    __slots__ = ("q",)
    def __init__(self):
        self.q = []
    def then_inc(self, semh, n=1):
        self.q.append((semh, n))
        return self


class EP:
    def __init__(self, blk, name, sem, cnt, buf):
        object.__setattr__(self, "_blk", blk)
        object.__setattr__(self, "_name", name)
        object.__setattr__(self, "_sem", sem)
        object.__setattr__(self, "_cnt", cnt)
        object.__setattr__(self, "_buf", buf)
    def __getattr__(self, attr):
        name, sem, cnt, buf = self._name, self._sem, self._cnt, self._buf
        own = _AUTOSEM.get(name)
        def call(*a, **k):
            gates = []
            selfwait = 0
            auto = False
            if attr in _NOINC:
                for s_ in ("sq", "sg", "out"):
                    if cnt[s_] > 0:
                        gates.append((s_, cnt[s_]))
            elif own is not None and attr not in _NOWRAP:
                if name != "tensor":
                    selfwait = cnt[own]
                auto = True
                cnt[own] += 1
            px = InstProxy()
            def emit(e, attr=attr, a=a, k=k, gates=gates, selfwait=selfwait,
                     auto=auto, px=px):
                for s_, v_ in gates:
                    e.wait_ge(sem[s_], v_)
                ins = getattr(e, attr)(*a, **k)
                if selfwait > 0:
                    ins._wait_ge(sem[own], selfwait)
                if auto:
                    ins.then_inc(sem[own], 1)
                for semh, n_ in px.q:
                    ins.then_inc(semh, n_)
                return ins
            buf[name].append(emit)
            return px
        return call


def host_consts(cfg):
    c = {}
    c["ident"] = np.eye(128, dtype=BF_NP)
    i = np.arange(cfg.W)[:, None]
    j = np.arange(2 * cfg.W)[None, :]
    base = (j > i) & (j <= i + cfg.W)
    c["winm"] = base.astype(BF_NP)
    c["winm_first"] = (base & (j >= cfg.W)).astype(BF_NP)
    pp = np.zeros((cfg.GTS, 128, cfg.LPG), np.float32)
    for g in range(cfg.GTS):
        for ii in range(128):
            pp[g, ii, g * (128 // cfg.TPL) + ii // cfg.TPL] = 1.0 / cfg.TPL
    c["poolpat"] = pp.astype(BF_NP)
    return c


def build(cfg):
    nc = bass.Bass("TRN2", target_bir_lowering=False)
    T, S, D, F, NH = cfg.T, cfg.S, cfg.D, cfg.F, cfg.NH
    lpg = cfg.LPG

    ap = {}
    def di(name, shape, dt=F32):
        ap[name] = nc.dram_tensor(name, shape, dt, kind="ExternalInput").ap()
    di("x", [T, D])
    for n, sh in [("wq", [D, D]), ("wk", [D, D]), ("wv", [D, D]),
                  ("wgq", [D, 512]), ("wgk", [D, 512]), ("wgv", [D, 512]),
                  ("r_out_w", [512, D]), ("glw", [16, 64]), ("grw", [16, 64]),
                  ("w_out", [D, D]), ("bWq", [D, D]), ("bWk", [D, D]),
                  ("bWv", [D, D]), ("cwq", [D, 4]), ("cwk", [D, 4]),
                  ("cwv", [D, 4]), ("bWb", [D, NH]), ("bWid", [D, D]),
                  ("alpha_id", [NH]), ("firs", [NH, 256, 3]), ("firl", [NH, 256, 63]),
                  ("gW1", [D + 16, cfg.GH]), ("gb1", [cfg.GH]), ("gW2", [cfg.GH, NH]),
                  ("gb2", [NH]), ("log_temp", [NH, 4]), ("floor_param", [NH, 4]),
                  ("crl", [NH]), ("onw", [256]), ("bWo", [D, D]),
                  ("w1", [F, D]), ("b1", [F]), ("w2", [D, F]), ("b2", [D])]:
        di(n, sh)
    di("ident", [128, 128], BF16)
    di("winm", [256, 512], BF16)
    di("winm_first", [256, 512], BF16)
    di("poolpat", [cfg.GTS, 128, lpg], BF16)

    out = nc.dram_tensor("out", [T, D], F32, kind="ExternalOutput").ap()

    dr = {}
    dr["h"] = nc.dram_tensor("h_sp", [F, T], F32).ap()
    dr["w1iT"] = nc.dram_tensor("w1iT", [D, F], BF16).ap()
    dr["w2qT"] = nc.dram_tensor("w2qT", [F, D], BF16).ap()
    dr["aQ"] = nc.dram_tensor("aQ", [D, T], BF16).ap()
    dr["aK"] = nc.dram_tensor("aK", [D, T], BF16).ap()
    dr["aV"] = nc.dram_tensor("aV", [T, 16 * 65], BF16).ap()
    dr["aGQ"] = nc.dram_tensor("aGQ", [512, T], BF16).ap()
    dr["bq"] = nc.dram_tensor("bq", [D, T], BF16).ap()
    dr["bk"] = nc.dram_tensor("bk", [D, T], BF16).ap()
    dr["bv"] = nc.dram_tensor("bv", [D, T], BF16).ap()
    dr["bid"] = nc.dram_tensor("bid", [T, D], BF16).ap()
    dr["bgt"] = nc.dram_tensor("bgt", [cfg.GH, T], BF16).ap()
    dr["bls"] = nc.dram_tensor("bls", [D, T], BF16).ap()
    dr["bll"] = nc.dram_tensor("bll", [D, T], BF16).ap()
    dr["bounce"] = nc.dram_tensor("bounce", [128, 128], F32).ap()
    dr["dbg"] = nc.dram_tensor("dbg", [128, 4096], BF16).ap()
    dr["dbg2"] = nc.dram_tensor("dbg2", [128, 16384], BF16).ap()
    cc_in_t = nc.dram_tensor("cc_in", [128, 128], F32)
    cc_out_t = nc.dram_tensor("cc_out", [128, 128], F32)

    ectx = []
    def sb(name, p, f, dt=BF16):
        t = nc.sbuf_tensor(name, [p, f], dt)
        h = t.__enter__(); ectx.append(t); return h
    def psal(name, p, f, dt=F32):
        t = nc.psum_tensor(name, [p, f], dt)
        h = t.__enter__(); ectx.append(t); return h
    def mksem(name):
        t = nc.semaphore(name)
        h = t.__enter__(); ectx.append(t); return h

    stream = sb("stream", 128, cfg.NT * D)
    mu = sb("mu", 128, cfg.NT, F32)
    rstd = sb("rstd", 128, cfg.NT, F32)
    maxc = sb("maxc", 128, 4, F32)
    identt = sb("identt", 128, 128)
    identf = sb("identf", 128, 128, F32)
    onesf = sb("onesf", 1, 128, F32)
    winmt = sb("winmt", 128, 2 * 512)
    winft = sb("winft", 128, 2 * 512)
    poolt = sb("poolt", 128, cfg.GTS * lpg)
    g_exp = sb("g_exp", 128, 8 * 32)
    cw = sb("cw", 128, 8 * 12, F32)
    firsc = sb("firsc", 128, 8 * 3, F32)
    firlc = sb("firlc", 128, 8 * 63, F32)
    gb1c = sb("gb1c", 128, 8, F32)
    gW1b = sb("gW1b", 16, cfg.GH)
    gW2c = sb("gW2c", 128, 8 * NH)
    bc16 = sb("bc16", 128, 16, F32)
    bcfl = sb("bcfl", 128, 16, F32)
    bcmisc = sb("bcmisc", 128, 16, F32)
    onwb = sb("onwb", 128, NH * 256, F32)
    s_w1c = sb("s_w1c", 128, F // 128, F32)
    b1c = sb("b1c", 128, F // 128, F32)
    sw2r = sb("sw2r", 1, D, F32)
    b2r = sb("b2r", 1, D, F32)
    srow = sb("srow", 1, 128, F32)
    bq2b = sb("bq2b", 128, D, F32)
    sInv = sb("sInv", 128, 4, F32)
    ccst = sb("ccst", 128, 128, F32)
    ffnc = sb("ffnc", 128, 96, F32)
    epsc = sb("epsc", 128, 4, F32)
    betac = sb("betac", 128, cfg.NT * NH, F32)
    GKs = sb("GKs", 128, 4 * cfg.NLAT)
    GVs = sb("GVs", cfg.NLAT, 8 * 65)
    pooled = sb("pooled", cfg.NLAT, D)
    pooledT = sb("pooledT", 128, 8 * cfg.NLAT)

    AB1 = sb("AB1", 128, 32768)
    AB2 = sb("AB2", 128, 6144)
    AFX = sb("AFX", 128, 5120, F32)
    AU8 = sb("AU8", 128, 2048, U8)
    AI32 = sb("AI32", 128, 1024, mybir.dt.int32)

    class Carve:
        def __init__(self, base):
            self.base = base; self.off = 0
        def take(self, ncols):
            v = self.base[:, self.off:self.off + ncols]
            self.off += ncols
            assert self.off <= self.base.shape[1], f"arena overflow {self.off}"
            return v

    PS = [psal(f"ps{i}", 128, 512, F32) for i in range(6)]
    PB = [psal(f"pb{i}", 128, 1024, BF16) for i in range(2)]

    sem = {n: mksem(n) for n in ["sq", "sg", "pe", "dve", "act", "pool", "cc", "out"]}
    cnt = {n: 0 for n in sem}

    blk_t = nc.Block()
    blk = blk_t.__enter__(); ectx.append(blk_t)
    EBUF = {n: [] for n in ["vector", "scalar", "tensor", "gpsimd", "sync"]}
    V = EP(blk, "vector", sem, cnt, EBUF); SC = EP(blk, "scalar", sem, cnt, EBUF)
    TE = EP(blk, "tensor", sem, cnt, EBUF); G = EP(blk, "gpsimd", sem, cnt, EBUF)
    SY = EP(blk, "sync", sem, cnt, EBUF)

    _AUTO = {"dve", "act", "pool", "pe"}
    def inc(instr, s, n=1):
        if s in _AUTO:
            return instr
        instr.then_inc(sem[s], n)
        cnt[s] += n
        return instr

    def snap():
        return dict(cnt)

    def wait(eng, *names, at=None):
        own = _AUTOSEM.get(eng._name)
        if eng._name == "tensor":
            own = None
        for s in names:
            if s == own:
                continue
            v = cnt[s] if at is None else at[s]
            if v > 0:
                eng.wait_ge(sem[s], v)

    wstg = AFX[:, 0:1024]
    fsc = AFX[:, 1024:2048]
    fsc2 = AFX[:, 2048:3072]
    b_sc = AFX[:, 3072:3584]
    b_sc2 = AFX[:, 3584:4096]

    inc(SY.dma_start(out=identt[:], in_=ap["ident"]), "sq", 16)
    inc(SY.dma_start(out=winmt[:, 0:512], in_=ap["winm"][0:128, :]), "sq", 16)
    inc(SY.dma_start(out=winmt[:, 512:1024], in_=ap["winm"][128:256, :]), "sq", 16)
    inc(SY.dma_start(out=winft[:, 0:512], in_=ap["winm_first"][0:128, :]), "sq", 16)
    inc(SY.dma_start(out=winft[:, 512:1024], in_=ap["winm_first"][128:256, :]), "sq", 16)
    inc(SY.dma_start(out=poolt[:].rearrange("p (g l) -> p g l", g=cfg.GTS),
                     in_=ap["poolpat"].rearrange("g p l -> p g l")), "sq", 16)
    V.memset(onesf[:], 1.0)
    V.memset(maxc[:], 0.0)
    V.memset(epsc[:, 0:1], 1e-5)
    V.memset(epsc[:, 1:2], 1e-6)
    G.memset(identf[:], 1.0)
    inc(G.affine_select(identf[:], identf[:], [[-1, 128]], ALU.is_equal, 0.0,
                        base=0, channel_multiplier=1), "pool")
    for i in range(cfg.NT):
        m = snap(); wait(SY, "act", at=m)
        inc(SY.dma_start(out=wstg[:], in_=ap["x"][128 * i:128 * (i + 1), :]), "sq", 16)
        m = snap(); wait(SC, "sq", at=m)
        inc(SC.copy(stream[:, D * i:D * (i + 1)], wstg[:]), "act")

    for ti, nm in enumerate(["cwq", "cwk", "cwv"]):
        for c in range(8):
            inc(SY.dma_start(out=cw[:, (ti * 8 + c) * 4:(ti * 8 + c) * 4 + 4],
                             in_=ap[nm][128 * c:128 * (c + 1), :]), "sq", 16)
    for c in range(8):
        inc(SY.dma_start(out=firsc[:, c * 3:(c + 1) * 3],
                         in_=ap["firs"].rearrange("h d k -> (h d) k")[128 * c:128 * (c + 1), :]), "sq", 16)
        inc(SY.dma_start(out=firlc[:, c * 63:(c + 1) * 63],
                         in_=ap["firl"].rearrange("h d k -> (h d) k")[128 * c:128 * (c + 1), :]), "sq", 16)
    m = snap(); wait(SY, "act", at=m)
    inc(SY.dma_start(out=wstg[0:16, :], in_=ap["gW1"][D:D + 16, :]), "sq", 16)
    m = snap(); wait(SC, "sq", at=m)
    inc(SC.copy(gW1b[:, :], wstg[0:16, :]), "act")
    for c in range(8):
        m = snap(); wait(SY, "act", at=m)
        inc(SY.dma_start(out=wstg[:, 0:NH], in_=ap["gW2"][128 * c:128 * (c + 1), :]), "sq", 16)
        m = snap(); wait(SC, "sq", at=m)
        inc(SC.copy(gW2c[:, NH * c:NH * (c + 1)], wstg[:, 0:NH]), "act")
    m = snap(); wait(SY, "act", at=m)
    inc(SY.dma_start(out=wstg[0:8, 0:128], in_=ap["gb1"].rearrange("(a b) -> a b", b=128)), "sq", 16)
    inc(SY.dma_start(out=wstg[32:64, 0:128], in_=ap["b1"].rearrange("(a b) -> a b", b=128)), "sq", 16)
    inc(SY.dma_start(out=b2r[0:1, :], in_=ap["b2"][None, :]), "sq", 16)
    m = snap(); wait(TE, "sq", "pool", at=m)
    TE.transpose(PS[0][0:128, 0:8], wstg[0:8, 0:128], identf[0:8, 0:8])
    inc(TE.transpose(PS[1][0:128, 0:32], wstg[32:64, 0:128], identf[32:64, 32:64]), "pe")
    m = snap(); wait(V, "pe", at=m)
    V.tensor_copy(gb1c[:, :], PS[0][0:128, 0:8])
    inc(V.tensor_copy(b1c[:, :], PS[1][0:128, 0:32]), "dve")

    m = snap(); wait(V, "dve", at=m)
    inc(V.memset(g_exp[:], 0.0), "dve")
    for gi, nm in enumerate(["glw", "grw"]):
        m = snap(); wait(SY, "act", "pe", at=m)
        inc(SY.dma_start(out=wstg[0:16, 0:64], in_=ap[nm]), "sq", 16)
        m = snap(); wait(SC, "sq", at=m)
        inc(SC.copy(AB2[0:16, 0:64], wstg[0:16, 0:64]), "act")
        m = snap(); wait(TE, "act", "dve", at=m)
        inc(TE.transpose(PB[0][0:64, 0:16], AB2[0:16, 0:64], identt[0:16, 0:16]), "pe")
        m = snap(); wait(V, "pe", "dve", at=m)
        inc(V.tensor_copy(AB2[0:64, 64:80], PB[0][0:64, 0:16]), "dve")
        m = snap(); wait(V, "dve", at=m)
        for c in range(8):
            V.tensor_copy(g_exp[0:64, 32 * c + 16 * gi + 2 * c:32 * c + 16 * gi + 2 * c + 1],
                          AB2[0:64, 64 + 2 * c:64 + 2 * c + 1])
            inc(V.tensor_copy(g_exp[64:128, 32 * c + 16 * gi + 2 * c + 1:32 * c + 16 * gi + 2 * c + 2],
                              AB2[0:64, 64 + 2 * c + 1:64 + 2 * c + 2]), "dve")

    inc(SY.dma_start(out=srow[0:1, 0:16], in_=ap["log_temp"].rearrange("a b -> (a b)")[None, :]), "sq", 16)
    m = snap(); wait(SC, "sq", at=m)
    inc(SC.activation(srow[0:1, 16:32], srow[0:1, 0:16], AF.Exp), "act")
    m = snap(); wait(V, "act", at=m)
    inc(V.reciprocal(srow[0:1, 32:48], srow[0:1, 16:32]), "dve")
    m = snap(); wait(TE, "dve", at=m)
    inc(TE.matmul(PS[0][:, 0:16], onesf[0:1, :], srow[0:1, 32:48], start=True, stop=True), "pe")
    m = snap(); wait(V, "pe", at=m)
    inc(V.tensor_copy(bc16[:], PS[0][:, 0:16]), "dve")
    m = snap(); wait(SY, "dve", at=m)
    inc(SY.dma_start(out=srow[0:1, 0:16], in_=ap["floor_param"].rearrange("a b -> (a b)")[None, :]), "sq", 16)
    m = snap(); wait(SC, "sq", at=m)
    inc(SC.activation(srow[0:1, 16:32], srow[0:1, 0:16], AF.Sigmoid), "act")
    m = snap(); wait(V, "act", at=m)
    inc(V.tensor_scalar(srow[0:1, 32:48], srow[0:1, 16:32], 0.05, 0.0, ALU.mult, ALU.add), "dve")
    m = snap(); wait(V, "dve", at=m)
    inc(V.reduce_sum(srow[0:1, 48:52],
                     srow[0:1, 32:48].rearrange("p (h c) -> p h c", h=4), axis=AX.X), "dve")
    m = snap(); wait(V, "dve", at=m)
    inc(V.tensor_scalar(srow[0:1, 52:56], srow[0:1, 48:52], -1.0, 1.0, ALU.mult, ALU.add), "dve")
    inc(SY.dma_start(out=srow[0:1, 72:76], in_=ap["crl"][None, :]), "sq", 16)
    inc(SY.dma_start(out=srow[0:1, 60:64], in_=ap["alpha_id"][None, :]), "sq", 16)
    inc(SY.dma_start(out=srow[0:1, 64:68], in_=ap["gb2"][None, :]), "sq", 16)
    m = snap(); wait(SC, "sq", at=m); wait(SC, "dve", at=m)
    inc(SC.activation(srow[0:1, 56:60], srow[0:1, 72:76], AF.Sigmoid), "act")
    m = snap(); wait(TE, "act", at=m)
    inc(TE.matmul(PS[0][:, 0:16], onesf[0:1, :], srow[0:1, 32:48], start=True, stop=True), "pe")
    inc(TE.matmul(PS[1][:, 0:16], onesf[0:1, :], srow[0:1, 52:68], start=True, stop=True), "pe")
    m = snap(); wait(V, "pe", at=m)
    V.tensor_copy(bcfl[:], PS[0][:, 0:16])
    inc(V.tensor_copy(bcmisc[:], PS[1][:, 0:16]), "dve")
    m = snap(); wait(SY, "dve", at=m)
    inc(SY.dma_start(out=srow[0:1, 0:128], in_=ap["onw"][None, 0:128]), "sq", 16)
    m = snap(); wait(TE, "sq", at=m)
    inc(TE.matmul(PS[2][:, 0:128], onesf[0:1, :], srow[0:1, 0:128], start=True, stop=True), "pe")
    m = snap(); wait(SY, "pe", at=m)
    inc(SY.dma_start(out=srow[0:1, 0:128], in_=ap["onw"][None, 128:256]), "sq", 16)
    m = snap(); wait(TE, "sq", at=m)
    inc(TE.matmul(PS[2][:, 128:256], onesf[0:1, :], srow[0:1, 0:128], start=True, stop=True), "pe")
    m = snap(); wait(V, "pe", at=m)
    for h in range(NH):
        inc(V.tensor_copy(onwb[:, 256 * h:256 * (h + 1)], PS[2][:, 0:256]), "dve")

    for c in range(F // 128):
        m = snap(); wait(SY, "act", "dve", "sg", "pe", at=m)
        inc(SY.dma_start(out=wstg[:], in_=ap["w1"][128 * c:128 * (c + 1), :]), "sq", 16)
        m = snap(); wait(V, "sq", at=m)
        inc(V.tensor_reduce(fsc[:, 0:1], wstg[:], axis=AX.X, op=ALU.max,
                            apply_absolute_value=True), "dve")
        m = snap(); wait(V, "dve", at=m)
        inc(V.tensor_scalar(fsc[:, 1:2], fsc[:, 0:1], 1e-8, 1.0 / 127.0, ALU.max, ALU.mult), "dve")
        m = snap(); wait(V, "dve", at=m)
        V.tensor_copy(s_w1c[:, c:c + 1], fsc[:, 1:2])
        inc(V.reciprocal(fsc[:, 2:3], fsc[:, 1:2]), "dve")
        m = snap(); wait(V, "dve", at=m)
        inc(V.tensor_scalar(fsc2[:, 0:1024], wstg[:], fsc[:, 2:3], 0.0, ALU.mult, ALU.add), "dve")
        m = snap(); wait(V, "dve", at=m)
        inc(V.tensor_copy(AI32[:, 0:1024], fsc2[:, 0:1024]), "dve")
        m = snap(); wait(V, "dve", at=m)
        inc(V.tensor_copy(AB2[:, 0:1024], AI32[:, 0:1024]), "dve")
        m = snap(); wait(TE, "dve", at=m)
        for d8 in range(8):
            inc(TE.transpose(PB[0][:, 128 * d8:128 * (d8 + 1)],
                             AB2[:, 128 * d8:128 * (d8 + 1)], identt[:]), "pe")
        m = snap(); wait(V, "pe", at=m)
        inc(V.tensor_copy(AB2[:, 1024:2048], PB[0][:, 0:1024]), "dve")
        m = snap(); wait(G, "dve", at=m)
        for d8 in range(8):
            inc(G.dma_start(out=dr["w1iT"][128 * d8:128 * (d8 + 1), 128 * c:128 * (c + 1)],
                            in_=AB2[:, 1024 + 128 * d8:1024 + 128 * (d8 + 1)]), "sg", 16)
    for c in range(D // 128):
        for q4 in range(4):
            m = snap(); wait(SY, "dve", "sg", at=m)
            inc(SY.dma_start(out=wstg[:], in_=ap["w2"][128 * c:128 * (c + 1),
                                                       1024 * q4:1024 * (q4 + 1)]), "sq", 16)
            m = snap(); wait(V, "sq", at=m)
            inc(V.tensor_reduce(fsc[:, q4:q4 + 1], wstg[:], axis=AX.X, op=ALU.max,
                                apply_absolute_value=True), "dve")
        m = snap(); wait(V, "dve", at=m)
        inc(V.tensor_reduce(fsc[:, 4:5], fsc[:, 0:4], axis=AX.X, op=ALU.max), "dve")
        m = snap(); wait(V, "dve", at=m)
        inc(V.tensor_scalar(fsc[:, 5:6], fsc[:, 4:5], 1e-8, 1.0 / 127.0, ALU.max, ALU.mult), "dve")
        m = snap(); wait(V, "dve", at=m)
        V.tensor_copy(bq2b[:, c:c + 1], fsc[:, 5:6])
        inc(V.reciprocal(fsc[:, 6:7], fsc[:, 5:6]), "dve")
        for q4 in range(4):
            m = snap(); wait(SY, "dve", "sg", at=m)
            inc(SY.dma_start(out=wstg[:], in_=ap["w2"][128 * c:128 * (c + 1),
                                                       1024 * q4:1024 * (q4 + 1)]), "sq", 16)
            m = snap(); wait(V, "sq", at=m)
            inc(V.tensor_scalar(fsc2[:, 0:1024], wstg[:], fsc[:, 6:7], 0.0, ALU.mult, ALU.add), "dve")
            m = snap(); wait(SC, "dve", at=m)
            m = snap(); wait(V, "dve", at=m)
            inc(V.tensor_copy(AI32[:, 0:1024], fsc2[:, 0:1024]), "dve")
            m = snap(); wait(V, "dve", at=m)
            inc(V.tensor_copy(AB2[:, 0:1024], AI32[:, 0:1024]), "dve")
            m = snap(); wait(V, "dve", at=m)
            inc(V.tensor_scalar(AB2[:, 0:1024], AB2[:, 0:1024], bq2b[:, c:c + 1], 0.0,
                                ALU.mult, ALU.add), "dve")
            m = snap(); wait(TE, "dve", at=m)
            for d8 in range(8):
                inc(TE.transpose(PB[0][:, 128 * d8:128 * (d8 + 1)],
                                 AB2[:, 128 * d8:128 * (d8 + 1)], identt[:]), "pe")
            m = snap(); wait(V, "pe", at=m)
            inc(V.tensor_copy(AB2[:, 1024:2048], PB[0][:, 0:1024]), "dve")
            m = snap(); wait(G, "dve", at=m)
            for d8 in range(8):
                inc(G.dma_start(out=dr["w2qT"][1024 * q4 + 128 * d8:1024 * q4 + 128 * (d8 + 1),
                                               128 * c:128 * (c + 1)],
                                in_=AB2[:, 1024 + 128 * d8:1024 + 128 * (d8 + 1)]), "sg", 16)
    m = snap(); wait(TE, "dve", at=m)
    inc(TE.transpose(PS[3][0:8, 0:128], bq2b[:, 0:8], identf[:]), "pe")
    m = snap(); wait(V, "pe", at=m)
    inc(V.tensor_copy(fsc[0:8, 0:128], PS[3][0:8, 0:128]), "dve")
    m = snap(); wait(G, "dve", at=m)
    inc(G.dma_start(out=dr["bounce"][0:8, 0:128], in_=fsc[0:8, 0:128]), "sg", 16)
    m = snap(); wait(SY, "sg", at=m)
    inc(SY.dma_start(out=sw2r[0:1, :],
                     in_=dr["bounce"].rearrange("a b -> (a b)")[None, 0:1024]), "sq", 16)
    m = snap(); wait(V, "sq", at=m)

    def emit_ln_stats(with_rowmax):
        for i in range(cfg.NT):
            st = stream[:, D * i:D * (i + 1)]
            m0 = snap(); wait(V, "act", at=m0)
            inc(V.reduce_sum(fsc[:, 0:1], st, axis=AX.X), "dve")
            m = snap(); wait(SC, "dve", at=m)
            inc(SC.activation(fsc2[:, 0:1024], st, AF.Square, accum_out=fsc[:, 1:2]), "act")
            m = snap(); wait(V, "act", at=m)
            V.tensor_scalar(fsc[:, 2:3], fsc[:, 0:1], 1.0 / D, 0.0, ALU.mult, ALU.add)
            inc(V.tensor_scalar(fsc[:, 3:4], fsc[:, 1:2], 1.0 / D, 0.0, ALU.mult, ALU.add), "dve")
            m = snap(); wait(V, "dve", at=m)
            V.tensor_copy(mu[:, i:i + 1], fsc[:, 2:3])
            inc(V.tensor_tensor(fsc[:, 4:5], fsc[:, 2:3], fsc[:, 2:3], ALU.mult), "dve")
            m = snap(); wait(V, "dve", at=m)
            inc(V.tensor_sub(fsc[:, 5:6], fsc[:, 3:4], fsc[:, 4:5]), "dve")
            m = snap(); wait(SC, "dve", at=m)
            inc(SC.activation(fsc[:, 6:7], fsc[:, 5:6], AF.Sqrt, bias=epsc[:, 0:1]), "act")
            m = snap(); wait(V, "act", at=m)
            inc(V.reciprocal(rstd[:, i:i + 1], fsc[:, 6:7]), "dve")
            if with_rowmax:
                inc(V.reduce_max(fsc[:, 8:9], st, axis=AX.X), "dve")
                m = snap(); wait(V, "dve", at=m)
                inc(V.tensor_scalar(fsc[:, 9:10], fsc[:, 8:9], mu[:, i:i + 1],
                                    rstd[:, i:i + 1], ALU.subtract, ALU.mult), "dve")
                m = snap(); wait(V, "dve", at=m)
                inc(V.tensor_max(maxc[:, 0:1], maxc[:, 0:1], fsc[:, 9:10]), "dve")

    def norm_tile(i, dst):
        return V.tensor_scalar(dst, stream[:, D * i:D * (i + 1)],
                               mu[:, i:i + 1], rstd[:, i:i + 1], ALU.subtract, ALU.mult)

    def emit_global_max(out_recip_col, out_val_col):
        m = snap(); wait(V, "dve", at=m)
        inc(V.memset(ccst[:], 0.0), "dve")
        m = snap(); wait(V, "dve", at=m)
        inc(V.tensor_copy(ccst[:, 0:1], maxc[:, 0:1]), "dve")
        m = snap(); wait(G, "dve", at=m)
        inc(G.dma_start(out=cc_in_t.ap(), in_=ccst[:]), "sg", 16)
        m = snap(); wait(G, "sg", at=m)
        inc(G.collective_compute("AllReduce", ALU.max, [list(cfg.cc_group)],
                                 [cc_in_t.ap().opt()], [cc_out_t.ap().opt()]), "cc", 1)
        m = snap(); wait(G, "cc", at=m)
        inc(G.dma_start(out=ccst[:], in_=cc_out_t.ap()), "sg", 16)
        m = snap(); wait(TE, "sg", at=m)
        inc(TE.transpose(PS[4][0:1, 0:128], ccst[:, 0:1], identf[:]), "pe")
        m = snap(); wait(V, "pe", at=m)
        inc(V.reduce_max(srow[0:1, 0:1], PS[4][0:1, 0:128], axis=AX.X), "dve")
        m = snap(); wait(V, "dve", at=m)
        inc(V.tensor_scalar(srow[0:1, 1:2], srow[0:1, 0:1], 1e-8, 1.0 / 255.0,
                            ALU.max, ALU.mult), "dve")
        m = snap(); wait(V, "dve", at=m)
        inc(V.reciprocal(srow[0:1, 2:3], srow[0:1, 1:2]), "dve")
        m = snap(); wait(TE, "dve", at=m)
        inc(TE.matmul(PS[4][:, 0:1], onesf[0:1, :], srow[0:1, 2:3], start=True, stop=True), "pe")
        inc(TE.matmul(PS[4][:, 1:2], onesf[0:1, :], srow[0:1, 1:2], start=True, stop=True), "pe")
        m = snap(); wait(V, "pe", at=m)
        V.tensor_copy(sInv[:, out_recip_col:out_recip_col + 1], PS[4][:, 0:1])
        inc(V.tensor_copy(sInv[:, out_val_col:out_val_col + 1], PS[4][:, 1:2]), "dve")
        m = snap(); wait(V, "dve", at=m)
        inc(V.memset(maxc[:, 0:1], 0.0), "dve")

    def load_w_chunks(name, dst, fdim, nchunk):
        for c in range(nchunk):
            m = snap(); wait(SY, "act", at=m)
            inc(SY.dma_start(out=wstg[:, 0:fdim], in_=ap[name][128 * c:128 * (c + 1), :]), "sq", 16)
            m = snap(); wait(SC, "sq", "sg", "pe", "dve", at=m)
            inc(SC.copy(dst[:, fdim * c:fdim * (c + 1)], wstg[:, 0:fdim]), "act")

    def xnT_tile(i, dstb, pbi=0):
        m = snap(); wait(V, "act", at=m)
        inc(norm_tile(i, fsc[:, 0:1024]), "dve")
        m = snap(); wait(SC, "dve", "pe", at=m)
        inc(SC.copy(AB2[:, 0:1024], fsc[:, 0:1024]), "act")
        m = snap(); wait(TE, "act", "dve", at=m)
        for d8 in range(8):
            inc(TE.transpose(PB[pbi][:, 128 * d8:128 * (d8 + 1)],
                             AB2[:, 128 * d8:128 * (d8 + 1)], identt[:]), "pe")
        m = snap(); wait(V, "pe", at=m)
        for d8 in range(8):
            inc(V.tensor_copy(dstb[:, 128 * d8:128 * (d8 + 1)],
                              PB[pbi][:, 128 * d8:128 * (d8 + 1)]), "dve")

    GKs2 = sb("GKs2", 128, 4 * cfg.NLAT)
    GVs2 = sb("GVs2", cfg.NLAT, 8 * 65)

    def emit_attn_a2():
        ca = Carve(AB1)
        wres = ca.take(8 * 1024)
        xnTb = ca.take(1024)
        evb = ca.take(1040)

        emit_ln_stats(with_rowmax=False)

        ngrp = cfg.NTS // cfg.GTS
        for sq_ in range(cfg.nseq):
            t0 = sq_ * cfg.NTS
            for g in range(ngrp):
                for it in range(cfg.GTS):
                    i = t0 + g * cfg.GTS + it
                    m = snap(); wait(V, "act", at=m)
                    inc(norm_tile(i, fsc[:, 0:1024]), "dve")
                    m = snap(); wait(SC, "dve", "pe", at=m)
                    inc(SC.copy(AB2[:, 0:1024], fsc[:, 0:1024]), "act")
                    m = snap(); wait(TE, "act", at=m)
                    if it == 0:
                        wait(TE, "dve", at=m)
                    for hf in range(2):
                        inc(TE.matmul(PS[hf][0:lpg, 0:512],
                                      poolt[:, lpg * it:lpg * (it + 1)],
                                      AB2[:, 512 * hf:512 * (hf + 1)],
                                      start=(it == 0), stop=(it == cfg.GTS - 1)), "pe")
                m = snap(); wait(V, "pe", at=m)
                for hf in range(2):
                    inc(V.tensor_copy(pooled[lpg * g:lpg * (g + 1), 512 * hf:512 * (hf + 1)],
                                      PS[hf][0:lpg, 0:512]), "dve")
            m = snap(); wait(TE, "dve", at=m)
            for d8 in range(8):
                inc(TE.transpose(PB[0][0:128, 128 * d8:128 * d8 + cfg.NLAT],
                                 pooled[0:cfg.NLAT, 128 * d8:128 * (d8 + 1)],
                                 identt[0:cfg.NLAT, 0:cfg.NLAT]), "pe")
            m = snap(); wait(V, "pe", at=m)
            for d8 in range(8):
                dcol = cfg.NLAT * d8
                if sq_ == 0:
                    inc(V.tensor_copy(pooledT[:, dcol:dcol + cfg.NLAT],
                                      PB[0][0:128, 128 * d8:128 * d8 + cfg.NLAT]), "dve")
                else:
                    inc(V.tensor_copy(evb[:, dcol:dcol + cfg.NLAT],
                                      PB[0][0:128, 128 * d8:128 * d8 + cfg.NLAT]), "dve")
        pooledT2 = evb

        for wn, kind in [("wgk", "K"), ("wgv", "V")]:
            load_w_chunks(wn, wres, 512, 8)
            for sq_ in range(cfg.nseq):
                pt = pooledT if sq_ == 0 else pooledT2
                gk_dst = GKs if sq_ == 0 else GKs2
                gv_dst = GVs if sq_ == 0 else GVs2
                m = snap(); wait(TE, "act", "dve", at=m)
                if kind == "K":
                    for fc in range(4):
                        for kc in range(8):
                            inc(TE.matmul(PS[fc % 2][0:128, 0:cfg.NLAT],
                                          wres[:, 512 * kc + 128 * fc:512 * kc + 128 * (fc + 1)],
                                          pt[:, cfg.NLAT * kc:cfg.NLAT * (kc + 1)],
                                          start=(kc == 0), stop=(kc == 7)), "pe")
                        m = snap(); wait(V, "pe", at=m)
                        inc(V.tensor_copy(gk_dst[:, cfg.NLAT * fc:cfg.NLAT * (fc + 1)],
                                          PS[fc % 2][0:128, 0:cfg.NLAT]), "dve")
                        m = snap(); wait(TE, "dve", at=m)
                else:
                    for kc in range(8):
                        inc(TE.matmul(PS[2][0:cfg.NLAT, 0:512],
                                      pt[:, cfg.NLAT * kc:cfg.NLAT * (kc + 1)],
                                      wres[:, 512 * kc:512 * (kc + 1)],
                                      start=(kc == 0), stop=(kc == 7)), "pe")
                    m = snap(); wait(V, "pe", at=m)
                    for lh in range(8):
                        inc(V.tensor_copy(gv_dst[0:cfg.NLAT, 65 * lh:65 * lh + 64],
                                          PS[2][0:cfg.NLAT, 64 * lh:64 * (lh + 1)]), "dve")
                    m = snap(); wait(V, "dve", at=m)
                    for lh in range(8):
                        inc(V.memset(gv_dst[0:cfg.NLAT, 65 * lh + 64:65 * (lh + 1)], 1.0), "dve")

        caA = Carve(AB1)
        wqR = caA.take(8 * 1024)
        wkR = caA.take(8 * 1024)
        wgqR = caA.take(8 * 512)
        wvR = caA.take(8 * 1024)
        xnTblk = caA.take(8 * 512)
        evS = [AB2[:, 1024:1536], AB2[:, 1536:2048]]
        evT = AB2[:, 2048:3088]
        load_w_chunks("wq", wqR, 1024, 8)
        load_w_chunks("wk", wkR, 1024, 8)
        load_w_chunks("wgq", wgqR, 512, 8)
        load_w_chunks("wv", wvR, 1024, 8)
        m = snap(); wait(V, "act", at=m)
        for h in range(16):
            inc(V.memset(evT[:, 65 * h + 64:65 * (h + 1)], 1.0), "dve")
        for bi in range(T // 512):
            for it in range(4):
                i = 4 * bi + it
                m = snap(); wait(V, "act", at=m)
                inc(norm_tile(i, fsc[:, 0:1024]), "dve")
                m = snap(); wait(SC, "dve", "pe", at=m)
                inc(SC.copy(AB2[:, 0:1024], fsc[:, 0:1024]), "act")
                m = snap(); wait(TE, "act", "dve", at=m)
                for d8 in range(8):
                    inc(TE.transpose(PB[it % 2][:, 128 * d8:128 * (d8 + 1)],
                                     AB2[:, 128 * d8:128 * (d8 + 1)], identt[:]), "pe")
                m = snap(); wait(V, "pe", at=m)
                for d8 in range(8):
                    inc(V.tensor_copy(xnTblk[:, 512 * d8 + 128 * it:512 * d8 + 128 * (it + 1)],
                                      PB[it % 2][:, 128 * d8:128 * (d8 + 1)]), "dve")
            m = snap(); wait(TE, "dve", at=m)
            for wres, dst, fdim in [(wqR, dr["aQ"], 1024), (wkR, dr["aK"], 1024),
                                    (wgqR, dr["aGQ"], 512)]:
                for fc in range(fdim // 128):
                    ps_q = PS[fc % 2]
                    m = snap(); wait(TE, "act", at=m)
                    for kc in range(8):
                        inc(TE.matmul(ps_q[:, 0:512],
                                      wres[:, fdim * kc + 128 * fc:fdim * kc + 128 * (fc + 1)],
                                      xnTblk[:, 512 * kc:512 * (kc + 1)],
                                      start=(kc == 0), stop=(kc == 7)), "pe")
                    m = snap(); wait(SC, "pe", "sg", at=m)
                    inc(SC.copy(evS[fc % 2][:, 0:512], ps_q[:, 0:512]), "act")
                    m = snap(); wait(G, "act", at=m)
                    inc(G.dma_start(out=dst[128 * fc:128 * (fc + 1), 512 * bi:512 * (bi + 1)],
                                    in_=evS[fc % 2][:, 0:512]), "sg", 16)
            for it in range(4):
                i = 4 * bi + it
                m = snap(); wait(TE, "dve", "act", at=m)
                for nh2 in range(2):
                    ps_v = PS[2 + nh2]
                    for kc in range(8):
                        inc(TE.matmul(ps_v[:, 0:512],
                                      xnTblk[:, 512 * kc + 128 * it:512 * kc + 128 * (it + 1)],
                                      wvR[:, 1024 * kc + 512 * nh2:1024 * kc + 512 * (nh2 + 1)],
                                      start=(kc == 0), stop=(kc == 7)), "pe")
                m = snap(); wait(V, "pe", "sg", at=m)
                for h in range(16):
                    inc(V.tensor_copy(evT[:, 65 * h:65 * h + 64],
                                      PS[2 + h // 8][:, 64 * (h % 8):64 * (h % 8) + 64]), "dve")
                m = snap(); wait(G, "dve", at=m)
                inc(G.dma_start(out=dr["aV"][128 * i:128 * (i + 1), :], in_=evT[:, 0:1040]), "sg", 16)

        ca2 = Carve(AB1)
        rW = ca2.take(4 * 1024)
        oW = ca2.take(8 * 1024)
        Qb = ca2.take(8 * 256)
        Kb = [ca2.take(8 * 256), ca2.take(8 * 256)]
        Vb = [ca2.take(2 * 1040), ca2.take(2 * 1040)]
        GQb = ca2.take(4 * 256)
        Ptb = ca2.take(512)
        PTb = ca2.take(512)
        remtb = ca2.take(2 * 512)
        remfb = ca2.take(4 * 256)
        combb = ca2.take(2 * 1024)
        gat = AFX[:, 4096:4096 + 64]

        load_w_chunks("r_out_w", rW, 1024, 4)
        load_w_chunks("w_out", oW, 1024, 8)
        mA2w = snap()

        for sq_ in range(cfg.nseq):
            gk = GKs if sq_ == 0 else GKs2
            gv = GVs if sq_ == 0 else GVs2
            for b in range(cfg.NB):
                cur, prv = b % 2, 1 - (b % 2)
                trow0 = (sq_ * S + b * 256)
                m = snap()
                wait(SY, "pe", at=m)
                inc(SY.dma_start(out=Qb[:, 0:2048],
                                 in_=dr["aQ"].rearrange("(c p) t -> p c t", p=128)
                                 [:, :, trow0:trow0 + 256]), "sq", 16)
                inc(SY.dma_start(out=Kb[cur][:, 0:2048],
                                 in_=dr["aK"].rearrange("(c p) t -> p c t", p=128)
                                 [:, :, trow0:trow0 + 256]), "sq", 16)
                inc(SY.dma_start(out=GQb[:, 0:1024],
                                 in_=dr["aGQ"].rearrange("(c p) t -> p c t", p=128)
                                 [:, :, trow0:trow0 + 256]), "sq", 16)
                inc(SY.dma_start(out=Vb[cur][:, 0:1040], in_=dr["aV"][trow0:trow0 + 128, :]), "sq", 16)
                inc(SY.dma_start(out=Vb[cur][:, 1040:2080], in_=dr["aV"][trow0 + 128:trow0 + 256, :]), "sq", 16)
                mLd = snap()
                if b == 0:
                    wait(V, "sq", at=mLd)
                    inc(V.memset(Kb[prv][:, :], 0.0), "dve")
                    inc(V.memset(Vb[prv][:, :], 0.0), "dve")
                wait(TE, "sq", at=mLd)
                wait(TE, "pe", at=mA2w)
                m = snap(); wait(TE, "dve", at=m)
                for mt in range(2):
                    for kc in range(8):
                        inc(TE.matmul(PS[4][:, 32 * mt:32 * (mt + 1)],
                                      Qb[:, 256 * kc + 128 * mt:256 * kc + 128 * (mt + 1)],
                                      g_exp[:, 32 * kc:32 * (kc + 1)],
                                      start=(kc == 0), stop=(kc == 7)), "pe")
                m = snap(); wait(SC, "pe", at=m)
                inc(SC.activation(gat[:, 0:64], PS[4][:, 0:64], AF.Sigmoid), "act")
                for lh in range(8):
                    fc, of = (64 * lh) // 128, (64 * lh) % 128
                    for mt in range(2):
                        inc(TE.matmul(PS[2][0:128, 64 * mt:64 * (mt + 1)],
                                      GQb[of:of + 64, 256 * fc + 128 * mt:256 * fc + 128 * (mt + 1)],
                                      gk[of:of + 64, 64 * fc:64 * fc + cfg.NLAT],
                                      start=True, stop=True), "pe")
                    m = snap(); wait(V, "pe", at=m)
                    V.reduce_max(fsc[:, 0:1], PS[2][:, 0:64], axis=AX.X)
                    inc(V.reduce_max(fsc[:, 1:2], PS[2][:, 64:128], axis=AX.X), "dve")
                    m = snap(); wait(V, "dve", at=m)
                    inc(V.tensor_scalar(fsc[:, 2:4], fsc[:, 0:2], -0.125, 0.0, ALU.mult, ALU.add), "dve")
                    for mt in range(2):
                        m = snap(); wait(SC, "dve" if mt == 0 else "pe", at=m)
                        wait(SC, "pool")
                        inc(SC.activation(Ptb[:, 0:cfg.NLAT], PS[2][:, 64 * mt:64 * (mt + 1)],
                                          AF.Exp, bias=fsc[:, 2 + mt:3 + mt], scale=0.125), "act")
                        m = snap(); wait(G, "act", at=m)
                        tbase = 256 * b + 128 * mt
                        inc(G.affine_select(Ptb[:, 0:cfg.NLAT], Ptb[:, 0:cfg.NLAT],
                                            [[-cfg.TPL, cfg.NLAT]], ALU.is_ge, 0.0,
                                            base=tbase, channel_multiplier=1), "pool")
                        m = snap(); wait(TE, "pool", at=m)
                        inc(TE.transpose(PB[0][0:cfg.NLAT, 128 * mt:128 * (mt + 1)],
                                         Ptb[:, 0:cfg.NLAT], identt[:]), "pe")
                        m = snap(); wait(V, "pe", at=m)
                        inc(V.tensor_copy(PTb[0:cfg.NLAT, 128 * mt:128 * (mt + 1)],
                                          PB[0][0:cfg.NLAT, 128 * mt:128 * (mt + 1)]), "dve")
                    m = snap(); wait(TE, "dve", at=m)
                    for mt in range(2):
                        inc(TE.matmul(PS[3][0:128, 65 * mt:65 * (mt + 1)],
                                      PTb[0:cfg.NLAT, 128 * mt:128 * (mt + 1)],
                                      gv[0:cfg.NLAT, 65 * lh:65 * (lh + 1)],
                                      start=True, stop=True), "pe")
                    m = snap(); wait(V, "pe", at=m)
                    for mt in range(2):
                        inc(V.reciprocal(fsc[:, 4 + mt:5 + mt],
                                         PS[3][:, 65 * mt + 64:65 * mt + 65]), "dve")
                    m = snap(); wait(V, "dve", at=m)
                    for mt in range(2):
                        inc(V.tensor_scalar(remtb[:, 512 * mt + 64 * lh:512 * mt + 64 * (lh + 1)],
                                            PS[3][:, 65 * mt:65 * mt + 64],
                                            fsc[:, 4 + mt:5 + mt], 0.0, ALU.mult, ALU.add), "dve")
                    m = snap(); wait(TE, "dve", at=m)
                for mt in range(2):
                    for d4 in range(4):
                        inc(TE.transpose(PB[mt][:, 128 * d4:128 * (d4 + 1)],
                                         remtb[:, 512 * mt + 128 * d4:512 * mt + 128 * (d4 + 1)],
                                         identt[:]), "pe")
                m = snap(); wait(V, "pe", at=m)
                for mt in range(2):
                    for d4 in range(4):
                        inc(V.tensor_copy(remfb[:, 256 * d4 + 128 * mt:256 * d4 + 128 * (mt + 1)],
                                          PB[mt][:, 128 * d4:128 * (d4 + 1)]), "dve")
                m = snap(); wait(TE, "dve", at=m)
                for mt in range(2):
                    for nh2 in range(2):
                        ps_r = PS[2 * mt + nh2]
                        for kc in range(4):
                            inc(TE.matmul(ps_r[:, 0:512],
                                          remfb[:, 256 * kc + 128 * mt:256 * kc + 128 * (mt + 1)],
                                          rW[:, 1024 * kc + 512 * nh2:1024 * kc + 512 * (nh2 + 1)],
                                          start=(kc == 0), stop=(kc == 3)), "pe")
                mRem = snap()
                for h in range(16):
                    fc, of = (64 * h) // 128, (64 * h) % 128
                    for mt in range(2):
                        ps_s = PS[4]
                        inc(TE.matmul(ps_s[:, 0:256],
                                      Qb[of:of + 64, 256 * fc + 128 * mt:256 * fc + 128 * (mt + 1)],
                                      Kb[prv][of:of + 64, 256 * fc:256 * fc + 256],
                                      start=True, stop=True), "pe")
                        inc(TE.matmul(ps_s[:, 256:512],
                                      Qb[of:of + 64, 256 * fc + 128 * mt:256 * fc + 128 * (mt + 1)],
                                      Kb[cur][of:of + 64, 256 * fc:256 * fc + 256],
                                      start=True, stop=True), "pe")
                        m = snap(); wait(V, "pe", at=m)
                        inc(V.reduce_max(fsc[:, 0:1], ps_s[:, 0:512], axis=AX.X), "dve")
                        m = snap(); wait(V, "dve", at=m)
                        inc(V.tensor_scalar(fsc[:, 1:2], fsc[:, 0:1], -0.125, 0.0,
                                            ALU.mult, ALU.add), "dve")
                        m = snap(); wait(SC, "dve", at=m)
                        inc(SC.activation(Ptb[:, 0:512], ps_s[:, 0:512], AF.Exp,
                                          bias=fsc[:, 1:2], scale=0.125), "act")
                        m = snap(); wait(G, "act", at=m)
                        wm = winft if b == 0 else winmt
                        inc(G.tensor_tensor(Ptb[:, 0:512], Ptb[:, 0:512],
                                            wm[:, 512 * mt:512 * (mt + 1)], ALU.mult), "pool")
                        m = snap(); wait(TE, "pool", at=m)
                        for k4 in range(4):
                            inc(TE.transpose(PB[0][:, 128 * k4:128 * (k4 + 1)],
                                             Ptb[:, 128 * k4:128 * (k4 + 1)], identt[:]), "pe")
                        m = snap(); wait(V, "pe", at=m)
                        inc(V.tensor_copy(PTb[:, 0:512], PB[0][:, 0:512]), "dve")
                        m = snap(); wait(TE, "dve", at=m)
                        for k4 in range(4):
                            vt = Vb[prv] if k4 < 2 else Vb[cur]
                            vof = 1040 * (k4 % 2)
                            inc(TE.matmul(PS[5][:, 0:65],
                                          PTb[:, 128 * k4:128 * (k4 + 1)],
                                          vt[:, vof + 65 * h:vof + 65 * (h + 1)],
                                          start=(k4 == 0), stop=(k4 == 3)), "pe")
                        m = snap(); wait(V, "pe", at=m)
                        inc(V.reciprocal(fsc[:, 2:3], PS[5][:, 64:65]), "dve")
                        m = snap(); wait(V, "dve", at=m)
                        inc(V.tensor_scalar(fsc[:, 4:68], PS[5][:, 0:64], fsc[:, 2:3],
                                            gat[:, 32 * mt + h:32 * mt + h + 1],
                                            ALU.mult, ALU.mult), "dve")
                        m = snap()
                        wait(V, "pe", at=mRem)
                        wait(V, "dve", at=m)
                        rps = PS[2 * mt + (64 * h) // 512]
                        rof = (64 * h) % 512
                        inc(V.scalar_tensor_tensor(
                            combb[:, 1024 * mt + 64 * h:1024 * mt + 64 * (h + 1)],
                            rps[:, rof:rof + 64],
                            gat[:, 32 * mt + 16 + h:32 * mt + 16 + h + 1],
                            fsc[:, 4:68], ALU.mult, ALU.add), "dve")
                m = snap(); wait(TE, "dve", at=m)
                for mt in range(2):
                    for d8 in range(8):
                        inc(TE.transpose(PB[mt][:, 128 * d8:128 * (d8 + 1)],
                                         combb[:, 1024 * mt + 128 * d8:1024 * mt + 128 * (d8 + 1)],
                                         identt[:]), "pe")
                m = snap(); wait(V, "pe", at=m)
                for mt in range(2):
                    for d8 in range(8):
                        inc(V.tensor_copy(Qb[:, 1024 * mt + 128 * d8:1024 * mt + 128 * (d8 + 1)],
                                          PB[mt][:, 128 * d8:128 * (d8 + 1)]), "dve")
                m = snap(); wait(TE, "dve", at=m)
                for mt in range(2):
                    for nh2 in range(2):
                        ps_o = PS[2 * mt + nh2]
                        for kc in range(8):
                            inc(TE.matmul(ps_o[:, 0:512],
                                          Qb[:, 1024 * mt + 128 * kc:1024 * mt + 128 * (kc + 1)],
                                          oW[:, 1024 * kc + 512 * nh2:1024 * kc + 512 * (nh2 + 1)],
                                          start=(kc == 0), stop=(kc == 7)), "pe")
                m = snap(); wait(V, "pe", at=m)
                for mt in range(2):
                    i = (trow0 // 128) + mt
                    for nh2 in range(2):
                        inc(V.tensor_tensor(stream[:, D * i + 512 * nh2:D * i + 512 * (nh2 + 1)],
                                            stream[:, D * i + 512 * nh2:D * i + 512 * (nh2 + 1)],
                                            PS[2 * mt + nh2][:, 0:512], ALU.add), "dve")
                m = snap(); wait(TE, "dve", at=m)
                wait(SY, "dve", at=m)

    def emit_ffn(final):
        ca = Carve(AB1)
        wbig = ca.take(8 * 2048)
        wbig2 = ca.take(8 * 2048)
        caB = Carve(AB2)
        xqTb = caB.take(4096)
        xqb = caB.take(1024)
        hqb = caB.take(0)

        emit_ln_stats(with_rowmax=True)
        emit_global_max(0, 1)
        m = snap(); wait(V, "dve", at=m)
        inc(V.tensor_scalar(ffnc[:, 0:32], s_w1c[:], sInv[:, 1:2], 0.0, ALU.mult, ALU.add), "dve")
        m = snap(); wait(V, "dve", at=m)
        inc(V.reciprocal(ffnc[:, 64:96], ffnc[:, 0:32]), "dve")
        m = snap(); wait(V, "dve", at=m)
        inc(V.tensor_tensor(fsc[:, 0:32], b1c[:], ffnc[:, 64:96], ALU.mult), "dve")
        m = snap(); wait(V, "dve", at=m)
        inc(V.tensor_copy(AI32[:, 0:32], fsc[:, 0:32]), "dve")
        m = snap(); wait(V, "dve", at=m)
        inc(V.tensor_copy(fsc[:, 32:64], AI32[:, 0:32]), "dve")
        m = snap(); wait(V, "dve", at=m)
        inc(V.tensor_tensor(ffnc[:, 32:64], fsc[:, 32:64], ffnc[:, 0:32], ALU.mult), "dve")

        m = snap(); wait(SY, "pe", "act", "dve", "pool", at=m)
        for d8 in range(8):
            inc(SY.dma_start(out=wbig[:, 2048 * d8:2048 * (d8 + 1)],
                             in_=dr["w1iT"][128 * d8:128 * (d8 + 1), 0:2048]), "sq", 16)
            inc(SY.dma_start(out=wbig2[:, 2048 * d8:2048 * (d8 + 1)],
                             in_=dr["w1iT"][128 * d8:128 * (d8 + 1), 2048:4096]), "sq", 16)
        mW1 = snap()
        m = snap(); wait(V, "dve", at=m)
        inc(V.memset(maxc[:, 0:1], 0.0), "dve")

        nblk = T // 512
        for bk in range(nblk):
            for it in range(4):
                i = 4 * bk + it
                m = snap(); wait(V, "pe", at=m)
                inc(norm_tile(i, fsc[:, 0:1024]), "dve")
                m = snap(); wait(V, "dve", at=m)
                inc(V.tensor_scalar(AU8[:, 0:1024], fsc[:, 0:1024], sInv[:, 0:1],
                                    0.0, ALU.mult, ALU.max), "dve")
                m = snap(); wait(V, "dve", at=m)
                inc(V.tensor_copy(xqb[:, 0:1024], AU8[:, 0:1024]), "dve")
                m = snap(); wait(TE, "dve", at=m)
                for d8 in range(8):
                    inc(TE.transpose(PB[it % 2][:, 128 * d8:128 * (d8 + 1)],
                                     xqb[:, 128 * d8:128 * (d8 + 1)], identt[:]), "pe")
                m = snap(); wait(V, "pe", at=m)
                for d8 in range(8):
                    inc(V.tensor_copy(xqTb[:, 512 * d8 + 128 * it:512 * d8 + 128 * (it + 1)],
                                      PB[it % 2][:, 128 * d8:128 * (d8 + 1)]), "dve")
            if bk == 0:
                wait(TE, "sq", at=mW1)
                if getattr(cfg, "dbg_xqt", False):
                    m = snap()
                    inc(G.dma_start(out=dr["dbg"], in_=xqTb[:, 0:4096]), "sg", 16)
            m = snap(); wait(TE, "dve", at=m)
            for fc in range(F // 128):
                wsrc = wbig if fc < 16 else wbig2
                fof = (fc % 16) * 128
                ps_f = PS[fc % 4]
                m = snap(); wait(TE, "act", at=m)
                for kc in range(8):
                    inc(TE.matmul(ps_f[:, 0:512],
                                  wsrc[:, 2048 * kc + fof:2048 * kc + fof + 128],
                                  xqTb[:, 512 * kc:512 * (kc + 1)],
                                  start=(kc == 0), stop=(kc == 7)), "pe")
                m = snap(); wait(SC, "pe", "sg", "dve", at=m)
                inc(SC.activation(b_sc[:, 0:512], ps_f[:, 0:512], AF.Relu,
                                  bias=ffnc[:, 32 + fc:33 + fc], scale=ffnc[:, fc:fc + 1]), "act")
                m = snap(); wait(V, "act", at=m)
                inc(V.reduce_max(fsc[:, 100:101], b_sc[:, 0:512], axis=AX.X), "dve")
                m = snap(); wait(V, "dve", at=m)
                inc(V.tensor_max(maxc[:, 0:1], maxc[:, 0:1], fsc[:, 100:101]), "dve")
                m = snap(); wait(G, "act", at=m)
                inc(G.dma_start(out=dr["h"][128 * fc:128 * (fc + 1), 512 * bk:512 * (bk + 1)],
                                in_=b_sc[:, 0:512]), "sg", 16)
        emit_global_max(2, 3)
        sh_sc = srow[0:1, 1:2]
        m = snap(); wait(V, "dve", at=m)
        inc(V.tensor_scalar(fsc[0:1, 0:1024], sw2r[0:1, :], sh_sc, 0.0, ALU.mult, ALU.add), "dve")
        m = snap(); wait(V, "dve", at=m)
        inc(V.reciprocal(fsc2[0:1, 0:1024], fsc[0:1, 0:1024]), "dve")
        m = snap(); wait(V, "dve", at=m)
        inc(V.tensor_tensor(fsc2[0:1, 0:1024], b2r[0:1, :], fsc2[0:1, 0:1024], ALU.mult), "dve")
        m = snap(); wait(V, "dve", at=m)
        inc(V.tensor_copy(AI32[0:1, 0:1024], fsc2[0:1, 0:1024]), "dve")
        m = snap(); wait(V, "dve", at=m)
        inc(V.tensor_copy(fsc2[0:1, 0:1024], AI32[0:1, 0:1024]), "dve")
        m = snap(); wait(V, "dve", at=m)
        inc(V.tensor_mul(fsc2[0:1, 0:1024], fsc2[0:1, 0:1024], fsc[0:1, 0:1024]), "dve")
        m = snap(); wait(TE, "dve", at=m)
        inc(TE.matmul(PS[4][:, 0:512], onesf[0:1, :], fsc2[0:1, 0:512], start=True, stop=True), "pe")
        inc(TE.matmul(PS[5][:, 0:512], onesf[0:1, :], fsc2[0:1, 512:1024], start=True, stop=True), "pe")
        m = snap(); wait(V, "pe", at=m)
        V.tensor_copy(bq2b[:, 0:512], PS[4][:, 0:512])
        inc(V.tensor_copy(bq2b[:, 512:1024], PS[5][:, 0:512]), "dve")

        m = snap(); wait(SY, "pe", at=m)
        m = snap(); wait(SY, "pe", "act", "dve", "pool", at=m)
        for kc in range(16):
            inc(SY.dma_start(out=wbig[:, 1024 * kc:1024 * (kc + 1)],
                             in_=dr["w2qT"][128 * kc:128 * (kc + 1), :]), "sq", 16)
            inc(SY.dma_start(out=wbig2[:, 1024 * kc:1024 * (kc + 1)],
                             in_=dr["w2qT"][128 * (16 + kc):128 * (17 + kc), :]), "sq", 16)
        mW2 = snap()

        hqcols = xqTb
        for i in range(cfg.NT):
            for f4 in range(F // 512):
                m = snap(); wait(SY, "dve", at=m)
                inc(SY.dma_start(out=fsc[:, 0:512].rearrange("p (c t) -> p c t", c=4),
                                 in_=dr["h"].rearrange("(c p) t -> p c t", p=128)
                                 [:, 4 * f4:4 * (f4 + 1), 128 * i:128 * (i + 1)]), "sq", 16)
                m = snap(); wait(V, "sq", at=m)
                inc(V.tensor_scalar(AU8[:, 0:512], fsc[:, 0:512], sInv[:, 2:3], 0.0,
                                    ALU.mult, ALU.max), "dve")
                m = snap(); wait(V, "dve", at=m)
                inc(V.tensor_copy(hqcols[:, 512 * f4:512 * (f4 + 1)], AU8[:, 0:512]), "dve")
            if i == 0:
                wait(TE, "sq", at=mW2)
            m = snap(); wait(TE, "dve", at=m)
            for nh2 in range(2):
                ps_o = PS[nh2]
                for kc in range(32):
                    wsrc = wbig if kc < 16 else wbig2
                    inc(TE.matmul(ps_o[:, 0:512],
                                  hqcols[:, 128 * kc:128 * (kc + 1)],
                                  wsrc[:, 1024 * (kc % 16) + 512 * nh2:1024 * (kc % 16) + 512 * (nh2 + 1)],
                                  start=(kc == 0), stop=(kc == 31)), "pe")
            m = snap(); wait(V, "pe", at=m)
            for nh2 in range(2):
                inc(V.scalar_tensor_tensor(fsc[:, 512 * nh2:512 * (nh2 + 1)],
                                           PS[nh2][:, 0:512], sInv[:, 3:4],
                                           bq2b[:, 512 * nh2:512 * (nh2 + 1)],
                                           ALU.mult, ALU.add), "dve")
            m = snap(); wait(V, "dve", at=m)
            if final:
                inc(V.tensor_tensor(fsc2[:, 0:1024], fsc[:, 0:1024],
                                    stream[:, D * i:D * (i + 1)], ALU.add), "dve")
                m = snap(); wait(G, "dve", at=m)
                inc(G.dma_start(out=out[128 * i:128 * (i + 1), :], in_=fsc2[:, 0:1024]), "out", 16)
                m = snap(); wait(V, "out", at=m)
            else:
                inc(V.tensor_tensor(stream[:, D * i:D * (i + 1)],
                                    stream[:, D * i:D * (i + 1)], fsc[:, 0:1024], ALU.add), "dve")
            m = snap(); wait(TE, "dve", at=m)

    def emit_attn_b():
        emit_ln_stats(with_rowmax=False)

        caB1 = Carve(AB1)
        WqR = caB1.take(8 * 1024)
        WkR = caB1.take(8 * 1024)
        WvR = caB1.take(8 * 1024)
        xnTblk2 = caB1.take(8 * 512)
        evC = AB2[:, 1024:1536]
        vdb = AB2[:, 1536:2112]
        fcast = AB2[:, 2112:2624]
        vhist = AB2[:, 2624:3136]
        chist = AFX[:, 4096:4168]
        load_w_chunks("bWq", WqR, 1024, 8)
        load_w_chunks("bWk", WkR, 1024, 8)
        load_w_chunks("bWv", WvR, 1024, 8)
        for sq_ in range(cfg.nseq):
            m = snap(); wait(V, "act", "pool", at=m)
            inc(V.memset(chist[:, :], 0.0), "dve")
            inc(V.memset(vhist[:, 0:512], 0.0), "dve")
            for bi in range(S // 512):
                blk_g = sq_ * (S // 512) + bi
                for it in range(4):
                    i = 4 * blk_g + it
                    m = snap(); wait(V, "act", at=m)
                    inc(norm_tile(i, fsc[:, 0:1024]), "dve")
                    m = snap(); wait(SC, "dve", "pe", at=m)
                    inc(SC.copy(AB2[:, 0:1024], fsc[:, 0:1024]), "act")
                    m = snap(); wait(TE, "act", "dve", at=m)
                    for d8 in range(8):
                        inc(TE.transpose(PB[it % 2][:, 128 * d8:128 * (d8 + 1)],
                                         AB2[:, 128 * d8:128 * (d8 + 1)], identt[:]), "pe")
                    m = snap(); wait(V, "pe", at=m)
                    for d8 in range(8):
                        inc(V.tensor_copy(xnTblk2[:, 512 * d8 + 128 * it:512 * d8 + 128 * (it + 1)],
                                          PB[it % 2][:, 128 * d8:128 * (d8 + 1)]), "dve")
                m = snap(); wait(TE, "dve", at=m)
                for wi, (wres2, dst) in enumerate([(WqR, dr["bq"]), (WkR, dr["bk"]),
                                                   (WvR, dr["bv"])]):
                    for fc in range(8):
                        ps_q = PS[fc % 2]
                        m = snap(); wait(TE, "dve", "act", at=m)
                        for kc in range(8):
                            inc(TE.matmul(ps_q[:, 0:512],
                                          wres2[:, 1024 * kc + 128 * fc:1024 * kc + 128 * (fc + 1)],
                                          xnTblk2[:, 512 * kc:512 * (kc + 1)],
                                          start=(kc == 0), stop=(kc == 7)), "pe")
                        cwc = cw[:, (wi * 8 + fc) * 4:(wi * 8 + fc) * 4 + 4]
                        hc3 = chist[:, (wi * 8 + fc) * 3:(wi * 8 + fc) * 3 + 3]
                        m = snap(); wait(V, "pe", "act", at=m)
                        inc(V.tensor_scalar(b_sc[:, 0:512], ps_q[:, 0:512], cwc[:, 3:4],
                                            0.0, ALU.mult, ALU.add), "dve")
                        for tp in range(1, 4):
                            inc(V.scalar_tensor_tensor(b_sc[:, tp:512], ps_q[:, 0:512 - tp],
                                                       cwc[:, 3 - tp:4 - tp],
                                                       b_sc[:, tp:512], ALU.mult, ALU.add), "dve")
                            inc(V.scalar_tensor_tensor(b_sc[:, 0:tp], hc3[:, 3 - tp:3],
                                                       cwc[:, 3 - tp:4 - tp],
                                                       b_sc[:, 0:tp], ALU.mult, ALU.add), "dve")
                        inc(V.tensor_copy(hc3[:, :], ps_q[:, 509:512]), "dve")
                        m = snap(); wait(SC, "dve", "sg", at=m)
                        inc(SC.activation(evC[:, 0:512], b_sc[:, 0:512], AF.Silu), "act")
                        m = snap(); wait(G, "act", at=m)
                        inc(G.dma_start(out=dst[128 * fc:128 * (fc + 1), 512 * blk_g:512 * (blk_g + 1)],
                                        in_=evC[:, 0:512]), "sg", 16)
                        if wi == 2:
                            m = snap(); wait(V, "act", at=m)
                            V.tensor_copy(vdb[:, 0:64], vhist[:, 64 * fc:64 * (fc + 1)])
                            inc(V.tensor_copy(vdb[:, 64:576], evC[:, 0:512]), "dve")
                            inc(V.tensor_copy(vhist[:, 64 * fc:64 * (fc + 1)],
                                              vdb[:, 512:576]), "dve")
                            for (filt, Kf, dspill) in [(firsc, 3, dr["bls"]), (firlc, 63, dr["bll"])]:
                                fco = filt[:, Kf * fc:Kf * (fc + 1)]
                                m = snap(); wait(V, "dve", "act", at=m)
                                inc(V.tensor_scalar(b_sc[:, 0:512], vdb[:, 64:576],
                                                    fco[:, Kf - 1:Kf], 0.0, ALU.mult, ALU.add), "dve")
                                for tp in range(1, Kf):
                                    inc(V.scalar_tensor_tensor(b_sc[:, 0:512],
                                                               vdb[:, 64 - tp:576 - tp],
                                                               fco[:, Kf - 1 - tp:Kf - tp],
                                                               b_sc[:, 0:512], ALU.mult, ALU.add), "dve")
                                m = snap(); wait(SC, "dve", "sg", at=m)
                                inc(SC.copy(fcast[:, 0:512], b_sc[:, 0:512]), "act")
                                m = snap(); wait(G, "act", at=m)
                                inc(G.dma_start(out=dspill[128 * fc:128 * (fc + 1),
                                                          512 * blk_g:512 * (blk_g + 1)],
                                                in_=fcast[:, 0:512]), "sg", 16)

        caB2 = Carve(AB1)
        WidR = caB2.take(8 * 1024)
        GW1R = caB2.take(8 * 1024)
        WbR = caB2.take(8 * NH)
        xnTblk3 = caB2.take(8 * 512)
        load_w_chunks("bWid", WidR, 1024, 8)
        load_w_chunks("gW1", GW1R, 1024, 8)
        load_w_chunks("bWb", WbR, NH, 8)
        for blk_g in range(T // 512):
            for it in range(4):
                i = 4 * blk_g + it
                m = snap(); wait(V, "act", at=m)
                inc(norm_tile(i, fsc[:, 0:1024]), "dve")
                m = snap(); wait(SC, "dve", "pe", at=m)
                inc(SC.copy(AB2[:, 0:1024], fsc[:, 0:1024]), "act")
                m = snap(); wait(TE, "act", "dve", at=m)
                for d8 in range(8):
                    inc(TE.transpose(PB[it % 2][:, 128 * d8:128 * (d8 + 1)],
                                     AB2[:, 128 * d8:128 * (d8 + 1)], identt[:]), "pe")
                m = snap(); wait(V, "pe", at=m)
                for d8 in range(8):
                    inc(V.tensor_copy(xnTblk3[:, 512 * d8 + 128 * it:512 * d8 + 128 * (it + 1)],
                                      PB[it % 2][:, 128 * d8:128 * (d8 + 1)]), "dve")
            m = snap(); wait(TE, "dve", at=m)
            for it in range(4):
                i = 4 * blk_g + it
                for kc in range(8):
                    inc(TE.matmul(PS[4][:, 0:NH],
                                  xnTblk3[:, 512 * kc + 128 * it:512 * kc + 128 * (it + 1)],
                                  WbR[:, NH * kc:NH * (kc + 1)],
                                  start=(kc == 0), stop=(kc == 7)), "pe")
                for nh2 in range(2):
                    ps_v = PS[2 + nh2]
                    for kc in range(8):
                        inc(TE.matmul(ps_v[:, 0:512],
                                      xnTblk3[:, 512 * kc + 128 * it:512 * kc + 128 * (it + 1)],
                                      WidR[:, 1024 * kc + 512 * nh2:1024 * kc + 512 * (nh2 + 1)],
                                      start=(kc == 0), stop=(kc == 7)), "pe")
                m = snap(); wait(SC, "pe", "sg", at=m)
                inc(SC.activation(betac[:, NH * i:NH * (i + 1)], PS[4][:, 0:NH], AF.Sigmoid), "act")
                SC.copy(evC[:, 0:512], PS[2][:, 0:512])
                inc(SC.copy(fcast[:, 0:512], PS[3][:, 0:512]), "act")
                m = snap(); wait(G, "act", at=m)
                inc(G.dma_start(out=dr["bid"][128 * i:128 * (i + 1), 0:512], in_=evC[:, 0:512]), "sg", 16)
                inc(G.dma_start(out=dr["bid"][128 * i:128 * (i + 1), 512:1024], in_=fcast[:, 0:512]), "sg", 16)
                m = snap(); wait(TE, "act", at=m)
            for fc in range(8):
                ps_q = PS[fc % 2]
                m = snap(); wait(TE, "act", at=m)
                for kc in range(8):
                    inc(TE.matmul(ps_q[:, 0:512],
                                  GW1R[:, 1024 * kc + 128 * fc:1024 * kc + 128 * (fc + 1)],
                                  xnTblk3[:, 512 * kc:512 * (kc + 1)],
                                  start=(kc == 0), stop=(kc == 7)), "pe")
                m = snap(); wait(SC, "pe", "sg", at=m)
                inc(SC.copy(evC[:, 0:512], ps_q[:, 0:512]), "act")
                m = snap(); wait(G, "act", at=m)
                inc(G.dma_start(out=dr["bgt"][128 * fc:128 * (fc + 1), 512 * blk_g:512 * (blk_g + 1)],
                                in_=evC[:, 0:512]), "sg", 16)

        ca2 = Carve(AB1)
        oW = ca2.take(8 * 1024)
        qfb = ca2.take(1024)
        kfb = ca2.take(1024)
        ktb = ca2.take(1024)
        vtb = ca2.take(1024)
        kbt = ca2.take(1024)
        vbt = ca2.take(1024)
        lstb = ca2.take(1024)
        lltb = ca2.take(1024)
        dotb = ca2.take(1024)
        idtb = ca2.take(1024)
        gtb = ca2.take(1024)
        gelb = ca2.take(1024)
        otb = ca2.take(1024)
        Sbf_ = [ca2.take(512) for _ in range(2 * NH)]
        dL = ca2.take(512)
        dT = ca2.take(512)
        dTT = ca2.take(512)
        dP2 = ca2.take(512)
        dP2T = ca2.take(512)
        dAttnT = ca2.take(512)
        dU = ca2.take(1024)
        dUadj = ca2.take(1024)
        dWTT = ca2.take(1024)
        dKbf = ca2.take(1024)
        vfb = dKbf
        qtb = dU
        fstg2 = dWTT
        ofb = otb
        statb = AB2
        probs_f = AFX[:, 4120:4136]
        stat_sc = AFX[:, 4136:4200]
        b2fs = AFX[:, 4200:4712]
        b2fs2 = AFX[:, 4712:5120]

        load_w_chunks("bWo", oW, 1024, 8)
        mWo = snap()
        for sq_ in range(cfg.nseq):
            m = snap(); wait(V, "act", "pool", "pe", at=m)
            for k in range(2 * NH):
                inc(V.memset(Sbf_[k][:, :], 0.0), "dve")
            for sg_i in range(cfg.NTS):
                i = sq_ * cfg.NTS + sg_i
                trow0 = 128 * i
                m = snap()
                wait(SY, "pe", "dve", at=m)
                inc(SY.dma_start(out=qfb[:, 0:1024],
                                 in_=dr["bq"].rearrange("(c p) t -> p c t", p=128)
                                 [:, :, trow0:trow0 + 128]), "sq", 16)
                inc(SY.dma_start(out=kfb[:, 0:1024],
                                 in_=dr["bk"].rearrange("(c p) t -> p c t", p=128)
                                 [:, :, trow0:trow0 + 128]), "sq", 16)
                inc(SY.dma_start(out=gtb[:, 0:1024],
                                 in_=dr["bgt"].rearrange("(c p) t -> p c t", p=128)
                                 [:, :, trow0:trow0 + 128]), "sq", 16)
                inc(SY.dma_start(out=idtb[:, 0:1024], in_=dr["bid"][trow0:trow0 + 128, :]), "sq", 16)
                inc(SY.dma_start(out=vfb[:, 0:1024],
                                 in_=dr["bv"].rearrange("(c p) t -> p c t", p=128)
                                 [:, :, trow0:trow0 + 128]), "sq", 16)
                mLd = snap()
                wait(TE, "sq", at=mLd)
                m = snap(); wait(TE, "dve", at=m)
                for d8 in range(8):
                    inc(TE.transpose(PB[0][:, 128 * d8:128 * (d8 + 1)],
                                     qfb[:, 128 * d8:128 * (d8 + 1)], identt[:]), "pe")
                m = snap(); wait(V, "pe", at=m)
                for d8 in range(8):
                    inc(V.tensor_copy(qtb[:, 128 * d8:128 * (d8 + 1)],
                                      PB[0][:, 128 * d8:128 * (d8 + 1)]), "dve")
                m = snap(); wait(TE, "dve", at=m)
                for d8 in range(8):
                    inc(TE.transpose(PB[0][:, 128 * d8:128 * (d8 + 1)],
                                     kfb[:, 128 * d8:128 * (d8 + 1)], identt[:]), "pe")
                m = snap(); wait(V, "pe", at=m)
                for d8 in range(8):
                    inc(V.tensor_copy(ktb[:, 128 * d8:128 * (d8 + 1)],
                                      PB[0][:, 128 * d8:128 * (d8 + 1)]), "dve")
                m = snap(); wait(TE, "dve", at=m)
                for d8 in range(8):
                    inc(TE.transpose(PB[0][:, 128 * d8:128 * (d8 + 1)],
                                     vfb[:, 128 * d8:128 * (d8 + 1)], identt[:]), "pe")
                m = snap(); wait(V, "pe", at=m)
                for d8 in range(8):
                    inc(V.tensor_copy(vtb[:, 128 * d8:128 * (d8 + 1)],
                                      PB[0][:, 128 * d8:128 * (d8 + 1)]), "dve")
                for tb, fb in [(qtb, qfb), (ktb, kfb)]:
                    for h in range(NH):
                        m = snap(); wait(SC, "dve", at=m)
                        inc(SC.activation(b2fs[:, 0:256], tb[:, 256 * h:256 * (h + 1)],
                                          AF.Square, accum_out=stat_sc[:, h:h + 1]), "act")
                    m = snap(); wait(SC, "act", at=m)
                    inc(SC.activation(stat_sc[:, 4:8], stat_sc[:, 0:4], AF.Sqrt, bias=epsc[:, 1:2]), "act")
                    m = snap(); wait(V, "act", at=m)
                    inc(V.reciprocal(stat_sc[:, 8:12], stat_sc[:, 4:8]), "dve")
                    m = snap(); wait(V, "dve", at=m)
                    for h in range(NH):
                        inc(V.tensor_scalar(tb[:, 256 * h:256 * (h + 1)],
                                            tb[:, 256 * h:256 * (h + 1)],
                                            stat_sc[:, 8 + h:9 + h], 0.0, ALU.mult, ALU.add), "dve")
                    m = snap(); wait(TE, "dve", at=m)
                    for d8 in range(8):
                        inc(TE.transpose(PB[0][:, 128 * d8:128 * (d8 + 1)],
                                         tb[:, 128 * d8:128 * (d8 + 1)], identt[:]), "pe")
                    m = snap(); wait(V, "pe", at=m)
                    for d8 in range(8):
                        inc(V.tensor_copy(fb[:, 128 * d8:128 * (d8 + 1)],
                                          PB[0][:, 128 * d8:128 * (d8 + 1)]), "dve")
                m = snap(); wait(V, "dve", at=m)
                for h in range(NH):
                    V.tensor_scalar(kbt[:, 256 * h:256 * (h + 1)], ktb[:, 256 * h:256 * (h + 1)],
                                    betac[:, NH * i + h:NH * i + h + 1], 0.0, ALU.mult, ALU.add)
                    inc(V.tensor_scalar(vbt[:, 256 * h:256 * (h + 1)], vtb[:, 256 * h:256 * (h + 1)],
                                        betac[:, NH * i + h:NH * i + h + 1], 0.0, ALU.mult, ALU.add), "dve")
                for (dspill, dstt) in [(dr["bls"], lstb), (dr["bll"], lltb)]:
                    m = snap(); wait(SY, "pe", "dve", at=m)
                    inc(SY.dma_start(out=fstg2[:, 0:1024],
                                     in_=dspill.rearrange("(c p) t -> p c t", p=128)
                                     [:, :, trow0:trow0 + 128]), "sq", 16)
                    m = snap(); wait(TE, "sq", "dve", at=m)
                    for d8 in range(8):
                        inc(TE.transpose(PB[1][:, 128 * d8:128 * (d8 + 1)],
                                         fstg2[:, 128 * d8:128 * (d8 + 1)], identt[:]), "pe")
                    m = snap(); wait(V, "pe", at=m)
                    for d8 in range(8):
                        inc(V.tensor_copy(dstt[:, 128 * d8:128 * (d8 + 1)],
                                          PB[1][:, 128 * d8:128 * (d8 + 1)]), "dve")
                if getattr(cfg, "delta_off", False):
                    m = snap(); wait(V, "pe", at=m)
                    inc(V.memset(dotb[:, 0:1024], 0.0), "dve")
                else:
                    m = snap(); wait(TE, "dve", at=m)
                    for h in range(NH):
                        for d2 in range(2):
                            inc(TE.transpose(PB[0][:, 256 * h + 128 * d2:256 * h + 128 * (d2 + 1)],
                                             kbt[:, 256 * h + 128 * d2:256 * h + 128 * (d2 + 1)],
                                             identt[:]), "pe")
                    m = snap(); wait(V, "pe", at=m)
                    inc(V.tensor_copy(dKbf[:, 0:1024], PB[0][:, 0:1024]), "dve")

                    if getattr(cfg, 'delta_upto', 99) >= 1:
                        m = snap(); wait(TE, "dve", at=m)
                        for h in range(NH):
                            for d2 in range(2):
                                inc(TE.matmul(PS[h][:, 0:128],
                                              dKbf[:, 256 * h + 128 * d2:256 * h + 128 * (d2 + 1)],
                                              kfb[:, 256 * h + 128 * d2:256 * h + 128 * (d2 + 1)],
                                              start=(d2 == 0), stop=(d2 == 1)), "pe")
                        m = snap(); wait(SC, "pe", at=m)
                        for h in range(NH):
                            inc(SC.mul(dL[:, 128 * h:128 * (h + 1)], PS[h][:, 0:128], -1.0), "act")
                        m = snap(); wait(G, "act", at=m)
                        inc(G.affine_select(dL[:, 0:512], dL[:, 0:512], [[0, 4], [-1, 128]],
                                            ALU.is_ge, 0.0, base=-1, channel_multiplier=1), "pool")


                    if getattr(cfg, 'delta_upto', 99) >= 2:
                        for h in range(NH):
                            hc0 = 128 * h
                            m = snap(); wait(V, "pool", "act", at=m)
                            inc(V.memset(dT[0:64, hc0 + 64:hc0 + 128], 0.0), "dve")
                            for hf_ in range(2):
                                b0 = 64 * hf_
                                slb = slice(b0, b0 + 64)
                                idb = identt[slb, slb]
                                m = snap(); wait(V, "act", at=m)
                                inc(V.tensor_add(dT[slb, hc0 + b0:hc0 + b0 + 64],
                                                 dL[slb, hc0 + b0:hc0 + b0 + 64], idb), "dve")
                                inc(V.tensor_copy(dP2[slb, hc0:hc0 + 64],
                                                  dL[slb, hc0 + b0:hc0 + b0 + 64]), "dve")
                                m = snap(); wait(TE, "dve", at=m)
                                inc(TE.transpose(PB[0][0:64, 0:64], dP2[slb, hc0:hc0 + 64], idb), "pe")
                                m = snap(); wait(V, "pe", at=m)
                                inc(V.tensor_copy(dP2T[slb, hc0:hc0 + 64], PB[0][0:64, 0:64]), "dve")
                                for st_ in range(5):
                                    m = snap(); wait(TE, "dve", at=m)
                                    inc(TE.matmul(PS[1][0:64, 0:64], dP2T[slb, hc0:hc0 + 64],
                                                  dP2[slb, hc0:hc0 + 64], start=True, stop=True), "pe")
                                    m = snap(); wait(SC, "pe", at=m)
                                    inc(SC.copy(dP2[slb, hc0:hc0 + 64], PS[1][0:64, 0:64]), "act")
                                    m = snap(); wait(TE, "act", at=m)
                                    inc(TE.transpose(PB[0][0:64, 0:64], dP2[slb, hc0:hc0 + 64], idb), "pe")
                                    m = snap(); wait(V, "pe", at=m)
                                    inc(V.tensor_copy(dP2T[slb, hc0:hc0 + 64], PB[0][0:64, 0:64]), "dve")
                                    m = snap(); wait(TE, "dve", "act", at=m)
                                    inc(TE.matmul(PS[2][0:64, 0:64], dP2T[slb, hc0:hc0 + 64],
                                                  dT[slb, hc0 + b0:hc0 + b0 + 64],
                                                  start=True, stop=False), "pe")
                                    inc(TE.matmul(PS[2][0:64, 0:64], idb,
                                                  dT[slb, hc0 + b0:hc0 + b0 + 64],
                                                  start=False, stop=True), "pe")
                                    m = snap(); wait(SC, "pe", at=m)
                                    inc(SC.copy(dT[slb, hc0 + b0:hc0 + b0 + 64], PS[2][0:64, 0:64]), "act")
                                    m = snap(); wait(TE, "act", at=m)
                            idb1 = identt[64:128, 64:128]
                            inc(TE.transpose(PB[0][0:64, 0:64], dL[64:128, hc0:hc0 + 64], idb1), "pe")
                            m = snap(); wait(V, "pe", at=m)
                            inc(V.tensor_copy(dP2[0:64, hc0:hc0 + 64], PB[0][0:64, 0:64]), "dve")
                            m = snap(); wait(TE, "dve", at=m)
                            inc(TE.matmul(PS[1][0:64, 0:64], dP2[0:64, hc0:hc0 + 64],
                                          dT[0:64, hc0:hc0 + 64], start=True, stop=True), "pe")
                            m = snap(); wait(SC, "pe", at=m)
                            inc(SC.copy(dP2T[0:64, hc0:hc0 + 64], PS[1][0:64, 0:64]), "act")
                            m = snap(); wait(TE, "act", at=m)
                            inc(TE.transpose(PB[0][0:64, 0:64], dT[64:128, hc0 + 64:hc0 + 128], idb1), "pe")
                            m = snap(); wait(V, "pe", at=m)
                            inc(V.tensor_copy(dP2[0:64, hc0:hc0 + 64], PB[0][0:64, 0:64]), "dve")
                            m = snap(); wait(TE, "dve", at=m)
                            inc(TE.matmul(PS[2][0:64, 0:64], dP2[0:64, hc0:hc0 + 64],
                                          dP2T[0:64, hc0:hc0 + 64], start=True, stop=True), "pe")
                            m = snap(); wait(SC, "pe", at=m)
                            inc(SC.copy(dT[64:128, hc0:hc0 + 64], PS[2][0:64, 0:64]), "act")
                    if getattr(cfg, 'delta_upto', 99) >= 5:
                        m = snap(); wait(TE, "act", "dve", at=m)
                        for h in range(NH):
                            inc(TE.transpose(PB[0][:, 128 * h:128 * (h + 1)],
                                             dT[:, 128 * h:128 * (h + 1)], identt[:]), "pe")
                        m = snap(); wait(V, "pe", at=m)
                        inc(V.tensor_copy(dTT[:, 0:512], PB[0][:, 0:512]), "dve")
                        if getattr(cfg, "dbg_b", -1) == i:
                            inc(G.dma_start(out=dr["dbg"][:, 0:128], in_=dT[:, 0:128]), "sg", 16)
                            inc(G.dma_start(out=dr["dbg"][:, 128:256], in_=dL[:, 0:128]), "sg", 16)
                            inc(G.dma_start(out=dr["dbg"][:, 256:512], in_=dKbf[:, 0:256]), "sg", 16)
                            inc(G.dma_start(out=dr["dbg"][:, 1152:1408], in_=kfb[:, 0:256]), "sg", 16)
                            inc(G.dma_start(out=dr["dbg"][:, 1408:1664], in_=qfb[:, 0:256]), "sg", 16)
                            m2 = snap(); wait(SC, "sg", at=m2); wait(V, "sg", at=m2)


                    if getattr(cfg, 'delta_upto', 99) >= 6:
                        m = snap(); wait(TE, "dve", at=m)
                        for h in range(NH):
                            inc(TE.matmul(PS[h][:, 0:256], dTT[:, 128 * h:128 * (h + 1)],
                                          vbt[:, 256 * h:256 * (h + 1)], start=True, stop=True), "pe")
                            for d2 in range(2):
                                inc(TE.matmul(PS[4 + h // 2][:, 512 * 0 + 256 * (h % 2) + 0:0]
                                              if False else
                                              PS[4 + h // 2][:, 256 * (h % 2):256 * (h % 2) + 128]
                                              if False else
                                              PS[4 + h // 2][:, 256 * (h % 2) + 128 * d2:
                                                             256 * (h % 2) + 128 * (d2 + 1)],
                                              kbt[:, 256 * h + 128 * d2:256 * h + 128 * (d2 + 1)],
                                              dTT[:, 128 * h:128 * (h + 1)], start=True, stop=True), "pe")
                        m = snap(); wait(SC, "pe", at=m)
                        for h in range(NH):
                            SC.copy(dU[:, 256 * h:256 * (h + 1)], PS[h][:, 0:256])
                            inc(SC.copy(dWTT[:, 256 * h:256 * (h + 1)],
                                        PS[4 + h // 2][:, 256 * (h % 2):256 * (h % 2) + 256]), "act")


                    if getattr(cfg, 'delta_upto', 99) >= 7:
                        m = snap(); wait(TE, "act", at=m)
                        for h in range(NH):
                            for d2 in range(2):
                                inc(TE.matmul(PS[h][:, 0:128],
                                              kfb[:, 256 * h + 128 * d2:256 * h + 128 * (d2 + 1)],
                                              qfb[:, 256 * h + 128 * d2:256 * h + 128 * (d2 + 1)],
                                              start=(d2 == 0), stop=(d2 == 1)), "pe")
                        m = snap(); wait(SC, "pe", at=m)
                        for h in range(NH):
                            inc(SC.copy(dAttnT[:, 128 * h:128 * (h + 1)], PS[h][:, 0:128]), "act")
                        m = snap(); wait(G, "act", at=m)
                        inc(G.affine_select(dAttnT[:, 0:512], dAttnT[:, 0:512], [[0, 4], [1, 128]],
                                            ALU.is_ge, 0.0, base=0, channel_multiplier=-1), "pool")


                    if getattr(cfg, 'delta_upto', 99) >= 8:
                        m = snap(); wait(TE, "pool", "dve", at=m)
                        for h in range(NH):
                            for d2 in range(2):
                                inc(TE.matmul(PS[h][:, 0:256],
                                              dWTT[:, 256 * h + 128 * d2:256 * h + 128 * (d2 + 1)],
                                              Sbf_[h][:, 256 * d2:256 * (d2 + 1)],
                                              start=(d2 == 0), stop=(d2 == 1)), "pe")
                        m = snap(); wait(V, "pe", at=m)
                        for h in range(NH):
                            inc(V.tensor_sub(dUadj[:, 256 * h:256 * (h + 1)],
                                             dU[:, 256 * h:256 * (h + 1)], PS[h][:, 0:256]), "dve")


                    if getattr(cfg, 'delta_upto', 99) >= 9:
                        m = snap(); wait(TE, "dve", at=m)
                        for h in range(NH):
                            for d2 in range(2):
                                inc(TE.matmul(PS[h][:, 0:256],
                                              qfb[:, 256 * h + 128 * d2:256 * h + 128 * (d2 + 1)],
                                              Sbf_[h][:, 256 * d2:256 * (d2 + 1)],
                                              start=(d2 == 0), stop=False), "pe")
                            inc(TE.matmul(PS[h][:, 0:256], dAttnT[:, 128 * h:128 * (h + 1)],
                                          dUadj[:, 256 * h:256 * (h + 1)],
                                          start=False, stop=True), "pe")
                        m = snap(); wait(V, "pe", at=m)
                        for h in range(NH):
                            inc(V.tensor_copy(dotb[:, 256 * h:256 * (h + 1)], PS[h][:, 0:256]), "dve")


                    if getattr(cfg, 'delta_upto', 99) >= 10:
                        for hp in range(2):
                            m = snap(); wait(TE, "dve", at=m)
                            for hh in range(2):
                                h = 2 * hp + hh
                                for d2 in range(2):
                                    inc(TE.matmul(PS[4 + hh][:, 256 * d2:256 * (d2 + 1)],
                                                  ktb[:, 256 * h + 128 * d2:256 * h + 128 * (d2 + 1)],
                                                  dUadj[:, 256 * h:256 * (h + 1)],
                                                  start=True, stop=True), "pe")
                            m = snap(); wait(V, "pe", at=m)
                            for hh in range(2):
                                h = 2 * hp + hh
                                for d2 in range(2):
                                    inc(V.tensor_tensor(Sbf_[h][:, 256 * d2:256 * (d2 + 1)],
                                                        Sbf_[h][:, 256 * d2:256 * (d2 + 1)],
                                                        PS[4 + hh][:, 256 * d2:256 * (d2 + 1)],
                                                        ALU.add), "dve")

                    if getattr(cfg, 'delta_upto', 99) < 10:
                        m = snap(); wait(V, "pe", at=m)
                        inc(V.memset(dotb[:, 0:1024], 0.0), "dve")
                if getattr(cfg, "stats_off", False):
                    m = snap(); wait(V, "act", at=m)
                    inc(V.memset(b2fs2[:, 0:64], 0.5), "dve")
                else:
                    tensors = [(lstb, 0), (lltb, 4), (dotb, 8), (vtb, 12)]
                    svw = b2fs2[:, 0:64].rearrange("p (h s) -> p s h", s=16)
                    for (tt, so) in tensors:
                        t3 = tt[:, 0:1024].rearrange("p (h d) -> p h d", h=NH)
                        m = snap(); wait(V, "act", at=m)
                        inc(V.reduce_sum(stat_sc[:, 0:4], t3, axis=AX.X), "dve")
                        inc(V.reduce_sum(stat_sc[:, 4:8], t3, axis=AX.X,
                                         apply_absolute_value=True), "dve")
                        m = snap(); wait(SC, "dve", at=m)
                        inc(SC.activation(fsc[:, 0:1024], tt[:, 0:1024], AF.Square), "act")
                        m = snap(); wait(V, "act", at=m)
                        inc(V.reduce_sum(stat_sc[:, 8:12],
                                         fsc[:, 0:1024].rearrange("p (h d) -> p h d", h=NH),
                                         axis=AX.X), "dve")
                        m = snap(); wait(V, "dve", at=m)
                        V.tensor_scalar(svw[:, so + 0, :], stat_sc[:, 0:4], 1.0 / 256.0,
                                        0.0, ALU.mult, ALU.add)
                        V.tensor_scalar(svw[:, so + 2, :], stat_sc[:, 4:8], 1.0 / 256.0,
                                        0.0, ALU.mult, ALU.add)
                        V.tensor_scalar(stat_sc[:, 12:16], stat_sc[:, 8:12], 1.0 / 256.0,
                                        0.0, ALU.mult, ALU.add)
                        inc(V.tensor_scalar(stat_sc[:, 16:20], stat_sc[:, 0:4], 1.0 / 256.0,
                                            0.0, ALU.mult, ALU.add), "dve")
                        inc(V.tensor_tensor(stat_sc[:, 20:24], stat_sc[:, 16:20],
                                            stat_sc[:, 16:20], ALU.mult), "dve")
                        inc(V.tensor_sub(svw[:, so + 1, :], stat_sc[:, 12:16],
                                         stat_sc[:, 20:24]), "dve")
                        m = snap(); wait(SC, "dve", at=m)
                        inc(SC.activation(svw[:, so + 3, :], stat_sc[:, 8:12], AF.Sqrt), "act")
                m = snap(); wait(TE, "dve", "act", at=m)
                for h in range(NH):
                    inc(TE.transpose(PS[4][0:16, 0:128], b2fs2[:, 16 * h:16 * (h + 1)],
                                     identf[:]), "pe")
                    m = snap(); wait(SC, "pe", at=m)
                    inc(SC.copy(statb[0:16, 128 * h:128 * (h + 1)], PS[4][0:16, 0:128]), "act")
                    m = snap(); wait(TE, "act", at=m)
                for h in range(NH):
                    for hc in range(8):
                        ps_g = PS[hc % 2]
                        inc(TE.matmul(ps_g[:, 0:128],
                                      gW1b[:, 128 * hc:128 * (hc + 1)],
                                      statb[0:16, 128 * h:128 * (h + 1)],
                                      start=True, stop=True), "pe")
                        m = snap(); wait(V, "pe", at=m)
                        inc(V.scalar_tensor_tensor(b2fs[:, 0:128], ps_g[:, 0:128],
                                                   gb1c[:, hc:hc + 1],
                                                   gtb[:, 128 * hc:128 * (hc + 1)],
                                                   ALU.add, ALU.add), "dve")
                        m = snap(); wait(SC, "dve", at=m)
                        inc(SC.activation(gelb[:, 128 * hc:128 * (hc + 1)], b2fs[:, 0:128],
                                          AF.Gelu), "act")
                        m = snap(); wait(TE, "act", at=m)
                    for hc in range(8):
                        inc(TE.matmul(PS[2][:, 0:NH], gelb[:, 128 * hc:128 * (hc + 1)],
                                      gW2c[:, NH * hc:NH * (hc + 1)],
                                      start=(hc == 0), stop=(hc == 7)), "pe")
                    m = snap(); wait(V, "pe", at=m)
                    inc(V.tensor_tensor(stat_sc[:, 0:4], PS[2][:, 0:NH],
                                        bcmisc[:, 12:16], ALU.add), "dve")
                    m = snap(); wait(V, "dve", at=m)
                    inc(V.tensor_tensor(stat_sc[:, 4:8], stat_sc[:, 0:4],
                                        bc16[:, 4 * h:4 * (h + 1)], ALU.mult), "dve")
                    m = snap(); wait(V, "dve", at=m)
                    inc(V.reduce_max(stat_sc[:, 8:9], stat_sc[:, 4:8], axis=AX.X), "dve")
                    m = snap(); wait(V, "dve", at=m)
                    inc(V.tensor_scalar(stat_sc[:, 9:10], stat_sc[:, 8:9], -1.0, 0.0,
                                        ALU.mult, ALU.add), "dve")
                    m = snap(); wait(SC, "dve", at=m)
                    inc(SC.activation(stat_sc[:, 12:16], stat_sc[:, 4:8], AF.Exp,
                                      bias=stat_sc[:, 9:10], accum_out=stat_sc[:, 10:11]), "act")
                    m = snap(); wait(V, "act", at=m)
                    inc(V.reciprocal(stat_sc[:, 11:12], stat_sc[:, 10:11]), "dve")
                    m = snap(); wait(V, "dve", at=m)
                    inc(V.tensor_scalar(stat_sc[:, 12:16], stat_sc[:, 12:16],
                                        stat_sc[:, 11:12], bcmisc[:, h:h + 1],
                                        ALU.mult, ALU.mult), "dve")
                    m = snap(); wait(V, "dve", at=m)
                    inc(V.tensor_tensor(probs_f[:, 4 * h:4 * (h + 1)], stat_sc[:, 12:16],
                                        bcfl[:, 4 * h:4 * (h + 1)], ALU.add), "dve")
                m = snap(); wait(V, "dve", at=m)
                for h in range(NH):
                    inc(V.tensor_scalar(stat_sc[:, 40 + h:41 + h], probs_f[:, 4 * h:4 * h + 1],
                                        bcmisc[:, 4 + h:5 + h], 0.0, ALU.add, ALU.add), "dve")
                for h in range(NH):
                    sl = slice(256 * h, 256 * (h + 1))
                    m = snap(); wait(V, "dve", at=m)
                    inc(V.tensor_scalar(b2fs[:, 0:256], lstb[:, sl], stat_sc[:, 40 + h:41 + h],
                                        0.0, ALU.mult, ALU.add), "dve")
                    inc(V.scalar_tensor_tensor(b2fs[:, 0:256], lltb[:, sl],
                                               probs_f[:, 4 * h + 1:4 * h + 2],
                                               b2fs[:, 0:256], ALU.mult, ALU.add), "dve")
                    inc(V.scalar_tensor_tensor(b2fs[:, 0:256], dotb[:, sl],
                                               probs_f[:, 4 * h + 2:4 * h + 3],
                                               b2fs[:, 0:256], ALU.mult, ALU.add), "dve")
                    inc(V.scalar_tensor_tensor(b2fs[:, 0:256], vtb[:, sl],
                                               probs_f[:, 4 * h + 3:4 * h + 4],
                                               b2fs[:, 0:256], ALU.mult, ALU.add), "dve")
                    inc(V.scalar_tensor_tensor(b2fs[:, 0:256], idtb[:, sl],
                                               bcmisc[:, 8 + h:9 + h],
                                               b2fs[:, 0:256], ALU.mult, ALU.add), "dve")
                    m = snap(); wait(SC, "dve", at=m)
                    inc(SC.activation(b2fs2[:, 0:256], b2fs[:, 0:256], AF.Square,
                                      accum_out=stat_sc[:, 48:49]), "act")
                    m = snap(); wait(V, "act", at=m)
                    inc(V.tensor_scalar(stat_sc[:, 49:50], stat_sc[:, 48:49], 1.0 / 256.0,
                                        0.0, ALU.mult, ALU.add), "dve")
                    m = snap(); wait(SC, "dve", at=m)
                    inc(SC.activation(stat_sc[:, 50:51], stat_sc[:, 49:50], AF.Sqrt,
                                      bias=epsc[:, 0:1]), "act")
                    m = snap(); wait(V, "act", at=m)
                    inc(V.reciprocal(stat_sc[:, 51:52], stat_sc[:, 50:51]), "dve")
                    m = snap(); wait(V, "dve", at=m)
                    inc(V.tensor_scalar(b2fs[:, 0:256], b2fs[:, 0:256], stat_sc[:, 51:52],
                                        0.0, ALU.mult, ALU.add), "dve")
                    m = snap(); wait(V, "dve", at=m)
                    inc(V.tensor_tensor(otb[:, sl], b2fs[:, 0:256], onwb[:, sl], ALU.mult), "dve")
                if getattr(cfg, "dbg_b", -1) == i:
                    for j_, tt_ in enumerate([qtb, ktb, vtb, kbt, vbt, lstb, lltb,
                                              dotb, idtb, otb, gelb, gtb, qfb, kfb]):
                        inc(G.dma_start(out=dr["dbg2"][:, 1024 * j_:1024 * (j_ + 1)],
                                        in_=tt_[:, 0:1024]), "sg", 16)
                m = snap(); wait(TE, "dve", at=m)
                for d8 in range(8):
                    inc(TE.transpose(PB[0][:, 128 * d8:128 * (d8 + 1)],
                                     otb[:, 128 * d8:128 * (d8 + 1)], identt[:]), "pe")
                m = snap(); wait(V, "pe", at=m)
                for d8 in range(8):
                    inc(V.tensor_copy(ofb[:, 128 * d8:128 * (d8 + 1)],
                                      PB[0][:, 128 * d8:128 * (d8 + 1)]), "dve")
                if sq_ == 0 and sg_i == 0:
                    wait(TE, "act", at=mWo)
                m = snap(); wait(TE, "dve", at=m)
                for nh2 in range(2):
                    ps_o = PS[nh2]
                    for kc in range(8):
                        inc(TE.matmul(ps_o[:, 0:512],
                                      ofb[:, 128 * kc:128 * (kc + 1)],
                                      oW[:, 1024 * kc + 512 * nh2:1024 * kc + 512 * (nh2 + 1)],
                                      start=(kc == 0), stop=(kc == 7)), "pe")
                m = snap(); wait(V, "pe", at=m)
                for nh2 in range(2):
                    inc(V.tensor_tensor(stream[:, D * i + 512 * nh2:D * i + 512 * (nh2 + 1)],
                                        stream[:, D * i + 512 * nh2:D * i + 512 * (nh2 + 1)],
                                        PS[nh2][:, 0:512], ALU.add), "dve")
                m = snap(); wait(TE, "dve", at=m)

    stages = getattr(cfg, "stages", ("a", "f1", "b", "f2"))
    for st_name in stages:
        if st_name == "a":
            emit_attn_a2()
        elif st_name == "f1":
            emit_ffn(final=False)
        elif st_name == "b":
            emit_attn_b()
        elif st_name == "f2":
            emit_ffn(final=True)
    if stages[-1] != "f2":
        for i in range(cfg.NT):
            m = snap(); wait(SC, "out", "dve", at=m)
            inc(SC.copy(fsc[:, 0:1024], stream[:, D * i:D * (i + 1)]), "act")
            m = snap(); wait(G, "act", at=m)
            inc(G.dma_start(out=out[128 * i:128 * (i + 1), :], in_=fsc[:, 0:1024]), "out", 16)

    m = snap()
    wait(G, "out", at=m)
    wait(G, "sq", "sg", "dve", "act", "pe", "pool", at=m)

    for en in ["sync", "gpsimd", "vector", "scalar", "tensor"]:
        fns = EBUF[en]
        getattr(blk, en)(lambda e, fns=fns: [f(e) for f in fns])

    blk_t.__exit__(None, None, None)
    for t in reversed(ectx[:-1]):
        try:
            t.__exit__(None, None, None)
        except Exception:
            pass
    return nc


def kernel(**inputs):
    cfg = CFG()
    consts = host_consts(cfg)
    nc = build(cfg)
    x = np.ascontiguousarray(inputs["L0_attn_a_x"], dtype=np.float32)
    namemap = {
        "wq": "L0_attn_a_wq", "wk": "L0_attn_a_wk", "wv": "L0_attn_a_wv",
        "wgq": "L0_attn_a_wgq", "wgk": "L0_attn_a_wgk", "wgv": "L0_attn_a_wgv",
        "r_out_w": "L0_attn_a_r_out_w", "glw": "L0_attn_a_gate_local_w",
        "grw": "L0_attn_a_gate_remote_w", "w_out": "L0_attn_a_w_out",
        "bWq": "L0_attn_b_Wq", "bWk": "L0_attn_b_Wk", "bWv": "L0_attn_b_Wv",
        "cwq": "L0_attn_b_cwq", "cwk": "L0_attn_b_cwk", "cwv": "L0_attn_b_cwv",
        "bWb": "L0_attn_b_Wb", "bWid": "L0_attn_b_Wid", "alpha_id": "L0_attn_b_alpha_id",
        "firs": "L0_attn_b_firs", "firl": "L0_attn_b_firl",
        "gW1": "L0_attn_b_gW1", "gb1": "L0_attn_b_gb1", "gW2": "L0_attn_b_gW2",
        "gb2": "L0_attn_b_gb2", "log_temp": "L0_attn_b_log_temp",
        "floor_param": "L0_attn_b_floor_param", "crl": "L0_attn_b_conv_res_logit",
        "onw": "L0_attn_b_o_norm_w", "bWo": "L0_attn_b_Wo",
        "w1": "L0_ffn_w1", "b1": "L0_ffn_b1", "w2": "L0_ffn_w2", "b2": "L0_ffn_b2",
    }
    shared = {k: np.ascontiguousarray(inputs[v], dtype=np.float32) for k, v in namemap.items()}
    shared.update(consts)
    in_maps = []
    for c in range(8):
        mm = dict(shared)
        mm["x"] = np.ascontiguousarray(x[2 * c:2 * c + 2].reshape(cfg.T, cfg.D))
        in_maps.append(mm)
    res = run_bass_kernel_spmd(nc, in_maps, core_ids=list(range(8)))
    outs = [res.results[c]["out"].reshape(2, cfg.S, cfg.D) for c in range(8)]
    return np.concatenate(outs, axis=0)


# version 05 (final)
# speedup vs baseline: 1.2658x; 1.0287x over previous
import numpy as np
import ml_dtypes
import concourse.bass as bass
import concourse.mybir as mybir
from concourse.bass_utils import run_bass_kernel_spmd

F32 = mybir.dt.float32
BF16 = mybir.dt.bfloat16
U8 = mybir.dt.uint8
AF = mybir.ActivationFunctionType
ALU = mybir.AluOpType
AX = mybir.AxisListType
BF_NP = ml_dtypes.bfloat16


class CFG:
    def __init__(self, nseq=2, S=2048, ncc=8):
        self.nseq, self.S, self.D = nseq, S, 1024
        self.T = nseq * S
        self.NT = self.T // 128
        self.NTS = S // 128
        self.W = 256
        self.NB = S // self.W
        self.NLAT = 64
        self.TPL = S // self.NLAT
        self.F = 4096
        self.NH = 4
        self.GH = 1024
        self.GTS = min(8, self.NTS)
        self.LPG = self.GTS * 128 // self.TPL
        self.cc_group = list(range(ncc))


_AUTOSEM = {"vector": "dve", "scalar": "act", "gpsimd": "pool", "tensor": "pe"}
_NOWRAP = {"wait_ge", "sem_inc"}
_NOINC = {"dma_start", "indirect_dma_start", "collective_compute", "dma_start_transpose"}


class InstProxy:
    __slots__ = ("q",)
    def __init__(self):
        self.q = []
    def then_inc(self, semh, n=1):
        self.q.append((semh, n))
        return self


class EP:
    def __init__(self, blk, name, sem, cnt, buf):
        object.__setattr__(self, "_blk", blk)
        object.__setattr__(self, "_name", name)
        object.__setattr__(self, "_sem", sem)
        object.__setattr__(self, "_cnt", cnt)
        object.__setattr__(self, "_buf", buf)
    def __getattr__(self, attr):
        name, sem, cnt, buf = self._name, self._sem, self._cnt, self._buf
        own = _AUTOSEM.get(name)
        def call(*a, **k):
            gates = []
            selfwait = 0
            auto = False
            if attr in _NOINC:
                for s_ in ("sq", "sg", "out"):
                    if cnt[s_] > 0:
                        gates.append((s_, cnt[s_]))
            elif own is not None and attr not in _NOWRAP:
                if name != "tensor":
                    selfwait = cnt[own]
                auto = True
                cnt[own] += 1
            px = InstProxy()
            def emit(e, attr=attr, a=a, k=k, gates=gates, selfwait=selfwait,
                     auto=auto, px=px):
                for s_, v_ in gates:
                    e.wait_ge(sem[s_], v_)
                ins = getattr(e, attr)(*a, **k)
                if selfwait > 0:
                    ins._wait_ge(sem[own], selfwait)
                if auto:
                    ins.then_inc(sem[own], 1)
                for semh, n_ in px.q:
                    ins.then_inc(semh, n_)
                return ins
            buf[name].append(emit)
            return px
        return call


def host_consts(cfg):
    c = {}
    c["ident"] = np.eye(128, dtype=BF_NP)
    i = np.arange(cfg.W)[:, None]
    j = np.arange(2 * cfg.W)[None, :]
    base = (j > i) & (j <= i + cfg.W)
    c["winm"] = base.astype(BF_NP)
    c["winm_first"] = (base & (j >= cfg.W)).astype(BF_NP)
    pp = np.zeros((cfg.GTS, 128, cfg.LPG), np.float32)
    for g in range(cfg.GTS):
        for ii in range(128):
            pp[g, ii, g * (128 // cfg.TPL) + ii // cfg.TPL] = 1.0 / cfg.TPL
    c["poolpat"] = pp.astype(BF_NP)
    return c


def build(cfg):
    nc = bass.Bass("TRN2", target_bir_lowering=False)
    T, S, D, F, NH = cfg.T, cfg.S, cfg.D, cfg.F, cfg.NH
    lpg = cfg.LPG

    ap = {}
    def di(name, shape, dt=F32):
        ap[name] = nc.dram_tensor(name, shape, dt, kind="ExternalInput").ap()
    di("x", [T, D])
    for n, sh in [("wq", [D, D]), ("wk", [D, D]), ("wv", [D, D]),
                  ("wgq", [D, 512]), ("wgk", [D, 512]), ("wgv", [D, 512]),
                  ("r_out_w", [512, D]), ("glw", [16, 64]), ("grw", [16, 64]),
                  ("w_out", [D, D]), ("bWq", [D, D]), ("bWk", [D, D]),
                  ("bWv", [D, D]), ("cwq", [D, 4]), ("cwk", [D, 4]),
                  ("cwv", [D, 4]), ("bWb", [D, NH]), ("bWid", [D, D]),
                  ("alpha_id", [NH]), ("firs", [NH, 256, 3]), ("firl", [NH, 256, 63]),
                  ("gW1", [D + 16, cfg.GH]), ("gb1", [cfg.GH]), ("gW2", [cfg.GH, NH]),
                  ("gb2", [NH]), ("log_temp", [NH, 4]), ("floor_param", [NH, 4]),
                  ("crl", [NH]), ("onw", [256]), ("bWo", [D, D]),
                  ("w1", [F, D]), ("b1", [F]), ("w2", [D, F]), ("b2", [D])]:
        di(n, sh)
    di("ident", [128, 128], BF16)
    di("winm", [256, 512], BF16)
    di("winm_first", [256, 512], BF16)
    di("poolpat", [cfg.GTS, 128, lpg], BF16)

    out = nc.dram_tensor("out", [T, D], F32, kind="ExternalOutput").ap()

    dr = {}
    dr["h"] = nc.dram_tensor("h_sp", [F, T], F32).ap()
    dr["w1iT"] = nc.dram_tensor("w1iT", [D, F], BF16).ap()
    dr["w2qT"] = nc.dram_tensor("w2qT", [F, D], BF16).ap()
    dr["aQ"] = nc.dram_tensor("aQ", [D, T], BF16).ap()
    dr["aK"] = nc.dram_tensor("aK", [D, T], BF16).ap()
    dr["aV"] = nc.dram_tensor("aV", [T, 16 * 65], BF16).ap()
    dr["aGQ"] = nc.dram_tensor("aGQ", [512, T], BF16).ap()
    dr["bq"] = nc.dram_tensor("bq", [D, T], BF16).ap()
    dr["bk"] = nc.dram_tensor("bk", [D, T], BF16).ap()
    dr["bv"] = nc.dram_tensor("bv", [D, T], BF16).ap()
    dr["bid"] = nc.dram_tensor("bid", [T, D], BF16).ap()
    dr["bgt"] = nc.dram_tensor("bgt", [cfg.GH, T], BF16).ap()
    dr["bls"] = nc.dram_tensor("bls", [D, T], BF16).ap()
    dr["bll"] = nc.dram_tensor("bll", [D, T], BF16).ap()
    dr["bounce"] = nc.dram_tensor("bounce", [128, 128], F32).ap()
    dr["dbg"] = nc.dram_tensor("dbg", [128, 4096], BF16).ap()
    dr["dbg2"] = nc.dram_tensor("dbg2", [128, 16384], BF16).ap()
    cc_in_t = nc.dram_tensor("cc_in", [128, 128], F32)
    cc_out_t = nc.dram_tensor("cc_out", [128, 128], F32)

    ectx = []
    def sb(name, p, f, dt=BF16):
        t = nc.sbuf_tensor(name, [p, f], dt)
        h = t.__enter__(); ectx.append(t); return h
    def psal(name, p, f, dt=F32):
        t = nc.psum_tensor(name, [p, f], dt)
        h = t.__enter__(); ectx.append(t); return h
    def mksem(name):
        t = nc.semaphore(name)
        h = t.__enter__(); ectx.append(t); return h

    stream = sb("stream", 128, cfg.NT * D)
    mu = sb("mu", 128, cfg.NT, F32)
    rstd = sb("rstd", 128, cfg.NT, F32)
    maxc = sb("maxc", 128, 4, F32)
    identt = sb("identt", 128, 128)
    identf = sb("identf", 128, 128, F32)
    onesf = sb("onesf", 1, 128, F32)
    winmt = sb("winmt", 128, 2 * 512)
    winft = sb("winft", 128, 2 * 512)
    poolt = sb("poolt", 128, cfg.GTS * lpg)
    g_exp = sb("g_exp", 128, 8 * 32)
    cw = sb("cw", 128, 8 * 12, F32)
    firsc = sb("firsc", 128, 8 * 3, F32)
    firlc = sb("firlc", 128, 8 * 63, F32)
    gb1c = sb("gb1c", 128, 8, F32)
    gW1b = sb("gW1b", 16, cfg.GH)
    gW2c = sb("gW2c", 128, 8 * NH)
    bc16 = sb("bc16", 128, 16, F32)
    bcfl = sb("bcfl", 128, 16, F32)
    bcmisc = sb("bcmisc", 128, 16, F32)
    onwb = sb("onwb", 128, NH * 256, F32)
    s_w1c = sb("s_w1c", 128, F // 128, F32)
    b1c = sb("b1c", 128, F // 128, F32)
    sw2r = sb("sw2r", 1, D, F32)
    b2r = sb("b2r", 1, D, F32)
    srow = sb("srow", 1, 128, F32)
    bq2b = sb("bq2b", 128, D, F32)
    sInv = sb("sInv", 128, 4, F32)
    ccst = sb("ccst", 128, 128, F32)
    ffnc = sb("ffnc", 128, 96, F32)
    epsc = sb("epsc", 128, 4, F32)
    betac = sb("betac", 128, cfg.NT * NH, F32)
    GKs = sb("GKs", 128, 4 * cfg.NLAT)
    GVs = sb("GVs", cfg.NLAT, 8 * 65)
    pooled = sb("pooled", cfg.NLAT, D)
    pooledT = sb("pooledT", 128, 8 * cfg.NLAT)

    AB1 = sb("AB1", 128, 32768)
    AB2 = sb("AB2", 128, 6144)
    AFX = sb("AFX", 128, 5120, F32)
    AU8 = sb("AU8", 128, 2048, U8)
    AI32 = sb("AI32", 128, 1024, mybir.dt.int32)

    class Carve:
        def __init__(self, base):
            self.base = base; self.off = 0
        def take(self, ncols):
            v = self.base[:, self.off:self.off + ncols]
            self.off += ncols
            assert self.off <= self.base.shape[1], f"arena overflow {self.off}"
            return v

    PS = [psal(f"ps{i}", 128, 512, F32) for i in range(6)]
    PB = [psal(f"pb{i}", 128, 1024, BF16) for i in range(2)]

    sem = {n: mksem(n) for n in ["sq", "sg", "pe", "dve", "act", "pool", "cc", "out"]}
    cnt = {n: 0 for n in sem}

    blk_t = nc.Block()
    blk = blk_t.__enter__(); ectx.append(blk_t)
    EBUF = {n: [] for n in ["vector", "scalar", "tensor", "gpsimd", "sync"]}
    V = EP(blk, "vector", sem, cnt, EBUF); SC = EP(blk, "scalar", sem, cnt, EBUF)
    TE = EP(blk, "tensor", sem, cnt, EBUF); G = EP(blk, "gpsimd", sem, cnt, EBUF)
    SY = EP(blk, "sync", sem, cnt, EBUF)

    _AUTO = {"dve", "act", "pool", "pe"}
    def inc(instr, s, n=1):
        if s in _AUTO:
            return instr
        instr.then_inc(sem[s], n)
        cnt[s] += n
        return instr

    def snap():
        return dict(cnt)

    def wait(eng, *names, at=None):
        own = _AUTOSEM.get(eng._name)
        if eng._name == "tensor":
            own = None
        for s in names:
            if s == own:
                continue
            v = cnt[s] if at is None else at[s]
            if v > 0:
                eng.wait_ge(sem[s], v)

    wstg = AFX[:, 0:1024]
    fsc = AFX[:, 1024:2048]
    fsc2 = AFX[:, 2048:3072]
    b_sc = AFX[:, 3072:3584]
    b_sc2 = AFX[:, 3584:4096]

    inc(SY.dma_start(out=identt[:], in_=ap["ident"]), "sq", 16)
    inc(SY.dma_start(out=winmt[:, 0:512], in_=ap["winm"][0:128, :]), "sq", 16)
    inc(SY.dma_start(out=winmt[:, 512:1024], in_=ap["winm"][128:256, :]), "sq", 16)
    inc(SY.dma_start(out=winft[:, 0:512], in_=ap["winm_first"][0:128, :]), "sq", 16)
    inc(SY.dma_start(out=winft[:, 512:1024], in_=ap["winm_first"][128:256, :]), "sq", 16)
    inc(SY.dma_start(out=poolt[:].rearrange("p (g l) -> p g l", g=cfg.GTS),
                     in_=ap["poolpat"].rearrange("g p l -> p g l")), "sq", 16)
    V.memset(onesf[:], 1.0)
    V.memset(maxc[:], 0.0)
    V.memset(epsc[:, 0:1], 1e-5)
    V.memset(epsc[:, 1:2], 1e-6)
    G.memset(identf[:], 1.0)
    inc(G.affine_select(identf[:], identf[:], [[-1, 128]], ALU.is_equal, 0.0,
                        base=0, channel_multiplier=1), "pool")
    for i in range(cfg.NT):
        m = snap(); wait(SY, "act", at=m)
        inc(SY.dma_start(out=wstg[:], in_=ap["x"][128 * i:128 * (i + 1), :]), "sq", 16)
        m = snap(); wait(SC, "sq", at=m)
        inc(SC.copy(stream[:, D * i:D * (i + 1)], wstg[:]), "act")

    for ti, nm in enumerate(["cwq", "cwk", "cwv"]):
        for c in range(8):
            inc(SY.dma_start(out=cw[:, (ti * 8 + c) * 4:(ti * 8 + c) * 4 + 4],
                             in_=ap[nm][128 * c:128 * (c + 1), :]), "sq", 16)
    for c in range(8):
        inc(SY.dma_start(out=firsc[:, c * 3:(c + 1) * 3],
                         in_=ap["firs"].rearrange("h d k -> (h d) k")[128 * c:128 * (c + 1), :]), "sq", 16)
        inc(SY.dma_start(out=firlc[:, c * 63:(c + 1) * 63],
                         in_=ap["firl"].rearrange("h d k -> (h d) k")[128 * c:128 * (c + 1), :]), "sq", 16)
    m = snap(); wait(SY, "act", at=m)
    inc(SY.dma_start(out=wstg[0:16, :], in_=ap["gW1"][D:D + 16, :]), "sq", 16)
    m = snap(); wait(SC, "sq", at=m)
    inc(SC.copy(gW1b[:, :], wstg[0:16, :]), "act")
    for c in range(8):
        m = snap(); wait(SY, "act", at=m)
        inc(SY.dma_start(out=wstg[:, 0:NH], in_=ap["gW2"][128 * c:128 * (c + 1), :]), "sq", 16)
        m = snap(); wait(SC, "sq", at=m)
        inc(SC.copy(gW2c[:, NH * c:NH * (c + 1)], wstg[:, 0:NH]), "act")
    m = snap(); wait(SY, "act", at=m)
    inc(SY.dma_start(out=wstg[0:8, 0:128], in_=ap["gb1"].rearrange("(a b) -> a b", b=128)), "sq", 16)
    inc(SY.dma_start(out=wstg[32:64, 0:128], in_=ap["b1"].rearrange("(a b) -> a b", b=128)), "sq", 16)
    inc(SY.dma_start(out=b2r[0:1, :], in_=ap["b2"][None, :]), "sq", 16)
    m = snap(); wait(TE, "sq", "pool", at=m)
    TE.transpose(PS[0][0:128, 0:8], wstg[0:8, 0:128], identf[0:8, 0:8])
    inc(TE.transpose(PS[1][0:128, 0:32], wstg[32:64, 0:128], identf[32:64, 32:64]), "pe")
    m = snap(); wait(V, "pe", at=m)
    V.tensor_copy(gb1c[:, :], PS[0][0:128, 0:8])
    inc(V.tensor_copy(b1c[:, :], PS[1][0:128, 0:32]), "dve")

    m = snap(); wait(V, "dve", at=m)
    inc(V.memset(g_exp[:], 0.0), "dve")
    for gi, nm in enumerate(["glw", "grw"]):
        m = snap(); wait(SY, "act", "pe", at=m)
        inc(SY.dma_start(out=wstg[0:16, 0:64], in_=ap[nm]), "sq", 16)
        m = snap(); wait(SC, "sq", at=m)
        inc(SC.copy(AB2[0:16, 0:64], wstg[0:16, 0:64]), "act")
        m = snap(); wait(TE, "act", "dve", at=m)
        inc(TE.transpose(PB[0][0:64, 0:16], AB2[0:16, 0:64], identt[0:16, 0:16]), "pe")
        m = snap(); wait(V, "pe", "dve", at=m)
        inc(V.tensor_copy(AB2[0:64, 64:80], PB[0][0:64, 0:16]), "dve")
        m = snap(); wait(V, "dve", at=m)
        for c in range(8):
            V.tensor_copy(g_exp[0:64, 32 * c + 16 * gi + 2 * c:32 * c + 16 * gi + 2 * c + 1],
                          AB2[0:64, 64 + 2 * c:64 + 2 * c + 1])
            inc(V.tensor_copy(g_exp[64:128, 32 * c + 16 * gi + 2 * c + 1:32 * c + 16 * gi + 2 * c + 2],
                              AB2[0:64, 64 + 2 * c + 1:64 + 2 * c + 2]), "dve")

    inc(SY.dma_start(out=srow[0:1, 0:16], in_=ap["log_temp"].rearrange("a b -> (a b)")[None, :]), "sq", 16)
    m = snap(); wait(SC, "sq", at=m)
    inc(SC.activation(srow[0:1, 16:32], srow[0:1, 0:16], AF.Exp), "act")
    m = snap(); wait(V, "act", at=m)
    inc(V.reciprocal(srow[0:1, 32:48], srow[0:1, 16:32]), "dve")
    m = snap(); wait(TE, "dve", at=m)
    inc(TE.matmul(PS[0][:, 0:16], onesf[0:1, :], srow[0:1, 32:48], start=True, stop=True), "pe")
    m = snap(); wait(V, "pe", at=m)
    inc(V.tensor_copy(bc16[:], PS[0][:, 0:16]), "dve")
    m = snap(); wait(SY, "dve", at=m)
    inc(SY.dma_start(out=srow[0:1, 0:16], in_=ap["floor_param"].rearrange("a b -> (a b)")[None, :]), "sq", 16)
    m = snap(); wait(SC, "sq", at=m)
    inc(SC.activation(srow[0:1, 16:32], srow[0:1, 0:16], AF.Sigmoid), "act")
    m = snap(); wait(V, "act", at=m)
    inc(V.tensor_scalar(srow[0:1, 32:48], srow[0:1, 16:32], 0.05, 0.0, ALU.mult, ALU.add), "dve")
    m = snap(); wait(V, "dve", at=m)
    inc(V.reduce_sum(srow[0:1, 48:52],
                     srow[0:1, 32:48].rearrange("p (h c) -> p h c", h=4), axis=AX.X), "dve")
    m = snap(); wait(V, "dve", at=m)
    inc(V.tensor_scalar(srow[0:1, 52:56], srow[0:1, 48:52], -1.0, 1.0, ALU.mult, ALU.add), "dve")
    inc(SY.dma_start(out=srow[0:1, 72:76], in_=ap["crl"][None, :]), "sq", 16)
    inc(SY.dma_start(out=srow[0:1, 60:64], in_=ap["alpha_id"][None, :]), "sq", 16)
    inc(SY.dma_start(out=srow[0:1, 64:68], in_=ap["gb2"][None, :]), "sq", 16)
    m = snap(); wait(SC, "sq", at=m); wait(SC, "dve", at=m)
    inc(SC.activation(srow[0:1, 56:60], srow[0:1, 72:76], AF.Sigmoid), "act")
    m = snap(); wait(TE, "act", at=m)
    inc(TE.matmul(PS[0][:, 0:16], onesf[0:1, :], srow[0:1, 32:48], start=True, stop=True), "pe")
    inc(TE.matmul(PS[1][:, 0:16], onesf[0:1, :], srow[0:1, 52:68], start=True, stop=True), "pe")
    m = snap(); wait(V, "pe", at=m)
    V.tensor_copy(bcfl[:], PS[0][:, 0:16])
    inc(V.tensor_copy(bcmisc[:], PS[1][:, 0:16]), "dve")
    m = snap(); wait(SY, "dve", at=m)
    inc(SY.dma_start(out=srow[0:1, 0:128], in_=ap["onw"][None, 0:128]), "sq", 16)
    m = snap(); wait(TE, "sq", at=m)
    inc(TE.matmul(PS[2][:, 0:128], onesf[0:1, :], srow[0:1, 0:128], start=True, stop=True), "pe")
    m = snap(); wait(SY, "pe", at=m)
    inc(SY.dma_start(out=srow[0:1, 0:128], in_=ap["onw"][None, 128:256]), "sq", 16)
    m = snap(); wait(TE, "sq", at=m)
    inc(TE.matmul(PS[2][:, 128:256], onesf[0:1, :], srow[0:1, 0:128], start=True, stop=True), "pe")
    m = snap(); wait(V, "pe", at=m)
    for h in range(NH):
        inc(V.tensor_copy(onwb[:, 256 * h:256 * (h + 1)], PS[2][:, 0:256]), "dve")

    for c in range(F // 128):
        m = snap(); wait(SY, "act", "dve", "sg", "pe", at=m)
        inc(SY.dma_start(out=wstg[:], in_=ap["w1"][128 * c:128 * (c + 1), :]), "sq", 16)
        m = snap(); wait(V, "sq", at=m)
        inc(V.tensor_reduce(fsc[:, 0:1], wstg[:], axis=AX.X, op=ALU.max,
                            apply_absolute_value=True), "dve")
        m = snap(); wait(V, "dve", at=m)
        inc(V.tensor_scalar(fsc[:, 1:2], fsc[:, 0:1], 1e-8, 1.0 / 127.0, ALU.max, ALU.mult), "dve")
        m = snap(); wait(V, "dve", at=m)
        V.tensor_copy(s_w1c[:, c:c + 1], fsc[:, 1:2])
        inc(V.reciprocal(fsc[:, 2:3], fsc[:, 1:2]), "dve")
        m = snap(); wait(V, "dve", at=m)
        inc(V.tensor_scalar(fsc2[:, 0:1024], wstg[:], fsc[:, 2:3], 0.0, ALU.mult, ALU.add), "dve")
        m = snap(); wait(V, "dve", at=m)
        inc(V.tensor_copy(AI32[:, 0:1024], fsc2[:, 0:1024]), "dve")
        m = snap(); wait(V, "dve", at=m)
        inc(V.tensor_copy(AB2[:, 0:1024], AI32[:, 0:1024]), "dve")
        m = snap(); wait(TE, "dve", at=m)
        for d8 in range(8):
            inc(TE.transpose(PB[0][:, 128 * d8:128 * (d8 + 1)],
                             AB2[:, 128 * d8:128 * (d8 + 1)], identt[:]), "pe")
        m = snap(); wait(V, "pe", at=m)
        inc(V.tensor_copy(AB2[:, 1024:2048], PB[0][:, 0:1024]), "dve")
        m = snap(); wait(G, "dve", at=m)
        for d8 in range(8):
            inc(G.dma_start(out=dr["w1iT"][128 * d8:128 * (d8 + 1), 128 * c:128 * (c + 1)],
                            in_=AB2[:, 1024 + 128 * d8:1024 + 128 * (d8 + 1)]), "sg", 16)
    for c in range(D // 128):
        for q4 in range(4):
            m = snap(); wait(SY, "dve", "sg", at=m)
            inc(SY.dma_start(out=wstg[:], in_=ap["w2"][128 * c:128 * (c + 1),
                                                       1024 * q4:1024 * (q4 + 1)]), "sq", 16)
            m = snap(); wait(V, "sq", at=m)
            inc(V.tensor_reduce(fsc[:, q4:q4 + 1], wstg[:], axis=AX.X, op=ALU.max,
                                apply_absolute_value=True), "dve")
        m = snap(); wait(V, "dve", at=m)
        inc(V.tensor_reduce(fsc[:, 4:5], fsc[:, 0:4], axis=AX.X, op=ALU.max), "dve")
        m = snap(); wait(V, "dve", at=m)
        inc(V.tensor_scalar(fsc[:, 5:6], fsc[:, 4:5], 1e-8, 1.0 / 127.0, ALU.max, ALU.mult), "dve")
        m = snap(); wait(V, "dve", at=m)
        V.tensor_copy(bq2b[:, c:c + 1], fsc[:, 5:6])
        inc(V.reciprocal(fsc[:, 6:7], fsc[:, 5:6]), "dve")
        for q4 in range(4):
            m = snap(); wait(SY, "dve", "sg", at=m)
            inc(SY.dma_start(out=wstg[:], in_=ap["w2"][128 * c:128 * (c + 1),
                                                       1024 * q4:1024 * (q4 + 1)]), "sq", 16)
            m = snap(); wait(V, "sq", at=m)
            inc(V.tensor_scalar(fsc2[:, 0:1024], wstg[:], fsc[:, 6:7], 0.0, ALU.mult, ALU.add), "dve")
            m = snap(); wait(SC, "dve", at=m)
            m = snap(); wait(V, "dve", at=m)
            inc(V.tensor_copy(AI32[:, 0:1024], fsc2[:, 0:1024]), "dve")
            m = snap(); wait(V, "dve", at=m)
            inc(V.tensor_copy(AB2[:, 0:1024], AI32[:, 0:1024]), "dve")
            m = snap(); wait(V, "dve", at=m)
            inc(V.tensor_scalar(AB2[:, 0:1024], AB2[:, 0:1024], bq2b[:, c:c + 1], 0.0,
                                ALU.mult, ALU.add), "dve")
            m = snap(); wait(TE, "dve", at=m)
            for d8 in range(8):
                inc(TE.transpose(PB[0][:, 128 * d8:128 * (d8 + 1)],
                                 AB2[:, 128 * d8:128 * (d8 + 1)], identt[:]), "pe")
            m = snap(); wait(V, "pe", at=m)
            inc(V.tensor_copy(AB2[:, 1024:2048], PB[0][:, 0:1024]), "dve")
            m = snap(); wait(G, "dve", at=m)
            for d8 in range(8):
                inc(G.dma_start(out=dr["w2qT"][1024 * q4 + 128 * d8:1024 * q4 + 128 * (d8 + 1),
                                               128 * c:128 * (c + 1)],
                                in_=AB2[:, 1024 + 128 * d8:1024 + 128 * (d8 + 1)]), "sg", 16)
    m = snap(); wait(TE, "dve", at=m)
    inc(TE.transpose(PS[3][0:8, 0:128], bq2b[:, 0:8], identf[:]), "pe")
    m = snap(); wait(V, "pe", at=m)
    inc(V.tensor_copy(fsc[0:8, 0:128], PS[3][0:8, 0:128]), "dve")
    m = snap(); wait(G, "dve", at=m)
    inc(G.dma_start(out=dr["bounce"][0:8, 0:128], in_=fsc[0:8, 0:128]), "sg", 16)
    m = snap(); wait(SY, "sg", at=m)
    inc(SY.dma_start(out=sw2r[0:1, :],
                     in_=dr["bounce"].rearrange("a b -> (a b)")[None, 0:1024]), "sq", 16)
    m = snap(); wait(V, "sq", at=m)

    def emit_ln_stats(with_rowmax):
        for i in range(cfg.NT):
            st = stream[:, D * i:D * (i + 1)]
            m0 = snap(); wait(V, "act", at=m0)
            inc(V.reduce_sum(fsc[:, 0:1], st, axis=AX.X), "dve")
            m = snap(); wait(SC, "dve", at=m)
            inc(SC.activation(fsc2[:, 0:1024], st, AF.Square, accum_out=fsc[:, 1:2]), "act")
            m = snap(); wait(V, "act", at=m)
            V.tensor_scalar(fsc[:, 2:3], fsc[:, 0:1], 1.0 / D, 0.0, ALU.mult, ALU.add)
            inc(V.tensor_scalar(fsc[:, 3:4], fsc[:, 1:2], 1.0 / D, 0.0, ALU.mult, ALU.add), "dve")
            m = snap(); wait(V, "dve", at=m)
            V.tensor_copy(mu[:, i:i + 1], fsc[:, 2:3])
            inc(V.tensor_tensor(fsc[:, 4:5], fsc[:, 2:3], fsc[:, 2:3], ALU.mult), "dve")
            m = snap(); wait(V, "dve", at=m)
            inc(V.tensor_sub(fsc[:, 5:6], fsc[:, 3:4], fsc[:, 4:5]), "dve")
            m = snap(); wait(SC, "dve", at=m)
            inc(SC.activation(fsc[:, 6:7], fsc[:, 5:6], AF.Sqrt, bias=epsc[:, 0:1]), "act")
            m = snap(); wait(V, "act", at=m)
            inc(V.reciprocal(rstd[:, i:i + 1], fsc[:, 6:7]), "dve")
            if with_rowmax:
                inc(V.reduce_max(fsc[:, 8:9], st, axis=AX.X), "dve")
                m = snap(); wait(V, "dve", at=m)
                inc(V.tensor_scalar(fsc[:, 9:10], fsc[:, 8:9], mu[:, i:i + 1],
                                    rstd[:, i:i + 1], ALU.subtract, ALU.mult), "dve")
                m = snap(); wait(V, "dve", at=m)
                inc(V.tensor_max(maxc[:, 0:1], maxc[:, 0:1], fsc[:, 9:10]), "dve")

    def norm_tile(i, dst):
        return V.tensor_scalar(dst, stream[:, D * i:D * (i + 1)],
                               mu[:, i:i + 1], rstd[:, i:i + 1], ALU.subtract, ALU.mult)

    def emit_global_max(out_recip_col, out_val_col):
        m = snap(); wait(V, "dve", at=m)
        inc(V.memset(ccst[:], 0.0), "dve")
        m = snap(); wait(V, "dve", at=m)
        inc(V.tensor_copy(ccst[:, 0:1], maxc[:, 0:1]), "dve")
        m = snap(); wait(G, "dve", at=m)
        inc(G.dma_start(out=cc_in_t.ap(), in_=ccst[:]), "sg", 16)
        m = snap(); wait(G, "sg", at=m)
        inc(G.collective_compute("AllReduce", ALU.max, [list(cfg.cc_group)],
                                 [cc_in_t.ap().opt()], [cc_out_t.ap().opt()]), "cc", 1)
        m = snap(); wait(G, "cc", at=m)
        inc(G.dma_start(out=ccst[:], in_=cc_out_t.ap()), "sg", 16)
        m = snap(); wait(TE, "sg", at=m)
        inc(TE.transpose(PS[4][0:1, 0:128], ccst[:, 0:1], identf[:]), "pe")
        m = snap(); wait(V, "pe", at=m)
        inc(V.reduce_max(srow[0:1, 0:1], PS[4][0:1, 0:128], axis=AX.X), "dve")
        m = snap(); wait(V, "dve", at=m)
        inc(V.tensor_scalar(srow[0:1, 1:2], srow[0:1, 0:1], 1e-8, 1.0 / 255.0,
                            ALU.max, ALU.mult), "dve")
        m = snap(); wait(V, "dve", at=m)
        inc(V.reciprocal(srow[0:1, 2:3], srow[0:1, 1:2]), "dve")
        m = snap(); wait(TE, "dve", at=m)
        inc(TE.matmul(PS[4][:, 0:1], onesf[0:1, :], srow[0:1, 2:3], start=True, stop=True), "pe")
        inc(TE.matmul(PS[4][:, 1:2], onesf[0:1, :], srow[0:1, 1:2], start=True, stop=True), "pe")
        m = snap(); wait(V, "pe", at=m)
        V.tensor_copy(sInv[:, out_recip_col:out_recip_col + 1], PS[4][:, 0:1])
        inc(V.tensor_copy(sInv[:, out_val_col:out_val_col + 1], PS[4][:, 1:2]), "dve")
        m = snap(); wait(V, "dve", at=m)
        inc(V.memset(maxc[:, 0:1], 0.0), "dve")

    def load_w_chunks(name, dst, fdim, nchunk):
        for c in range(nchunk):
            m = snap(); wait(SY, "act", at=m)
            inc(SY.dma_start(out=wstg[:, 0:fdim], in_=ap[name][128 * c:128 * (c + 1), :]), "sq", 16)
            m = snap(); wait(SC, "sq", "sg", "pe", "dve", at=m)
            inc(SC.copy(dst[:, fdim * c:fdim * (c + 1)], wstg[:, 0:fdim]), "act")

    def xnT_tile(i, dstb, pbi=0):
        m = snap(); wait(V, "act", at=m)
        inc(norm_tile(i, fsc[:, 0:1024]), "dve")
        m = snap(); wait(SC, "dve", "pe", at=m)
        inc(SC.copy(AB2[:, 0:1024], fsc[:, 0:1024]), "act")
        m = snap(); wait(TE, "act", "dve", at=m)
        for d8 in range(8):
            inc(TE.transpose(PB[pbi][:, 128 * d8:128 * (d8 + 1)],
                             AB2[:, 128 * d8:128 * (d8 + 1)], identt[:]), "pe")
        m = snap(); wait(V, "pe", at=m)
        for d8 in range(8):
            inc(V.tensor_copy(dstb[:, 128 * d8:128 * (d8 + 1)],
                              PB[pbi][:, 128 * d8:128 * (d8 + 1)]), "dve")

    GKs2 = sb("GKs2", 128, 4 * cfg.NLAT)
    GVs2 = sb("GVs2", cfg.NLAT, 8 * 65)

    def emit_attn_a2():
        ca = Carve(AB1)
        wres = ca.take(8 * 1024)
        xnTb = ca.take(1024)
        evb = ca.take(1040)

        emit_ln_stats(with_rowmax=False)

        ngrp = cfg.NTS // cfg.GTS
        for sq_ in range(cfg.nseq):
            t0 = sq_ * cfg.NTS
            for g in range(ngrp):
                for it in range(cfg.GTS):
                    i = t0 + g * cfg.GTS + it
                    m = snap(); wait(V, "act", at=m)
                    inc(norm_tile(i, fsc[:, 0:1024]), "dve")
                    m = snap(); wait(SC, "dve", "pe", at=m)
                    inc(SC.copy(AB2[:, 0:1024], fsc[:, 0:1024]), "act")
                    m = snap(); wait(TE, "act", at=m)
                    if it == 0:
                        wait(TE, "dve", at=m)
                    for hf in range(2):
                        inc(TE.matmul(PS[hf][0:lpg, 0:512],
                                      poolt[:, lpg * it:lpg * (it + 1)],
                                      AB2[:, 512 * hf:512 * (hf + 1)],
                                      start=(it == 0), stop=(it == cfg.GTS - 1)), "pe")
                m = snap(); wait(V, "pe", at=m)
                for hf in range(2):
                    inc(V.tensor_copy(pooled[lpg * g:lpg * (g + 1), 512 * hf:512 * (hf + 1)],
                                      PS[hf][0:lpg, 0:512]), "dve")
            m = snap(); wait(TE, "dve", at=m)
            for d8 in range(8):
                inc(TE.transpose(PB[0][0:128, 128 * d8:128 * d8 + cfg.NLAT],
                                 pooled[0:cfg.NLAT, 128 * d8:128 * (d8 + 1)],
                                 identt[0:cfg.NLAT, 0:cfg.NLAT]), "pe")
            m = snap(); wait(V, "pe", at=m)
            for d8 in range(8):
                dcol = cfg.NLAT * d8
                if sq_ == 0:
                    inc(V.tensor_copy(pooledT[:, dcol:dcol + cfg.NLAT],
                                      PB[0][0:128, 128 * d8:128 * d8 + cfg.NLAT]), "dve")
                else:
                    inc(V.tensor_copy(evb[:, dcol:dcol + cfg.NLAT],
                                      PB[0][0:128, 128 * d8:128 * d8 + cfg.NLAT]), "dve")
        pooledT2 = evb

        for wn, kind in [("wgk", "K"), ("wgv", "V")]:
            load_w_chunks(wn, wres, 512, 8)
            for sq_ in range(cfg.nseq):
                pt = pooledT if sq_ == 0 else pooledT2
                gk_dst = GKs if sq_ == 0 else GKs2
                gv_dst = GVs if sq_ == 0 else GVs2
                m = snap(); wait(TE, "act", "dve", at=m)
                if kind == "K":
                    for fc in range(4):
                        for kc in range(8):
                            inc(TE.matmul(PS[fc % 2][0:128, 0:cfg.NLAT],
                                          wres[:, 512 * kc + 128 * fc:512 * kc + 128 * (fc + 1)],
                                          pt[:, cfg.NLAT * kc:cfg.NLAT * (kc + 1)],
                                          start=(kc == 0), stop=(kc == 7)), "pe")
                        m = snap(); wait(V, "pe", at=m)
                        inc(V.tensor_copy(gk_dst[:, cfg.NLAT * fc:cfg.NLAT * (fc + 1)],
                                          PS[fc % 2][0:128, 0:cfg.NLAT]), "dve")
                        m = snap(); wait(TE, "dve", at=m)
                else:
                    for kc in range(8):
                        inc(TE.matmul(PS[2][0:cfg.NLAT, 0:512],
                                      pt[:, cfg.NLAT * kc:cfg.NLAT * (kc + 1)],
                                      wres[:, 512 * kc:512 * (kc + 1)],
                                      start=(kc == 0), stop=(kc == 7)), "pe")
                    m = snap(); wait(V, "pe", at=m)
                    for lh in range(8):
                        inc(V.tensor_copy(gv_dst[0:cfg.NLAT, 65 * lh:65 * lh + 64],
                                          PS[2][0:cfg.NLAT, 64 * lh:64 * (lh + 1)]), "dve")
                    m = snap(); wait(V, "dve", at=m)
                    for lh in range(8):
                        inc(V.memset(gv_dst[0:cfg.NLAT, 65 * lh + 64:65 * (lh + 1)], 1.0), "dve")

        caA = Carve(AB1)
        wqR = caA.take(8 * 1024)
        wkR = caA.take(8 * 1024)
        wgqR = caA.take(8 * 512)
        wvR = caA.take(8 * 1024)
        xnTblk = caA.take(8 * 512)
        evS = [AB2[:, 1024:1536], AB2[:, 1536:2048]]
        evT = AB2[:, 2048:3088]
        load_w_chunks("wq", wqR, 1024, 8)
        load_w_chunks("wk", wkR, 1024, 8)
        load_w_chunks("wgq", wgqR, 512, 8)
        load_w_chunks("wv", wvR, 1024, 8)
        m = snap(); wait(V, "act", at=m)
        for h in range(16):
            inc(V.memset(evT[:, 65 * h + 64:65 * (h + 1)], 1.0), "dve")
        for bi in range(T // 512):
            for it in range(4):
                i = 4 * bi + it
                m = snap(); wait(V, "act", at=m)
                inc(norm_tile(i, fsc[:, 0:1024]), "dve")
                m = snap(); wait(SC, "dve", "pe", at=m)
                inc(SC.copy(AB2[:, 0:1024], fsc[:, 0:1024]), "act")
                m = snap(); wait(TE, "act", "dve", at=m)
                for d8 in range(8):
                    inc(TE.transpose(PB[it % 2][:, 128 * d8:128 * (d8 + 1)],
                                     AB2[:, 128 * d8:128 * (d8 + 1)], identt[:]), "pe")
                m = snap(); wait(V, "pe", at=m)
                for d8 in range(8):
                    inc(V.tensor_copy(xnTblk[:, 512 * d8 + 128 * it:512 * d8 + 128 * (it + 1)],
                                      PB[it % 2][:, 128 * d8:128 * (d8 + 1)]), "dve")
            m = snap(); wait(TE, "dve", at=m)
            for wres, dst, fdim in [(wqR, dr["aQ"], 1024), (wkR, dr["aK"], 1024),
                                    (wgqR, dr["aGQ"], 512)]:
                for fc in range(fdim // 128):
                    ps_q = PS[fc % 2]
                    m = snap(); wait(TE, "act", at=m)
                    for kc in range(8):
                        inc(TE.matmul(ps_q[:, 0:512],
                                      wres[:, fdim * kc + 128 * fc:fdim * kc + 128 * (fc + 1)],
                                      xnTblk[:, 512 * kc:512 * (kc + 1)],
                                      start=(kc == 0), stop=(kc == 7)), "pe")
                    m = snap(); wait(SC, "pe", "sg", at=m)
                    inc(SC.copy(evS[fc % 2][:, 0:512], ps_q[:, 0:512]), "act")
                    m = snap(); wait(G, "act", at=m)
                    inc(G.dma_start(out=dst[128 * fc:128 * (fc + 1), 512 * bi:512 * (bi + 1)],
                                    in_=evS[fc % 2][:, 0:512]), "sg", 16)
            for it in range(4):
                i = 4 * bi + it
                m = snap(); wait(TE, "dve", "act", at=m)
                for nh2 in range(2):
                    ps_v = PS[2 + nh2]
                    for kc in range(8):
                        inc(TE.matmul(ps_v[:, 0:512],
                                      xnTblk[:, 512 * kc + 128 * it:512 * kc + 128 * (it + 1)],
                                      wvR[:, 1024 * kc + 512 * nh2:1024 * kc + 512 * (nh2 + 1)],
                                      start=(kc == 0), stop=(kc == 7)), "pe")
                m = snap(); wait(V, "pe", "sg", at=m)
                for h in range(16):
                    inc(V.tensor_copy(evT[:, 65 * h:65 * h + 64],
                                      PS[2 + h // 8][:, 64 * (h % 8):64 * (h % 8) + 64]), "dve")
                m = snap(); wait(G, "dve", at=m)
                inc(G.dma_start(out=dr["aV"][128 * i:128 * (i + 1), :], in_=evT[:, 0:1040]), "sg", 16)

        ca2 = Carve(AB1)
        rW = ca2.take(4 * 1024)
        oW = ca2.take(8 * 1024)
        Qb = ca2.take(8 * 256)
        Kb = [ca2.take(8 * 256), ca2.take(8 * 256)]
        Vb = [ca2.take(2 * 1040), ca2.take(2 * 1040)]
        GQb = ca2.take(4 * 256)
        Ptb = ca2.take(512)
        PTb = ca2.take(512)
        remtb = ca2.take(2 * 512)
        remfb = ca2.take(4 * 256)
        combb = ca2.take(2 * 1024)
        gat = AFX[:, 4096:4096 + 64]

        load_w_chunks("r_out_w", rW, 1024, 4)
        load_w_chunks("w_out", oW, 1024, 8)
        mA2w = snap()

        for sq_ in range(cfg.nseq):
            gk = GKs if sq_ == 0 else GKs2
            gv = GVs if sq_ == 0 else GVs2
            for b in range(cfg.NB):
                cur, prv = b % 2, 1 - (b % 2)
                trow0 = (sq_ * S + b * 256)
                m = snap()
                wait(SY, "pe", at=m)
                inc(SY.dma_start(out=Qb[:, 0:2048],
                                 in_=dr["aQ"].rearrange("(c p) t -> p c t", p=128)
                                 [:, :, trow0:trow0 + 256]), "sq", 16)
                inc(SY.dma_start(out=Kb[cur][:, 0:2048],
                                 in_=dr["aK"].rearrange("(c p) t -> p c t", p=128)
                                 [:, :, trow0:trow0 + 256]), "sq", 16)
                inc(SY.dma_start(out=GQb[:, 0:1024],
                                 in_=dr["aGQ"].rearrange("(c p) t -> p c t", p=128)
                                 [:, :, trow0:trow0 + 256]), "sq", 16)
                inc(SY.dma_start(out=Vb[cur][:, 0:1040], in_=dr["aV"][trow0:trow0 + 128, :]), "sq", 16)
                inc(SY.dma_start(out=Vb[cur][:, 1040:2080], in_=dr["aV"][trow0 + 128:trow0 + 256, :]), "sq", 16)
                mLd = snap()
                if b == 0:
                    wait(V, "sq", at=mLd)
                    inc(V.memset(Kb[prv][:, :], 0.0), "dve")
                    inc(V.memset(Vb[prv][:, :], 0.0), "dve")
                wait(TE, "sq", at=mLd)
                wait(TE, "pe", at=mA2w)
                m = snap(); wait(TE, "dve", at=m)
                for mt in range(2):
                    for kc in range(8):
                        inc(TE.matmul(PS[4][:, 32 * mt:32 * (mt + 1)],
                                      Qb[:, 256 * kc + 128 * mt:256 * kc + 128 * (mt + 1)],
                                      g_exp[:, 32 * kc:32 * (kc + 1)],
                                      start=(kc == 0), stop=(kc == 7)), "pe")
                m = snap(); wait(SC, "pe", at=m)
                inc(SC.activation(gat[:, 0:64], PS[4][:, 0:64], AF.Sigmoid), "act")
                for lh in range(8):
                    fc, of = (64 * lh) // 128, (64 * lh) % 128
                    for mt in range(2):
                        inc(TE.matmul(PS[2][0:128, 64 * mt:64 * (mt + 1)],
                                      GQb[of:of + 64, 256 * fc + 128 * mt:256 * fc + 128 * (mt + 1)],
                                      gk[of:of + 64, 64 * fc:64 * fc + cfg.NLAT],
                                      start=True, stop=True), "pe")
                    m = snap(); wait(V, "pe", at=m)
                    V.reduce_max(fsc[:, 0:1], PS[2][:, 0:64], axis=AX.X)
                    inc(V.reduce_max(fsc[:, 1:2], PS[2][:, 64:128], axis=AX.X), "dve")
                    m = snap(); wait(V, "dve", at=m)
                    inc(V.tensor_scalar(fsc[:, 2:4], fsc[:, 0:2], -0.125, 0.0, ALU.mult, ALU.add), "dve")
                    for mt in range(2):
                        m = snap(); wait(SC, "dve" if mt == 0 else "pe", at=m)
                        wait(SC, "pool")
                        inc(SC.activation(Ptb[:, 0:cfg.NLAT], PS[2][:, 64 * mt:64 * (mt + 1)],
                                          AF.Exp, bias=fsc[:, 2 + mt:3 + mt], scale=0.125), "act")
                        m = snap(); wait(G, "act", at=m)
                        tbase = 256 * b + 128 * mt
                        inc(G.affine_select(Ptb[:, 0:cfg.NLAT], Ptb[:, 0:cfg.NLAT],
                                            [[-cfg.TPL, cfg.NLAT]], ALU.is_ge, 0.0,
                                            base=tbase, channel_multiplier=1), "pool")
                        m = snap(); wait(TE, "pool", at=m)
                        inc(TE.transpose(PB[0][0:cfg.NLAT, 128 * mt:128 * (mt + 1)],
                                         Ptb[:, 0:cfg.NLAT], identt[:]), "pe")
                        m = snap(); wait(V, "pe", at=m)
                        inc(V.tensor_copy(PTb[0:cfg.NLAT, 128 * mt:128 * (mt + 1)],
                                          PB[0][0:cfg.NLAT, 128 * mt:128 * (mt + 1)]), "dve")
                    m = snap(); wait(TE, "dve", at=m)
                    for mt in range(2):
                        inc(TE.matmul(PS[3][0:128, 65 * mt:65 * (mt + 1)],
                                      PTb[0:cfg.NLAT, 128 * mt:128 * (mt + 1)],
                                      gv[0:cfg.NLAT, 65 * lh:65 * (lh + 1)],
                                      start=True, stop=True), "pe")
                    m = snap(); wait(V, "pe", at=m)
                    for mt in range(2):
                        inc(V.reciprocal(fsc[:, 4 + mt:5 + mt],
                                         PS[3][:, 65 * mt + 64:65 * mt + 65]), "dve")
                    m = snap(); wait(V, "dve", at=m)
                    for mt in range(2):
                        inc(V.tensor_scalar(remtb[:, 512 * mt + 64 * lh:512 * mt + 64 * (lh + 1)],
                                            PS[3][:, 65 * mt:65 * mt + 64],
                                            fsc[:, 4 + mt:5 + mt], 0.0, ALU.mult, ALU.add), "dve")
                    m = snap(); wait(TE, "dve", at=m)
                for mt in range(2):
                    for d4 in range(4):
                        inc(TE.transpose(PB[mt][:, 128 * d4:128 * (d4 + 1)],
                                         remtb[:, 512 * mt + 128 * d4:512 * mt + 128 * (d4 + 1)],
                                         identt[:]), "pe")
                m = snap(); wait(V, "pe", at=m)
                for mt in range(2):
                    for d4 in range(4):
                        inc(V.tensor_copy(remfb[:, 256 * d4 + 128 * mt:256 * d4 + 128 * (mt + 1)],
                                          PB[mt][:, 128 * d4:128 * (d4 + 1)]), "dve")
                m = snap(); wait(TE, "dve", at=m)
                for mt in range(2):
                    for nh2 in range(2):
                        ps_r = PS[2 * mt + nh2]
                        for kc in range(4):
                            inc(TE.matmul(ps_r[:, 0:512],
                                          remfb[:, 256 * kc + 128 * mt:256 * kc + 128 * (mt + 1)],
                                          rW[:, 1024 * kc + 512 * nh2:1024 * kc + 512 * (nh2 + 1)],
                                          start=(kc == 0), stop=(kc == 3)), "pe")
                mRem = snap()
                for h in range(16):
                    fc, of = (64 * h) // 128, (64 * h) % 128
                    for mt in range(2):
                        ps_s = PS[4]
                        inc(TE.matmul(ps_s[:, 0:256],
                                      Qb[of:of + 64, 256 * fc + 128 * mt:256 * fc + 128 * (mt + 1)],
                                      Kb[prv][of:of + 64, 256 * fc:256 * fc + 256],
                                      start=True, stop=True), "pe")
                        inc(TE.matmul(ps_s[:, 256:512],
                                      Qb[of:of + 64, 256 * fc + 128 * mt:256 * fc + 128 * (mt + 1)],
                                      Kb[cur][of:of + 64, 256 * fc:256 * fc + 256],
                                      start=True, stop=True), "pe")
                        m = snap(); wait(V, "pe", at=m)
                        inc(V.reduce_max(fsc[:, 0:1], ps_s[:, 0:512], axis=AX.X), "dve")
                        m = snap(); wait(V, "dve", at=m)
                        inc(V.tensor_scalar(fsc[:, 1:2], fsc[:, 0:1], -0.125, 0.0,
                                            ALU.mult, ALU.add), "dve")
                        m = snap(); wait(SC, "dve", at=m)
                        inc(SC.activation(Ptb[:, 0:512], ps_s[:, 0:512], AF.Exp,
                                          bias=fsc[:, 1:2], scale=0.125), "act")
                        m = snap(); wait(G, "act", at=m)
                        wm = winft if b == 0 else winmt
                        inc(G.tensor_tensor(Ptb[:, 0:512], Ptb[:, 0:512],
                                            wm[:, 512 * mt:512 * (mt + 1)], ALU.mult), "pool")
                        m = snap(); wait(TE, "pool", at=m)
                        for k4 in range(4):
                            inc(TE.transpose(PB[0][:, 128 * k4:128 * (k4 + 1)],
                                             Ptb[:, 128 * k4:128 * (k4 + 1)], identt[:]), "pe")
                        m = snap(); wait(V, "pe", at=m)
                        inc(V.tensor_copy(PTb[:, 0:512], PB[0][:, 0:512]), "dve")
                        m = snap(); wait(TE, "dve", at=m)
                        for k4 in range(4):
                            vt = Vb[prv] if k4 < 2 else Vb[cur]
                            vof = 1040 * (k4 % 2)
                            inc(TE.matmul(PS[5][:, 0:65],
                                          PTb[:, 128 * k4:128 * (k4 + 1)],
                                          vt[:, vof + 65 * h:vof + 65 * (h + 1)],
                                          start=(k4 == 0), stop=(k4 == 3)), "pe")
                        m = snap(); wait(V, "pe", at=m)
                        inc(V.reciprocal(fsc[:, 2:3], PS[5][:, 64:65]), "dve")
                        m = snap(); wait(V, "dve", at=m)
                        inc(V.tensor_scalar(fsc[:, 4:68], PS[5][:, 0:64], fsc[:, 2:3],
                                            gat[:, 32 * mt + h:32 * mt + h + 1],
                                            ALU.mult, ALU.mult), "dve")
                        m = snap()
                        wait(V, "pe", at=mRem)
                        wait(V, "dve", at=m)
                        rps = PS[2 * mt + (64 * h) // 512]
                        rof = (64 * h) % 512
                        inc(V.scalar_tensor_tensor(
                            combb[:, 1024 * mt + 64 * h:1024 * mt + 64 * (h + 1)],
                            rps[:, rof:rof + 64],
                            gat[:, 32 * mt + 16 + h:32 * mt + 16 + h + 1],
                            fsc[:, 4:68], ALU.mult, ALU.add), "dve")
                m = snap(); wait(TE, "dve", at=m)
                for mt in range(2):
                    for d8 in range(8):
                        inc(TE.transpose(PB[mt][:, 128 * d8:128 * (d8 + 1)],
                                         combb[:, 1024 * mt + 128 * d8:1024 * mt + 128 * (d8 + 1)],
                                         identt[:]), "pe")
                m = snap(); wait(V, "pe", at=m)
                for mt in range(2):
                    for d8 in range(8):
                        inc(V.tensor_copy(Qb[:, 1024 * mt + 128 * d8:1024 * mt + 128 * (d8 + 1)],
                                          PB[mt][:, 128 * d8:128 * (d8 + 1)]), "dve")
                m = snap(); wait(TE, "dve", at=m)
                for mt in range(2):
                    for nh2 in range(2):
                        ps_o = PS[2 * mt + nh2]
                        for kc in range(8):
                            inc(TE.matmul(ps_o[:, 0:512],
                                          Qb[:, 1024 * mt + 128 * kc:1024 * mt + 128 * (kc + 1)],
                                          oW[:, 1024 * kc + 512 * nh2:1024 * kc + 512 * (nh2 + 1)],
                                          start=(kc == 0), stop=(kc == 7)), "pe")
                m = snap(); wait(V, "pe", at=m)
                for mt in range(2):
                    i = (trow0 // 128) + mt
                    for nh2 in range(2):
                        inc(V.tensor_tensor(stream[:, D * i + 512 * nh2:D * i + 512 * (nh2 + 1)],
                                            stream[:, D * i + 512 * nh2:D * i + 512 * (nh2 + 1)],
                                            PS[2 * mt + nh2][:, 0:512], ALU.add), "dve")
                m = snap(); wait(TE, "dve", at=m)
                wait(SY, "dve", at=m)

    def emit_ffn(final):
        ca = Carve(AB1)
        wbig = ca.take(8 * 2048)
        wbig2 = ca.take(8 * 2048)
        caB = Carve(AB2)
        xqTb = caB.take(4096)
        xqb = caB.take(1024)
        hqb = caB.take(0)

        emit_ln_stats(with_rowmax=True)
        emit_global_max(0, 1)
        m = snap(); wait(V, "dve", at=m)
        inc(V.tensor_scalar(ffnc[:, 0:32], s_w1c[:], sInv[:, 1:2], 0.0, ALU.mult, ALU.add), "dve")
        m = snap(); wait(V, "dve", at=m)
        inc(V.reciprocal(ffnc[:, 64:96], ffnc[:, 0:32]), "dve")
        m = snap(); wait(V, "dve", at=m)
        inc(V.tensor_tensor(fsc[:, 0:32], b1c[:], ffnc[:, 64:96], ALU.mult), "dve")
        m = snap(); wait(V, "dve", at=m)
        inc(V.tensor_copy(AI32[:, 0:32], fsc[:, 0:32]), "dve")
        m = snap(); wait(V, "dve", at=m)
        inc(V.tensor_copy(fsc[:, 32:64], AI32[:, 0:32]), "dve")
        m = snap(); wait(V, "dve", at=m)
        inc(V.tensor_tensor(ffnc[:, 32:64], fsc[:, 32:64], ffnc[:, 0:32], ALU.mult), "dve")

        m = snap(); wait(SY, "pe", "act", "dve", "pool", at=m)
        for d8 in range(8):
            inc(SY.dma_start(out=wbig[:, 2048 * d8:2048 * (d8 + 1)],
                             in_=dr["w1iT"][128 * d8:128 * (d8 + 1), 0:2048]), "sq", 16)
            inc(SY.dma_start(out=wbig2[:, 2048 * d8:2048 * (d8 + 1)],
                             in_=dr["w1iT"][128 * d8:128 * (d8 + 1), 2048:4096]), "sq", 16)
        mW1 = snap()
        m = snap(); wait(V, "dve", at=m)
        inc(V.memset(maxc[:, 0:1], 0.0), "dve")

        nblk = T // 512
        for bk in range(nblk):
            for it in range(4):
                i = 4 * bk + it
                m = snap(); wait(V, "pe", at=m)
                inc(norm_tile(i, fsc[:, 0:1024]), "dve")
                m = snap(); wait(V, "dve", at=m)
                inc(V.tensor_scalar(AU8[:, 0:1024], fsc[:, 0:1024], sInv[:, 0:1],
                                    0.0, ALU.mult, ALU.max), "dve")
                m = snap(); wait(V, "dve", at=m)
                inc(V.tensor_copy(xqb[:, 0:1024], AU8[:, 0:1024]), "dve")
                m = snap(); wait(TE, "dve", at=m)
                for d8 in range(8):
                    inc(TE.transpose(PB[it % 2][:, 128 * d8:128 * (d8 + 1)],
                                     xqb[:, 128 * d8:128 * (d8 + 1)], identt[:]), "pe")
                m = snap(); wait(V, "pe", at=m)
                for d8 in range(8):
                    inc(V.tensor_copy(xqTb[:, 512 * d8 + 128 * it:512 * d8 + 128 * (it + 1)],
                                      PB[it % 2][:, 128 * d8:128 * (d8 + 1)]), "dve")
            if bk == 0:
                wait(TE, "sq", at=mW1)
                if getattr(cfg, "dbg_xqt", False):
                    m = snap()
                    inc(G.dma_start(out=dr["dbg"], in_=xqTb[:, 0:4096]), "sg", 16)
            m = snap(); wait(TE, "dve", at=m)
            for fc in range(F // 128):
                wsrc = wbig if fc < 16 else wbig2
                fof = (fc % 16) * 128
                ps_f = PS[fc % 4]
                m = snap(); wait(TE, "act", at=m)
                for kc in range(8):
                    inc(TE.matmul(ps_f[:, 0:512],
                                  wsrc[:, 2048 * kc + fof:2048 * kc + fof + 128],
                                  xqTb[:, 512 * kc:512 * (kc + 1)],
                                  start=(kc == 0), stop=(kc == 7)), "pe")
                m = snap(); wait(SC, "pe", "sg", "dve", at=m)
                inc(SC.activation(b_sc[:, 0:512], ps_f[:, 0:512], AF.Relu,
                                  bias=ffnc[:, 32 + fc:33 + fc], scale=ffnc[:, fc:fc + 1]), "act")
                m = snap(); wait(V, "act", at=m)
                inc(V.reduce_max(fsc[:, 100:101], b_sc[:, 0:512], axis=AX.X), "dve")
                m = snap(); wait(V, "dve", at=m)
                inc(V.tensor_max(maxc[:, 0:1], maxc[:, 0:1], fsc[:, 100:101]), "dve")
                m = snap(); wait(G, "act", at=m)
                inc(G.dma_start(out=dr["h"][128 * fc:128 * (fc + 1), 512 * bk:512 * (bk + 1)],
                                in_=b_sc[:, 0:512]), "sg", 16)
        emit_global_max(2, 3)
        sh_sc = srow[0:1, 1:2]
        m = snap(); wait(V, "dve", at=m)
        inc(V.tensor_scalar(fsc[0:1, 0:1024], sw2r[0:1, :], sh_sc, 0.0, ALU.mult, ALU.add), "dve")
        m = snap(); wait(V, "dve", at=m)
        inc(V.reciprocal(fsc2[0:1, 0:1024], fsc[0:1, 0:1024]), "dve")
        m = snap(); wait(V, "dve", at=m)
        inc(V.tensor_tensor(fsc2[0:1, 0:1024], b2r[0:1, :], fsc2[0:1, 0:1024], ALU.mult), "dve")
        m = snap(); wait(V, "dve", at=m)
        inc(V.tensor_copy(AI32[0:1, 0:1024], fsc2[0:1, 0:1024]), "dve")
        m = snap(); wait(V, "dve", at=m)
        inc(V.tensor_copy(fsc2[0:1, 0:1024], AI32[0:1, 0:1024]), "dve")
        m = snap(); wait(V, "dve", at=m)
        inc(V.tensor_mul(fsc2[0:1, 0:1024], fsc2[0:1, 0:1024], fsc[0:1, 0:1024]), "dve")
        m = snap(); wait(TE, "dve", at=m)
        inc(TE.matmul(PS[4][:, 0:512], onesf[0:1, :], fsc2[0:1, 0:512], start=True, stop=True), "pe")
        inc(TE.matmul(PS[5][:, 0:512], onesf[0:1, :], fsc2[0:1, 512:1024], start=True, stop=True), "pe")
        m = snap(); wait(V, "pe", at=m)
        V.tensor_copy(bq2b[:, 0:512], PS[4][:, 0:512])
        inc(V.tensor_copy(bq2b[:, 512:1024], PS[5][:, 0:512]), "dve")

        m = snap(); wait(SY, "pe", at=m)
        m = snap(); wait(SY, "pe", "act", "dve", "pool", at=m)
        for kc in range(16):
            inc(SY.dma_start(out=wbig[:, 1024 * kc:1024 * (kc + 1)],
                             in_=dr["w2qT"][128 * kc:128 * (kc + 1), :]), "sq", 16)
            inc(SY.dma_start(out=wbig2[:, 1024 * kc:1024 * (kc + 1)],
                             in_=dr["w2qT"][128 * (16 + kc):128 * (17 + kc), :]), "sq", 16)
        mW2 = snap()

        hqcols = xqTb
        for i in range(cfg.NT):
            for f4 in range(F // 512):
                m = snap(); wait(SY, "dve", at=m)
                inc(SY.dma_start(out=fsc[:, 0:512].rearrange("p (c t) -> p c t", c=4),
                                 in_=dr["h"].rearrange("(c p) t -> p c t", p=128)
                                 [:, 4 * f4:4 * (f4 + 1), 128 * i:128 * (i + 1)]), "sq", 16)
                m = snap(); wait(V, "sq", at=m)
                inc(V.tensor_scalar(AU8[:, 0:512], fsc[:, 0:512], sInv[:, 2:3], 0.0,
                                    ALU.mult, ALU.max), "dve")
                m = snap(); wait(V, "dve", at=m)
                inc(V.tensor_copy(hqcols[:, 512 * f4:512 * (f4 + 1)], AU8[:, 0:512]), "dve")
            if i == 0:
                wait(TE, "sq", at=mW2)
            m = snap(); wait(TE, "dve", at=m)
            for nh2 in range(2):
                ps_o = PS[nh2]
                for kc in range(32):
                    wsrc = wbig if kc < 16 else wbig2
                    inc(TE.matmul(ps_o[:, 0:512],
                                  hqcols[:, 128 * kc:128 * (kc + 1)],
                                  wsrc[:, 1024 * (kc % 16) + 512 * nh2:1024 * (kc % 16) + 512 * (nh2 + 1)],
                                  start=(kc == 0), stop=(kc == 31)), "pe")
            m = snap(); wait(V, "pe", at=m)
            for nh2 in range(2):
                inc(V.scalar_tensor_tensor(fsc[:, 512 * nh2:512 * (nh2 + 1)],
                                           PS[nh2][:, 0:512], sInv[:, 3:4],
                                           bq2b[:, 512 * nh2:512 * (nh2 + 1)],
                                           ALU.mult, ALU.add), "dve")
            m = snap(); wait(V, "dve", at=m)
            if final:
                inc(V.tensor_tensor(fsc2[:, 0:1024], fsc[:, 0:1024],
                                    stream[:, D * i:D * (i + 1)], ALU.add), "dve")
                m = snap(); wait(G, "dve", at=m)
                inc(G.dma_start(out=out[128 * i:128 * (i + 1), :], in_=fsc2[:, 0:1024]), "out", 16)
                m = snap(); wait(V, "out", at=m)
            else:
                inc(V.tensor_tensor(stream[:, D * i:D * (i + 1)],
                                    stream[:, D * i:D * (i + 1)], fsc[:, 0:1024], ALU.add), "dve")
            m = snap(); wait(TE, "dve", at=m)

    def emit_attn_b():
        emit_ln_stats(with_rowmax=False)

        caB1 = Carve(AB1)
        WqR = caB1.take(8 * 1024)
        WkR = caB1.take(8 * 1024)
        WvR = caB1.take(8 * 1024)
        xnTblk2 = caB1.take(8 * 512)
        evC = AB2[:, 1024:1536]
        vdb = AB2[:, 1536:2112]
        fcast = AB2[:, 2112:2624]
        vhist = AB2[:, 2624:3136]
        chist = AFX[:, 4096:4168]
        load_w_chunks("bWq", WqR, 1024, 8)
        load_w_chunks("bWk", WkR, 1024, 8)
        load_w_chunks("bWv", WvR, 1024, 8)
        for sq_ in range(cfg.nseq):
            m = snap(); wait(V, "act", "pool", at=m)
            inc(V.memset(chist[:, :], 0.0), "dve")
            inc(V.memset(vhist[:, 0:512], 0.0), "dve")
            for bi in range(S // 512):
                blk_g = sq_ * (S // 512) + bi
                for it in range(4):
                    i = 4 * blk_g + it
                    m = snap(); wait(V, "act", at=m)
                    inc(norm_tile(i, fsc[:, 0:1024]), "dve")
                    m = snap(); wait(SC, "dve", "pe", at=m)
                    inc(SC.copy(AB2[:, 0:1024], fsc[:, 0:1024]), "act")
                    m = snap(); wait(TE, "act", "dve", at=m)
                    for d8 in range(8):
                        inc(TE.transpose(PB[it % 2][:, 128 * d8:128 * (d8 + 1)],
                                         AB2[:, 128 * d8:128 * (d8 + 1)], identt[:]), "pe")
                    m = snap(); wait(V, "pe", at=m)
                    for d8 in range(8):
                        inc(V.tensor_copy(xnTblk2[:, 512 * d8 + 128 * it:512 * d8 + 128 * (it + 1)],
                                          PB[it % 2][:, 128 * d8:128 * (d8 + 1)]), "dve")
                m = snap(); wait(TE, "dve", at=m)
                for wi, (wres2, dst) in enumerate([(WqR, dr["bq"]), (WkR, dr["bk"]),
                                                   (WvR, dr["bv"])]):
                    for fc in range(8):
                        ps_q = PS[fc % 2]
                        m = snap(); wait(TE, "dve", "act", at=m)
                        for kc in range(8):
                            inc(TE.matmul(ps_q[:, 0:512],
                                          wres2[:, 1024 * kc + 128 * fc:1024 * kc + 128 * (fc + 1)],
                                          xnTblk2[:, 512 * kc:512 * (kc + 1)],
                                          start=(kc == 0), stop=(kc == 7)), "pe")
                        cwc = cw[:, (wi * 8 + fc) * 4:(wi * 8 + fc) * 4 + 4]
                        hc3 = chist[:, (wi * 8 + fc) * 3:(wi * 8 + fc) * 3 + 3]
                        m = snap(); wait(V, "pe", "act", at=m)
                        inc(V.tensor_scalar(b_sc[:, 0:512], ps_q[:, 0:512], cwc[:, 3:4],
                                            0.0, ALU.mult, ALU.add), "dve")
                        for tp in range(1, 4):
                            inc(V.scalar_tensor_tensor(b_sc[:, tp:512], ps_q[:, 0:512 - tp],
                                                       cwc[:, 3 - tp:4 - tp],
                                                       b_sc[:, tp:512], ALU.mult, ALU.add), "dve")
                            inc(V.scalar_tensor_tensor(b_sc[:, 0:tp], hc3[:, 3 - tp:3],
                                                       cwc[:, 3 - tp:4 - tp],
                                                       b_sc[:, 0:tp], ALU.mult, ALU.add), "dve")
                        inc(V.tensor_copy(hc3[:, :], ps_q[:, 509:512]), "dve")
                        m = snap(); wait(SC, "dve", "sg", at=m)
                        inc(SC.activation(evC[:, 0:512], b_sc[:, 0:512], AF.Silu), "act")
                        m = snap(); wait(G, "act", at=m)
                        inc(G.dma_start(out=dst[128 * fc:128 * (fc + 1), 512 * blk_g:512 * (blk_g + 1)],
                                        in_=evC[:, 0:512]), "sg", 16)
                        if wi == 2:
                            m = snap(); wait(V, "act", at=m)
                            V.tensor_copy(vdb[:, 0:64], vhist[:, 64 * fc:64 * (fc + 1)])
                            inc(V.tensor_copy(vdb[:, 64:576], evC[:, 0:512]), "dve")
                            inc(V.tensor_copy(vhist[:, 64 * fc:64 * (fc + 1)],
                                              vdb[:, 512:576]), "dve")
                            for (filt, Kf, dspill) in [(firsc, 3, dr["bls"]), (firlc, 63, dr["bll"])]:
                                fco = filt[:, Kf * fc:Kf * (fc + 1)]
                                m = snap(); wait(V, "dve", "act", at=m)
                                inc(V.tensor_scalar(b_sc[:, 0:512], vdb[:, 64:576],
                                                    fco[:, Kf - 1:Kf], 0.0, ALU.mult, ALU.add), "dve")
                                for tp in range(1, Kf):
                                    inc(V.scalar_tensor_tensor(b_sc[:, 0:512],
                                                               vdb[:, 64 - tp:576 - tp],
                                                               fco[:, Kf - 1 - tp:Kf - tp],
                                                               b_sc[:, 0:512], ALU.mult, ALU.add), "dve")
                                m = snap(); wait(SC, "dve", "sg", at=m)
                                inc(SC.copy(fcast[:, 0:512], b_sc[:, 0:512]), "act")
                                m = snap(); wait(G, "act", at=m)
                                inc(G.dma_start(out=dspill[128 * fc:128 * (fc + 1),
                                                          512 * blk_g:512 * (blk_g + 1)],
                                                in_=fcast[:, 0:512]), "sg", 16)

        caB2 = Carve(AB1)
        WidR = caB2.take(8 * 1024)
        GW1R = caB2.take(8 * 1024)
        WbR = caB2.take(8 * NH)
        xnTblk3 = caB2.take(8 * 512)
        load_w_chunks("bWid", WidR, 1024, 8)
        load_w_chunks("gW1", GW1R, 1024, 8)
        load_w_chunks("bWb", WbR, NH, 8)
        for blk_g in range(T // 512):
            for it in range(4):
                i = 4 * blk_g + it
                m = snap(); wait(V, "act", at=m)
                inc(norm_tile(i, fsc[:, 0:1024]), "dve")
                m = snap(); wait(SC, "dve", "pe", at=m)
                inc(SC.copy(AB2[:, 0:1024], fsc[:, 0:1024]), "act")
                m = snap(); wait(TE, "act", "dve", at=m)
                for d8 in range(8):
                    inc(TE.transpose(PB[it % 2][:, 128 * d8:128 * (d8 + 1)],
                                     AB2[:, 128 * d8:128 * (d8 + 1)], identt[:]), "pe")
                m = snap(); wait(V, "pe", at=m)
                for d8 in range(8):
                    inc(V.tensor_copy(xnTblk3[:, 512 * d8 + 128 * it:512 * d8 + 128 * (it + 1)],
                                      PB[it % 2][:, 128 * d8:128 * (d8 + 1)]), "dve")
            m = snap(); wait(TE, "dve", at=m)
            for it in range(4):
                i = 4 * blk_g + it
                for kc in range(8):
                    inc(TE.matmul(PS[4][:, 0:NH],
                                  xnTblk3[:, 512 * kc + 128 * it:512 * kc + 128 * (it + 1)],
                                  WbR[:, NH * kc:NH * (kc + 1)],
                                  start=(kc == 0), stop=(kc == 7)), "pe")
                for nh2 in range(2):
                    ps_v = PS[2 + nh2]
                    for kc in range(8):
                        inc(TE.matmul(ps_v[:, 0:512],
                                      xnTblk3[:, 512 * kc + 128 * it:512 * kc + 128 * (it + 1)],
                                      WidR[:, 1024 * kc + 512 * nh2:1024 * kc + 512 * (nh2 + 1)],
                                      start=(kc == 0), stop=(kc == 7)), "pe")
                m = snap(); wait(SC, "pe", "sg", at=m)
                inc(SC.activation(betac[:, NH * i:NH * (i + 1)], PS[4][:, 0:NH], AF.Sigmoid), "act")
                SC.copy(evC[:, 0:512], PS[2][:, 0:512])
                inc(SC.copy(fcast[:, 0:512], PS[3][:, 0:512]), "act")
                m = snap(); wait(G, "act", at=m)
                inc(G.dma_start(out=dr["bid"][128 * i:128 * (i + 1), 0:512], in_=evC[:, 0:512]), "sg", 16)
                inc(G.dma_start(out=dr["bid"][128 * i:128 * (i + 1), 512:1024], in_=fcast[:, 0:512]), "sg", 16)
                m = snap(); wait(TE, "act", at=m)
            for fc in range(8):
                ps_q = PS[fc % 2]
                m = snap(); wait(TE, "act", at=m)
                for kc in range(8):
                    inc(TE.matmul(ps_q[:, 0:512],
                                  GW1R[:, 1024 * kc + 128 * fc:1024 * kc + 128 * (fc + 1)],
                                  xnTblk3[:, 512 * kc:512 * (kc + 1)],
                                  start=(kc == 0), stop=(kc == 7)), "pe")
                m = snap(); wait(SC, "pe", "sg", at=m)
                inc(SC.copy(evC[:, 0:512], ps_q[:, 0:512]), "act")
                m = snap(); wait(G, "act", at=m)
                inc(G.dma_start(out=dr["bgt"][128 * fc:128 * (fc + 1), 512 * blk_g:512 * (blk_g + 1)],
                                in_=evC[:, 0:512]), "sg", 16)

        ca2 = Carve(AB1)
        oW = ca2.take(8 * 1024)
        qfb = ca2.take(1024)
        kfb = ca2.take(1024)
        ktb = ca2.take(1024)
        vtb = ca2.take(1024)
        kbt = ca2.take(1024)
        vbt = ca2.take(1024)
        lstb = ca2.take(1024)
        lltb = ca2.take(1024)
        dotb = ca2.take(1024)
        idtb = ca2.take(1024)
        gtb = ca2.take(1024)
        gelb = ca2.take(1024)
        otb = ca2.take(1024)
        Sbf_ = [ca2.take(512) for _ in range(2 * NH)]
        dL = ca2.take(512)
        dT = ca2.take(512)
        dTT = ca2.take(512)
        dP2 = ca2.take(512)
        dP2T = ca2.take(512)
        dAttnT = ca2.take(512)
        dU = ca2.take(1024)
        dUadj = ca2.take(1024)
        dWTT = ca2.take(1024)
        dKbf = ca2.take(1024)
        vfb = dKbf
        qtb = dU
        fstg2 = dWTT
        ofb = otb
        statb = AB2
        probs_f = AFX[:, 4120:4136]
        stat_sc = AFX[:, 4136:4200]
        b2fs = AFX[:, 4200:4712]
        b2fs2 = AFX[:, 4712:5120]

        load_w_chunks("bWo", oW, 1024, 8)
        mWo = snap()
        for sq_ in range(cfg.nseq):
            m = snap(); wait(V, "act", "pool", "pe", at=m)
            for k in range(2 * NH):
                inc(V.memset(Sbf_[k][:, :], 0.0), "dve")
            for sg_i in range(cfg.NTS):
                i = sq_ * cfg.NTS + sg_i
                trow0 = 128 * i
                m = snap()
                wait(SY, "pe", "dve", at=m)
                inc(SY.dma_start(out=qfb[:, 0:1024],
                                 in_=dr["bq"].rearrange("(c p) t -> p c t", p=128)
                                 [:, :, trow0:trow0 + 128]), "sq", 16)
                inc(SY.dma_start(out=kfb[:, 0:1024],
                                 in_=dr["bk"].rearrange("(c p) t -> p c t", p=128)
                                 [:, :, trow0:trow0 + 128]), "sq", 16)
                inc(SY.dma_start(out=gtb[:, 0:1024],
                                 in_=dr["bgt"].rearrange("(c p) t -> p c t", p=128)
                                 [:, :, trow0:trow0 + 128]), "sq", 16)
                inc(SY.dma_start(out=idtb[:, 0:1024], in_=dr["bid"][trow0:trow0 + 128, :]), "sq", 16)
                inc(SY.dma_start(out=vfb[:, 0:1024],
                                 in_=dr["bv"].rearrange("(c p) t -> p c t", p=128)
                                 [:, :, trow0:trow0 + 128]), "sq", 16)
                mLd = snap()
                wait(TE, "sq", at=mLd)
                m = snap(); wait(TE, "dve", at=m)
                for d8 in range(8):
                    inc(TE.transpose(PB[0][:, 128 * d8:128 * (d8 + 1)],
                                     qfb[:, 128 * d8:128 * (d8 + 1)], identt[:]), "pe")
                m = snap(); wait(V, "pe", at=m)
                for d8 in range(8):
                    inc(V.tensor_copy(qtb[:, 128 * d8:128 * (d8 + 1)],
                                      PB[0][:, 128 * d8:128 * (d8 + 1)]), "dve")
                m = snap(); wait(TE, "dve", at=m)
                for d8 in range(8):
                    inc(TE.transpose(PB[0][:, 128 * d8:128 * (d8 + 1)],
                                     kfb[:, 128 * d8:128 * (d8 + 1)], identt[:]), "pe")
                m = snap(); wait(V, "pe", at=m)
                for d8 in range(8):
                    inc(V.tensor_copy(ktb[:, 128 * d8:128 * (d8 + 1)],
                                      PB[0][:, 128 * d8:128 * (d8 + 1)]), "dve")
                m = snap(); wait(TE, "dve", at=m)
                for d8 in range(8):
                    inc(TE.transpose(PB[0][:, 128 * d8:128 * (d8 + 1)],
                                     vfb[:, 128 * d8:128 * (d8 + 1)], identt[:]), "pe")
                m = snap(); wait(V, "pe", at=m)
                for d8 in range(8):
                    inc(V.tensor_copy(vtb[:, 128 * d8:128 * (d8 + 1)],
                                      PB[0][:, 128 * d8:128 * (d8 + 1)]), "dve")
                for tb, fb in [(qtb, qfb), (ktb, kfb)]:
                    for h in range(NH):
                        m = snap(); wait(SC, "dve", at=m)
                        inc(SC.activation(b2fs[:, 0:256], tb[:, 256 * h:256 * (h + 1)],
                                          AF.Square, accum_out=stat_sc[:, h:h + 1]), "act")
                    m = snap(); wait(SC, "act", at=m)
                    inc(SC.activation(stat_sc[:, 4:8], stat_sc[:, 0:4], AF.Sqrt, bias=epsc[:, 1:2]), "act")
                    m = snap(); wait(V, "act", at=m)
                    inc(V.reciprocal(stat_sc[:, 8:12], stat_sc[:, 4:8]), "dve")
                    m = snap(); wait(V, "dve", at=m)
                    for h in range(NH):
                        inc(V.tensor_scalar(tb[:, 256 * h:256 * (h + 1)],
                                            tb[:, 256 * h:256 * (h + 1)],
                                            stat_sc[:, 8 + h:9 + h], 0.0, ALU.mult, ALU.add), "dve")
                    m = snap(); wait(TE, "dve", at=m)
                    for d8 in range(8):
                        inc(TE.transpose(PB[0][:, 128 * d8:128 * (d8 + 1)],
                                         tb[:, 128 * d8:128 * (d8 + 1)], identt[:]), "pe")
                    m = snap(); wait(V, "pe", at=m)
                    for d8 in range(8):
                        inc(V.tensor_copy(fb[:, 128 * d8:128 * (d8 + 1)],
                                          PB[0][:, 128 * d8:128 * (d8 + 1)]), "dve")
                m = snap(); wait(V, "dve", at=m)
                for h in range(NH):
                    V.tensor_scalar(kbt[:, 256 * h:256 * (h + 1)], ktb[:, 256 * h:256 * (h + 1)],
                                    betac[:, NH * i + h:NH * i + h + 1], 0.0, ALU.mult, ALU.add)
                    inc(V.tensor_scalar(vbt[:, 256 * h:256 * (h + 1)], vtb[:, 256 * h:256 * (h + 1)],
                                        betac[:, NH * i + h:NH * i + h + 1], 0.0, ALU.mult, ALU.add), "dve")
                for (dspill, dstt) in [(dr["bls"], lstb), (dr["bll"], lltb)]:
                    m = snap(); wait(SY, "pe", "dve", at=m)
                    inc(SY.dma_start(out=fstg2[:, 0:1024],
                                     in_=dspill.rearrange("(c p) t -> p c t", p=128)
                                     [:, :, trow0:trow0 + 128]), "sq", 16)
                    m = snap(); wait(TE, "sq", "dve", at=m)
                    for d8 in range(8):
                        inc(TE.transpose(PB[1][:, 128 * d8:128 * (d8 + 1)],
                                         fstg2[:, 128 * d8:128 * (d8 + 1)], identt[:]), "pe")
                    m = snap(); wait(V, "pe", at=m)
                    for d8 in range(8):
                        inc(V.tensor_copy(dstt[:, 128 * d8:128 * (d8 + 1)],
                                          PB[1][:, 128 * d8:128 * (d8 + 1)]), "dve")
                if getattr(cfg, "delta_off", False):
                    m = snap(); wait(V, "pe", at=m)
                    inc(V.memset(dotb[:, 0:1024], 0.0), "dve")
                else:
                    m = snap(); wait(TE, "dve", at=m)
                    for h in range(NH):
                        for d2 in range(2):
                            inc(TE.transpose(PB[0][:, 256 * h + 128 * d2:256 * h + 128 * (d2 + 1)],
                                             kbt[:, 256 * h + 128 * d2:256 * h + 128 * (d2 + 1)],
                                             identt[:]), "pe")
                    m = snap(); wait(V, "pe", at=m)
                    inc(V.tensor_copy(dKbf[:, 0:1024], PB[0][:, 0:1024]), "dve")

                    if getattr(cfg, 'delta_upto', 99) >= 1:
                        m = snap(); wait(TE, "dve", at=m)
                        for h in range(NH):
                            for d2 in range(2):
                                inc(TE.matmul(PS[h][:, 0:128],
                                              dKbf[:, 256 * h + 128 * d2:256 * h + 128 * (d2 + 1)],
                                              kfb[:, 256 * h + 128 * d2:256 * h + 128 * (d2 + 1)],
                                              start=(d2 == 0), stop=(d2 == 1)), "pe")
                        m = snap(); wait(SC, "pe", at=m)
                        for h in range(NH):
                            inc(SC.mul(dL[:, 128 * h:128 * (h + 1)], PS[h][:, 0:128], -1.0), "act")
                        m = snap(); wait(G, "act", at=m)
                        inc(G.affine_select(dL[:, 0:512], dL[:, 0:512], [[0, 4], [-1, 128]],
                                            ALU.is_ge, 0.0, base=-1, channel_multiplier=1), "pool")


                    if getattr(cfg, 'delta_upto', 99) >= 2:
                        for h in range(NH):
                            hc0 = 128 * h
                            m = snap(); wait(V, "pool", "act", at=m)
                            inc(V.memset(dT[0:64, hc0 + 64:hc0 + 128], 0.0), "dve")
                            for hf_ in range(2):
                                b0 = 64 * hf_
                                slb = slice(b0, b0 + 64)
                                idb = identt[slb, slb]
                                m = snap(); wait(V, "act", at=m)
                                inc(V.tensor_copy(dT[slb, hc0 + b0:hc0 + b0 + 64], idb), "dve")
                                inc(V.tensor_copy(dP2[slb, hc0:hc0 + 64],
                                                  dL[slb, hc0 + b0:hc0 + b0 + 64]), "dve")
                                m = snap(); wait(TE, "dve", at=m)
                                inc(TE.transpose(PB[0][0:64, 0:64], dP2[slb, hc0:hc0 + 64], idb), "pe")
                                m = snap(); wait(V, "pe", at=m)
                                inc(V.tensor_copy(dP2T[slb, hc0:hc0 + 64], PB[0][0:64, 0:64]), "dve")
                                for st_ in range(6):
                                    m = snap(); wait(TE, "dve", "act", at=m)
                                    inc(TE.matmul(PS[2][0:64, 0:64], dP2T[slb, hc0:hc0 + 64],
                                                  dT[slb, hc0 + b0:hc0 + b0 + 64],
                                                  start=True, stop=False), "pe")
                                    inc(TE.matmul(PS[2][0:64, 0:64], idb,
                                                  dT[slb, hc0 + b0:hc0 + b0 + 64],
                                                  start=False, stop=True), "pe")
                                    if st_ < 5:
                                        inc(TE.matmul(PS[1][0:64, 0:64], dP2T[slb, hc0:hc0 + 64],
                                                      dP2[slb, hc0:hc0 + 64],
                                                      start=True, stop=True), "pe")
                                        inc(TE.matmul(PS[3][0:64, 0:64], dP2[slb, hc0:hc0 + 64],
                                                      dP2T[slb, hc0:hc0 + 64],
                                                      start=True, stop=True), "pe")
                                    m = snap(); wait(SC, "pe", at=m)
                                    inc(SC.copy(dT[slb, hc0 + b0:hc0 + b0 + 64], PS[2][0:64, 0:64]), "act")
                                    if st_ < 5:
                                        inc(SC.copy(dP2[slb, hc0:hc0 + 64], PS[1][0:64, 0:64]), "act")
                                        inc(SC.copy(dP2T[slb, hc0:hc0 + 64], PS[3][0:64, 0:64]), "act")
                                m = snap(); wait(TE, "act", at=m)
                            idb1 = identt[64:128, 64:128]
                            inc(TE.transpose(PB[0][0:64, 0:64], dL[64:128, hc0:hc0 + 64], idb1), "pe")
                            m = snap(); wait(V, "pe", at=m)
                            inc(V.tensor_copy(dP2[0:64, hc0:hc0 + 64], PB[0][0:64, 0:64]), "dve")
                            m = snap(); wait(TE, "dve", at=m)
                            inc(TE.matmul(PS[1][0:64, 0:64], dP2[0:64, hc0:hc0 + 64],
                                          dT[0:64, hc0:hc0 + 64], start=True, stop=True), "pe")
                            m = snap(); wait(SC, "pe", at=m)
                            inc(SC.copy(dP2T[0:64, hc0:hc0 + 64], PS[1][0:64, 0:64]), "act")
                            m = snap(); wait(TE, "act", at=m)
                            inc(TE.transpose(PB[0][0:64, 0:64], dT[64:128, hc0 + 64:hc0 + 128], idb1), "pe")
                            m = snap(); wait(V, "pe", at=m)
                            inc(V.tensor_copy(dP2[0:64, hc0:hc0 + 64], PB[0][0:64, 0:64]), "dve")
                            m = snap(); wait(TE, "dve", at=m)
                            inc(TE.matmul(PS[2][0:64, 0:64], dP2[0:64, hc0:hc0 + 64],
                                          dP2T[0:64, hc0:hc0 + 64], start=True, stop=True), "pe")
                            m = snap(); wait(SC, "pe", at=m)
                            inc(SC.copy(dT[64:128, hc0:hc0 + 64], PS[2][0:64, 0:64]), "act")
                    if getattr(cfg, 'delta_upto', 99) >= 5:
                        m = snap(); wait(TE, "act", "dve", at=m)
                        for h in range(NH):
                            inc(TE.transpose(PB[0][:, 128 * h:128 * (h + 1)],
                                             dT[:, 128 * h:128 * (h + 1)], identt[:]), "pe")
                        m = snap(); wait(V, "pe", at=m)
                        inc(V.tensor_copy(dTT[:, 0:512], PB[0][:, 0:512]), "dve")
                        if getattr(cfg, "dbg_b", -1) == i:
                            inc(G.dma_start(out=dr["dbg"][:, 0:128], in_=dT[:, 0:128]), "sg", 16)
                            inc(G.dma_start(out=dr["dbg"][:, 128:256], in_=dL[:, 0:128]), "sg", 16)
                            inc(G.dma_start(out=dr["dbg"][:, 256:512], in_=dKbf[:, 0:256]), "sg", 16)
                            inc(G.dma_start(out=dr["dbg"][:, 1152:1408], in_=kfb[:, 0:256]), "sg", 16)
                            inc(G.dma_start(out=dr["dbg"][:, 1408:1664], in_=qfb[:, 0:256]), "sg", 16)
                            m2 = snap(); wait(SC, "sg", at=m2); wait(V, "sg", at=m2)


                    if getattr(cfg, 'delta_upto', 99) >= 6:
                        m = snap(); wait(TE, "dve", at=m)
                        for h in range(NH):
                            inc(TE.matmul(PS[h][:, 0:256], dTT[:, 128 * h:128 * (h + 1)],
                                          vbt[:, 256 * h:256 * (h + 1)], start=True, stop=True), "pe")
                            for d2 in range(2):
                                inc(TE.matmul(PS[4 + h // 2][:, 512 * 0 + 256 * (h % 2) + 0:0]
                                              if False else
                                              PS[4 + h // 2][:, 256 * (h % 2):256 * (h % 2) + 128]
                                              if False else
                                              PS[4 + h // 2][:, 256 * (h % 2) + 128 * d2:
                                                             256 * (h % 2) + 128 * (d2 + 1)],
                                              kbt[:, 256 * h + 128 * d2:256 * h + 128 * (d2 + 1)],
                                              dTT[:, 128 * h:128 * (h + 1)], start=True, stop=True), "pe")
                        m = snap(); wait(SC, "pe", at=m)
                        for h in range(NH):
                            SC.copy(dU[:, 256 * h:256 * (h + 1)], PS[h][:, 0:256])
                            inc(SC.copy(dWTT[:, 256 * h:256 * (h + 1)],
                                        PS[4 + h // 2][:, 256 * (h % 2):256 * (h % 2) + 256]), "act")


                    if getattr(cfg, 'delta_upto', 99) >= 7:
                        m = snap(); wait(TE, "act", at=m)
                        for h in range(NH):
                            for d2 in range(2):
                                inc(TE.matmul(PS[h][:, 0:128],
                                              kfb[:, 256 * h + 128 * d2:256 * h + 128 * (d2 + 1)],
                                              qfb[:, 256 * h + 128 * d2:256 * h + 128 * (d2 + 1)],
                                              start=(d2 == 0), stop=(d2 == 1)), "pe")
                        m = snap(); wait(SC, "pe", at=m)
                        for h in range(NH):
                            inc(SC.copy(dAttnT[:, 128 * h:128 * (h + 1)], PS[h][:, 0:128]), "act")
                        m = snap(); wait(G, "act", at=m)
                        inc(G.affine_select(dAttnT[:, 0:512], dAttnT[:, 0:512], [[0, 4], [1, 128]],
                                            ALU.is_ge, 0.0, base=0, channel_multiplier=-1), "pool")


                    if getattr(cfg, 'delta_upto', 99) >= 8:
                        m = snap(); wait(TE, "pool", "dve", at=m)
                        for h in range(NH):
                            for d2 in range(2):
                                inc(TE.matmul(PS[h][:, 0:256],
                                              dWTT[:, 256 * h + 128 * d2:256 * h + 128 * (d2 + 1)],
                                              Sbf_[h][:, 256 * d2:256 * (d2 + 1)],
                                              start=(d2 == 0), stop=(d2 == 1)), "pe")
                        m = snap(); wait(V, "pe", at=m)
                        for h in range(NH):
                            inc(V.tensor_sub(dUadj[:, 256 * h:256 * (h + 1)],
                                             dU[:, 256 * h:256 * (h + 1)], PS[h][:, 0:256]), "dve")


                    if getattr(cfg, 'delta_upto', 99) >= 9:
                        m = snap(); wait(TE, "dve", at=m)
                        for h in range(NH):
                            for d2 in range(2):
                                inc(TE.matmul(PS[h][:, 0:256],
                                              qfb[:, 256 * h + 128 * d2:256 * h + 128 * (d2 + 1)],
                                              Sbf_[h][:, 256 * d2:256 * (d2 + 1)],
                                              start=(d2 == 0), stop=False), "pe")
                            inc(TE.matmul(PS[h][:, 0:256], dAttnT[:, 128 * h:128 * (h + 1)],
                                          dUadj[:, 256 * h:256 * (h + 1)],
                                          start=False, stop=True), "pe")
                        m = snap(); wait(V, "pe", at=m)
                        for h in range(NH):
                            inc(V.tensor_copy(dotb[:, 256 * h:256 * (h + 1)], PS[h][:, 0:256]), "dve")


                    if getattr(cfg, 'delta_upto', 99) >= 10:
                        for hp in range(2):
                            m = snap(); wait(TE, "dve", at=m)
                            for hh in range(2):
                                h = 2 * hp + hh
                                for d2 in range(2):
                                    inc(TE.matmul(PS[4 + hh][:, 256 * d2:256 * (d2 + 1)],
                                                  ktb[:, 256 * h + 128 * d2:256 * h + 128 * (d2 + 1)],
                                                  dUadj[:, 256 * h:256 * (h + 1)],
                                                  start=True, stop=True), "pe")
                            m = snap(); wait(V, "pe", at=m)
                            for hh in range(2):
                                h = 2 * hp + hh
                                for d2 in range(2):
                                    inc(V.tensor_tensor(Sbf_[h][:, 256 * d2:256 * (d2 + 1)],
                                                        Sbf_[h][:, 256 * d2:256 * (d2 + 1)],
                                                        PS[4 + hh][:, 256 * d2:256 * (d2 + 1)],
                                                        ALU.add), "dve")

                    if getattr(cfg, 'delta_upto', 99) < 10:
                        m = snap(); wait(V, "pe", at=m)
                        inc(V.memset(dotb[:, 0:1024], 0.0), "dve")
                if getattr(cfg, "stats_off", False):
                    m = snap(); wait(V, "act", at=m)
                    inc(V.memset(b2fs2[:, 0:64], 0.5), "dve")
                else:
                    tensors = [(lstb, 0), (lltb, 4), (dotb, 8), (vtb, 12)]
                    svw = b2fs2[:, 0:64].rearrange("p (h s) -> p s h", s=16)
                    for (tt, so) in tensors:
                        t3 = tt[:, 0:1024].rearrange("p (h d) -> p h d", h=NH)
                        m = snap(); wait(V, "act", at=m)
                        inc(V.reduce_sum(stat_sc[:, 0:4], t3, axis=AX.X), "dve")
                        inc(V.reduce_sum(stat_sc[:, 4:8], t3, axis=AX.X,
                                         apply_absolute_value=True), "dve")
                        m = snap(); wait(SC, "dve", at=m)
                        inc(SC.activation(fsc[:, 0:1024], tt[:, 0:1024], AF.Square), "act")
                        m = snap(); wait(V, "act", at=m)
                        inc(V.reduce_sum(stat_sc[:, 8:12],
                                         fsc[:, 0:1024].rearrange("p (h d) -> p h d", h=NH),
                                         axis=AX.X), "dve")
                        m = snap(); wait(V, "dve", at=m)
                        V.tensor_scalar(svw[:, so + 0, :], stat_sc[:, 0:4], 1.0 / 256.0,
                                        0.0, ALU.mult, ALU.add)
                        V.tensor_scalar(svw[:, so + 2, :], stat_sc[:, 4:8], 1.0 / 256.0,
                                        0.0, ALU.mult, ALU.add)
                        V.tensor_scalar(stat_sc[:, 12:16], stat_sc[:, 8:12], 1.0 / 256.0,
                                        0.0, ALU.mult, ALU.add)
                        inc(V.tensor_scalar(stat_sc[:, 16:20], stat_sc[:, 0:4], 1.0 / 256.0,
                                            0.0, ALU.mult, ALU.add), "dve")
                        inc(V.tensor_tensor(stat_sc[:, 20:24], stat_sc[:, 16:20],
                                            stat_sc[:, 16:20], ALU.mult), "dve")
                        inc(V.tensor_sub(svw[:, so + 1, :], stat_sc[:, 12:16],
                                         stat_sc[:, 20:24]), "dve")
                        m = snap(); wait(SC, "dve", at=m)
                        inc(SC.activation(svw[:, so + 3, :], stat_sc[:, 8:12], AF.Sqrt), "act")
                m = snap(); wait(TE, "dve", "act", at=m)
                for h in range(NH):
                    inc(TE.transpose(PS[4][0:16, 0:128], b2fs2[:, 16 * h:16 * (h + 1)],
                                     identf[:]), "pe")
                    m = snap(); wait(SC, "pe", at=m)
                    inc(SC.copy(statb[0:16, 128 * h:128 * (h + 1)], PS[4][0:16, 0:128]), "act")
                    m = snap(); wait(TE, "act", at=m)
                for h in range(NH):
                    for hc in range(8):
                        ps_g = PS[hc % 2]
                        inc(TE.matmul(ps_g[:, 0:128],
                                      gW1b[:, 128 * hc:128 * (hc + 1)],
                                      statb[0:16, 128 * h:128 * (h + 1)],
                                      start=True, stop=True), "pe")
                        m = snap(); wait(V, "pe", at=m)
                        inc(V.scalar_tensor_tensor(b2fs[:, 0:128], ps_g[:, 0:128],
                                                   gb1c[:, hc:hc + 1],
                                                   gtb[:, 128 * hc:128 * (hc + 1)],
                                                   ALU.add, ALU.add), "dve")
                        m = snap(); wait(SC, "dve", at=m)
                        inc(SC.activation(gelb[:, 128 * hc:128 * (hc + 1)], b2fs[:, 0:128],
                                          AF.Gelu), "act")
                        m = snap(); wait(TE, "act", at=m)
                    for hc in range(8):
                        inc(TE.matmul(PS[2][:, 0:NH], gelb[:, 128 * hc:128 * (hc + 1)],
                                      gW2c[:, NH * hc:NH * (hc + 1)],
                                      start=(hc == 0), stop=(hc == 7)), "pe")
                    m = snap(); wait(V, "pe", at=m)
                    inc(V.tensor_tensor(stat_sc[:, 0:4], PS[2][:, 0:NH],
                                        bcmisc[:, 12:16], ALU.add), "dve")
                    m = snap(); wait(V, "dve", at=m)
                    inc(V.tensor_tensor(stat_sc[:, 4:8], stat_sc[:, 0:4],
                                        bc16[:, 4 * h:4 * (h + 1)], ALU.mult), "dve")
                    m = snap(); wait(V, "dve", at=m)
                    inc(V.reduce_max(stat_sc[:, 8:9], stat_sc[:, 4:8], axis=AX.X), "dve")
                    m = snap(); wait(V, "dve", at=m)
                    inc(V.tensor_scalar(stat_sc[:, 9:10], stat_sc[:, 8:9], -1.0, 0.0,
                                        ALU.mult, ALU.add), "dve")
                    m = snap(); wait(SC, "dve", at=m)
                    inc(SC.activation(stat_sc[:, 12:16], stat_sc[:, 4:8], AF.Exp,
                                      bias=stat_sc[:, 9:10], accum_out=stat_sc[:, 10:11]), "act")
                    m = snap(); wait(V, "act", at=m)
                    inc(V.reciprocal(stat_sc[:, 11:12], stat_sc[:, 10:11]), "dve")
                    m = snap(); wait(V, "dve", at=m)
                    inc(V.tensor_scalar(stat_sc[:, 12:16], stat_sc[:, 12:16],
                                        stat_sc[:, 11:12], bcmisc[:, h:h + 1],
                                        ALU.mult, ALU.mult), "dve")
                    m = snap(); wait(V, "dve", at=m)
                    inc(V.tensor_tensor(probs_f[:, 4 * h:4 * (h + 1)], stat_sc[:, 12:16],
                                        bcfl[:, 4 * h:4 * (h + 1)], ALU.add), "dve")
                m = snap(); wait(V, "dve", at=m)
                for h in range(NH):
                    inc(V.tensor_scalar(stat_sc[:, 40 + h:41 + h], probs_f[:, 4 * h:4 * h + 1],
                                        bcmisc[:, 4 + h:5 + h], 0.0, ALU.add, ALU.add), "dve")
                for h in range(NH):
                    sl = slice(256 * h, 256 * (h + 1))
                    m = snap(); wait(V, "dve", at=m)
                    inc(V.tensor_scalar(b2fs[:, 0:256], lstb[:, sl], stat_sc[:, 40 + h:41 + h],
                                        0.0, ALU.mult, ALU.add), "dve")
                    inc(V.scalar_tensor_tensor(b2fs[:, 0:256], lltb[:, sl],
                                               probs_f[:, 4 * h + 1:4 * h + 2],
                                               b2fs[:, 0:256], ALU.mult, ALU.add), "dve")
                    inc(V.scalar_tensor_tensor(b2fs[:, 0:256], dotb[:, sl],
                                               probs_f[:, 4 * h + 2:4 * h + 3],
                                               b2fs[:, 0:256], ALU.mult, ALU.add), "dve")
                    inc(V.scalar_tensor_tensor(b2fs[:, 0:256], vtb[:, sl],
                                               probs_f[:, 4 * h + 3:4 * h + 4],
                                               b2fs[:, 0:256], ALU.mult, ALU.add), "dve")
                    inc(V.scalar_tensor_tensor(b2fs[:, 0:256], idtb[:, sl],
                                               bcmisc[:, 8 + h:9 + h],
                                               b2fs[:, 0:256], ALU.mult, ALU.add), "dve")
                    m = snap(); wait(SC, "dve", at=m)
                    inc(SC.activation(b2fs2[:, 0:256], b2fs[:, 0:256], AF.Square,
                                      accum_out=stat_sc[:, 48:49]), "act")
                    m = snap(); wait(V, "act", at=m)
                    inc(V.tensor_scalar(stat_sc[:, 49:50], stat_sc[:, 48:49], 1.0 / 256.0,
                                        0.0, ALU.mult, ALU.add), "dve")
                    m = snap(); wait(SC, "dve", at=m)
                    inc(SC.activation(stat_sc[:, 50:51], stat_sc[:, 49:50], AF.Sqrt,
                                      bias=epsc[:, 0:1]), "act")
                    m = snap(); wait(V, "act", at=m)
                    inc(V.reciprocal(stat_sc[:, 51:52], stat_sc[:, 50:51]), "dve")
                    m = snap(); wait(V, "dve", at=m)
                    inc(V.tensor_scalar(b2fs[:, 0:256], b2fs[:, 0:256], stat_sc[:, 51:52],
                                        0.0, ALU.mult, ALU.add), "dve")
                    m = snap(); wait(V, "dve", at=m)
                    inc(V.tensor_tensor(otb[:, sl], b2fs[:, 0:256], onwb[:, sl], ALU.mult), "dve")
                if getattr(cfg, "dbg_b", -1) == i:
                    for j_, tt_ in enumerate([qtb, ktb, vtb, kbt, vbt, lstb, lltb,
                                              dotb, idtb, otb, gelb, gtb, qfb, kfb]):
                        inc(G.dma_start(out=dr["dbg2"][:, 1024 * j_:1024 * (j_ + 1)],
                                        in_=tt_[:, 0:1024]), "sg", 16)
                m = snap(); wait(TE, "dve", at=m)
                for d8 in range(8):
                    inc(TE.transpose(PB[0][:, 128 * d8:128 * (d8 + 1)],
                                     otb[:, 128 * d8:128 * (d8 + 1)], identt[:]), "pe")
                m = snap(); wait(V, "pe", at=m)
                for d8 in range(8):
                    inc(V.tensor_copy(ofb[:, 128 * d8:128 * (d8 + 1)],
                                      PB[0][:, 128 * d8:128 * (d8 + 1)]), "dve")
                if sq_ == 0 and sg_i == 0:
                    wait(TE, "act", at=mWo)
                m = snap(); wait(TE, "dve", at=m)
                for nh2 in range(2):
                    ps_o = PS[nh2]
                    for kc in range(8):
                        inc(TE.matmul(ps_o[:, 0:512],
                                      ofb[:, 128 * kc:128 * (kc + 1)],
                                      oW[:, 1024 * kc + 512 * nh2:1024 * kc + 512 * (nh2 + 1)],
                                      start=(kc == 0), stop=(kc == 7)), "pe")
                m = snap(); wait(V, "pe", at=m)
                for nh2 in range(2):
                    inc(V.tensor_tensor(stream[:, D * i + 512 * nh2:D * i + 512 * (nh2 + 1)],
                                        stream[:, D * i + 512 * nh2:D * i + 512 * (nh2 + 1)],
                                        PS[nh2][:, 0:512], ALU.add), "dve")
                m = snap(); wait(TE, "dve", at=m)

    stages = getattr(cfg, "stages", ("a", "f1", "b", "f2"))
    for st_name in stages:
        if st_name == "a":
            emit_attn_a2()
        elif st_name == "f1":
            emit_ffn(final=False)
        elif st_name == "b":
            emit_attn_b()
        elif st_name == "f2":
            emit_ffn(final=True)
    if stages[-1] != "f2":
        for i in range(cfg.NT):
            m = snap(); wait(SC, "out", "dve", at=m)
            inc(SC.copy(fsc[:, 0:1024], stream[:, D * i:D * (i + 1)]), "act")
            m = snap(); wait(G, "act", at=m)
            inc(G.dma_start(out=out[128 * i:128 * (i + 1), :], in_=fsc[:, 0:1024]), "out", 16)

    m = snap()
    wait(G, "out", at=m)
    wait(G, "sq", "sg", "dve", "act", "pe", "pool", at=m)

    for en in ["sync", "gpsimd", "vector", "scalar", "tensor"]:
        fns = EBUF[en]
        getattr(blk, en)(lambda e, fns=fns: [f(e) for f in fns])

    blk_t.__exit__(None, None, None)
    for t in reversed(ectx[:-1]):
        try:
            t.__exit__(None, None, None)
        except Exception:
            pass
    return nc


def kernel(**inputs):
    cfg = CFG()
    consts = host_consts(cfg)
    nc = build(cfg)
    x = np.ascontiguousarray(inputs["L0_attn_a_x"], dtype=np.float32)
    namemap = {
        "wq": "L0_attn_a_wq", "wk": "L0_attn_a_wk", "wv": "L0_attn_a_wv",
        "wgq": "L0_attn_a_wgq", "wgk": "L0_attn_a_wgk", "wgv": "L0_attn_a_wgv",
        "r_out_w": "L0_attn_a_r_out_w", "glw": "L0_attn_a_gate_local_w",
        "grw": "L0_attn_a_gate_remote_w", "w_out": "L0_attn_a_w_out",
        "bWq": "L0_attn_b_Wq", "bWk": "L0_attn_b_Wk", "bWv": "L0_attn_b_Wv",
        "cwq": "L0_attn_b_cwq", "cwk": "L0_attn_b_cwk", "cwv": "L0_attn_b_cwv",
        "bWb": "L0_attn_b_Wb", "bWid": "L0_attn_b_Wid", "alpha_id": "L0_attn_b_alpha_id",
        "firs": "L0_attn_b_firs", "firl": "L0_attn_b_firl",
        "gW1": "L0_attn_b_gW1", "gb1": "L0_attn_b_gb1", "gW2": "L0_attn_b_gW2",
        "gb2": "L0_attn_b_gb2", "log_temp": "L0_attn_b_log_temp",
        "floor_param": "L0_attn_b_floor_param", "crl": "L0_attn_b_conv_res_logit",
        "onw": "L0_attn_b_o_norm_w", "bWo": "L0_attn_b_Wo",
        "w1": "L0_ffn_w1", "b1": "L0_ffn_b1", "w2": "L0_ffn_w2", "b2": "L0_ffn_b2",
    }
    shared = {k: np.ascontiguousarray(inputs[v], dtype=np.float32) for k, v in namemap.items()}
    shared.update(consts)
    in_maps = []
    for c in range(8):
        mm = dict(shared)
        mm["x"] = np.ascontiguousarray(x[2 * c:2 * c + 2].reshape(cfg.T, cfg.D))
        in_maps.append(mm)
    res = run_bass_kernel_spmd(nc, in_maps, core_ids=list(range(8)))
    outs = [res.results[c]["out"].reshape(2, cfg.S, cfg.D) for c in range(8)]
    return np.concatenate(outs, axis=0)
